# version 09
# baseline (speedup 1.0000x reference)
import numpy as np
import ml_dtypes
from contextlib import ExitStack

import concourse.bass as bass
import concourse.mybir as mybir
from concourse.bass_utils import run_bass_kernel_spmd

F32 = mybir.dt.float32
BF16 = mybir.dt.bfloat16
I32 = mybir.dt.int32
AF = mybir.ActivationFunctionType
ALU = mybir.AluOpType

LN2C = float(np.log(2.0))

N_CORES = 8
NSEQ = 4
S = 2048
D = 1024
DC = D // 128
T = 512
NT = S // T
HJ = 4096
HJC = HJ // 128
F8 = np.float32


class Sched:

    ENGINES = ("tensor", "vector", "scalar", "gpsimd", "sync")

    def __init__(self):
        self.ops = {e: [] for e in self.ENGINES}
        self.cnt = {}

    def add(self, eng, fn, waits=(), inc=None, n=1):
        w = [t for t in waits if t is not None]
        tok = None
        if inc is not None:
            self.cnt[inc] = self.cnt.get(inc, 0) + n
            tok = (inc, self.cnt[inc])
        self.ops[eng].append({"fn": fn, "waits": w, "inc": inc, "n": n})
        return tok

    def check(self):
        idx = {e: 0 for e in self.ENGINES}
        cnt = {s: 0 for s in self.cnt}
        total = sum(len(v) for v in self.ops.values())
        done = 0
        progress = True
        while progress:
            progress = False
            for e in self.ENGINES:
                while idx[e] < len(self.ops[e]):
                    op = self.ops[e][idx[e]]
                    if all(cnt.get(s, 0) >= v for (s, v) in op["waits"]):
                        if op["inc"]:
                            cnt[op["inc"]] = cnt.get(op["inc"], 0) + op["n"]
                        idx[e] += 1
                        done += 1
                        progress = True
                    else:
                        break
        if done != total:
            stuck = {e: (idx[e], len(self.ops[e]),
                         self.ops[e][idx[e]]["waits"] if idx[e] < len(self.ops[e]) else None)
                     for e in self.ENGINES}
            raise RuntimeError(f"schedule deadlock: done={done}/{total} stuck={stuck} cnt_sample={dict(list(cnt.items())[:10])}")

    def emit(self, block, sems):
        for ename in self.ENGINES:
            ops = self.ops[ename]

            def make(ops=ops):
                def body(eng):
                    last_wait = {}
                    for op in ops:
                        for (s, v) in op["waits"]:
                            if last_wait.get(s, -1) < v:
                                eng.wait_ge(sems[s], v)
                                last_wait[s] = v
                        ins = op["fn"](eng)
                        if op["inc"] is not None:
                            assert ins is not None, "inc op must return instruction"
                            ins.then_inc(sems[op["inc"]], op["n"])
                return body

            getattr(block, ename)(make())


def build_nc(debug_taps=False):
    nc = bass.Bass(target_bir_lowering=False)

    x_h = nc.dram_tensor("x", [NSEQ, D, S], F32, kind="ExternalInput")
    wqkv_h = nc.dram_tensor("wqkvT", [D, 3 * D], F32, kind="ExternalInput")
    wproj_h = nc.dram_tensor("wprojT", [D, D], F32, kind="ExternalInput")
    w1_h = nc.dram_tensor("w1T", [D, HJ], F32, kind="ExternalInput")
    w2_h = nc.dram_tensor("w2T", [HJ, D], F32, kind="ExternalInput")
    bias_h = nc.dram_tensor("biases", [128, 48], F32, kind="ExternalInput")
    sel_h = nc.dram_tensor("sel", [16, 8 * 128], F32, kind="ExternalInput")
    cb_h = nc.dram_tensor("cb", [128, 8], F32, kind="ExternalInput")
    out_h = nc.dram_tensor("out", [NSEQ, D, S], F32, kind="ExternalOutput")
    if debug_taps:
        dbg_bq = nc.dram_tensor("dbg_bq", [128, 8], F32, kind="ExternalOutput")
        dbg_w2 = nc.dram_tensor("dbg_w2", [8, 128, 4096], BF16, kind="ExternalOutput")
        dbg_w1 = nc.dram_tensor("dbg_w1", [8, 128, 4096], BF16, kind="ExternalOutput")
        dbg_xb = nc.dram_tensor("dbg_xb", [NSEQ, 128, 16384], BF16, kind="ExternalOutput")
        dbg_qf = nc.dram_tensor("dbg_qf", [NSEQ, 128, 16384], BF16, kind="ExternalOutput")
        dbg_kv2 = nc.dram_tensor("dbg_kv2", [NSEQ, 128, 1024], BF16, kind="ExternalOutput")
        dbg_bd = nc.dram_tensor("dbg_bd", [NSEQ, 128, 128], BF16, kind="ExternalOutput")
        dbg_at = nc.dram_tensor("dbg_at", [NSEQ, 128, 16384], BF16, kind="ExternalOutput")
        dbg_z = nc.dram_tensor("dbg_z", [NSEQ, NT, 16, T], BF16, kind="ExternalOutput")

    sp_q = nc.dram_tensor("sp_q", [8, 128, 1024], BF16, kind="Internal")
    sp_kv = nc.dram_tensor("sp_kv", [4, 128, 4096], BF16, kind="Internal")
    sp_pr = nc.dram_tensor("sp_pr", [8, 128, 1024], BF16, kind="Internal")
    sp_w1 = nc.dram_tensor("sp_w1", [8, 128, 4096], BF16, kind="Internal")
    sp_w2 = nc.dram_tensor("sp_w2", [8, 128, 4096], BF16, kind="Internal")

    es = ExitStack()

    def sb(name, shape, dt=BF16):
        return es.enter_context(nc.sbuf_tensor(name, shape, dt))

    wstream = sb("wstream", [128, 16384], BF16)
    xa_hq = sb("xa_hq", [128, 16384], BF16)
    qf = sb("qf", [128, 16384], BF16)
    xb = sb("xb", [128, 16384], BF16)
    r_t = [sb(f"r{i}", [128, 1024], F32) for i in range(3)]
    l_t = [sb(f"l{i}", [128, 1024], F32) for i in range(2)]
    e_t = [sb(f"e{i}", [128, 1024], I32) for i in range(2)]
    inv_t = [sb(f"inv{i}", [128, 1024], BF16) for i in range(3)]
    g_t = [sb(f"g{i}", [128, 1024], BF16) for i in range(3)]
    xn0 = sb("xn0", [128, 8 * T], BF16)
    xn = [xn0, None]
    kf_nb = [inv_t[0], g_t[0], inv_t[1], g_t[1]]
    kv2 = inv_t[2]
    z_f = r_t[2][0:16, 0:T]
    kvacc_sb = r_t[2][0:128, T:2 * T]
    z_b = g_t[2][0:16, 0:T]
    zb_t = [g_t[2][0:128, T:2 * T]]
    v_nb = [sb(f"v{i}", [128, 1024], BF16) for i in range(4)]
    stg = [sb(f"stg{i}", [128, T], F32) for i in range(2)]
    sqs = [sb(f"sqs{i}", [128, T], F32) for i in range(2)]
    lnsq0 = sb("lnsq0", [128, T], BF16)
    lnsq = [lnsq0, lnsq0]
    bd = sb("bd", [128, 8 * 16], BF16)
    sel_sb = sb("sel_sb", [16, 8 * 128], BF16)
    bias_sb = sb("bias_sb", [128, 48], F32)
    bq_sb = sb("bq_sb", [128, 8], F32)
    cb = sb("cb_sb", [128, 8], F32)
    ones1 = sb("ones1", [1, 128], BF16)
    negones = sb("negones", [1, 128], BF16)
    onescol = sb("onescol", [128, 1], BF16)
    rowF = sb("rowF", [1, 3 * T], F32)
    rowBf = sb("rowBf", [1, 2 * T], BF16)
    m_row = rowF[0:1, 0:T]
    s2_row = rowF[0:1, T:2 * T]
    mm_row = rowF[0:1, 2 * T:3 * T]
    rstd_row = rowF[0:1, T:2 * T]
    nb_rowf = rowF[0:1, 0:T]
    a_rowb = rowBf[0:1, 0:T]
    nb_rowb = rowBf[0:1, T:2 * T]
    ab_b = sb("ab_b", [128, T], BF16)
    bb_b = sb("bb_b", [128, T], BF16)
    ksacc_sb = sb("ksacc", [128, 8], F32)

    pes = ExitStack()

    def ps(name, shape):
        return pes.enter_context(nc.psum_tensor(name, shape, F32))

    pA = ps("pA", [128, 512])
    pB = ps("pB", [128, 512])
    pC = ps("pC", [128, 512])
    pD = ps("pD", [128, 512])
    pK1 = ps("pK1", [128, 512])
    pK2 = ps("pK2", [128, 512])
    pL1 = ps("pL1", [128, 512])
    pL2 = ps("pL2", [128, 512])

    sem_names = ["dl", "ds", "mm", "ea", "ev", "ou", "eg"]
    sems = {n: es.enter_context(nc.semaphore(n)) for n in sem_names}

    sch = Sched()

    qbuf = [wstream[:, i * 1024:(i + 1) * 1024] for i in range(2)]
    kvbuf = [wstream[:, 2048 + i * 4096: 2048 + (i + 1) * 4096] for i in range(2)]
    prbuf = [wstream[:, i * 1024:(i + 1) * 1024] for i in range(2)]
    w1buf = [wstream[:, i * 4096:(i + 1) * 4096] for i in range(2)]
    w2buf = [wstream[:, 8192 + i * 4096: 8192 + (i + 1) * 4096] for i in range(2)]
    xn[1] = wstream[:, 12288:16384]

    TINY4 = float(2.0 ** -80)
    TINY10 = float(2.0 ** -120)
    CLAMP4 = float(np.float32(-5.45 * LN2C))
    CLAMP10 = float(np.float32(-13.45 * 2 * LN2C))
    C2 = float(np.float32(1.0 / LN2C))
    C22 = float(np.float32(1.0 / (2 * LN2C)))
    CB_M3, CB_P3, CB_M10, CB_P10, CB_T4, CB_T10, CB_EPS = range(7)

    def cbc(i):
        return cb[:, i:i + 1]

    def quantize(src_ap, out_ap, man, signed, bufi, n, waits, extra_first_waits=(), helper="vector"):
        lt, et, it_, gt = l_t[bufi][:, :n], e_t[bufi][:, :n], inv_t[bufi][:, :n], g_t[bufi][:, :n]
        ef = l_t[bufi][:, :n]
        w0 = list(waits) + list(extra_first_waits)
        min_exp = -6 if man == 3 else -14
        cbm = CB_M3 if man == 3 else CB_M10
        cbp = CB_P3 if man == 3 else CB_P10
        if signed:
            sqt = r_t[bufi][:, :n]
            t_sq = sch.add("scalar", lambda e, o=sqt, i=src_ap: e.activation(o, i, AF.Square),
                           waits=w0, inc="ea")
            t_l = sch.add("scalar", lambda e, o=lt, i=sqt, b=cbc(CB_T10): e.activation(o, i, AF.Ln, bias=b),
                          waits=[t_sq], inc="ea")
            clamp = float(np.float32((min_exp + 0.55) * 2 * LN2C))
            cc = C22
        else:
            t_l = sch.add("scalar", lambda e, o=lt, i=src_ap, b=cbc(CB_T4): e.activation(o, i, AF.Ln, bias=b),
                          waits=w0, inc="ea")
            t_sq = None
            clamp = float(np.float32((min_exp + 0.55) * LN2C))
            cc = C2
        hsem = "ev" if helper == "vector" else "eg"
        t_e = sch.add(helper, lambda e, o=et, i=lt, c=cc: e.tensor_scalar(o, i, c, 0.5, op0=ALU.mult, op1=ALU.subtract),
                      waits=[t_l], inc=hsem)
        t_ef = sch.add(helper, lambda e, o=ef, i=et, c=float(min_exp): e.tensor_scalar(o, i, c, None, op0=ALU.max),
                       waits=[t_e], inc=hsem)
        t_inv = sch.add("scalar", lambda e, o=it_, i=ef, b=cbc(cbp): e.activation(o, i, AF.Exp, scale=-LN2C, bias=b),
                        waits=[t_ef], inc="ea")
        t_g = sch.add("scalar", lambda e, o=gt, i=ef, b=cbc(cbm): e.activation(o, i, AF.Exp, scale=LN2C, bias=b),
                      waits=[t_ef], inc="ea")
        t_y = sch.add("vector", lambda e, o=et, a=src_ap, b=it_: e.tensor_tensor(o, a, b, ALU.mult),
                      waits=[t_inv], inc="ev")
        t_q = sch.add("vector", lambda e, o=out_ap, a=et, b=gt: e.tensor_tensor(o, a, b, ALU.mult),
                      waits=[t_g, t_y], inc="ev")
        return t_q, t_y

    def ln_sums(x_tile_of, waits_extra=()):
        tok_mm1 = tok_mm2 = None
        for c in range(DC):
            xc = x_tile_of(c)
            sq = lnsq[c % 2][:, :]
            tw = [w for w in [lnsq_tok[c % 2]] if w] + (list(waits_extra) if c == 0 else [])
            tsq = sch.add("vector", lambda e, o=sq, i=xc: e.tensor_tensor(o, i, i, ALU.mult),
                          waits=tw, inc="ev")
            w1_ = ([ln_bank_free[0], t_o3] + list(waits_extra)) if c == 0 else []
            tok_mm1 = sch.add("tensor",
                              lambda e, r=xc, s=(c == 0), p=(c == 7): e.matmul(pL1[0:1, :], onescol[:, :], r, start=s, stop=p),
                              waits=[w for w in w1_ if w], inc="mm" if c == 7 else None)
            w2_ = [tsq] + ([ln_bank_free[1]] if c == 0 and ln_bank_free[1] else [])
            tok_mm2 = sch.add("tensor",
                              lambda e, r=sq, s=(c == 0), p=(c == 7): e.matmul(pL2[0:1, :], onescol[:, :], r, start=s, stop=p),
                              waits=w2_, inc="mm")
            lnsq_tok[0] = lnsq_tok[1] = tok_mm2
        return tok_mm1, tok_mm2

    def ln_tail(x_tile_of, xn_buf, sums_toks, xn_guard=()):
        tok_mm1, tok_mm2 = sums_toks
        t1 = sch.add("vector", lambda e: e.tensor_scalar(m_row[:, :], pL1[0:1, :], 1.0 / D, None, op0=ALU.mult),
                     waits=[tok_mm1], inc="ev")
        t2 = sch.add("vector", lambda e: e.tensor_scalar(s2_row[:, :], pL2[0:1, :], 1.0 / D, None, op0=ALU.mult),
                     waits=[tok_mm2], inc="ev")
        t3 = sch.add("vector", lambda e: e.tensor_tensor(mm_row[:, :], m_row[:, :], m_row[:, :], ALU.mult),
                     waits=[t1], inc="ev")
        t4 = sch.add("vector", lambda e: e.tensor_tensor(mm_row[:, :], s2_row[:, :], mm_row[:, :], ALU.subtract),
                     waits=[t2, t3], inc="ev")
        t5 = sch.add("scalar", lambda e: e.activation(mm_row[:, :], mm_row[:, :], AF.Ln, bias=cb[0:1, CB_EPS:CB_EPS + 1]),
                     waits=[t4], inc="ea")
        t6 = sch.add("scalar", lambda e: e.activation(rstd_row[:, :], mm_row[:, :], AF.Exp, scale=-0.5),
                     waits=[t5], inc="ea")
        t7 = sch.add("vector", lambda e: e.tensor_copy(a_rowb[:, :], rstd_row[:, :]), waits=[t6], inc="ev")
        t8 = sch.add("vector", lambda e: e.tensor_tensor(nb_rowf[:, :], m_row[:, :], rstd_row[:, :], ALU.mult),
                     waits=[t6], inc="ev")
        t9 = sch.add("vector", lambda e: e.tensor_copy(nb_rowb[:, :], nb_rowf[:, :]), waits=[t8], inc="ev")
        ta = sch.add("tensor", lambda e: e.matmul(pL1[:, :], ones1[0:1, 0:128], a_rowb[:, :], start=True, stop=True),
                     waits=[t7, t_o1], inc="mm")
        tb = sch.add("tensor", lambda e: e.matmul(pL2[:, :], negones[0:1, 0:128], nb_rowb[:, :], start=True, stop=True),
                     waits=[t9, t_o2], inc="mm")
        tc = sch.add("vector", lambda e: e.tensor_copy(ab_b[:, :], pL1[:, :]), waits=[ta], inc="ev")
        td = sch.add("vector", lambda e: e.tensor_copy(bb_b[:, :], pL2[:, :]), waits=[tb], inc="ev")
        ln_bank_free[0] = tc
        ln_bank_free[1] = td
        tok_xn = []
        for c in range(DC):
            xc = x_tile_of(c)
            dst = xn_buf[:, c * T:(c + 1) * T]
            tm = sch.add("vector", lambda e, o=dst, i=xc: e.tensor_tensor(o, i, ab_b[:, :], ALU.mult),
                         waits=[tc] + (list(xn_guard) if c == 0 else []), inc="ev")
            tn = sch.add("vector", lambda e, o=dst: e.tensor_tensor(o, o, bb_b[:, :], ALU.add),
                         waits=[td, tm], inc="ev")
            tok_xn.append(tn)
        return tok_xn

    bank_free = {k: None for k in ["pA", "pB", "pC", "pD", "pK1", "pK2"]}
    ln_bank_free = [None, None]
    xn_free = [None, None]
    _lt0 = [None]
    lnsq_tok = _LnsqTok = type("LT", (), {})()
    lnsq_tok = _lt0 * 1 if False else [None, None]
    sqs_tok = [None, None]
    temps_free = [None, None, None]

    tok_bias = sch.add("sync", lambda e: e.dma_start(out=bias_sb[:, :], in_=bias_h.ap()), inc="ds", n=16)
    tok_cb = sch.add("sync", lambda e: e.dma_start(out=cb[:, :], in_=cb_h.ap()), inc="ds", n=16)
    tok_sel = sch.add("gpsimd", lambda e: e.dma_start(out=sel_sb[:, :], in_=sel_h.ap()), inc="dl", n=16)
    t_o1 = sch.add("vector", lambda e: e.memset(ones1[:, :], 1.0), inc="ev")
    t_o2 = sch.add("vector", lambda e: e.memset(negones[:, :], -1.0), inc="ev")
    t_o3 = sch.add("vector", lambda e: e.memset(onescol[:, :], 1.0), inc="ev")

    tok_bq, _ = quantize(bias_sb[:, 40:48], bq_sb[:, :], man=10, signed=True, bufi=0, n=8,
                         waits=[tok_bias, tok_cb])

    spill_toks = []

    def spill(dst_view, src_view, stage_ap, waits):
        tg = sch.add("gpsimd", lambda e, o=stage_ap, i=src_view: e.dma_start(out=o, in_=i),
                     waits=waits, inc="dl", n=16)
        ts_ = sch.add("sync", lambda e, o=dst_view, i=stage_ap: e.dma_start(out=o, in_=i),
                      waits=[tg], inc="ds", n=16)
        return ts_

    stage_a = wstream[:, 0:4096]
    stage_b = wstream[:, 4096:8192]
    prev = [None, None]
    for jc in range(8):
        src = wqkv_h.ap()[:, jc * 128:(jc + 1) * 128].rearrange("(c p) q -> p c q", p=128)
        stg_ap = (stage_a if jc % 2 == 0 else stage_b)[:, 0:1024].rearrange("p (c q) -> p c q", c=8)
        tok = spill(sp_q.ap()[jc], src, stg_ap, [w for w in [prev[jc % 2]] if w])
        prev[jc % 2] = tok
        spill_toks.append(tok)
    for jc in range(8):
        src = wproj_h.ap()[:, jc * 128:(jc + 1) * 128].rearrange("(c p) q -> p c q", p=128)
        stg_ap = (stage_a if jc % 2 == 0 else stage_b)[:, 0:1024].rearrange("p (c q) -> p c q", c=8)
        tok = spill(sp_pr.ap()[jc], src, stg_ap, [prev[jc % 2]])
        prev[jc % 2] = tok
        spill_toks.append(tok)
    for s_ in range(4):
        src = wqkv_h.ap()[:, 1024 + s_ * 512: 1024 + (s_ + 1) * 512].rearrange("(c p) j -> p c j", p=128)
        stg_ap = (stage_a if s_ % 2 == 0 else stage_b).rearrange("p (c j) -> p c j", c=8)
        tok = spill(sp_kv.ap()[s_], src, stg_ap, [prev[s_ % 2]])
        prev[s_ % 2] = tok
        spill_toks.append(tok)
    for c in range(8):
        src = w1_h.ap()[c * 128:(c + 1) * 128, :]
        stg_flat = (stage_a if c % 2 == 0 else stage_b)
        tok_ld = sch.add("gpsimd", lambda e, o=stg_flat, i=src: e.dma_start(out=o, in_=i),
                         waits=[prev[c % 2]], inc="dl", n=16)
        tok = None
        for pc in range(8):
            dst = sp_w1.ap()[pc].rearrange("p (jj rest) -> p jj rest", jj=4)[:, :, c * 128:(c + 1) * 128]
            src2 = stg_flat[:, pc * 512:(pc + 1) * 512].rearrange("p (jj q) -> p jj q", jj=4)
            tok = sch.add("sync", lambda e, o=dst, i=src2: e.dma_start(out=o, in_=i),
                          waits=[tok_ld], inc="ds", n=16)
        prev[c % 2] = tok
        spill_toks.append(tok)
    w2stage = [xa_hq[:, 0:512], xa_hq[:, 512:1024]]
    prevq = [None, None]
    for jc in range(HJC):
        for h in range(2):
            bi = (2 * jc + h) % 2
            src = w2_h.ap()[jc * 128:(jc + 1) * 128, h * 512:(h + 1) * 512]
            tok_ld = sch.add("sync", lambda e, o=sqs[bi][:, :], i=src: e.dma_start(out=o, in_=i),
                             waits=[w for w in [prevq[bi], sqs_tok[bi],
                                                tok_bq if (jc == 0) else None] if w],
                             inc="ds", n=16)
            tq, ty = quantize(sqs[bi][:, :], w2stage[bi][:, :], man=3, signed=True, bufi=bi, n=512,
                              waits=[tok_ld])
            sqs_tok[bi] = ty
            temps_free[bi] = tq
            dst = sp_w2.ap()[4 * h:4 * h + 4, :, jc * 128:(jc + 1) * 128].rearrange("a p q -> p a q")
            src2 = w2stage[bi].rearrange("p (a q) -> p a q", a=4)
            tok = sch.add("sync", lambda e, o=dst, i=src2: e.dma_start(out=o, in_=i),
                          waits=[tq], inc="ds", n=16)
            prevq[bi] = tok
            spill_toks.append(tok)

    if debug_taps:
        sch.add("sync", lambda e: e.dma_start(out=dbg_bq.ap(), in_=bq_sb[:, :]),
                waits=[tok_bq], inc="ou", n=16)
        tok_out_dbg = [16]
        sch.add("sync", lambda e: e.dma_start(out=dbg_w2.ap(), in_=sp_w2.ap()),
                waits=[last_spill := spill_toks[-1]], inc="ou", n=16)
        sch.add("sync", lambda e: e.dma_start(out=dbg_w1.ap(), in_=sp_w1.ap()),
                waits=[spill_toks[-1]], inc="ou", n=16)
        tok_out_dbg[0] += 32
    else:
        tok_out_dbg = [0]

    tok_xa = [None]

    def load_x(s_, waits):
        src = x_h.ap()[s_].rearrange("(c p) t -> p c t", p=128)
        tgt = xa_hq if s_ % 2 == 0 else qf
        dst = tgt[:, :].rearrange("p (c t) -> p c t", c=8)
        return sch.add("gpsimd", lambda e, o=dst, i=src: e.dma_start(out=o, in_=i),
                       waits=waits, inc="dl", n=16)

    tok_xa[0] = load_x(0, [tok_sel] + [t for t in [prevq[0], prevq[1]] if t])

    qbuf_free = [None, None]
    kvbuf_free = [None, None]
    prbuf_free = [None, None]
    w1buf_free = [None, None]
    w2buf_free = [None, None]
    last_spill = spill_toks[-1]

    def spill_done_tok():
        return last_spill

    for lst in (qbuf_free, kvbuf_free, prbuf_free, w1buf_free, w2buf_free):
        lst[0] = lst[1] = spill_done_tok()


    tok_out_total = [0]
    hq_read_done = None
    tok_stgfree = [None, None]
    kfv_free = [None] * 4
    attn_over_qf_ok = {}
    x_next_tok = tok_xa[0]
    xb_read_done = None
    kv2_read_done = None
    bd_read_done = None
    zrow_free = None
    zb_free = [None, None]
    ab_free = None

    for s_ in range(NSEQ):
        x_ready = x_next_tok
        qf_free_at = hq_read_done
        XA = xa_hq if s_ % 2 == 0 else qf
        QF = qf if s_ % 2 == 0 else xa_hq

        def x_of_t(t):
            def x_of(c, t=t):
                return XA[:, c * S + t * T: c * S + t * T + T]
            return x_of

        extra0 = [x_ready] + [w for w in temps_free if w]
        sums_t = {0: ln_sums(x_of_t(0), waits_extra=extra0)}
        tails_t = {0: ln_tail(x_of_t(0), xn[0], sums_t[0],
                              xn_guard=[w for w in [xn_free[0]] if w])}
        for t in range(NT):
            tok_xn = tails_t[t]
            if t + 1 < NT:
                sums_t[t + 1] = ln_sums(x_of_t(t + 1))

            qrot = [pA, pB, pC, pD]
            qrot_names = ["pA", "pB", "pC", "pD"]
            for jc in range(8):
                bi = jc % 2
                wg = [qbuf_free[bi]]
                if t == 0 and jc == 0:
                    wg += w1buf_free + w2buf_free
                tok_w = sch.add("sync", lambda e, o=qbuf[bi], i=sp_q.ap()[jc]: e.dma_start(out=o, in_=i),
                                waits=[w for w in wg if w], inc="ds", n=16)
                mmw = [tok_w, bank_free[qrot_names[jc % 4]]] + tok_xn
                tok_mm = None
                for c in range(DC):
                    tok_mm = sch.add("tensor",
                                     lambda e, l_=qbuf[bi][:, c * 128:(c + 1) * 128],
                                     r=xn[t % 2][:, c * T:(c + 1) * T], p=qrot[jc % 4],
                                     s0=(c == 0), s1=(c == 7): e.matmul(p[:, :], l_, r, start=s0, stop=s1),
                                     waits=[w for w in mmw if w] if c == 0 else [],
                                     inc="mm" if c == 7 else None)
                qbuf_free[bi] = tok_mm
                sqt = sqs[bi][:, :]
                tevq = sch.add("scalar", lambda e, o=sqt, i=qrot[jc % 4][:, :]: e.activation(o, i, AF.Square),
                               waits=[tok_mm] + [w for w in [sqs_tok[bi]] if w]
                               + ([w for w in [qf_free_at] if w] if (t == 0 and jc == 0) else []),
                               inc="ea")
                bank_free[qrot_names[jc % 4]] = tevq
                dst = QF[:, jc * S + t * T: jc * S + t * T + T]
                tevq2 = sch.add("scalar", lambda e, o=dst, i=sqt: e.activation(o, i, AF.Exp, scale=-0.5),
                                waits=[tevq], inc="ea")
                sqs_tok[bi] = tevq2
                if jc == 7:
                    qf_tile_done = tevq2

            if t + 1 < NT:
                guard = [w for w in [xn_free[(t + 1) % 2]] if w]
                if (t + 1) % 2 == 1:
                    guard += [w for w in [w2buf_free[1]] if w]
                tails_t[t + 1] = ln_tail(x_of_t(t + 1), xn[(t + 1) % 2], sums_t[t + 1], xn_guard=guard)

            pnb = [pA, pB, pC, pD]
            pnb_names = ["pA", "pB", "pC", "pD"]
            kv_evac_toks = {}
            for sl in range(4):
                bi = sl % 2
                tok_w = sch.add("sync", lambda e, o=kvbuf[bi], i=sp_kv.ap()[sl]: e.dma_start(out=o, in_=i),
                                waits=[w for w in [kvbuf_free[bi]] if w], inc="ds", n=16)
                last_mm_of_nb = {}
                for nb in range(4):
                    tok_mm = None
                    w0 = [tok_w, bank_free[pnb_names[nb]]]
                    for c in range(DC):
                        tok_mm = sch.add("tensor",
                                         lambda e, l_=xn[t % 2][:, c * T + nb * 128: c * T + nb * 128 + 128],
                                         r=kvbuf[bi][:, c * 512:(c + 1) * 512], p=pnb[nb],
                                         s0=(c == 0), s1=(c == 7): e.matmul(p[:, :], l_, r, start=s0, stop=s1),
                                         waits=[w for w in w0 if w] if c == 0 else [],
                                         inc="mm" if c == 7 else None)
                    last_mm_of_nb[nb] = tok_mm
                kvbuf_free[bi] = last_mm_of_nb[3]
                for nb in range(4):
                    tok_mm = last_mm_of_nb[nb]
                    if sl < 2:
                        sqt = sqs[nb % 2][:, :]
                        te1 = sch.add("scalar", lambda e, o=sqt, i=pnb[nb][:, :]: e.activation(o, i, AF.Square),
                                      waits=[tok_mm] + [w for w in [sqs_tok[nb % 2]] if w],
                                      inc="ea")
                        dst = kf_nb[nb][:, sl * 512:(sl + 1) * 512]
                        kf_guard = ([kfv_free[nb]] if kfv_free[nb] else []) + \
                                   ([temps_free[nb // 2]] if temps_free[nb // 2] else [])
                        te2 = sch.add("scalar", lambda e, o=dst, i=sqt: e.activation(o, i, AF.Exp, scale=-0.5),
                                      waits=[te1] + (kf_guard if (sl == 0 and t == 0) else
                                                     ([kfv_free[nb]] if (sl == 0 and kfv_free[nb]) else [])),
                                      inc="ea")
                        sqs_tok[nb % 2] = te2
                        bank_free[pnb_names[nb]] = te1
                        kv_evac_toks[("k", sl, nb)] = te2
                    else:
                        dst = v_nb[nb][:, (sl - 2) * 512:(sl - 1) * 512]
                        te = sch.add("vector", lambda e, o=dst, i=pnb[nb][:, :]: e.tensor_copy(o, i),
                                     waits=[tok_mm] + ([kfv_free[nb]] if (sl == 2 and kfv_free[nb]) else []),
                                     inc="ev")
                        bank_free[pnb_names[nb]] = te
                        kv_evac_toks[("v", sl, nb)] = te

            wv0 = [kv_evac_toks[(kk, slx, nbx)] for nbx in range(4)
                   for (kk, slx) in (("k", 0), ("k", 1), ("v", 2), ("v", 3))]
            if t == 0:
                tz_kv = sch.add("vector", lambda e: e.memset(kvacc_sb[:, :], 0.0), waits=wv0, inc="ev")
                tz_ks = sch.add("vector", lambda e: e.memset(ksacc_sb[:, :], 0.0), waits=[tz_kv], inc="ev")
            pKband = [pK1, pK2]
            pKname = ["pK1", "pK2"]
            for hp in range(8):
                for half in range(2):
                    pb = pKband[half]
                    rows = slice(half * 64, (half + 1) * 64)
                    w0 = ([w for w in wv0 if w] if (hp == 0) else []) + \
                         [w for w in [bank_free[pKname[half]]] if w]
                    tok_kv = None
                    for nb in range(4):
                        tok_kv = sch.add("tensor",
                                         lambda e, l_=kf_nb[nb][:, hp * 128 + half * 64: hp * 128 + half * 64 + 64],
                                         r=v_nb[nb][:, hp * 128 + half * 64: hp * 128 + half * 64 + 64],
                                         p=pb[rows, 0:64],
                                         s0=(nb == 0), s1=(nb == 3), tp=(0, 64) if half else None:
                                         e.matmul(p, l_, r, start=s0, stop=s1, tile_position=tp),
                                         waits=w0 if nb == 0 else [],
                                         inc="mm" if nb == 3 else None)
                    dst = kvacc_sb[rows, hp * 64:(hp + 1) * 64]
                    tacc = sch.add("vector",
                                   lambda e, o=dst, p=pb[rows, 0:64]:
                                   e.tensor_tensor(o, o, p, ALU.add),
                                   waits=[tok_kv] + ([tz_ks] if (t == 0 and hp == 0) else []), inc="ev")
                    bank_free[pKname[half]] = tacc
            for c in range(DC):
                pb = pKband[c % 2]
                w0 = [w for w in [bank_free[pKname[c % 2]], t_o3] if w]
                tok_ks = None
                for nb in range(4):
                    tok_ks = sch.add("tensor",
                                     lambda e, l_=kf_nb[nb][:, c * 128:(c + 1) * 128],
                                     p=pb[:, 0:1], s0=(nb == 0), s1=(nb == 3):
                                     e.matmul(p, l_, onescol[:, :], start=s0, stop=s1),
                                     waits=w0 if nb == 0 else [],
                                     inc="mm" if nb == 3 else None)
                dst = ksacc_sb[:, c:c + 1]
                tacc = sch.add("vector",
                               lambda e, o=dst, p=pb[:, 0:1]: e.tensor_tensor(o, o, p, ALU.add),
                               waits=[tok_ks], inc="ev")
                bank_free[pKname[c % 2]] = tacc
            for nb in range(4):
                kfv_free[nb] = tok_ks
            xn_free[t % 2] = tok_ks
            kvks_done = bank_free["pK1"] if bank_free["pK1"][1] > bank_free["pK2"][1] else bank_free["pK2"]

        w_prep = [kvks_done] + [w for w in [kv2_read_done, bd_read_done] if w]
        t_z0 = sch.add("vector", lambda e: e.memset(kv2[:, :], 0.0), waits=w_prep, inc="ev")
        t_z1 = sch.add("vector", lambda e: e.memset(bd[:, :], 0.0), waits=[t_z0], inc="ev")
        kv2_toks = []
        for c in range(DC):
            ta_ = sch.add("vector", lambda e, c=c: e.tensor_copy(kv2[0:64, c * 128: c * 128 + 64],
                                                                 kvacc_sb[0:64, c * 64:(c + 1) * 64]),
                          waits=[t_z0], inc="ev")
            tb_ = sch.add("vector", lambda e, c=c: e.tensor_copy(kv2[64:128, c * 128 + 64: c * 128 + 128],
                                                                 kvacc_sb[64:128, c * 64:(c + 1) * 64]),
                          waits=[t_z0], inc="ev")
            kv2_toks += [ta_, tb_]
        bd_toks = []
        for c in range(DC):
            ta_ = sch.add("vector", lambda e, c=c: e.tensor_copy(bd[0:64, c * 16 + 2 * c: c * 16 + 2 * c + 1],
                                                                 ksacc_sb[0:64, c:c + 1]),
                          waits=[t_z1], inc="ev")
            tb_ = sch.add("vector", lambda e, c=c: e.tensor_copy(bd[64:128, c * 16 + 2 * c + 1: c * 16 + 2 * c + 2],
                                                                 ksacc_sb[64:128, c:c + 1]),
                          waits=[t_z1], inc="ev")
            bd_toks += [ta_, tb_]
        kv2_done = kv2_toks[-1]
        bd_done = bd_toks[-1]
        if debug_taps:
            sch.add("sync", lambda e, s_=s_: e.dma_start(out=dbg_kv2.ap()[s_], in_=kv2[:, :]),
                    waits=[kv2_done], inc="ou", n=16)
            sch.add("sync", lambda e, s_=s_: e.dma_start(out=dbg_bd.ap()[s_], in_=bd[:, :]),
                    waits=[bd_done], inc="ou", n=16)
            tok_out_dbg[0] += 32

        qf_dump_gate = []
        if debug_taps:
            tqd = sch.add("sync", lambda e, s_=s_, q=QF: e.dma_start(out=dbg_qf.ap()[s_], in_=q[:, :]),
                    waits=[qf_tile_done], inc="ou", n=16)
            tok_out_dbg[0] += 16
            qf_dump_gate = [tqd]
        proj_last = None
        attn_toks_t = {}
        for t in range(NT):
            tok_zd = None
            for c in range(DC):
                w0 = [bd_done, qf_tile_done, bank_free["pC"]] if c == 0 else []
                tok_zd = sch.add("tensor",
                                 lambda e, l_=bd[:, c * 16:(c + 1) * 16],
                                 r=QF[:, c * S + t * T: c * S + t * T + T],
                                 s0=(c == 0), s1=(c == 7): e.matmul(pC[0:16, :], l_, r, start=s0, stop=s1),
                                 waits=[w for w in w0 if w], inc="mm" if c == 7 else None)
            tz1 = sch.add("vector", lambda e: e.tensor_scalar(z_f[:, :], pC[0:16, :], 1e-6, None, op0=ALU.add),
                          waits=[tok_zd] + [w for w in [zrow_free, temps_free[2]] if w], inc="ev")
            bank_free["pC"] = tz1
            tz2 = sch.add("vector", lambda e: e.reciprocal(z_f[:, :], z_f[:, :]), waits=[tz1], inc="ev")
            tz3 = sch.add("vector", lambda e: e.tensor_copy(z_b[:, :], z_f[:, :]), waits=[tz2], inc="ev")
            zrow_free = tz3
            if debug_taps:
                tzd = sch.add("sync", lambda e, s_=s_, t=t: e.dma_start(out=dbg_z.ap()[s_, t], in_=z_b[:, :]),
                              waits=[tz3], inc="ou", n=16)
                tok_out_dbg[0] += 16
                zrow_free = tzd
            attn_toks = []
            for c in range(DC):
                bi = 0
                tz4 = sch.add("tensor", lambda e, c=c: e.matmul(pC[:, :], sel_sb[:, c * 128:(c + 1) * 128],
                                                                z_b[:, :], start=True, stop=True),
                              waits=[tz3, kv2_done, bank_free["pC"]] + qf_dump_gate, inc="mm")
                tz5 = sch.add("vector", lambda e, o=zb_t[bi][:, :]: e.tensor_copy(o, pC[:, :]),
                              waits=[tz4] + [w for w in [zb_free[bi]] if w], inc="ev")
                bank_free["pC"] = tz5
                ta_ = sch.add("tensor", lambda e, c=c, r=QF[:, c * S + t * T: c * S + t * T + T]:
                              e.matmul(pD[:, :], kv2[:, c * 128:(c + 1) * 128], r, start=True, stop=True),
                              waits=[bank_free["pD"], kv2_done], inc="mm")
                dst = QF[:, c * S + t * T: c * S + t * T + T]
                tm_ = sch.add("vector", lambda e, o=dst, b=zb_t[bi][:, :]: e.tensor_tensor(o, pD[:, :], b, ALU.mult),
                              waits=[ta_, tz5], inc="ev")
                bank_free["pD"] = tm_
                zb_free[bi] = tm_
                attn_toks.append(tm_)
            attn_toks_t[t] = attn_toks
        dbg_gate = []
        if debug_taps:
            tdump = sch.add("sync", lambda e, s_=s_, q=QF: e.dma_start(out=dbg_at.ap()[s_], in_=q[:, :]),
                            waits=[tok for tl in attn_toks_t.values() for tok in tl], inc="ou", n=16)
            tok_out_dbg[0] += 16
            dbg_gate = [tdump]
        for t in range(NT):
            attn_toks = attn_toks_t[t]
            prot = [pA, pB, pK1, pK2]
            prot_names = ["pA", "pB", "pK1", "pK2"]
            for jc in range(8):
                bi = jc % 2
                wg = [prbuf_free[bi]]
                if t == 0 and jc == 0:
                    wg += qbuf_free + kvbuf_free
                tok_w = sch.add("sync", lambda e, o=prbuf[bi], i=sp_pr.ap()[jc]: e.dma_start(out=o, in_=i),
                                waits=[w for w in wg if w], inc="ds", n=16)
                ri = jc % 4
                w0 = [tok_w, bank_free[prot_names[ri]]] + attn_toks + dbg_gate
                tok_mm = None
                for cE in range(DC):
                    tok_mm = sch.add("tensor",
                                     lambda e, l_=prbuf[bi][:, cE * 128:(cE + 1) * 128],
                                     r=QF[:, cE * S + t * T: cE * S + t * T + T], p=prot[ri],
                                     s0=(cE == 0), s1=(cE == 7): e.matmul(p[:, :], l_, r, start=s0, stop=s1),
                                     waits=[w for w in w0 if w] if cE == 0 else [],
                                     inc="mm" if cE == 7 else None)
                prbuf_free[bi] = tok_mm
                dst = xb[:, jc * S + t * T: jc * S + t * T + T]
                xw = [tok_mm] + ([xb_read_done] if (xb_read_done and t == 0 and jc == 0) else [])
                te = sch.add("vector",
                             lambda e, o=dst, p=prot[ri], b=bias_sb[:, jc:jc + 1],
                             xc=XA[:, jc * S + t * T: jc * S + t * T + T]:
                             e.scalar_tensor_tensor(o, p[:, :], b, xc, op0=ALU.add, op1=ALU.add),
                             waits=[w for w in xw if w], inc="ev")
                bank_free[prot_names[ri]] = te
                proj_last = te
        if debug_taps:
            sch.add("sync", lambda e, s_=s_: e.dma_start(out=dbg_xb.ap()[s_], in_=xb[:, :]),
                    waits=[proj_last], inc="ou", n=16)
            tok_out_dbg[0] += 16
        attn_over_qf_ok[s_] = proj_last
        kv2_read_done = ta_
        bd_read_done = tok_zd
        xa_read_done = proj_last
        if s_ + 1 < NSEQ:
            x_next_tok = load_x(s_ + 1, [proj_last])

        def xb_of_t(t):
            def xb_of(c, t=t):
                return xb[:, c * S + t * T: c * S + t * T + T]
            return xb_of

        ln2_sums = ln_sums(xb_of_t(0), waits_extra=[proj_last])
        ln2_tail = {0: ln_tail(xb_of_t(0), xn[0], ln2_sums,
                               xn_guard=[w for w in [xn_free[0]] if w])}
        xb_last_read = None
        for t in range(NT):
            tok_xn2 = ln2_tail[t]
            if t + 1 < NT:
                ln2_sums_next = ln_sums(xb_of_t(t + 1))

            frot = [pA, pB, pK1, pK2]
            frot_names = ["pA", "pB", "pK1", "pK2"]
            hq_toks = []
            for grp in range(16):
                gbuf = grp % 2
                if grp % 2 == 0:
                    pc = grp // 2
                    wbi = pc % 2
                    wg = [w1buf_free[wbi]]
                    if t == 0 and grp == 0:
                        wg += prbuf_free + kvbuf_free
                    tok_w1 = sch.add("sync", lambda e, o=w1buf[wbi], i=sp_w1.ap()[pc]: e.dma_start(out=o, in_=i),
                                     waits=[w for w in wg if w], inc="ds", n=16)
                    w1tok_cur = tok_w1
                grp_last_mm = []
                for jj in range(2):
                    j = grp * 2 + jj
                    bi = j % 2
                    pcbuf = (j // 4) % 2
                    off = (j % 4) * 1024
                    ri = j % 4
                    w0 = [w1tok_cur, bank_free[frot_names[ri]]] + (tok_xn2 if j == 0 else [])
                    tok_mm = None
                    for c in range(DC):
                        tok_mm = sch.add("tensor",
                                         lambda e, l_=w1buf[pcbuf][:, off + c * 128: off + (c + 1) * 128],
                                         r=xn[0][:, c * T:(c + 1) * T], p=frot[ri],
                                         s0=(c == 0), s1=(c == 7): e.matmul(p[:, :], l_, r, start=s0, stop=s1),
                                         waits=[w for w in w0 if w] if c == 0 else [],
                                         inc="mm" if c == 7 else None)
                    grp_last_mm.append(tok_mm)
                    dst = r_t[gbuf][:, jj * T:(jj + 1) * T]
                    ew = [tok_mm] + ([w for w in [temps_free[gbuf]] if w] if jj == 0 or j % 2 == 1 else [])
                    if j % 2 == 0:
                        te = sch.add("vector",
                                     lambda e, o=dst, p=frot[ri], b=bias_sb[:, 8 + j:8 + j + 1]:
                                     e.tensor_scalar(o, p[:, :], b, 0.0, op0=ALU.add, op1=ALU.max),
                                     waits=ew, inc="ev")
                    else:
                        te = sch.add("scalar",
                                     lambda e, o=dst, p=frot[ri], b=bias_sb[:, 8 + j:8 + j + 1]:
                                     e.activation(o, p[:, :], AF.Relu, bias=b),
                                     waits=ew, inc="ea")
                    bank_free[frot_names[ri]] = te
                    if jj == 0:
                        evac_done0 = te
                    else:
                        evac_done = te
                if grp % 2 == 1:
                    w1buf_free[(grp // 2) % 2] = grp_last_mm[-1]
                hq_dst = XA[:, grp * 1024:(grp + 1) * 1024]
                tq, _ = quantize(r_t[gbuf][:, :], hq_dst, man=3, signed=False, bufi=gbuf, n=1024,
                                 waits=[evac_done, evac_done0],
                                 extra_first_waits=([xa_read_done] if (t == 0 and grp == 0) else []))
                temps_free[gbuf] = tq
                hq_toks.append(tq)

            if t + 1 < NT:
                ln2_tail[t + 1] = ln_tail(xb_of_t(t + 1), xn[0], ln2_sums_next,
                                          xn_guard=[grp_last_mm[-1]])

            for eo in range(DC):
                ebi = eo % 2
                pe_ = pC if ebi == 0 else pD
                pe_name = "pC" if ebi == 0 else "pD"
                wg = [w2buf_free[ebi]]
                if t == 0 and eo == 0:
                    wg += kvbuf_free + [xn_free[1]]
                tok_w2 = sch.add("sync", lambda e, o=w2buf[ebi], i=sp_w2.ap()[eo]: e.dma_start(out=o, in_=i),
                                 waits=[w for w in wg if w], inc="ds", n=16)
                tok_mm = None
                w0 = [tok_w2, bank_free[pe_name]]
                for j in range(HJC):
                    jw = ([w for w in w0 if w] if j == 0 else []) + \
                         ([hq_toks[j // 2]] if eo == 0 else [])
                    tok_mm = sch.add("tensor",
                                     lambda e, l_=w2buf[ebi][:, j * 128:(j + 1) * 128],
                                     r=XA[:, j * T:(j + 1) * T], p=pe_,
                                     s0=(j == 0), s1=(j == HJC - 1): e.matmul(p[:, :], l_, r, start=s0, stop=s1),
                                     waits=jw, inc="mm" if j == HJC - 1 else None)
                w2buf_free[ebi] = tok_mm
                vtile = sqs[ebi][:, :]
                tv = sch.add("vector", lambda e, o=vtile, p=pe_, b=bq_sb[:, eo:eo + 1]:
                             e.tensor_scalar(o, p[:, :], b, None, op0=ALU.add),
                             waits=[tok_mm] + [w for w in [tok_stgfree[ebi], sqs_tok[ebi]] if w], inc="ev")
                bank_free[pe_name] = tv
                qdst = stg[ebi][:, :]
                tq, ty = quantize(vtile, qdst, man=10, signed=True, bufi=ebi, n=512,
                                  waits=[tv] + [w for w in [temps_free[ebi]] if w])
                temps_free[ebi] = tq
                sqs_tok[ebi] = ty
                tr = sch.add("vector", lambda e, o=qdst, xc=xb[:, eo * S + t * T: eo * S + t * T + T]:
                             e.tensor_tensor(o, o, xc, ALU.add),
                             waits=[tq], inc="ev")
                xb_last_read = tr
                tdo = sch.add("gpsimd", lambda e, i=qdst, o=out_h.ap()[s_, eo * 128:(eo + 1) * 128, t * T:(t + 1) * T]:
                              e.dma_start(out=o, in_=i),
                              waits=[tr], inc="ou", n=16)
                tok_out_total[0] += 16
                tok_stgfree[ebi] = tdo
            if t == NT - 1:
                xn_free[0] = grp_last_mm[-1]

        xb_read_done = xb_last_read
        hq_read_done = tok_mm

    sch.add("sync", lambda e: e.nop(), waits=[("ou", tok_out_total[0] + tok_out_dbg[0])])

    sch.check()
    with es, pes, nc.Block() as block:
        sch.emit(block, sems)
    return nc


def _host_inputs(core, L0_attn_x, W_qkv, W_proj, b_proj, w1, b1, w2, b2):
    xs = L0_attn_x[core * NSEQ:(core + 1) * NSEQ]
    x_fm = np.ascontiguousarray(xs.transpose(0, 2, 1))
    biases = np.zeros((128, 48), dtype=np.float32)
    biases[:, 0:8] = b_proj.reshape(8, 128).T
    biases[:, 8:40] = b1.reshape(32, 128).T
    biases[:, 40:48] = b2.reshape(8, 128).T
    sel = np.zeros((16, 8 * 128), dtype=np.float32)
    for c in range(8):
        sel[2 * c, c * 128: c * 128 + 64] = 1.0
        sel[2 * c + 1, c * 128 + 64: c * 128 + 128] = 1.0
    cbv = np.zeros((128, 8), dtype=np.float32)
    cbv[:, 0] = -3 * LN2C
    cbv[:, 1] = 3 * LN2C
    cbv[:, 2] = -10 * LN2C
    cbv[:, 3] = 10 * LN2C
    cbv[:, 4] = 2.0 ** -80
    cbv[:, 5] = 2.0 ** -120
    cbv[:, 6] = 1e-5
    return {
        "x": x_fm,
        "wqkvT": np.ascontiguousarray(W_qkv.T),
        "wprojT": np.ascontiguousarray(W_proj.T),
        "w1T": np.ascontiguousarray(w1.T),
        "w2T": np.ascontiguousarray(w2.T),
        "biases": biases,
        "sel": sel,
        "cb": cbv,
    }


_NC_CACHE = {}


def kernel(L0_attn_x, L0_attn_W_qkv, L0_attn_W_proj, L0_attn_b_proj,
           L0_ffn_w1, L0_ffn_b1, L0_ffn_w2, L0_ffn_b2):
    args = [np.asarray(a, dtype=np.float32) for a in
            (L0_attn_x, L0_attn_W_qkv, L0_attn_W_proj, L0_attn_b_proj,
             L0_ffn_w1, L0_ffn_b1, L0_ffn_w2, L0_ffn_b2)]
    if "nc" not in _NC_CACHE:
        _NC_CACHE["nc"] = build_nc()
    nc = _NC_CACHE["nc"]
    in_maps = [_host_inputs(c, *args) for c in range(N_CORES)]
    res = run_bass_kernel_spmd(nc, in_maps, core_ids=list(range(N_CORES)))
    outs = [r["out"].transpose(0, 2, 1) for r in res.results]
    return np.ascontiguousarray(np.concatenate(outs, axis=0))


# version 10
# speedup vs baseline: 1.0108x; 1.0108x over previous
import numpy as np
import ml_dtypes
from contextlib import ExitStack

import concourse.bass as bass
import concourse.mybir as mybir
from concourse.bass_utils import run_bass_kernel_spmd

F32 = mybir.dt.float32
BF16 = mybir.dt.bfloat16
I32 = mybir.dt.int32
AF = mybir.ActivationFunctionType
ALU = mybir.AluOpType

LN2C = float(np.log(2.0))

N_CORES = 8
NSEQ = 4
S = 2048
D = 1024
DC = D // 128
T = 512
NT = S // T
HJ = 4096
HJC = HJ // 128
F8 = np.float32


class Sched:

    ENGINES = ("tensor", "vector", "scalar", "gpsimd", "sync")

    def __init__(self):
        self.ops = {e: [] for e in self.ENGINES}
        self.cnt = {}

    def add(self, eng, fn, waits=(), inc=None, n=1):
        w = [t for t in waits if t is not None]
        tok = None
        if inc is not None:
            self.cnt[inc] = self.cnt.get(inc, 0) + n
            tok = (inc, self.cnt[inc])
        self.ops[eng].append({"fn": fn, "waits": w, "inc": inc, "n": n})
        return tok

    def check(self):
        idx = {e: 0 for e in self.ENGINES}
        cnt = {s: 0 for s in self.cnt}
        total = sum(len(v) for v in self.ops.values())
        done = 0
        progress = True
        while progress:
            progress = False
            for e in self.ENGINES:
                while idx[e] < len(self.ops[e]):
                    op = self.ops[e][idx[e]]
                    if all(cnt.get(s, 0) >= v for (s, v) in op["waits"]):
                        if op["inc"]:
                            cnt[op["inc"]] = cnt.get(op["inc"], 0) + op["n"]
                        idx[e] += 1
                        done += 1
                        progress = True
                    else:
                        break
        if done != total:
            stuck = {e: (idx[e], len(self.ops[e]),
                         self.ops[e][idx[e]]["waits"] if idx[e] < len(self.ops[e]) else None)
                     for e in self.ENGINES}
            raise RuntimeError(f"schedule deadlock: done={done}/{total} stuck={stuck} cnt_sample={dict(list(cnt.items())[:10])}")

    def emit(self, block, sems):
        for ename in self.ENGINES:
            ops = self.ops[ename]

            def make(ops=ops):
                def body(eng):
                    last_wait = {}
                    for op in ops:
                        for (s, v) in op["waits"]:
                            if last_wait.get(s, -1) < v:
                                eng.wait_ge(sems[s], v)
                                last_wait[s] = v
                        ins = op["fn"](eng)
                        if op["inc"] is not None:
                            assert ins is not None, "inc op must return instruction"
                            ins.then_inc(sems[op["inc"]], op["n"])
                return body

            getattr(block, ename)(make())


def build_nc(debug_taps=False):
    nc = bass.Bass(target_bir_lowering=False)

    x_h = nc.dram_tensor("x", [NSEQ, D, S], F32, kind="ExternalInput")
    wqkv_h = nc.dram_tensor("wqkvT", [D, 3 * D], F32, kind="ExternalInput")
    wproj_h = nc.dram_tensor("wprojT", [D, D], F32, kind="ExternalInput")
    w1_h = nc.dram_tensor("w1T", [D, HJ], F32, kind="ExternalInput")
    w2_h = nc.dram_tensor("w2T", [HJ, D], F32, kind="ExternalInput")
    bias_h = nc.dram_tensor("biases", [128, 48], F32, kind="ExternalInput")
    sel_h = nc.dram_tensor("sel", [16, 8 * 128], F32, kind="ExternalInput")
    cb_h = nc.dram_tensor("cb", [128, 8], F32, kind="ExternalInput")
    out_h = nc.dram_tensor("out", [NSEQ, D, S], F32, kind="ExternalOutput")
    if debug_taps:
        dbg_bq = nc.dram_tensor("dbg_bq", [128, 8], F32, kind="ExternalOutput")
        dbg_w2 = nc.dram_tensor("dbg_w2", [8, 128, 4096], BF16, kind="ExternalOutput")
        dbg_w1 = nc.dram_tensor("dbg_w1", [8, 128, 4096], BF16, kind="ExternalOutput")
        dbg_xb = nc.dram_tensor("dbg_xb", [NSEQ, 128, 16384], BF16, kind="ExternalOutput")
        dbg_qf = nc.dram_tensor("dbg_qf", [NSEQ, 128, 16384], BF16, kind="ExternalOutput")
        dbg_kv2 = nc.dram_tensor("dbg_kv2", [NSEQ, 128, 1024], BF16, kind="ExternalOutput")
        dbg_bd = nc.dram_tensor("dbg_bd", [NSEQ, 128, 128], BF16, kind="ExternalOutput")
        dbg_at = nc.dram_tensor("dbg_at", [NSEQ, 128, 16384], BF16, kind="ExternalOutput")
        dbg_z = nc.dram_tensor("dbg_z", [NSEQ, NT, 16, T], BF16, kind="ExternalOutput")

    sp_q = nc.dram_tensor("sp_q", [8, 128, 1024], BF16, kind="Internal")
    sp_kv = nc.dram_tensor("sp_kv", [4, 128, 4096], BF16, kind="Internal")
    sp_pr = nc.dram_tensor("sp_pr", [8, 128, 1024], BF16, kind="Internal")
    sp_w1 = nc.dram_tensor("sp_w1", [8, 128, 4096], BF16, kind="Internal")
    sp_w2 = nc.dram_tensor("sp_w2", [8, 128, 4096], BF16, kind="Internal")

    es = ExitStack()

    def sb(name, shape, dt=BF16):
        return es.enter_context(nc.sbuf_tensor(name, shape, dt))

    wstream = sb("wstream", [128, 16384], BF16)
    xa_hq = sb("xa_hq", [128, 16384], BF16)
    qf = sb("qf", [128, 16384], BF16)
    xb = sb("xb", [128, 16384], BF16)
    r_t = [sb(f"r{i}", [128, 1024], F32) for i in range(3)]
    l_t = [sb(f"l{i}", [128, 1024], F32) for i in range(2)]
    e_t = [sb(f"e{i}", [128, 1024], I32) for i in range(2)]
    inv_t = [sb(f"inv{i}", [128, 1024], BF16) for i in range(3)]
    g_t = [sb(f"g{i}", [128, 1024], BF16) for i in range(3)]
    xn0 = sb("xn0", [128, 8 * T], BF16)
    xn = [xn0, None]
    kf_nb = [inv_t[0], g_t[0], inv_t[1], g_t[1]]
    kv2 = inv_t[2]
    z_f = r_t[2][0:16, 0:T]
    kvacc_sb = r_t[2][0:128, T:2 * T]
    z_b = g_t[2][0:16, 0:T]
    zb_t = [g_t[2][0:128, T:2 * T]]
    v_nb = [sb(f"v{i}", [128, 1024], BF16) for i in range(4)]
    stg = [sb(f"stg{i}", [128, T], F32) for i in range(2)]
    sqs = [sb(f"sqs{i}", [128, T], F32) for i in range(2)]
    lnsq0 = sb("lnsq0", [128, T], BF16)
    lnsq = [lnsq0, lnsq0]
    bd = sb("bd", [128, 8 * 16], BF16)
    sel_sb = sb("sel_sb", [16, 8 * 128], BF16)
    bias_sb = sb("bias_sb", [128, 48], F32)
    bq_sb = sb("bq_sb", [128, 8], F32)
    cb = sb("cb_sb", [128, 8], F32)
    ones1 = sb("ones1", [1, 128], BF16)
    negones = sb("negones", [1, 128], BF16)
    onescol = sb("onescol", [128, 1], BF16)
    rowF = sb("rowF", [1, 3 * T], F32)
    rowBf = sb("rowBf", [1, 2 * T], BF16)
    m_row = rowF[0:1, 0:T]
    s2_row = rowF[0:1, T:2 * T]
    mm_row = rowF[0:1, 2 * T:3 * T]
    rstd_row = rowF[0:1, T:2 * T]
    nb_rowf = rowF[0:1, 0:T]
    a_rowb = rowBf[0:1, 0:T]
    nb_rowb = rowBf[0:1, T:2 * T]
    ab_b = sb("ab_b", [128, T], BF16)
    bb_b = sb("bb_b", [128, T], BF16)
    ksacc_sb = sb("ksacc", [128, 8], F32)

    pes = ExitStack()

    def ps(name, shape):
        return pes.enter_context(nc.psum_tensor(name, shape, F32))

    pA = ps("pA", [128, 512])
    pB = ps("pB", [128, 512])
    pC = ps("pC", [128, 512])
    pD = ps("pD", [128, 512])
    pK1 = ps("pK1", [128, 512])
    pK2 = ps("pK2", [128, 512])
    pL1 = ps("pL1", [128, 512])
    pL2 = ps("pL2", [128, 512])

    sem_names = ["dl", "ds", "mm", "ea", "ev", "ou", "eg"]
    sems = {n: es.enter_context(nc.semaphore(n)) for n in sem_names}

    sch = Sched()

    qbuf = [wstream[:, i * 1024:(i + 1) * 1024] for i in range(2)]
    kvbuf = [wstream[:, 2048 + i * 4096: 2048 + (i + 1) * 4096] for i in range(2)]
    prbuf = [wstream[:, i * 1024:(i + 1) * 1024] for i in range(2)]
    w1buf = [wstream[:, i * 4096:(i + 1) * 4096] for i in range(2)]
    w2buf = [wstream[:, 8192 + i * 4096: 8192 + (i + 1) * 4096] for i in range(2)]
    xn[1] = wstream[:, 12288:16384]

    TINY4 = float(2.0 ** -80)
    TINY10 = float(2.0 ** -120)
    CLAMP4 = float(np.float32(-5.45 * LN2C))
    CLAMP10 = float(np.float32(-13.45 * 2 * LN2C))
    C2 = float(np.float32(1.0 / LN2C))
    C22 = float(np.float32(1.0 / (2 * LN2C)))
    CB_M3, CB_P3, CB_M10, CB_P10, CB_T4, CB_T10, CB_EPS = range(7)

    def cbc(i):
        return cb[:, i:i + 1]

    def quantize(src_ap, out_ap, man, signed, bufi, n, waits, extra_first_waits=(),
                 sq_src=None, sq_bias=None, sq_waits=()):
        lt, et, it_, gt = l_t[bufi][:, :n], e_t[bufi][:, :n], inv_t[bufi][:, :n], g_t[bufi][:, :n]
        ef = l_t[bufi][:, :n]
        w0 = list(waits) + list(extra_first_waits)
        min_exp = -6 if man == 3 else -14
        cbm = CB_M3 if man == 3 else CB_M10
        cbp = CB_P3 if man == 3 else CB_P10
        if sq_src == "pre":
            t_sq = None
        else:
            ssrc = sq_src if sq_src is not None else src_ap
            t_sq = sch.add("scalar",
                           (lambda e, o=lt, i=ssrc, b=sq_bias: e.activation(o, i, AF.Square, bias=b))
                           if sq_bias is not None else
                           (lambda e, o=lt, i=ssrc: e.activation(o, i, AF.Square)),
                           waits=w0 + list(sq_waits), inc="ea")
        t_l = sch.add("scalar", lambda e, o=lt, i=lt, b=cbc(CB_T10): e.activation(o, i, AF.Ln, bias=b, scale=0.5),
                      waits=([t_sq] if t_sq else w0), inc="ea")
        clampv = float(np.float32(min_exp - 0.45))
        t_e = sch.add("vector", lambda e, o=et, i=lt, c=C22, cl=clampv:
                      e.tensor_scalar(o, i, c, cl, op0=ALU.mult, op1=ALU.max),
                      waits=[t_l], inc="ev")
        t_ef = sch.add("vector", lambda e, o=ef, i=et: e.tensor_copy(o, i),
                       waits=[t_e], inc="ev")
        t_inv = sch.add("scalar", lambda e, o=it_, i=ef, b=cbc(cbp): e.activation(o, i, AF.Exp, scale=-LN2C, bias=b),
                        waits=[t_ef], inc="ea")
        t_g = sch.add("scalar", lambda e, o=gt, i=ef, b=cbc(cbm): e.activation(o, i, AF.Exp, scale=LN2C, bias=b),
                      waits=[t_ef], inc="ea")
        t_y = sch.add("vector", lambda e, o=et, a=src_ap, b=it_: e.tensor_tensor(o, a, b, ALU.mult),
                      waits=[t_inv], inc="ev")
        t_q = sch.add("vector", lambda e, o=out_ap, a=et, b=gt: e.tensor_tensor(o, a, b, ALU.mult),
                      waits=[t_g, t_y], inc="ev")
        return t_q, t_y

    def ln_sums(x_tile_of, waits_extra=()):
        tok_mm1 = tok_mm2 = None
        for c in range(DC):
            xc = x_tile_of(c)
            sq = lnsq[c % 2][:, :]
            tw = [w for w in [lnsq_tok[c % 2]] if w] + (list(waits_extra) if c == 0 else [])
            tsq = sch.add("vector", lambda e, o=sq, i=xc: e.tensor_tensor(o, i, i, ALU.mult),
                          waits=tw, inc="ev")
            w1_ = ([ln_bank_free[0], t_o3] + list(waits_extra)) if c == 0 else []
            tok_mm1 = sch.add("tensor",
                              lambda e, r=xc, s=(c == 0), p=(c == 7): e.matmul(pL1[0:1, :], onescol[:, :], r, start=s, stop=p),
                              waits=[w for w in w1_ if w], inc="mm" if c == 7 else None)
            w2_ = [tsq] + ([ln_bank_free[1]] if c == 0 and ln_bank_free[1] else [])
            tok_mm2 = sch.add("tensor",
                              lambda e, r=sq, s=(c == 0), p=(c == 7): e.matmul(pL2[0:1, :], onescol[:, :], r, start=s, stop=p),
                              waits=w2_, inc="mm")
            lnsq_tok[0] = lnsq_tok[1] = tok_mm2
        return tok_mm1, tok_mm2

    def ln_tail(x_tile_of, xn_buf, sums_toks, xn_guard=()):
        tok_mm1, tok_mm2 = sums_toks
        t1 = sch.add("vector", lambda e: e.tensor_scalar(m_row[:, :], pL1[0:1, :], 1.0 / D, None, op0=ALU.mult),
                     waits=[tok_mm1], inc="ev")
        t2 = sch.add("vector", lambda e: e.tensor_scalar(s2_row[:, :], pL2[0:1, :], 1.0 / D, None, op0=ALU.mult),
                     waits=[tok_mm2], inc="ev")
        t3 = sch.add("vector", lambda e: e.tensor_tensor(mm_row[:, :], m_row[:, :], m_row[:, :], ALU.mult),
                     waits=[t1], inc="ev")
        t4 = sch.add("vector", lambda e: e.tensor_tensor(mm_row[:, :], s2_row[:, :], mm_row[:, :], ALU.subtract),
                     waits=[t2, t3], inc="ev")
        t5 = sch.add("scalar", lambda e: e.activation(mm_row[:, :], mm_row[:, :], AF.Ln, bias=cb[0:1, CB_EPS:CB_EPS + 1]),
                     waits=[t4], inc="ea")
        t6 = sch.add("scalar", lambda e: e.activation(rstd_row[:, :], mm_row[:, :], AF.Exp, scale=-0.5),
                     waits=[t5], inc="ea")
        t7 = sch.add("vector", lambda e: e.tensor_copy(a_rowb[:, :], rstd_row[:, :]), waits=[t6], inc="ev")
        t8 = sch.add("vector", lambda e: e.tensor_tensor(nb_rowf[:, :], m_row[:, :], rstd_row[:, :], ALU.mult),
                     waits=[t6], inc="ev")
        t9 = sch.add("vector", lambda e: e.tensor_copy(nb_rowb[:, :], nb_rowf[:, :]), waits=[t8], inc="ev")
        ta = sch.add("tensor", lambda e: e.matmul(pL1[:, :], ones1[0:1, 0:128], a_rowb[:, :], start=True, stop=True),
                     waits=[t7, t_o1], inc="mm")
        tb = sch.add("tensor", lambda e: e.matmul(pL2[:, :], negones[0:1, 0:128], nb_rowb[:, :], start=True, stop=True),
                     waits=[t9, t_o2], inc="mm")
        tc = sch.add("vector", lambda e: e.tensor_copy(ab_b[:, :], pL1[:, :]), waits=[ta], inc="ev")
        td = sch.add("vector", lambda e: e.tensor_copy(bb_b[:, :], pL2[:, :]), waits=[tb], inc="ev")
        ln_bank_free[0] = tc
        ln_bank_free[1] = td
        tok_xn = []
        for c in range(DC):
            xc = x_tile_of(c)
            dst = xn_buf[:, c * T:(c + 1) * T]
            tm = sch.add("vector", lambda e, o=dst, i=xc: e.tensor_tensor(o, i, ab_b[:, :], ALU.mult),
                         waits=[tc] + (list(xn_guard) if c == 0 else []), inc="ev")
            tn = sch.add("vector", lambda e, o=dst: e.tensor_tensor(o, o, bb_b[:, :], ALU.add),
                         waits=[td, tm], inc="ev")
            tok_xn.append(tn)
        return tok_xn

    bank_free = {k: None for k in ["pA", "pB", "pC", "pD", "pK1", "pK2"]}
    ln_bank_free = [None, None]
    xn_free = [None, None]
    _lt0 = [None]
    lnsq_tok = _LnsqTok = type("LT", (), {})()
    lnsq_tok = _lt0 * 1 if False else [None, None]
    sqs_tok = [None, None]
    temps_free = [None, None, None]

    tok_bias = sch.add("sync", lambda e: e.dma_start(out=bias_sb[:, :], in_=bias_h.ap()), inc="ds", n=16)
    tok_cb = sch.add("sync", lambda e: e.dma_start(out=cb[:, :], in_=cb_h.ap()), inc="ds", n=16)
    tok_sel = sch.add("gpsimd", lambda e: e.dma_start(out=sel_sb[:, :], in_=sel_h.ap()), inc="dl", n=16)
    t_o1 = sch.add("vector", lambda e: e.memset(ones1[:, :], 1.0), inc="ev")
    t_o2 = sch.add("vector", lambda e: e.memset(negones[:, :], -1.0), inc="ev")
    t_o3 = sch.add("vector", lambda e: e.memset(onescol[:, :], 1.0), inc="ev")

    tok_bq, _ = quantize(bias_sb[:, 40:48], bq_sb[:, :], man=10, signed=True, bufi=0, n=8,
                         waits=[tok_bias, tok_cb])

    spill_toks = []

    def spill(dst_view, src_view, stage_ap, waits):
        tg = sch.add("gpsimd", lambda e, o=stage_ap, i=src_view: e.dma_start(out=o, in_=i),
                     waits=waits, inc="dl", n=16)
        ts_ = sch.add("sync", lambda e, o=dst_view, i=stage_ap: e.dma_start(out=o, in_=i),
                      waits=[tg], inc="ds", n=16)
        return ts_

    stage_a = wstream[:, 0:4096]
    stage_b = wstream[:, 4096:8192]
    prev = [None, None]
    for jc in range(8):
        src = wqkv_h.ap()[:, jc * 128:(jc + 1) * 128].rearrange("(c p) q -> p c q", p=128)
        stg_ap = (stage_a if jc % 2 == 0 else stage_b)[:, 0:1024].rearrange("p (c q) -> p c q", c=8)
        tok = spill(sp_q.ap()[jc], src, stg_ap, [w for w in [prev[jc % 2]] if w])
        prev[jc % 2] = tok
        spill_toks.append(tok)
    for jc in range(8):
        src = wproj_h.ap()[:, jc * 128:(jc + 1) * 128].rearrange("(c p) q -> p c q", p=128)
        stg_ap = (stage_a if jc % 2 == 0 else stage_b)[:, 0:1024].rearrange("p (c q) -> p c q", c=8)
        tok = spill(sp_pr.ap()[jc], src, stg_ap, [prev[jc % 2]])
        prev[jc % 2] = tok
        spill_toks.append(tok)
    for s_ in range(4):
        src = wqkv_h.ap()[:, 1024 + s_ * 512: 1024 + (s_ + 1) * 512].rearrange("(c p) j -> p c j", p=128)
        stg_ap = (stage_a if s_ % 2 == 0 else stage_b).rearrange("p (c j) -> p c j", c=8)
        tok = spill(sp_kv.ap()[s_], src, stg_ap, [prev[s_ % 2]])
        prev[s_ % 2] = tok
        spill_toks.append(tok)
    for c in range(8):
        src = w1_h.ap()[c * 128:(c + 1) * 128, :]
        stg_flat = (stage_a if c % 2 == 0 else stage_b)
        tok_ld = sch.add("gpsimd", lambda e, o=stg_flat, i=src: e.dma_start(out=o, in_=i),
                         waits=[prev[c % 2]], inc="dl", n=16)
        tok = None
        for pc in range(8):
            dst = sp_w1.ap()[pc].rearrange("p (jj rest) -> p jj rest", jj=4)[:, :, c * 128:(c + 1) * 128]
            src2 = stg_flat[:, pc * 512:(pc + 1) * 512].rearrange("p (jj q) -> p jj q", jj=4)
            tok = sch.add("sync", lambda e, o=dst, i=src2: e.dma_start(out=o, in_=i),
                          waits=[tok_ld], inc="ds", n=16)
        prev[c % 2] = tok
        spill_toks.append(tok)
    w2stage = [xa_hq[:, 0:512], xa_hq[:, 512:1024]]
    prevq = [None, None]
    for jc in range(HJC):
        for h in range(2):
            bi = (2 * jc + h) % 2
            src = w2_h.ap()[jc * 128:(jc + 1) * 128, h * 512:(h + 1) * 512]
            tok_ld = sch.add("sync", lambda e, o=sqs[bi][:, :], i=src: e.dma_start(out=o, in_=i),
                             waits=[w for w in [prevq[bi], sqs_tok[bi],
                                                tok_bq if (jc == 0) else None] if w],
                             inc="ds", n=16)
            tq, ty = quantize(sqs[bi][:, :], w2stage[bi][:, :], man=3, signed=True, bufi=bi, n=512,
                              waits=[tok_ld])
            sqs_tok[bi] = ty
            temps_free[bi] = tq
            dst = sp_w2.ap()[4 * h:4 * h + 4, :, jc * 128:(jc + 1) * 128].rearrange("a p q -> p a q")
            src2 = w2stage[bi].rearrange("p (a q) -> p a q", a=4)
            tok = sch.add("sync", lambda e, o=dst, i=src2: e.dma_start(out=o, in_=i),
                          waits=[tq], inc="ds", n=16)
            prevq[bi] = tok
            spill_toks.append(tok)

    if debug_taps:
        sch.add("sync", lambda e: e.dma_start(out=dbg_bq.ap(), in_=bq_sb[:, :]),
                waits=[tok_bq], inc="ou", n=16)
        tok_out_dbg = [16]
        sch.add("sync", lambda e: e.dma_start(out=dbg_w2.ap(), in_=sp_w2.ap()),
                waits=[last_spill := spill_toks[-1]], inc="ou", n=16)
        sch.add("sync", lambda e: e.dma_start(out=dbg_w1.ap(), in_=sp_w1.ap()),
                waits=[spill_toks[-1]], inc="ou", n=16)
        tok_out_dbg[0] += 32
    else:
        tok_out_dbg = [0]

    tok_xa = [None]

    def load_x(s_, waits):
        src = x_h.ap()[s_].rearrange("(c p) t -> p c t", p=128)
        tgt = xa_hq if s_ % 2 == 0 else qf
        dst = tgt[:, :].rearrange("p (c t) -> p c t", c=8)
        return sch.add("gpsimd", lambda e, o=dst, i=src: e.dma_start(out=o, in_=i),
                       waits=waits, inc="dl", n=16)

    tok_xa[0] = load_x(0, [tok_sel] + [t for t in [prevq[0], prevq[1]] if t])

    qbuf_free = [None, None]
    kvbuf_free = [None, None]
    prbuf_free = [None, None]
    w1buf_free = [None, None]
    w2buf_free = [None, None]
    last_spill = spill_toks[-1]

    def spill_done_tok():
        return last_spill

    for lst in (qbuf_free, kvbuf_free, prbuf_free, w1buf_free, w2buf_free):
        lst[0] = lst[1] = spill_done_tok()


    tok_out_total = [0]
    hq_read_done = None
    tok_stgfree = [None, None]
    kfv_free = [None] * 4
    attn_over_qf_ok = {}
    x_next_tok = tok_xa[0]
    xb_read_done = None
    kv2_read_done = None
    bd_read_done = None
    zrow_free = None
    zb_free = [None, None]
    ab_free = None

    for s_ in range(NSEQ):
        x_ready = x_next_tok
        qf_free_at = hq_read_done
        XA = xa_hq if s_ % 2 == 0 else qf
        QF = qf if s_ % 2 == 0 else xa_hq

        def x_of_t(t):
            def x_of(c, t=t):
                return XA[:, c * S + t * T: c * S + t * T + T]
            return x_of

        extra0 = [x_ready] + [w for w in temps_free if w]
        sums_t = {0: ln_sums(x_of_t(0), waits_extra=extra0)}
        tails_t = {0: ln_tail(x_of_t(0), xn[0], sums_t[0],
                              xn_guard=[w for w in [xn_free[0]] if w])}
        for t in range(NT):
            tok_xn = tails_t[t]
            if t + 1 < NT:
                sums_t[t + 1] = ln_sums(x_of_t(t + 1))

            qrot = [pA, pB, pC, pD]
            qrot_names = ["pA", "pB", "pC", "pD"]
            for jc in range(8):
                bi = jc % 2
                wg = [qbuf_free[bi]]
                if t == 0 and jc == 0:
                    wg += w1buf_free + w2buf_free
                tok_w = sch.add("sync", lambda e, o=qbuf[bi], i=sp_q.ap()[jc]: e.dma_start(out=o, in_=i),
                                waits=[w for w in wg if w], inc="ds", n=16)
                mmw = [tok_w, bank_free[qrot_names[jc % 4]]] + tok_xn
                tok_mm = None
                for c in range(DC):
                    tok_mm = sch.add("tensor",
                                     lambda e, l_=qbuf[bi][:, c * 128:(c + 1) * 128],
                                     r=xn[t % 2][:, c * T:(c + 1) * T], p=qrot[jc % 4],
                                     s0=(c == 0), s1=(c == 7): e.matmul(p[:, :], l_, r, start=s0, stop=s1),
                                     waits=[w for w in mmw if w] if c == 0 else [],
                                     inc="mm" if c == 7 else None)
                qbuf_free[bi] = tok_mm
                sqt = sqs[bi][:, :]
                tevq = sch.add("scalar", lambda e, o=sqt, i=qrot[jc % 4][:, :]: e.activation(o, i, AF.Square),
                               waits=[tok_mm] + [w for w in [sqs_tok[bi]] if w]
                               + ([w for w in [qf_free_at] if w] if (t == 0 and jc == 0) else []),
                               inc="ea")
                bank_free[qrot_names[jc % 4]] = tevq
                dst = QF[:, jc * S + t * T: jc * S + t * T + T]
                tevq2 = sch.add("scalar", lambda e, o=dst, i=sqt: e.activation(o, i, AF.Exp, scale=-0.5),
                                waits=[tevq], inc="ea")
                sqs_tok[bi] = tevq2
                if jc == 7:
                    qf_tile_done = tevq2

            if t + 1 < NT:
                guard = [w for w in [xn_free[(t + 1) % 2]] if w]
                if (t + 1) % 2 == 1:
                    guard += [w for w in [w2buf_free[1]] if w]
                tails_t[t + 1] = ln_tail(x_of_t(t + 1), xn[(t + 1) % 2], sums_t[t + 1], xn_guard=guard)

            pnb = [pA, pB, pC, pD]
            pnb_names = ["pA", "pB", "pC", "pD"]
            kv_evac_toks = {}
            for sl in range(4):
                bi = sl % 2
                tok_w = sch.add("sync", lambda e, o=kvbuf[bi], i=sp_kv.ap()[sl]: e.dma_start(out=o, in_=i),
                                waits=[w for w in [kvbuf_free[bi]] if w], inc="ds", n=16)
                last_mm_of_nb = {}
                for nb in range(4):
                    tok_mm = None
                    w0 = [tok_w, bank_free[pnb_names[nb]]]
                    for c in range(DC):
                        tok_mm = sch.add("tensor",
                                         lambda e, l_=xn[t % 2][:, c * T + nb * 128: c * T + nb * 128 + 128],
                                         r=kvbuf[bi][:, c * 512:(c + 1) * 512], p=pnb[nb],
                                         s0=(c == 0), s1=(c == 7): e.matmul(p[:, :], l_, r, start=s0, stop=s1),
                                         waits=[w for w in w0 if w] if c == 0 else [],
                                         inc="mm" if c == 7 else None)
                    last_mm_of_nb[nb] = tok_mm
                kvbuf_free[bi] = last_mm_of_nb[3]
                for nb in range(4):
                    tok_mm = last_mm_of_nb[nb]
                    if sl < 2:
                        sqt = sqs[nb % 2][:, :]
                        te1 = sch.add("scalar", lambda e, o=sqt, i=pnb[nb][:, :]: e.activation(o, i, AF.Square),
                                      waits=[tok_mm] + [w for w in [sqs_tok[nb % 2]] if w],
                                      inc="ea")
                        dst = kf_nb[nb][:, sl * 512:(sl + 1) * 512]
                        kf_guard = ([kfv_free[nb]] if kfv_free[nb] else []) + \
                                   ([temps_free[nb // 2]] if temps_free[nb // 2] else [])
                        te2 = sch.add("scalar", lambda e, o=dst, i=sqt: e.activation(o, i, AF.Exp, scale=-0.5),
                                      waits=[te1] + (kf_guard if (sl == 0 and t == 0) else
                                                     ([kfv_free[nb]] if (sl == 0 and kfv_free[nb]) else [])),
                                      inc="ea")
                        sqs_tok[nb % 2] = te2
                        bank_free[pnb_names[nb]] = te1
                        kv_evac_toks[("k", sl, nb)] = te2
                    else:
                        dst = v_nb[nb][:, (sl - 2) * 512:(sl - 1) * 512]
                        te = sch.add("vector", lambda e, o=dst, i=pnb[nb][:, :]: e.tensor_copy(o, i),
                                     waits=[tok_mm] + ([kfv_free[nb]] if (sl == 2 and kfv_free[nb]) else []),
                                     inc="ev")
                        bank_free[pnb_names[nb]] = te
                        kv_evac_toks[("v", sl, nb)] = te

            wv0 = [kv_evac_toks[(kk, slx, nbx)] for nbx in range(4)
                   for (kk, slx) in (("k", 0), ("k", 1), ("v", 2), ("v", 3))]
            if t == 0:
                tz_kv = sch.add("vector", lambda e: e.memset(kvacc_sb[:, :], 0.0), waits=wv0, inc="ev")
                tz_ks = sch.add("vector", lambda e: e.memset(ksacc_sb[:, :], 0.0), waits=[tz_kv], inc="ev")
            pKband = [pK1, pK2]
            pKname = ["pK1", "pK2"]
            for hp in range(8):
                for half in range(2):
                    pb = pKband[half]
                    rows = slice(half * 64, (half + 1) * 64)
                    w0 = ([w for w in wv0 if w] if (hp == 0) else []) + \
                         [w for w in [bank_free[pKname[half]]] if w]
                    tok_kv = None
                    for nb in range(4):
                        tok_kv = sch.add("tensor",
                                         lambda e, l_=kf_nb[nb][:, hp * 128 + half * 64: hp * 128 + half * 64 + 64],
                                         r=v_nb[nb][:, hp * 128 + half * 64: hp * 128 + half * 64 + 64],
                                         p=pb[rows, 0:64],
                                         s0=(nb == 0), s1=(nb == 3), tp=(0, 64) if half else None:
                                         e.matmul(p, l_, r, start=s0, stop=s1, tile_position=tp),
                                         waits=w0 if nb == 0 else [],
                                         inc="mm" if nb == 3 else None)
                    dst = kvacc_sb[rows, hp * 64:(hp + 1) * 64]
                    tacc = sch.add("vector",
                                   lambda e, o=dst, p=pb[rows, 0:64]:
                                   e.tensor_tensor(o, o, p, ALU.add),
                                   waits=[tok_kv] + ([tz_ks] if (t == 0 and hp == 0) else []), inc="ev")
                    bank_free[pKname[half]] = tacc
            for c in range(DC):
                pb = pKband[c % 2]
                w0 = [w for w in [bank_free[pKname[c % 2]], t_o3] if w]
                tok_ks = None
                for nb in range(4):
                    tok_ks = sch.add("tensor",
                                     lambda e, l_=kf_nb[nb][:, c * 128:(c + 1) * 128],
                                     p=pb[:, 0:1], s0=(nb == 0), s1=(nb == 3):
                                     e.matmul(p, l_, onescol[:, :], start=s0, stop=s1),
                                     waits=w0 if nb == 0 else [],
                                     inc="mm" if nb == 3 else None)
                dst = ksacc_sb[:, c:c + 1]
                tacc = sch.add("vector",
                               lambda e, o=dst, p=pb[:, 0:1]: e.tensor_tensor(o, o, p, ALU.add),
                               waits=[tok_ks], inc="ev")
                bank_free[pKname[c % 2]] = tacc
            for nb in range(4):
                kfv_free[nb] = tok_ks
            xn_free[t % 2] = tok_ks
            kvks_done = bank_free["pK1"] if bank_free["pK1"][1] > bank_free["pK2"][1] else bank_free["pK2"]

        w_prep = [kvks_done] + [w for w in [kv2_read_done, bd_read_done] if w]
        t_z0 = sch.add("vector", lambda e: e.memset(kv2[:, :], 0.0), waits=w_prep, inc="ev")
        t_z1 = sch.add("vector", lambda e: e.memset(bd[:, :], 0.0), waits=[t_z0], inc="ev")
        kv2_toks = []
        for c in range(DC):
            ta_ = sch.add("vector", lambda e, c=c: e.tensor_copy(kv2[0:64, c * 128: c * 128 + 64],
                                                                 kvacc_sb[0:64, c * 64:(c + 1) * 64]),
                          waits=[t_z0], inc="ev")
            tb_ = sch.add("vector", lambda e, c=c: e.tensor_copy(kv2[64:128, c * 128 + 64: c * 128 + 128],
                                                                 kvacc_sb[64:128, c * 64:(c + 1) * 64]),
                          waits=[t_z0], inc="ev")
            kv2_toks += [ta_, tb_]
        bd_toks = []
        for c in range(DC):
            ta_ = sch.add("vector", lambda e, c=c: e.tensor_copy(bd[0:64, c * 16 + 2 * c: c * 16 + 2 * c + 1],
                                                                 ksacc_sb[0:64, c:c + 1]),
                          waits=[t_z1], inc="ev")
            tb_ = sch.add("vector", lambda e, c=c: e.tensor_copy(bd[64:128, c * 16 + 2 * c + 1: c * 16 + 2 * c + 2],
                                                                 ksacc_sb[64:128, c:c + 1]),
                          waits=[t_z1], inc="ev")
            bd_toks += [ta_, tb_]
        kv2_done = kv2_toks[-1]
        bd_done = bd_toks[-1]
        if debug_taps:
            sch.add("sync", lambda e, s_=s_: e.dma_start(out=dbg_kv2.ap()[s_], in_=kv2[:, :]),
                    waits=[kv2_done], inc="ou", n=16)
            sch.add("sync", lambda e, s_=s_: e.dma_start(out=dbg_bd.ap()[s_], in_=bd[:, :]),
                    waits=[bd_done], inc="ou", n=16)
            tok_out_dbg[0] += 32

        qf_dump_gate = []
        if debug_taps:
            tqd = sch.add("sync", lambda e, s_=s_, q=QF: e.dma_start(out=dbg_qf.ap()[s_], in_=q[:, :]),
                    waits=[qf_tile_done], inc="ou", n=16)
            tok_out_dbg[0] += 16
            qf_dump_gate = [tqd]
        proj_last = None
        attn_toks_t = {}
        for t in range(NT):
            tok_zd = None
            for c in range(DC):
                w0 = [bd_done, qf_tile_done, bank_free["pC"]] if c == 0 else []
                tok_zd = sch.add("tensor",
                                 lambda e, l_=bd[:, c * 16:(c + 1) * 16],
                                 r=QF[:, c * S + t * T: c * S + t * T + T],
                                 s0=(c == 0), s1=(c == 7): e.matmul(pC[0:16, :], l_, r, start=s0, stop=s1),
                                 waits=[w for w in w0 if w], inc="mm" if c == 7 else None)
            tz1 = sch.add("vector", lambda e: e.tensor_scalar(z_f[:, :], pC[0:16, :], 1e-6, None, op0=ALU.add),
                          waits=[tok_zd] + [w for w in [zrow_free, temps_free[2]] if w], inc="ev")
            bank_free["pC"] = tz1
            tz2 = sch.add("vector", lambda e: e.reciprocal(z_f[:, :], z_f[:, :]), waits=[tz1], inc="ev")
            tz3 = sch.add("vector", lambda e: e.tensor_copy(z_b[:, :], z_f[:, :]), waits=[tz2], inc="ev")
            zrow_free = tz3
            if debug_taps:
                tzd = sch.add("sync", lambda e, s_=s_, t=t: e.dma_start(out=dbg_z.ap()[s_, t], in_=z_b[:, :]),
                              waits=[tz3], inc="ou", n=16)
                tok_out_dbg[0] += 16
                zrow_free = tzd
            attn_toks = []
            for c in range(DC):
                bi = 0
                tz4 = sch.add("tensor", lambda e, c=c: e.matmul(pC[:, :], sel_sb[:, c * 128:(c + 1) * 128],
                                                                z_b[:, :], start=True, stop=True),
                              waits=[tz3, kv2_done, bank_free["pC"]] + qf_dump_gate, inc="mm")
                tz5 = sch.add("vector", lambda e, o=zb_t[bi][:, :]: e.tensor_copy(o, pC[:, :]),
                              waits=[tz4] + [w for w in [zb_free[bi]] if w], inc="ev")
                bank_free["pC"] = tz5
                ta_ = sch.add("tensor", lambda e, c=c, r=QF[:, c * S + t * T: c * S + t * T + T]:
                              e.matmul(pD[:, :], kv2[:, c * 128:(c + 1) * 128], r, start=True, stop=True),
                              waits=[bank_free["pD"], kv2_done], inc="mm")
                dst = QF[:, c * S + t * T: c * S + t * T + T]
                tm_ = sch.add("vector", lambda e, o=dst, b=zb_t[bi][:, :]: e.tensor_tensor(o, pD[:, :], b, ALU.mult),
                              waits=[ta_, tz5], inc="ev")
                bank_free["pD"] = tm_
                zb_free[bi] = tm_
                attn_toks.append(tm_)
            attn_toks_t[t] = attn_toks
        dbg_gate = []
        if debug_taps:
            tdump = sch.add("sync", lambda e, s_=s_, q=QF: e.dma_start(out=dbg_at.ap()[s_], in_=q[:, :]),
                            waits=[tok for tl in attn_toks_t.values() for tok in tl], inc="ou", n=16)
            tok_out_dbg[0] += 16
            dbg_gate = [tdump]
        for t in range(NT):
            attn_toks = attn_toks_t[t]
            prot = [pA, pB, pK1, pK2]
            prot_names = ["pA", "pB", "pK1", "pK2"]
            for jc in range(8):
                bi = jc % 2
                wg = [prbuf_free[bi]]
                if t == 0 and jc == 0:
                    wg += qbuf_free + kvbuf_free
                tok_w = sch.add("sync", lambda e, o=prbuf[bi], i=sp_pr.ap()[jc]: e.dma_start(out=o, in_=i),
                                waits=[w for w in wg if w], inc="ds", n=16)
                ri = jc % 4
                w0 = [tok_w, bank_free[prot_names[ri]]] + attn_toks + dbg_gate
                tok_mm = None
                for cE in range(DC):
                    tok_mm = sch.add("tensor",
                                     lambda e, l_=prbuf[bi][:, cE * 128:(cE + 1) * 128],
                                     r=QF[:, cE * S + t * T: cE * S + t * T + T], p=prot[ri],
                                     s0=(cE == 0), s1=(cE == 7): e.matmul(p[:, :], l_, r, start=s0, stop=s1),
                                     waits=[w for w in w0 if w] if cE == 0 else [],
                                     inc="mm" if cE == 7 else None)
                prbuf_free[bi] = tok_mm
                dst = xb[:, jc * S + t * T: jc * S + t * T + T]
                xw = [tok_mm] + ([xb_read_done] if (xb_read_done and t == 0 and jc == 0) else [])
                te = sch.add("vector",
                             lambda e, o=dst, p=prot[ri], b=bias_sb[:, jc:jc + 1],
                             xc=XA[:, jc * S + t * T: jc * S + t * T + T]:
                             e.scalar_tensor_tensor(o, p[:, :], b, xc, op0=ALU.add, op1=ALU.add),
                             waits=[w for w in xw if w], inc="ev")
                bank_free[prot_names[ri]] = te
                proj_last = te
        if debug_taps:
            sch.add("sync", lambda e, s_=s_: e.dma_start(out=dbg_xb.ap()[s_], in_=xb[:, :]),
                    waits=[proj_last], inc="ou", n=16)
            tok_out_dbg[0] += 16
        attn_over_qf_ok[s_] = proj_last
        kv2_read_done = ta_
        bd_read_done = tok_zd
        xa_read_done = proj_last
        if s_ + 1 < NSEQ:
            x_next_tok = load_x(s_ + 1, [proj_last])

        def xb_of_t(t):
            def xb_of(c, t=t):
                return xb[:, c * S + t * T: c * S + t * T + T]
            return xb_of

        ln2_sums = ln_sums(xb_of_t(0), waits_extra=[proj_last])
        ln2_tail = {0: ln_tail(xb_of_t(0), xn[0], ln2_sums,
                               xn_guard=[w for w in [xn_free[0]] if w])}
        xb_last_read = None
        for t in range(NT):
            tok_xn2 = ln2_tail[t]
            if t + 1 < NT:
                ln2_sums_next = ln_sums(xb_of_t(t + 1))

            frot = [pA, pB, pK1, pK2]
            frot_names = ["pA", "pB", "pK1", "pK2"]
            hq_toks = []
            for grp in range(16):
                gbuf = grp % 2
                if grp % 2 == 0:
                    pc = grp // 2
                    wbi = pc % 2
                    wg = [w1buf_free[wbi]]
                    if t == 0 and grp == 0:
                        wg += prbuf_free + kvbuf_free
                    tok_w1 = sch.add("sync", lambda e, o=w1buf[wbi], i=sp_w1.ap()[pc]: e.dma_start(out=o, in_=i),
                                     waits=[w for w in wg if w], inc="ds", n=16)
                    w1tok_cur = tok_w1
                grp_last_mm = []
                for jj in range(2):
                    j = grp * 2 + jj
                    bi = j % 2
                    pcbuf = (j // 4) % 2
                    off = (j % 4) * 1024
                    ri = j % 4
                    w0 = [w1tok_cur, bank_free[frot_names[ri]]] + (tok_xn2 if j == 0 else [])
                    tok_mm = None
                    for c in range(DC):
                        tok_mm = sch.add("tensor",
                                         lambda e, l_=w1buf[pcbuf][:, off + c * 128: off + (c + 1) * 128],
                                         r=xn[0][:, c * T:(c + 1) * T], p=frot[ri],
                                         s0=(c == 0), s1=(c == 7): e.matmul(p[:, :], l_, r, start=s0, stop=s1),
                                         waits=[w for w in w0 if w] if c == 0 else [],
                                         inc="mm" if c == 7 else None)
                    grp_last_mm.append(tok_mm)
                    sqdst = l_t[gbuf][:, jj * T:(jj + 1) * T]
                    sw = [tok_mm] + ([w for w in [temps_free[gbuf]] if w] if jj == 0 or j % 2 == 1 else [])
                    tsqj = sch.add("scalar",
                                   lambda e, o=sqdst, p=frot[ri], b=bias_sb[:, 8 + j:8 + j + 1]:
                                   e.activation(o, p[:, :], AF.Square, bias=b),
                                   waits=sw, inc="ea")
                    dst = r_t[gbuf][:, jj * T:(jj + 1) * T]
                    if j % 2 == 0:
                        te = sch.add("vector",
                                     lambda e, o=dst, p=frot[ri], b=bias_sb[:, 8 + j:8 + j + 1]:
                                     e.tensor_scalar(o, p[:, :], b, 0.0, op0=ALU.add, op1=ALU.max),
                                     waits=[tok_mm, tsqj] + ([w for w in [temps_free[gbuf]] if w] if jj == 0 else []),
                                     inc="ev")
                    else:
                        te = sch.add("scalar",
                                     lambda e, o=dst, p=frot[ri], b=bias_sb[:, 8 + j:8 + j + 1]:
                                     e.activation(o, p[:, :], AF.Relu, bias=b),
                                     waits=[tok_mm], inc="ea")
                    bank_free[frot_names[ri]] = te
                    if jj == 0:
                        evac_done0 = te
                        sq_done0 = tsqj
                    else:
                        evac_done = te
                        sq_done = tsqj
                if grp % 2 == 1:
                    w1buf_free[(grp // 2) % 2] = grp_last_mm[-1]
                hq_dst = XA[:, grp * 1024:(grp + 1) * 1024]
                tq, _ = quantize(r_t[gbuf][:, :], hq_dst, man=3, signed=False, bufi=gbuf, n=1024,
                                 waits=[evac_done, evac_done0, sq_done, sq_done0],
                                 extra_first_waits=([xa_read_done] if (t == 0 and grp == 0) else []),
                                 sq_src="pre")
                temps_free[gbuf] = tq
                hq_toks.append(tq)

            if t + 1 < NT:
                ln2_tail[t + 1] = ln_tail(xb_of_t(t + 1), xn[0], ln2_sums_next,
                                          xn_guard=[grp_last_mm[-1]])

            for eo in range(DC):
                ebi = eo % 2
                pe_ = pC if ebi == 0 else pD
                pe_name = "pC" if ebi == 0 else "pD"
                wg = [w2buf_free[ebi]]
                if t == 0 and eo == 0:
                    wg += kvbuf_free + [xn_free[1]]
                tok_w2 = sch.add("sync", lambda e, o=w2buf[ebi], i=sp_w2.ap()[eo]: e.dma_start(out=o, in_=i),
                                 waits=[w for w in wg if w], inc="ds", n=16)
                tok_mm = None
                w0 = [tok_w2, bank_free[pe_name]]
                for j in range(HJC):
                    jw = ([w for w in w0 if w] if j == 0 else []) + \
                         ([hq_toks[j // 2]] if eo == 0 else [])
                    tok_mm = sch.add("tensor",
                                     lambda e, l_=w2buf[ebi][:, j * 128:(j + 1) * 128],
                                     r=XA[:, j * T:(j + 1) * T], p=pe_,
                                     s0=(j == 0), s1=(j == HJC - 1): e.matmul(p[:, :], l_, r, start=s0, stop=s1),
                                     waits=jw, inc="mm" if j == HJC - 1 else None)
                w2buf_free[ebi] = tok_mm
                vtile = sqs[ebi][:, :]
                tv = sch.add("vector", lambda e, o=vtile, p=pe_, b=bq_sb[:, eo:eo + 1]:
                             e.tensor_scalar(o, p[:, :], b, None, op0=ALU.add),
                             waits=[tok_mm] + [w for w in [tok_stgfree[ebi], sqs_tok[ebi]] if w], inc="ev")
                bank_free[pe_name] = tv
                qdst = stg[ebi][:, :]
                tq, ty = quantize(vtile, qdst, man=10, signed=True, bufi=ebi, n=512,
                                  waits=[tv] + [w for w in [temps_free[ebi]] if w])
                temps_free[ebi] = tq
                sqs_tok[ebi] = ty
                tr = sch.add("vector", lambda e, o=qdst, xc=xb[:, eo * S + t * T: eo * S + t * T + T]:
                             e.tensor_tensor(o, o, xc, ALU.add),
                             waits=[tq], inc="ev")
                xb_last_read = tr
                tdo = sch.add("gpsimd", lambda e, i=qdst, o=out_h.ap()[s_, eo * 128:(eo + 1) * 128, t * T:(t + 1) * T]:
                              e.dma_start(out=o, in_=i),
                              waits=[tr], inc="ou", n=16)
                tok_out_total[0] += 16
                tok_stgfree[ebi] = tdo
            if t == NT - 1:
                xn_free[0] = grp_last_mm[-1]

        xb_read_done = xb_last_read
        hq_read_done = tok_mm

    sch.add("sync", lambda e: e.nop(), waits=[("ou", tok_out_total[0] + tok_out_dbg[0])])

    sch.check()
    with es, pes, nc.Block() as block:
        sch.emit(block, sems)
    return nc


def _host_inputs(core, L0_attn_x, W_qkv, W_proj, b_proj, w1, b1, w2, b2):
    xs = L0_attn_x[core * NSEQ:(core + 1) * NSEQ]
    x_fm = np.ascontiguousarray(xs.transpose(0, 2, 1))
    biases = np.zeros((128, 48), dtype=np.float32)
    biases[:, 0:8] = b_proj.reshape(8, 128).T
    biases[:, 8:40] = b1.reshape(32, 128).T
    biases[:, 40:48] = b2.reshape(8, 128).T
    sel = np.zeros((16, 8 * 128), dtype=np.float32)
    for c in range(8):
        sel[2 * c, c * 128: c * 128 + 64] = 1.0
        sel[2 * c + 1, c * 128 + 64: c * 128 + 128] = 1.0
    cbv = np.zeros((128, 8), dtype=np.float32)
    cbv[:, 0] = -3 * LN2C
    cbv[:, 1] = 3 * LN2C
    cbv[:, 2] = -10 * LN2C
    cbv[:, 3] = 10 * LN2C
    cbv[:, 4] = 2.0 ** -80
    cbv[:, 5] = 2.0 ** -120
    cbv[:, 6] = 1e-5
    return {
        "x": x_fm,
        "wqkvT": np.ascontiguousarray(W_qkv.T),
        "wprojT": np.ascontiguousarray(W_proj.T),
        "w1T": np.ascontiguousarray(w1.T),
        "w2T": np.ascontiguousarray(w2.T),
        "biases": biases,
        "sel": sel,
        "cb": cbv,
    }


_NC_CACHE = {}


def kernel(L0_attn_x, L0_attn_W_qkv, L0_attn_W_proj, L0_attn_b_proj,
           L0_ffn_w1, L0_ffn_b1, L0_ffn_w2, L0_ffn_b2):
    args = [np.asarray(a, dtype=np.float32) for a in
            (L0_attn_x, L0_attn_W_qkv, L0_attn_W_proj, L0_attn_b_proj,
             L0_ffn_w1, L0_ffn_b1, L0_ffn_w2, L0_ffn_b2)]
    if "nc" not in _NC_CACHE:
        _NC_CACHE["nc"] = build_nc()
    nc = _NC_CACHE["nc"]
    in_maps = [_host_inputs(c, *args) for c in range(N_CORES)]
    res = run_bass_kernel_spmd(nc, in_maps, core_ids=list(range(N_CORES)))
    outs = [r["out"].transpose(0, 2, 1) for r in res.results]
    return np.ascontiguousarray(np.concatenate(outs, axis=0))


# version 11
# speedup vs baseline: 1.0208x; 1.0098x over previous
import numpy as np
import ml_dtypes
from contextlib import ExitStack

import concourse.bass as bass
import concourse.mybir as mybir
from concourse.bass_utils import run_bass_kernel_spmd

F32 = mybir.dt.float32
BF16 = mybir.dt.bfloat16
I32 = mybir.dt.int32
AF = mybir.ActivationFunctionType
ALU = mybir.AluOpType

LN2C = float(np.log(2.0))

N_CORES = 8
NSEQ = 4
S = 2048
D = 1024
DC = D // 128
T = 512
NT = S // T
HJ = 4096
HJC = HJ // 128
F8 = np.float32


class Sched:

    ENGINES = ("tensor", "vector", "scalar", "gpsimd", "sync")

    def __init__(self):
        self.ops = {e: [] for e in self.ENGINES}
        self.cnt = {}

    def add(self, eng, fn, waits=(), inc=None, n=1):
        w = [t for t in waits if t is not None]
        tok = None
        if inc is not None:
            self.cnt[inc] = self.cnt.get(inc, 0) + n
            tok = (inc, self.cnt[inc])
        self.ops[eng].append({"fn": fn, "waits": w, "inc": inc, "n": n})
        return tok

    def check(self):
        idx = {e: 0 for e in self.ENGINES}
        cnt = {s: 0 for s in self.cnt}
        total = sum(len(v) for v in self.ops.values())
        done = 0
        progress = True
        while progress:
            progress = False
            for e in self.ENGINES:
                while idx[e] < len(self.ops[e]):
                    op = self.ops[e][idx[e]]
                    if all(cnt.get(s, 0) >= v for (s, v) in op["waits"]):
                        if op["inc"]:
                            cnt[op["inc"]] = cnt.get(op["inc"], 0) + op["n"]
                        idx[e] += 1
                        done += 1
                        progress = True
                    else:
                        break
        if done != total:
            stuck = {e: (idx[e], len(self.ops[e]),
                         self.ops[e][idx[e]]["waits"] if idx[e] < len(self.ops[e]) else None)
                     for e in self.ENGINES}
            raise RuntimeError(f"schedule deadlock: done={done}/{total} stuck={stuck} cnt_sample={dict(list(cnt.items())[:10])}")

    def emit(self, block, sems):
        for ename in self.ENGINES:
            ops = self.ops[ename]

            def make(ops=ops):
                def body(eng):
                    last_wait = {}
                    for op in ops:
                        for (s, v) in op["waits"]:
                            if last_wait.get(s, -1) < v:
                                eng.wait_ge(sems[s], v)
                                last_wait[s] = v
                        ins = op["fn"](eng)
                        if op["inc"] is not None:
                            assert ins is not None, "inc op must return instruction"
                            ins.then_inc(sems[op["inc"]], op["n"])
                return body

            getattr(block, ename)(make())


def build_nc(debug_taps=False):
    nc = bass.Bass(target_bir_lowering=False)

    x_h = nc.dram_tensor("x", [NSEQ, D, S], F32, kind="ExternalInput")
    wqkv_h = nc.dram_tensor("wqkvT", [D, 3 * D], F32, kind="ExternalInput")
    wproj_h = nc.dram_tensor("wprojT", [D, D], F32, kind="ExternalInput")
    w1_h = nc.dram_tensor("w1T", [D, HJ], F32, kind="ExternalInput")
    w2_h = nc.dram_tensor("w2T", [HJ, D], F32, kind="ExternalInput")
    bias_h = nc.dram_tensor("biases", [128, 48], F32, kind="ExternalInput")
    sel_h = nc.dram_tensor("sel", [16, 8 * 128], F32, kind="ExternalInput")
    cb_h = nc.dram_tensor("cb", [128, 8], F32, kind="ExternalInput")
    out_h = nc.dram_tensor("out", [NSEQ, D, S], F32, kind="ExternalOutput")
    if debug_taps:
        dbg_bq = nc.dram_tensor("dbg_bq", [128, 8], F32, kind="ExternalOutput")
        dbg_w2 = nc.dram_tensor("dbg_w2", [8, 128, 4096], BF16, kind="ExternalOutput")
        dbg_w1 = nc.dram_tensor("dbg_w1", [8, 128, 4096], BF16, kind="ExternalOutput")
        dbg_xb = nc.dram_tensor("dbg_xb", [NSEQ, 128, 16384], BF16, kind="ExternalOutput")
        dbg_qf = nc.dram_tensor("dbg_qf", [NSEQ, 128, 16384], BF16, kind="ExternalOutput")
        dbg_kv2 = nc.dram_tensor("dbg_kv2", [NSEQ, 128, 1024], BF16, kind="ExternalOutput")
        dbg_bd = nc.dram_tensor("dbg_bd", [NSEQ, 128, 128], BF16, kind="ExternalOutput")
        dbg_at = nc.dram_tensor("dbg_at", [NSEQ, 128, 16384], BF16, kind="ExternalOutput")
        dbg_z = nc.dram_tensor("dbg_z", [NSEQ, NT, 16, T], BF16, kind="ExternalOutput")

    sp_q = nc.dram_tensor("sp_q", [8, 128, 1024], BF16, kind="Internal")
    sp_kv = nc.dram_tensor("sp_kv", [4, 128, 4096], BF16, kind="Internal")
    sp_pr = nc.dram_tensor("sp_pr", [8, 128, 1024], BF16, kind="Internal")
    sp_w1 = nc.dram_tensor("sp_w1", [8, 128, 4096], BF16, kind="Internal")
    sp_w2 = nc.dram_tensor("sp_w2", [8, 128, 4096], BF16, kind="Internal")

    es = ExitStack()

    def sb(name, shape, dt=BF16):
        return es.enter_context(nc.sbuf_tensor(name, shape, dt))

    wstream = sb("wstream", [128, 16384], BF16)
    xa_hq = sb("xa_hq", [128, 16384], BF16)
    qf = sb("qf", [128, 16384], BF16)
    xb = sb("xb", [128, 16384], BF16)
    r_t = [sb(f"r{i}", [128, 1024], F32) for i in range(3)]
    l_t = [sb(f"l{i}", [128, 1024], F32) for i in range(2)]
    e_t = [sb(f"e{i}", [128, 1024], I32) for i in range(2)]
    inv_t = [sb(f"inv{i}", [128, 1024], BF16) for i in range(3)]
    g_t = [sb(f"g{i}", [128, 1024], BF16) for i in range(3)]
    xn0 = sb("xn0", [128, 8 * T], BF16)
    xn = [xn0, None]
    kf_nb = [inv_t[0], g_t[0], inv_t[1], g_t[1]]
    kv2 = inv_t[2]
    z_f = r_t[2][0:16, 0:T]
    kvacc_sb = r_t[2][0:128, T:2 * T]
    z_b = g_t[2][0:16, 0:T]
    zb_t = [g_t[2][0:128, T:2 * T]]
    v_nb = [sb(f"v{i}", [128, 1024], BF16) for i in range(4)]
    stg = [sb(f"stg{i}", [128, T], F32) for i in range(2)]
    sqs = [sb(f"sqs{i}", [128, T], F32) for i in range(2)]
    lnsq0 = sb("lnsq0", [128, T], BF16)
    lnsq = [lnsq0, lnsq0]
    bd = sb("bd", [128, 8 * 16], BF16)
    sel_sb = sb("sel_sb", [16, 8 * 128], BF16)
    bias_sb = sb("bias_sb", [128, 48], F32)
    bq_sb = sb("bq_sb", [128, 8], F32)
    cb = sb("cb_sb", [128, 8], F32)
    ones1 = sb("ones1", [1, 128], BF16)
    negones = sb("negones", [1, 128], BF16)
    onescol = sb("onescol", [128, 1], BF16)
    rowF = sb("rowF", [1, 3 * T], F32)
    rowBf = sb("rowBf", [1, 2 * T], BF16)
    m_row = rowF[0:1, 0:T]
    s2_row = rowF[0:1, T:2 * T]
    mm_row = rowF[0:1, 2 * T:3 * T]
    rstd_row = rowF[0:1, T:2 * T]
    nb_rowf = rowF[0:1, 0:T]
    a_rowb = rowBf[0:1, 0:T]
    nb_rowb = rowBf[0:1, T:2 * T]
    ab_b = sb("ab_b", [128, T], BF16)
    bb_b = sb("bb_b", [128, T], BF16)
    ksacc_sb = sb("ksacc", [128, 8], F32)

    pes = ExitStack()

    def ps(name, shape):
        return pes.enter_context(nc.psum_tensor(name, shape, F32))

    pA = ps("pA", [128, 512])
    pB = ps("pB", [128, 512])
    pC = ps("pC", [128, 512])
    pD = ps("pD", [128, 512])
    pK1 = ps("pK1", [128, 512])
    pK2 = ps("pK2", [128, 512])
    pL1 = ps("pL1", [128, 512])
    pL2 = ps("pL2", [128, 512])

    sem_names = ["dl", "ds", "mm", "ea", "ev", "ou", "eg"]
    sems = {n: es.enter_context(nc.semaphore(n)) for n in sem_names}

    sch = Sched()

    qbuf = [wstream[:, i * 1024:(i + 1) * 1024] for i in range(2)]
    kvbuf = [wstream[:, 2048 + i * 4096: 2048 + (i + 1) * 4096] for i in range(2)]
    prbuf = [wstream[:, i * 1024:(i + 1) * 1024] for i in range(2)]
    w1buf = [wstream[:, i * 4096:(i + 1) * 4096] for i in range(2)]
    w2buf = [wstream[:, 8192 + i * 4096: 8192 + (i + 1) * 4096] for i in range(2)]
    xn[1] = wstream[:, 12288:16384]

    TINY4 = float(2.0 ** -80)
    TINY10 = float(2.0 ** -120)
    CLAMP4 = float(np.float32(-5.45 * LN2C))
    CLAMP10 = float(np.float32(-13.45 * 2 * LN2C))
    C2 = float(np.float32(1.0 / LN2C))
    C22 = float(np.float32(1.0 / (2 * LN2C)))
    CB_M3, CB_P3, CB_M10, CB_P10, CB_T4, CB_T10, CB_EPS = range(7)

    def cbc(i):
        return cb[:, i:i + 1]

    def quantize(src_ap, out_ap, man, signed, bufi, n, waits, extra_first_waits=(), helper="vector"):
        lt, et, it_, gt = l_t[bufi][:, :n], e_t[bufi][:, :n], inv_t[bufi][:, :n], g_t[bufi][:, :n]
        ef = l_t[bufi][:, :n]
        w0 = list(waits) + list(extra_first_waits)
        min_exp = -6 if man == 3 else -14
        cbm = CB_M3 if man == 3 else CB_M10
        cbp = CB_P3 if man == 3 else CB_P10
        if signed:
            sqt = r_t[bufi][:, :n]
            t_sq = sch.add("scalar", lambda e, o=sqt, i=src_ap: e.activation(o, i, AF.Square),
                           waits=w0, inc="ea")
            t_l = sch.add("scalar", lambda e, o=lt, i=sqt, b=cbc(CB_T10): e.activation(o, i, AF.Ln, bias=b),
                          waits=[t_sq], inc="ea")
            clamp = float(np.float32((min_exp + 0.55) * 2 * LN2C))
            cc = C22
        else:
            t_l = sch.add("scalar", lambda e, o=lt, i=src_ap, b=cbc(CB_T4): e.activation(o, i, AF.Ln, bias=b),
                          waits=w0, inc="ea")
            t_sq = None
            clamp = float(np.float32((min_exp + 0.55) * LN2C))
            cc = C2
        hsem = "ev" if helper == "vector" else "eg"
        t_e = sch.add(helper, lambda e, o=et, i=lt, c=cc: e.tensor_scalar(o, i, c, 0.5, op0=ALU.mult, op1=ALU.subtract),
                      waits=[t_l], inc=hsem)
        t_ef = sch.add(helper, lambda e, o=ef, i=et, c=float(min_exp): e.tensor_scalar(o, i, c, None, op0=ALU.max),
                       waits=[t_e], inc=hsem)
        t_inv = sch.add("scalar", lambda e, o=it_, i=ef, b=cbc(cbp): e.activation(o, i, AF.Exp, scale=-LN2C, bias=b),
                        waits=[t_ef], inc="ea")
        t_g = sch.add("scalar", lambda e, o=gt, i=ef, b=cbc(cbm): e.activation(o, i, AF.Exp, scale=LN2C, bias=b),
                      waits=[t_ef], inc="ea")
        t_y = sch.add("vector", lambda e, o=et, a=src_ap, b=it_: e.tensor_tensor(o, a, b, ALU.mult),
                      waits=[t_inv], inc="ev")
        t_q = sch.add("vector", lambda e, o=out_ap, a=et, b=gt: e.tensor_tensor(o, a, b, ALU.mult),
                      waits=[t_g, t_y], inc="ev")
        return t_q, t_y

    def ln_sums(x_tile_of, waits_extra=()):
        tok_mm1 = tok_mm2 = None
        for c in range(DC):
            xc = x_tile_of(c)
            sq = lnsq[c % 2][:, :]
            tw = [w for w in [lnsq_tok[c % 2]] if w] + (list(waits_extra) if c == 0 else [])
            tsq = sch.add("vector", lambda e, o=sq, i=xc: e.tensor_tensor(o, i, i, ALU.mult),
                          waits=tw, inc="ev")
            w1_ = ([ln_bank_free[0], t_o3] + list(waits_extra)) if c == 0 else []
            tok_mm1 = sch.add("tensor",
                              lambda e, r=xc, s=(c == 0), p=(c == 7): e.matmul(pL1[0:1, :], onescol[:, :], r, start=s, stop=p),
                              waits=[w for w in w1_ if w], inc="mm" if c == 7 else None)
            w2_ = [tsq] + ([ln_bank_free[1]] if c == 0 and ln_bank_free[1] else [])
            tok_mm2 = sch.add("tensor",
                              lambda e, r=sq, s=(c == 0), p=(c == 7): e.matmul(pL2[0:1, :], onescol[:, :], r, start=s, stop=p),
                              waits=w2_, inc="mm")
            lnsq_tok[0] = lnsq_tok[1] = tok_mm2
        return tok_mm1, tok_mm2

    def ln_rows(sums_toks):
        tok_mm1, tok_mm2 = sums_toks
        t1 = sch.add("vector", lambda e: e.tensor_scalar(m_row[:, :], pL1[0:1, :], 1.0 / D, None, op0=ALU.mult),
                     waits=[tok_mm1], inc="ev")
        t2 = sch.add("vector", lambda e: e.tensor_scalar(s2_row[:, :], pL2[0:1, :], 1.0 / D, None, op0=ALU.mult),
                     waits=[tok_mm2], inc="ev")
        t3 = sch.add("vector", lambda e: e.tensor_tensor(mm_row[:, :], m_row[:, :], m_row[:, :], ALU.mult),
                     waits=[t1], inc="ev")
        t4 = sch.add("vector", lambda e: e.tensor_tensor(mm_row[:, :], s2_row[:, :], mm_row[:, :], ALU.subtract),
                     waits=[t2, t3], inc="ev")
        t5 = sch.add("scalar", lambda e: e.activation(mm_row[:, :], mm_row[:, :], AF.Ln, bias=cb[0:1, CB_EPS:CB_EPS + 1]),
                     waits=[t4], inc="ea")
        t6 = sch.add("scalar", lambda e: e.activation(rstd_row[:, :], mm_row[:, :], AF.Exp, scale=-0.5),
                     waits=[t5], inc="ea")
        t7 = sch.add("vector", lambda e: e.tensor_copy(a_rowb[:, :], rstd_row[:, :]), waits=[t6], inc="ev")
        t8 = sch.add("vector", lambda e: e.tensor_tensor(nb_rowf[:, :], m_row[:, :], rstd_row[:, :], ALU.mult),
                     waits=[t6], inc="ev")
        t9 = sch.add("vector", lambda e: e.tensor_copy(nb_rowb[:, :], nb_rowf[:, :]), waits=[t8], inc="ev")
        return (t1, t2, t7, t9)

    def ln_tail(x_tile_of, xn_buf, rows_toks, xn_guard=()):
        t1, t2, t7, t9 = rows_toks
        ta = sch.add("tensor", lambda e: e.matmul(pL1[:, :], ones1[0:1, 0:128], a_rowb[:, :], start=True, stop=True),
                     waits=[t7, t_o1], inc="mm")
        tb = sch.add("tensor", lambda e: e.matmul(pL2[:, :], negones[0:1, 0:128], nb_rowb[:, :], start=True, stop=True),
                     waits=[t9, t_o2], inc="mm")
        tc = sch.add("vector", lambda e: e.tensor_copy(ab_b[:, :], pL1[:, :]), waits=[ta], inc="ev")
        td = sch.add("vector", lambda e: e.tensor_copy(bb_b[:, :], pL2[:, :]), waits=[tb], inc="ev")
        ln_bank_free[0] = tc
        ln_bank_free[1] = td
        tok_xn = []
        for c in range(DC):
            xc = x_tile_of(c)
            dst = xn_buf[:, c * T:(c + 1) * T]
            tm = sch.add("vector", lambda e, o=dst, i=xc: e.tensor_tensor(o, i, ab_b[:, :], ALU.mult),
                         waits=[tc] + (list(xn_guard) if c == 0 else []), inc="ev")
            tn = sch.add("vector", lambda e, o=dst: e.tensor_tensor(o, o, bb_b[:, :], ALU.add),
                         waits=[td, tm], inc="ev")
            tok_xn.append(tn)
        return tok_xn

    bank_free = {k: None for k in ["pA", "pB", "pC", "pD", "pK1", "pK2"]}
    ln_bank_free = [None, None]
    xn_free = [None, None]
    _lt0 = [None]
    lnsq_tok = _LnsqTok = type("LT", (), {})()
    lnsq_tok = _lt0 * 1 if False else [None, None]
    sqs_tok = [None, None]
    temps_free = [None, None, None]

    tok_bias = sch.add("sync", lambda e: e.dma_start(out=bias_sb[:, :], in_=bias_h.ap()), inc="ds", n=16)
    tok_cb = sch.add("sync", lambda e: e.dma_start(out=cb[:, :], in_=cb_h.ap()), inc="ds", n=16)
    tok_sel = sch.add("gpsimd", lambda e: e.dma_start(out=sel_sb[:, :], in_=sel_h.ap()), inc="dl", n=16)
    t_o1 = sch.add("vector", lambda e: e.memset(ones1[:, :], 1.0), inc="ev")
    t_o2 = sch.add("vector", lambda e: e.memset(negones[:, :], -1.0), inc="ev")
    t_o3 = sch.add("vector", lambda e: e.memset(onescol[:, :], 1.0), inc="ev")

    tok_bq, _ = quantize(bias_sb[:, 40:48], bq_sb[:, :], man=10, signed=True, bufi=0, n=8,
                         waits=[tok_bias, tok_cb])

    spill_toks = []

    def spill(dst_view, src_view, stage_ap, waits):
        tg = sch.add("gpsimd", lambda e, o=stage_ap, i=src_view: e.dma_start(out=o, in_=i),
                     waits=waits, inc="dl", n=16)
        ts_ = sch.add("sync", lambda e, o=dst_view, i=stage_ap: e.dma_start(out=o, in_=i),
                      waits=[tg], inc="ds", n=16)
        return ts_

    stage_a = wstream[:, 0:4096]
    stage_b = wstream[:, 4096:8192]
    prev = [None, None]
    for jc in range(8):
        src = wqkv_h.ap()[:, jc * 128:(jc + 1) * 128].rearrange("(c p) q -> p c q", p=128)
        stg_ap = (stage_a if jc % 2 == 0 else stage_b)[:, 0:1024].rearrange("p (c q) -> p c q", c=8)
        tok = spill(sp_q.ap()[jc], src, stg_ap, [w for w in [prev[jc % 2]] if w])
        prev[jc % 2] = tok
        spill_toks.append(tok)
    for jc in range(8):
        src = wproj_h.ap()[:, jc * 128:(jc + 1) * 128].rearrange("(c p) q -> p c q", p=128)
        stg_ap = (stage_a if jc % 2 == 0 else stage_b)[:, 0:1024].rearrange("p (c q) -> p c q", c=8)
        tok = spill(sp_pr.ap()[jc], src, stg_ap, [prev[jc % 2]])
        prev[jc % 2] = tok
        spill_toks.append(tok)
    for s_ in range(4):
        src = wqkv_h.ap()[:, 1024 + s_ * 512: 1024 + (s_ + 1) * 512].rearrange("(c p) j -> p c j", p=128)
        stg_ap = (stage_a if s_ % 2 == 0 else stage_b).rearrange("p (c j) -> p c j", c=8)
        tok = spill(sp_kv.ap()[s_], src, stg_ap, [prev[s_ % 2]])
        prev[s_ % 2] = tok
        spill_toks.append(tok)
    for c in range(8):
        src = w1_h.ap()[c * 128:(c + 1) * 128, :]
        stg_flat = (stage_a if c % 2 == 0 else stage_b)
        tok_ld = sch.add("gpsimd", lambda e, o=stg_flat, i=src: e.dma_start(out=o, in_=i),
                         waits=[prev[c % 2]], inc="dl", n=16)
        tok = None
        for pc in range(8):
            dst = sp_w1.ap()[pc].rearrange("p (jj rest) -> p jj rest", jj=4)[:, :, c * 128:(c + 1) * 128]
            src2 = stg_flat[:, pc * 512:(pc + 1) * 512].rearrange("p (jj q) -> p jj q", jj=4)
            tok = sch.add("sync", lambda e, o=dst, i=src2: e.dma_start(out=o, in_=i),
                          waits=[tok_ld], inc="ds", n=16)
        prev[c % 2] = tok
        spill_toks.append(tok)
    w2stage = [xa_hq[:, 0:512], xa_hq[:, 512:1024]]
    prevq = [None, None]
    for jc in range(HJC):
        for h in range(2):
            bi = (2 * jc + h) % 2
            src = w2_h.ap()[jc * 128:(jc + 1) * 128, h * 512:(h + 1) * 512]
            tok_ld = sch.add("sync", lambda e, o=sqs[bi][:, :], i=src: e.dma_start(out=o, in_=i),
                             waits=[w for w in [prevq[bi], sqs_tok[bi],
                                                tok_bq if (jc == 0) else None] if w],
                             inc="ds", n=16)
            tq, ty = quantize(sqs[bi][:, :], w2stage[bi][:, :], man=3, signed=True, bufi=bi, n=512,
                              waits=[tok_ld])
            sqs_tok[bi] = ty
            temps_free[bi] = tq
            dst = sp_w2.ap()[4 * h:4 * h + 4, :, jc * 128:(jc + 1) * 128].rearrange("a p q -> p a q")
            src2 = w2stage[bi].rearrange("p (a q) -> p a q", a=4)
            tok = sch.add("sync", lambda e, o=dst, i=src2: e.dma_start(out=o, in_=i),
                          waits=[tq], inc="ds", n=16)
            prevq[bi] = tok
            spill_toks.append(tok)

    if debug_taps:
        sch.add("sync", lambda e: e.dma_start(out=dbg_bq.ap(), in_=bq_sb[:, :]),
                waits=[tok_bq], inc="ou", n=16)
        tok_out_dbg = [16]
        sch.add("sync", lambda e: e.dma_start(out=dbg_w2.ap(), in_=sp_w2.ap()),
                waits=[last_spill := spill_toks[-1]], inc="ou", n=16)
        sch.add("sync", lambda e: e.dma_start(out=dbg_w1.ap(), in_=sp_w1.ap()),
                waits=[spill_toks[-1]], inc="ou", n=16)
        tok_out_dbg[0] += 32
    else:
        tok_out_dbg = [0]

    tok_xa = [None]

    def load_x(s_, waits):
        src = x_h.ap()[s_].rearrange("(c p) t -> p c t", p=128)
        tgt = xa_hq if s_ % 2 == 0 else qf
        dst = tgt[:, :].rearrange("p (c t) -> p c t", c=8)
        return sch.add("gpsimd", lambda e, o=dst, i=src: e.dma_start(out=o, in_=i),
                       waits=waits, inc="dl", n=16)

    tok_xa[0] = load_x(0, [tok_sel] + [t for t in [prevq[0], prevq[1]] if t])

    qbuf_free = [None, None]
    kvbuf_free = [None, None]
    prbuf_free = [None, None]
    w1buf_free = [None, None]
    w2buf_free = [None, None]
    last_spill = spill_toks[-1]

    def spill_done_tok():
        return last_spill

    for lst in (qbuf_free, kvbuf_free, prbuf_free, w1buf_free, w2buf_free):
        lst[0] = lst[1] = spill_done_tok()


    tok_out_total = [0]
    hq_read_done = None
    tok_stgfree = [None, None]
    kfv_free = [None] * 4
    attn_over_qf_ok = {}
    x_next_tok = tok_xa[0]
    xb_read_done = None
    kv2_read_done = None
    bd_read_done = None
    zrow_free = None
    zb_free = [None, None]
    ab_free = None

    for s_ in range(NSEQ):
        x_ready = x_next_tok
        qf_free_at = hq_read_done
        XA = xa_hq if s_ % 2 == 0 else qf
        QF = qf if s_ % 2 == 0 else xa_hq

        def x_of_t(t):
            def x_of(c, t=t):
                return XA[:, c * S + t * T: c * S + t * T + T]
            return x_of

        extra0 = [x_ready] + [w for w in temps_free if w]
        sums_t = {0: ln_sums(x_of_t(0), waits_extra=extra0)}
        rows_t = {0: ln_rows(sums_t[0])}
        tails_t = {0: ln_tail(x_of_t(0), xn[0], rows_t[0],
                              xn_guard=[w for w in [xn_free[0]] if w])}
        for t in range(NT):
            tok_xn = tails_t[t]
            if t + 1 < NT:
                sums_t[t + 1] = ln_sums(x_of_t(t + 1))
                rows_t[t + 1] = ln_rows(sums_t[t + 1])

            qrot = [pA, pB, pC, pD]
            qrot_names = ["pA", "pB", "pC", "pD"]
            for jc in range(8):
                bi = jc % 2
                wg = [qbuf_free[bi]]
                if t == 0 and jc == 0:
                    wg += w1buf_free + w2buf_free
                tok_w = sch.add("sync", lambda e, o=qbuf[bi], i=sp_q.ap()[jc]: e.dma_start(out=o, in_=i),
                                waits=[w for w in wg if w], inc="ds", n=16)
                mmw = [tok_w, bank_free[qrot_names[jc % 4]]] + tok_xn
                tok_mm = None
                for c in range(DC):
                    tok_mm = sch.add("tensor",
                                     lambda e, l_=qbuf[bi][:, c * 128:(c + 1) * 128],
                                     r=xn[t % 2][:, c * T:(c + 1) * T], p=qrot[jc % 4],
                                     s0=(c == 0), s1=(c == 7): e.matmul(p[:, :], l_, r, start=s0, stop=s1),
                                     waits=[w for w in mmw if w] if c == 0 else [],
                                     inc="mm" if c == 7 else None)
                qbuf_free[bi] = tok_mm
                sqt = sqs[bi][:, :]
                tevq = sch.add("scalar", lambda e, o=sqt, i=qrot[jc % 4][:, :]: e.activation(o, i, AF.Square),
                               waits=[tok_mm] + [w for w in [sqs_tok[bi]] if w]
                               + ([w for w in [qf_free_at] if w] if (t == 0 and jc == 0) else []),
                               inc="ea")
                bank_free[qrot_names[jc % 4]] = tevq
                dst = QF[:, jc * S + t * T: jc * S + t * T + T]
                tevq2 = sch.add("scalar", lambda e, o=dst, i=sqt: e.activation(o, i, AF.Exp, scale=-0.5),
                                waits=[tevq], inc="ea")
                sqs_tok[bi] = tevq2
                if jc == 7:
                    qf_tile_done = tevq2

            if t + 1 < NT:
                guard = [w for w in [xn_free[(t + 1) % 2]] if w]
                if (t + 1) % 2 == 1:
                    guard += [w for w in [w2buf_free[1]] if w]
                tails_t[t + 1] = ln_tail(x_of_t(t + 1), xn[(t + 1) % 2], rows_t[t + 1], xn_guard=guard)

            pnb = [pA, pB, pC, pD]
            pnb_names = ["pA", "pB", "pC", "pD"]
            kv_evac_toks = {}
            for sl in range(4):
                bi = sl % 2
                tok_w = sch.add("sync", lambda e, o=kvbuf[bi], i=sp_kv.ap()[sl]: e.dma_start(out=o, in_=i),
                                waits=[w for w in [kvbuf_free[bi]] if w], inc="ds", n=16)
                last_mm_of_nb = {}
                for nb in range(4):
                    tok_mm = None
                    w0 = [tok_w, bank_free[pnb_names[nb]]]
                    for c in range(DC):
                        tok_mm = sch.add("tensor",
                                         lambda e, l_=xn[t % 2][:, c * T + nb * 128: c * T + nb * 128 + 128],
                                         r=kvbuf[bi][:, c * 512:(c + 1) * 512], p=pnb[nb],
                                         s0=(c == 0), s1=(c == 7): e.matmul(p[:, :], l_, r, start=s0, stop=s1),
                                         waits=[w for w in w0 if w] if c == 0 else [],
                                         inc="mm" if c == 7 else None)
                    last_mm_of_nb[nb] = tok_mm
                kvbuf_free[bi] = last_mm_of_nb[3]
                for nb in range(4):
                    tok_mm = last_mm_of_nb[nb]
                    if sl < 2:
                        sqt = sqs[nb % 2][:, :]
                        te1 = sch.add("scalar", lambda e, o=sqt, i=pnb[nb][:, :]: e.activation(o, i, AF.Square),
                                      waits=[tok_mm] + [w for w in [sqs_tok[nb % 2]] if w],
                                      inc="ea")
                        dst = kf_nb[nb][:, sl * 512:(sl + 1) * 512]
                        kf_guard = ([kfv_free[nb]] if kfv_free[nb] else []) + \
                                   ([temps_free[nb // 2]] if temps_free[nb // 2] else [])
                        te2 = sch.add("scalar", lambda e, o=dst, i=sqt: e.activation(o, i, AF.Exp, scale=-0.5),
                                      waits=[te1] + (kf_guard if (sl == 0 and t == 0) else
                                                     ([kfv_free[nb]] if (sl == 0 and kfv_free[nb]) else [])),
                                      inc="ea")
                        sqs_tok[nb % 2] = te2
                        bank_free[pnb_names[nb]] = te1
                        kv_evac_toks[("k", sl, nb)] = te2
                    else:
                        dst = v_nb[nb][:, (sl - 2) * 512:(sl - 1) * 512]
                        te = sch.add("vector", lambda e, o=dst, i=pnb[nb][:, :]: e.tensor_copy(o, i),
                                     waits=[tok_mm] + ([kfv_free[nb]] if (sl == 2 and kfv_free[nb]) else []),
                                     inc="ev")
                        bank_free[pnb_names[nb]] = te
                        kv_evac_toks[("v", sl, nb)] = te

            wv0 = [kv_evac_toks[(kk, slx, nbx)] for nbx in range(4)
                   for (kk, slx) in (("k", 0), ("k", 1), ("v", 2), ("v", 3))]
            if t == 0:
                tz_kv = sch.add("vector", lambda e: e.memset(kvacc_sb[:, :], 0.0), waits=wv0, inc="ev")
                tz_ks = sch.add("vector", lambda e: e.memset(ksacc_sb[:, :], 0.0), waits=[tz_kv], inc="ev")
            pKband = [pK1, pK2]
            pKname = ["pK1", "pK2"]
            for hp in range(8):
                for half in range(2):
                    pb = pKband[half]
                    rows = slice(half * 64, (half + 1) * 64)
                    w0 = ([w for w in wv0 if w] if (hp == 0) else []) + \
                         [w for w in [bank_free[pKname[half]]] if w]
                    tok_kv = None
                    for nb in range(4):
                        tok_kv = sch.add("tensor",
                                         lambda e, l_=kf_nb[nb][:, hp * 128 + half * 64: hp * 128 + half * 64 + 64],
                                         r=v_nb[nb][:, hp * 128 + half * 64: hp * 128 + half * 64 + 64],
                                         p=pb[rows, 0:64],
                                         s0=(nb == 0), s1=(nb == 3), tp=(0, 64) if half else None:
                                         e.matmul(p, l_, r, start=s0, stop=s1, tile_position=tp),
                                         waits=w0 if nb == 0 else [],
                                         inc="mm" if nb == 3 else None)
                    dst = kvacc_sb[rows, hp * 64:(hp + 1) * 64]
                    tacc = sch.add("vector",
                                   lambda e, o=dst, p=pb[rows, 0:64]:
                                   e.tensor_tensor(o, o, p, ALU.add),
                                   waits=[tok_kv] + ([tz_ks] if (t == 0 and hp == 0) else []), inc="ev")
                    bank_free[pKname[half]] = tacc
            for c in range(DC):
                pb = pKband[c % 2]
                w0 = [w for w in [bank_free[pKname[c % 2]], t_o3] if w]
                tok_ks = None
                for nb in range(4):
                    tok_ks = sch.add("tensor",
                                     lambda e, l_=kf_nb[nb][:, c * 128:(c + 1) * 128],
                                     p=pb[:, 0:1], s0=(nb == 0), s1=(nb == 3):
                                     e.matmul(p, l_, onescol[:, :], start=s0, stop=s1),
                                     waits=w0 if nb == 0 else [],
                                     inc="mm" if nb == 3 else None)
                dst = ksacc_sb[:, c:c + 1]
                tacc = sch.add("vector",
                               lambda e, o=dst, p=pb[:, 0:1]: e.tensor_tensor(o, o, p, ALU.add),
                               waits=[tok_ks], inc="ev")
                bank_free[pKname[c % 2]] = tacc
            for nb in range(4):
                kfv_free[nb] = tok_ks
            xn_free[t % 2] = tok_ks
            kvks_done = bank_free["pK1"] if bank_free["pK1"][1] > bank_free["pK2"][1] else bank_free["pK2"]

        w_prep = [kvks_done] + [w for w in [kv2_read_done, bd_read_done] if w]
        t_z0 = sch.add("vector", lambda e: e.memset(kv2[:, :], 0.0), waits=w_prep, inc="ev")
        t_z1 = sch.add("vector", lambda e: e.memset(bd[:, :], 0.0), waits=[t_z0], inc="ev")
        kv2_toks = []
        for c in range(DC):
            ta_ = sch.add("vector", lambda e, c=c: e.tensor_copy(kv2[0:64, c * 128: c * 128 + 64],
                                                                 kvacc_sb[0:64, c * 64:(c + 1) * 64]),
                          waits=[t_z0], inc="ev")
            tb_ = sch.add("vector", lambda e, c=c: e.tensor_copy(kv2[64:128, c * 128 + 64: c * 128 + 128],
                                                                 kvacc_sb[64:128, c * 64:(c + 1) * 64]),
                          waits=[t_z0], inc="ev")
            kv2_toks += [ta_, tb_]
        bd_toks = []
        for c in range(DC):
            ta_ = sch.add("vector", lambda e, c=c: e.tensor_copy(bd[0:64, c * 16 + 2 * c: c * 16 + 2 * c + 1],
                                                                 ksacc_sb[0:64, c:c + 1]),
                          waits=[t_z1], inc="ev")
            tb_ = sch.add("vector", lambda e, c=c: e.tensor_copy(bd[64:128, c * 16 + 2 * c + 1: c * 16 + 2 * c + 2],
                                                                 ksacc_sb[64:128, c:c + 1]),
                          waits=[t_z1], inc="ev")
            bd_toks += [ta_, tb_]
        kv2_done = kv2_toks[-1]
        bd_done = bd_toks[-1]
        if debug_taps:
            sch.add("sync", lambda e, s_=s_: e.dma_start(out=dbg_kv2.ap()[s_], in_=kv2[:, :]),
                    waits=[kv2_done], inc="ou", n=16)
            sch.add("sync", lambda e, s_=s_: e.dma_start(out=dbg_bd.ap()[s_], in_=bd[:, :]),
                    waits=[bd_done], inc="ou", n=16)
            tok_out_dbg[0] += 32

        qf_dump_gate = []
        if debug_taps:
            tqd = sch.add("sync", lambda e, s_=s_, q=QF: e.dma_start(out=dbg_qf.ap()[s_], in_=q[:, :]),
                    waits=[qf_tile_done], inc="ou", n=16)
            tok_out_dbg[0] += 16
            qf_dump_gate = [tqd]
        proj_last = None
        attn_toks_t = {}
        for t in range(NT):
            tok_zd = None
            for c in range(DC):
                w0 = [bd_done, qf_tile_done, bank_free["pC"]] if c == 0 else []
                tok_zd = sch.add("tensor",
                                 lambda e, l_=bd[:, c * 16:(c + 1) * 16],
                                 r=QF[:, c * S + t * T: c * S + t * T + T],
                                 s0=(c == 0), s1=(c == 7): e.matmul(pC[0:16, :], l_, r, start=s0, stop=s1),
                                 waits=[w for w in w0 if w], inc="mm" if c == 7 else None)
            tz1 = sch.add("vector", lambda e: e.tensor_scalar(z_f[:, :], pC[0:16, :], 1e-6, None, op0=ALU.add),
                          waits=[tok_zd] + [w for w in [zrow_free, temps_free[2]] if w], inc="ev")
            bank_free["pC"] = tz1
            tz2 = sch.add("vector", lambda e: e.reciprocal(z_f[:, :], z_f[:, :]), waits=[tz1], inc="ev")
            tz3 = sch.add("vector", lambda e: e.tensor_copy(z_b[:, :], z_f[:, :]), waits=[tz2], inc="ev")
            zrow_free = tz3
            if debug_taps:
                tzd = sch.add("sync", lambda e, s_=s_, t=t: e.dma_start(out=dbg_z.ap()[s_, t], in_=z_b[:, :]),
                              waits=[tz3], inc="ou", n=16)
                tok_out_dbg[0] += 16
                zrow_free = tzd
            attn_toks = []
            for c in range(DC):
                bi = 0
                tz4 = sch.add("tensor", lambda e, c=c: e.matmul(pC[:, :], sel_sb[:, c * 128:(c + 1) * 128],
                                                                z_b[:, :], start=True, stop=True),
                              waits=[tz3, kv2_done, bank_free["pC"]] + qf_dump_gate, inc="mm")
                tz5 = sch.add("vector", lambda e, o=zb_t[bi][:, :]: e.tensor_copy(o, pC[:, :]),
                              waits=[tz4] + [w for w in [zb_free[bi]] if w], inc="ev")
                bank_free["pC"] = tz5
                ta_ = sch.add("tensor", lambda e, c=c, r=QF[:, c * S + t * T: c * S + t * T + T]:
                              e.matmul(pD[:, :], kv2[:, c * 128:(c + 1) * 128], r, start=True, stop=True),
                              waits=[bank_free["pD"], kv2_done], inc="mm")
                dst = QF[:, c * S + t * T: c * S + t * T + T]
                tm_ = sch.add("vector", lambda e, o=dst, b=zb_t[bi][:, :]: e.tensor_tensor(o, pD[:, :], b, ALU.mult),
                              waits=[ta_, tz5], inc="ev")
                bank_free["pD"] = tm_
                zb_free[bi] = tm_
                attn_toks.append(tm_)
            attn_toks_t[t] = attn_toks
        dbg_gate = []
        if debug_taps:
            tdump = sch.add("sync", lambda e, s_=s_, q=QF: e.dma_start(out=dbg_at.ap()[s_], in_=q[:, :]),
                            waits=[tok for tl in attn_toks_t.values() for tok in tl], inc="ou", n=16)
            tok_out_dbg[0] += 16
            dbg_gate = [tdump]
        for t in range(NT):
            attn_toks = attn_toks_t[t]
            prot = [pA, pB, pK1, pK2]
            prot_names = ["pA", "pB", "pK1", "pK2"]
            for jc in range(8):
                bi = jc % 2
                wg = [prbuf_free[bi]]
                if t == 0 and jc == 0:
                    wg += qbuf_free + kvbuf_free
                tok_w = sch.add("sync", lambda e, o=prbuf[bi], i=sp_pr.ap()[jc]: e.dma_start(out=o, in_=i),
                                waits=[w for w in wg if w], inc="ds", n=16)
                ri = jc % 4
                w0 = [tok_w, bank_free[prot_names[ri]]] + attn_toks + dbg_gate
                tok_mm = None
                for cE in range(DC):
                    tok_mm = sch.add("tensor",
                                     lambda e, l_=prbuf[bi][:, cE * 128:(cE + 1) * 128],
                                     r=QF[:, cE * S + t * T: cE * S + t * T + T], p=prot[ri],
                                     s0=(cE == 0), s1=(cE == 7): e.matmul(p[:, :], l_, r, start=s0, stop=s1),
                                     waits=[w for w in w0 if w] if cE == 0 else [],
                                     inc="mm" if cE == 7 else None)
                prbuf_free[bi] = tok_mm
                dst = xb[:, jc * S + t * T: jc * S + t * T + T]
                xw = [tok_mm] + ([xb_read_done] if (xb_read_done and t == 0 and jc == 0) else [])
                te = sch.add("vector",
                             lambda e, o=dst, p=prot[ri], b=bias_sb[:, jc:jc + 1],
                             xc=XA[:, jc * S + t * T: jc * S + t * T + T]:
                             e.scalar_tensor_tensor(o, p[:, :], b, xc, op0=ALU.add, op1=ALU.add),
                             waits=[w for w in xw if w], inc="ev")
                bank_free[prot_names[ri]] = te
                proj_last = te
        if debug_taps:
            sch.add("sync", lambda e, s_=s_: e.dma_start(out=dbg_xb.ap()[s_], in_=xb[:, :]),
                    waits=[proj_last], inc="ou", n=16)
            tok_out_dbg[0] += 16
        attn_over_qf_ok[s_] = proj_last
        kv2_read_done = ta_
        bd_read_done = tok_zd
        xa_read_done = proj_last
        if s_ + 1 < NSEQ:
            x_next_tok = load_x(s_ + 1, [proj_last])

        def xb_of_t(t):
            def xb_of(c, t=t):
                return xb[:, c * S + t * T: c * S + t * T + T]
            return xb_of

        ln2_sums = ln_sums(xb_of_t(0), waits_extra=[proj_last])
        ln2_rows = ln_rows(ln2_sums)
        ln2_tail = {0: ln_tail(xb_of_t(0), xn[0], ln2_rows,
                               xn_guard=[w for w in [xn_free[0]] if w])}
        xb_last_read = None
        for t in range(NT):
            tok_xn2 = ln2_tail[t]
            if t + 1 < NT:
                ln2_sums_next = ln_sums(xb_of_t(t + 1))
                ln2_rows_next = ln_rows(ln2_sums_next)

            frot = [pA, pB, pK1, pK2]
            frot_names = ["pA", "pB", "pK1", "pK2"]
            hq_toks = []
            for grp in range(16):
                gbuf = grp % 2
                if grp % 2 == 0:
                    pc = grp // 2
                    wbi = pc % 2
                    wg = [w1buf_free[wbi]]
                    if t == 0 and grp == 0:
                        wg += prbuf_free + kvbuf_free
                    tok_w1 = sch.add("sync", lambda e, o=w1buf[wbi], i=sp_w1.ap()[pc]: e.dma_start(out=o, in_=i),
                                     waits=[w for w in wg if w], inc="ds", n=16)
                    w1tok_cur = tok_w1
                grp_last_mm = []
                for jj in range(2):
                    j = grp * 2 + jj
                    bi = j % 2
                    pcbuf = (j // 4) % 2
                    off = (j % 4) * 1024
                    ri = j % 4
                    w0 = [w1tok_cur, bank_free[frot_names[ri]]] + (tok_xn2 if j == 0 else [])
                    tok_mm = None
                    for c in range(DC):
                        tok_mm = sch.add("tensor",
                                         lambda e, l_=w1buf[pcbuf][:, off + c * 128: off + (c + 1) * 128],
                                         r=xn[0][:, c * T:(c + 1) * T], p=frot[ri],
                                         s0=(c == 0), s1=(c == 7): e.matmul(p[:, :], l_, r, start=s0, stop=s1),
                                         waits=[w for w in w0 if w] if c == 0 else [],
                                         inc="mm" if c == 7 else None)
                    grp_last_mm.append(tok_mm)
                    dst = r_t[gbuf][:, jj * T:(jj + 1) * T]
                    ew = [tok_mm] + ([w for w in [temps_free[gbuf]] if w] if jj == 0 or j % 2 == 1 else [])
                    if j % 2 == 0:
                        te = sch.add("vector",
                                     lambda e, o=dst, p=frot[ri], b=bias_sb[:, 8 + j:8 + j + 1]:
                                     e.tensor_scalar(o, p[:, :], b, 0.0, op0=ALU.add, op1=ALU.max),
                                     waits=ew, inc="ev")
                    else:
                        te = sch.add("scalar",
                                     lambda e, o=dst, p=frot[ri], b=bias_sb[:, 8 + j:8 + j + 1]:
                                     e.activation(o, p[:, :], AF.Relu, bias=b),
                                     waits=ew, inc="ea")
                    bank_free[frot_names[ri]] = te
                    if jj == 0:
                        evac_done0 = te
                    else:
                        evac_done = te
                if grp % 2 == 1:
                    w1buf_free[(grp // 2) % 2] = grp_last_mm[-1]
                hq_dst = XA[:, grp * 1024:(grp + 1) * 1024]
                tq, _ = quantize(r_t[gbuf][:, :], hq_dst, man=3, signed=False, bufi=gbuf, n=1024,
                                 waits=[evac_done, evac_done0],
                                 extra_first_waits=([xa_read_done] if (t == 0 and grp == 0) else []))
                temps_free[gbuf] = tq
                hq_toks.append(tq)

            if t + 1 < NT:
                ln2_tail[t + 1] = ln_tail(xb_of_t(t + 1), xn[0], ln2_rows_next,
                                          xn_guard=[grp_last_mm[-1]])

            for eo in range(DC):
                ebi = eo % 2
                pe_ = pC if ebi == 0 else pD
                pe_name = "pC" if ebi == 0 else "pD"
                wg = [w2buf_free[ebi]]
                if t == 0 and eo == 0:
                    wg += kvbuf_free + [xn_free[1]]
                tok_w2 = sch.add("sync", lambda e, o=w2buf[ebi], i=sp_w2.ap()[eo]: e.dma_start(out=o, in_=i),
                                 waits=[w for w in wg if w], inc="ds", n=16)
                tok_mm = None
                w0 = [tok_w2, bank_free[pe_name]]
                for j in range(HJC):
                    jw = ([w for w in w0 if w] if j == 0 else []) + \
                         ([hq_toks[j // 2]] if eo == 0 else [])
                    tok_mm = sch.add("tensor",
                                     lambda e, l_=w2buf[ebi][:, j * 128:(j + 1) * 128],
                                     r=XA[:, j * T:(j + 1) * T], p=pe_,
                                     s0=(j == 0), s1=(j == HJC - 1): e.matmul(p[:, :], l_, r, start=s0, stop=s1),
                                     waits=jw, inc="mm" if j == HJC - 1 else None)
                w2buf_free[ebi] = tok_mm
                vtile = sqs[ebi][:, :]
                tv = sch.add("vector", lambda e, o=vtile, p=pe_, b=bq_sb[:, eo:eo + 1]:
                             e.tensor_scalar(o, p[:, :], b, None, op0=ALU.add),
                             waits=[tok_mm] + [w for w in [tok_stgfree[ebi], sqs_tok[ebi]] if w], inc="ev")
                bank_free[pe_name] = tv
                qdst = stg[ebi][:, :]
                tq, ty = quantize(vtile, qdst, man=10, signed=True, bufi=ebi, n=512,
                                  waits=[tv] + [w for w in [temps_free[ebi]] if w])
                temps_free[ebi] = tq
                sqs_tok[ebi] = ty
                tr = sch.add("vector", lambda e, o=qdst, xc=xb[:, eo * S + t * T: eo * S + t * T + T]:
                             e.tensor_tensor(o, o, xc, ALU.add),
                             waits=[tq], inc="ev")
                xb_last_read = tr
                tdo = sch.add("gpsimd", lambda e, i=qdst, o=out_h.ap()[s_, eo * 128:(eo + 1) * 128, t * T:(t + 1) * T]:
                              e.dma_start(out=o, in_=i),
                              waits=[tr], inc="ou", n=16)
                tok_out_total[0] += 16
                tok_stgfree[ebi] = tdo
            if t == NT - 1:
                xn_free[0] = grp_last_mm[-1]

        xb_read_done = xb_last_read
        hq_read_done = tok_mm

    sch.add("sync", lambda e: e.nop(), waits=[("ou", tok_out_total[0] + tok_out_dbg[0])])

    sch.check()
    with es, pes, nc.Block() as block:
        sch.emit(block, sems)
    return nc


def _host_inputs(core, L0_attn_x, W_qkv, W_proj, b_proj, w1, b1, w2, b2):
    xs = L0_attn_x[core * NSEQ:(core + 1) * NSEQ]
    x_fm = np.ascontiguousarray(xs.transpose(0, 2, 1))
    biases = np.zeros((128, 48), dtype=np.float32)
    biases[:, 0:8] = b_proj.reshape(8, 128).T
    biases[:, 8:40] = b1.reshape(32, 128).T
    biases[:, 40:48] = b2.reshape(8, 128).T
    sel = np.zeros((16, 8 * 128), dtype=np.float32)
    for c in range(8):
        sel[2 * c, c * 128: c * 128 + 64] = 1.0
        sel[2 * c + 1, c * 128 + 64: c * 128 + 128] = 1.0
    cbv = np.zeros((128, 8), dtype=np.float32)
    cbv[:, 0] = -3 * LN2C
    cbv[:, 1] = 3 * LN2C
    cbv[:, 2] = -10 * LN2C
    cbv[:, 3] = 10 * LN2C
    cbv[:, 4] = 2.0 ** -80
    cbv[:, 5] = 2.0 ** -120
    cbv[:, 6] = 1e-5
    return {
        "x": x_fm,
        "wqkvT": np.ascontiguousarray(W_qkv.T),
        "wprojT": np.ascontiguousarray(W_proj.T),
        "w1T": np.ascontiguousarray(w1.T),
        "w2T": np.ascontiguousarray(w2.T),
        "biases": biases,
        "sel": sel,
        "cb": cbv,
    }


_NC_CACHE = {}


def kernel(L0_attn_x, L0_attn_W_qkv, L0_attn_W_proj, L0_attn_b_proj,
           L0_ffn_w1, L0_ffn_b1, L0_ffn_w2, L0_ffn_b2):
    args = [np.asarray(a, dtype=np.float32) for a in
            (L0_attn_x, L0_attn_W_qkv, L0_attn_W_proj, L0_attn_b_proj,
             L0_ffn_w1, L0_ffn_b1, L0_ffn_w2, L0_ffn_b2)]
    if "nc" not in _NC_CACHE:
        _NC_CACHE["nc"] = build_nc()
    nc = _NC_CACHE["nc"]
    in_maps = [_host_inputs(c, *args) for c in range(N_CORES)]
    res = run_bass_kernel_spmd(nc, in_maps, core_ids=list(range(N_CORES)))
    outs = [r["out"].transpose(0, 2, 1) for r in res.results]
    return np.ascontiguousarray(np.concatenate(outs, axis=0))


# version 12
# speedup vs baseline: 1.0532x; 1.0318x over previous
import numpy as np
import ml_dtypes
from contextlib import ExitStack

import concourse.bass as bass
import concourse.mybir as mybir
from concourse.bass_utils import run_bass_kernel_spmd

F32 = mybir.dt.float32
BF16 = mybir.dt.bfloat16
F8E4 = mybir.dt.float8e4
I32 = mybir.dt.int32
AF = mybir.ActivationFunctionType
ALU = mybir.AluOpType

LN2C = float(np.log(2.0))

N_CORES = 8
NSEQ = 4
S = 2048
D = 1024
DC = D // 128
T = 512
NT = S // T
HJ = 4096
HJC = HJ // 128
F8 = np.float32


class Sched:

    ENGINES = ("tensor", "vector", "scalar", "gpsimd", "sync")

    def __init__(self):
        self.ops = {e: [] for e in self.ENGINES}
        self.cnt = {}

    def add(self, eng, fn, waits=(), inc=None, n=1):
        w = [t for t in waits if t is not None]
        tok = None
        if inc is not None:
            self.cnt[inc] = self.cnt.get(inc, 0) + n
            tok = (inc, self.cnt[inc])
        self.ops[eng].append({"fn": fn, "waits": w, "inc": inc, "n": n})
        return tok

    def check(self):
        idx = {e: 0 for e in self.ENGINES}
        cnt = {s: 0 for s in self.cnt}
        total = sum(len(v) for v in self.ops.values())
        done = 0
        progress = True
        while progress:
            progress = False
            for e in self.ENGINES:
                while idx[e] < len(self.ops[e]):
                    op = self.ops[e][idx[e]]
                    if all(cnt.get(s, 0) >= v for (s, v) in op["waits"]):
                        if op["inc"]:
                            cnt[op["inc"]] = cnt.get(op["inc"], 0) + op["n"]
                        idx[e] += 1
                        done += 1
                        progress = True
                    else:
                        break
        if done != total:
            stuck = {e: (idx[e], len(self.ops[e]),
                         self.ops[e][idx[e]]["waits"] if idx[e] < len(self.ops[e]) else None)
                     for e in self.ENGINES}
            raise RuntimeError(f"schedule deadlock: done={done}/{total} stuck={stuck} cnt_sample={dict(list(cnt.items())[:10])}")

    def emit(self, block, sems):
        for ename in self.ENGINES:
            ops = self.ops[ename]

            def make(ops=ops):
                def body(eng):
                    last_wait = {}
                    for op in ops:
                        for (s, v) in op["waits"]:
                            if last_wait.get(s, -1) < v:
                                eng.wait_ge(sems[s], v)
                                last_wait[s] = v
                        ins = op["fn"](eng)
                        if op["inc"] is not None:
                            assert ins is not None, "inc op must return instruction"
                            ins.then_inc(sems[op["inc"]], op["n"])
                return body

            getattr(block, ename)(make())


def build_nc(debug_taps=False):
    nc = bass.Bass(target_bir_lowering=False)

    x_h = nc.dram_tensor("x", [NSEQ, D, S], F32, kind="ExternalInput")
    wqkv_h = nc.dram_tensor("wqkvT", [D, 3 * D], F32, kind="ExternalInput")
    wproj_h = nc.dram_tensor("wprojT", [D, D], F32, kind="ExternalInput")
    w1_h = nc.dram_tensor("w1T", [D, HJ], F32, kind="ExternalInput")
    w2_h = nc.dram_tensor("w2T", [HJ, D], F32, kind="ExternalInput")
    bias_h = nc.dram_tensor("biases", [128, 48], F32, kind="ExternalInput")
    sel_h = nc.dram_tensor("sel", [16, 8 * 128], F32, kind="ExternalInput")
    cb_h = nc.dram_tensor("cb", [128, 8], F32, kind="ExternalInput")
    out_h = nc.dram_tensor("out", [NSEQ, D, S], F32, kind="ExternalOutput")
    if debug_taps:
        dbg_bq = nc.dram_tensor("dbg_bq", [128, 8], F32, kind="ExternalOutput")
        dbg_w1 = nc.dram_tensor("dbg_w1", [8, 128, 4096], BF16, kind="ExternalOutput")
        dbg_xb = nc.dram_tensor("dbg_xb", [NSEQ, 128, 16384], BF16, kind="ExternalOutput")
        dbg_qf = nc.dram_tensor("dbg_qf", [NSEQ, 128, 16384], BF16, kind="ExternalOutput")
        dbg_kv2 = nc.dram_tensor("dbg_kv2", [NSEQ, 128, 1024], BF16, kind="ExternalOutput")
        dbg_bd = nc.dram_tensor("dbg_bd", [NSEQ, 128, 128], BF16, kind="ExternalOutput")
        dbg_at = nc.dram_tensor("dbg_at", [NSEQ, 128, 16384], BF16, kind="ExternalOutput")
        dbg_z = nc.dram_tensor("dbg_z", [NSEQ, NT, 16, T], BF16, kind="ExternalOutput")

    sp_q = nc.dram_tensor("sp_q", [8, 128, 1024], BF16, kind="Internal")
    sp_kv = nc.dram_tensor("sp_kv", [4, 128, 4096], BF16, kind="Internal")
    sp_pr = nc.dram_tensor("sp_pr", [8, 128, 1024], BF16, kind="Internal")
    sp_w1 = nc.dram_tensor("sp_w1", [8, 128, 4096], BF16, kind="Internal")
    sp_w2 = nc.dram_tensor("sp_w2", [8, 128, 4096], F8E4, kind="Internal")

    es = ExitStack()

    def sb(name, shape, dt=BF16):
        return es.enter_context(nc.sbuf_tensor(name, shape, dt))

    wstream = sb("wstream", [128, 16384], BF16)
    xa_hq = sb("xa_hq", [128, 16384], BF16)
    qf = sb("qf", [128, 16384], BF16)
    xb = sb("xb", [128, 16384], BF16)
    r_t = [sb(f"r{i}", [128, 1024], F32) for i in range(3)]
    l_t = [sb(f"l{i}", [128, 1024], F32) for i in range(2)]
    e_t = [sb(f"e{i}", [128, 1024], I32) for i in range(2)]
    inv_t = [sb(f"inv{i}", [128, 1024], BF16) for i in range(3)]
    g_t = [sb(f"g{i}", [128, 1024], BF16) for i in range(3)]
    xn0 = sb("xn0", [128, 8 * T], BF16)
    xn = [xn0, None]
    kf_nb = [inv_t[0], g_t[0], inv_t[1], g_t[1]]
    kv2 = inv_t[2]
    hq8 = {id(xa_hq): [xa_hq[:, 0:8192].bitcast(F8E4), xa_hq[:, 8192:16384].bitcast(F8E4)],
           id(qf): [qf[:, 0:8192].bitcast(F8E4), qf[:, 8192:16384].bitcast(F8E4)]}
    z_f = r_t[2][0:16, 0:T]
    kvacc_sb = r_t[2][0:128, T:2 * T]
    z_b = g_t[2][0:16, 0:T]
    zb_t = [g_t[2][0:128, T:2 * T]]
    v_nb = [sb(f"v{i}", [128, 1024], BF16) for i in range(4)]
    stg = [sb(f"stg{i}", [128, T], F32) for i in range(2)]
    sqs = [sb(f"sqs{i}", [128, T], F32) for i in range(2)]
    lnsq0 = sb("lnsq0", [128, T], BF16)
    lnsq = [lnsq0, lnsq0]
    bd = sb("bd", [128, 8 * 16], BF16)
    sel_sb = sb("sel_sb", [16, 8 * 128], BF16)
    bias_sb = sb("bias_sb", [128, 48], F32)
    bq_sb = sb("bq_sb", [128, 8], F32)
    cb = sb("cb_sb", [128, 8], F32)
    ones1 = sb("ones1", [1, 128], BF16)
    negones = sb("negones", [1, 128], BF16)
    onescol = sb("onescol", [128, 1], BF16)
    rowF = sb("rowF", [1, 3 * T], F32)
    rowBf = sb("rowBf", [1, 2 * T], BF16)
    m_row = rowF[0:1, 0:T]
    s2_row = rowF[0:1, T:2 * T]
    mm_row = rowF[0:1, 2 * T:3 * T]
    rstd_row = rowF[0:1, T:2 * T]
    nb_rowf = rowF[0:1, 0:T]
    a_rowb = rowBf[0:1, 0:T]
    nb_rowb = rowBf[0:1, T:2 * T]
    ab_b = sb("ab_b", [128, T], BF16)
    bb_b = sb("bb_b", [128, T], BF16)
    ksacc_sb = sb("ksacc", [128, 8], F32)

    pes = ExitStack()

    def ps(name, shape):
        return pes.enter_context(nc.psum_tensor(name, shape, F32))

    pA = ps("pA", [128, 512])
    pB = ps("pB", [128, 512])
    pC = ps("pC", [128, 512])
    pD = ps("pD", [128, 512])
    pK1 = ps("pK1", [128, 512])
    pK2 = ps("pK2", [128, 512])
    pL1 = ps("pL1", [128, 512])
    pL2 = ps("pL2", [128, 512])

    sem_names = ["dl", "ds", "mm", "ea", "ev", "ou", "eg"]
    sems = {n: es.enter_context(nc.semaphore(n)) for n in sem_names}

    sch = Sched()

    qbuf = [wstream[:, i * 1024:(i + 1) * 1024] for i in range(2)]
    kvbuf = [wstream[:, 2048 + i * 4096: 2048 + (i + 1) * 4096] for i in range(2)]
    prbuf = [wstream[:, i * 1024:(i + 1) * 1024] for i in range(2)]
    w1buf = [wstream[:, i * 4096:(i + 1) * 4096] for i in range(2)]
    w2buf = [wstream[:, 8192 + i * 2048: 8192 + (i + 1) * 2048].bitcast(F8E4) for i in range(2)]
    xn[1] = wstream[:, 12288:16384]

    TINY4 = float(2.0 ** -80)
    TINY10 = float(2.0 ** -120)
    CLAMP4 = float(np.float32(-5.45 * LN2C))
    CLAMP10 = float(np.float32(-13.45 * 2 * LN2C))
    C2 = float(np.float32(1.0 / LN2C))
    C22 = float(np.float32(1.0 / (2 * LN2C)))
    CB_M3, CB_P3, CB_M10, CB_P10, CB_T4, CB_T10, CB_EPS = range(7)

    def cbc(i):
        return cb[:, i:i + 1]

    def quantize(src_ap, out_ap, man, signed, bufi, n, waits, extra_first_waits=(), helper="vector"):
        lt, et, it_, gt = l_t[bufi][:, :n], e_t[bufi][:, :n], inv_t[bufi][:, :n], g_t[bufi][:, :n]
        ef = l_t[bufi][:, :n]
        w0 = list(waits) + list(extra_first_waits)
        min_exp = -6 if man == 3 else -14
        cbm = CB_M3 if man == 3 else CB_M10
        cbp = CB_P3 if man == 3 else CB_P10
        if signed:
            sqt = r_t[bufi][:, :n]
            t_sq = sch.add("scalar", lambda e, o=sqt, i=src_ap: e.activation(o, i, AF.Square),
                           waits=w0, inc="ea")
            t_l = sch.add("scalar", lambda e, o=lt, i=sqt, b=cbc(CB_T10): e.activation(o, i, AF.Ln, bias=b),
                          waits=[t_sq], inc="ea")
            clamp = float(np.float32((min_exp + 0.55) * 2 * LN2C))
            cc = C22
        else:
            t_l = sch.add("scalar", lambda e, o=lt, i=src_ap, b=cbc(CB_T4): e.activation(o, i, AF.Ln, bias=b),
                          waits=w0, inc="ea")
            t_sq = None
            clamp = float(np.float32((min_exp + 0.55) * LN2C))
            cc = C2
        hsem = "ev" if helper == "vector" else "eg"
        t_e = sch.add(helper, lambda e, o=et, i=lt, c=cc: e.tensor_scalar(o, i, c, 0.5, op0=ALU.mult, op1=ALU.subtract),
                      waits=[t_l], inc=hsem)
        t_ef = sch.add(helper, lambda e, o=ef, i=et, c=float(min_exp): e.tensor_scalar(o, i, c, None, op0=ALU.max),
                       waits=[t_e], inc=hsem)
        t_inv = sch.add("scalar", lambda e, o=it_, i=ef, b=cbc(cbp): e.activation(o, i, AF.Exp, scale=-LN2C, bias=b),
                        waits=[t_ef], inc="ea")
        t_g = sch.add("scalar", lambda e, o=gt, i=ef, b=cbc(cbm): e.activation(o, i, AF.Exp, scale=LN2C, bias=b),
                      waits=[t_ef], inc="ea")
        t_y = sch.add("vector", lambda e, o=et, a=src_ap, b=it_: e.tensor_tensor(o, a, b, ALU.mult),
                      waits=[t_inv], inc="ev")
        t_q = sch.add("vector", lambda e, o=out_ap, a=et, b=gt: e.tensor_tensor(o, a, b, ALU.mult),
                      waits=[t_g, t_y], inc="ev")
        return t_q, t_y

    def ln_sums(x_tile_of, waits_extra=()):
        tok_mm1 = tok_mm2 = None
        for c in range(DC):
            xc = x_tile_of(c)
            sq = lnsq[c % 2][:, :]
            tw = [w for w in [lnsq_tok[c % 2]] if w] + (list(waits_extra) if c == 0 else [])
            tsq = sch.add("vector", lambda e, o=sq, i=xc: e.tensor_tensor(o, i, i, ALU.mult),
                          waits=tw, inc="ev")
            w1_ = ([ln_bank_free[0], t_o3] + list(waits_extra)) if c == 0 else []
            tok_mm1 = sch.add("tensor",
                              lambda e, r=xc, s=(c == 0), p=(c == 7): e.matmul(pL1[0:1, :], onescol[:, :], r, start=s, stop=p),
                              waits=[w for w in w1_ if w], inc="mm" if c == 7 else None)
            w2_ = [tsq] + ([ln_bank_free[1]] if c == 0 and ln_bank_free[1] else [])
            tok_mm2 = sch.add("tensor",
                              lambda e, r=sq, s=(c == 0), p=(c == 7): e.matmul(pL2[0:1, :], onescol[:, :], r, start=s, stop=p),
                              waits=w2_, inc="mm")
            lnsq_tok[0] = lnsq_tok[1] = tok_mm2
        return tok_mm1, tok_mm2

    def ln_rows(sums_toks):
        tok_mm1, tok_mm2 = sums_toks
        t1 = sch.add("vector", lambda e: e.tensor_scalar(m_row[:, :], pL1[0:1, :], 1.0 / D, None, op0=ALU.mult),
                     waits=[tok_mm1], inc="ev")
        t2 = sch.add("vector", lambda e: e.tensor_scalar(s2_row[:, :], pL2[0:1, :], 1.0 / D, None, op0=ALU.mult),
                     waits=[tok_mm2], inc="ev")
        t3 = sch.add("vector", lambda e: e.tensor_tensor(mm_row[:, :], m_row[:, :], m_row[:, :], ALU.mult),
                     waits=[t1], inc="ev")
        t4 = sch.add("vector", lambda e: e.tensor_tensor(mm_row[:, :], s2_row[:, :], mm_row[:, :], ALU.subtract),
                     waits=[t2, t3], inc="ev")
        t5 = sch.add("scalar", lambda e: e.activation(mm_row[:, :], mm_row[:, :], AF.Ln, bias=cb[0:1, CB_EPS:CB_EPS + 1]),
                     waits=[t4], inc="ea")
        t6 = sch.add("scalar", lambda e: e.activation(rstd_row[:, :], mm_row[:, :], AF.Exp, scale=-0.5),
                     waits=[t5], inc="ea")
        t7 = sch.add("vector", lambda e: e.tensor_copy(a_rowb[:, :], rstd_row[:, :]), waits=[t6], inc="ev")
        t8 = sch.add("vector", lambda e: e.tensor_tensor(nb_rowf[:, :], m_row[:, :], rstd_row[:, :], ALU.mult),
                     waits=[t6], inc="ev")
        t9 = sch.add("vector", lambda e: e.tensor_copy(nb_rowb[:, :], nb_rowf[:, :]), waits=[t8], inc="ev")
        return (t1, t2, t7, t9)

    def ln_tail(x_tile_of, xn_buf, rows_toks, xn_guard=()):
        t1, t2, t7, t9 = rows_toks
        ta = sch.add("tensor", lambda e: e.matmul(pL1[:, :], ones1[0:1, 0:128], a_rowb[:, :], start=True, stop=True),
                     waits=[t7, t_o1], inc="mm")
        tb = sch.add("tensor", lambda e: e.matmul(pL2[:, :], negones[0:1, 0:128], nb_rowb[:, :], start=True, stop=True),
                     waits=[t9, t_o2], inc="mm")
        tc = sch.add("vector", lambda e: e.tensor_copy(ab_b[:, :], pL1[:, :]), waits=[ta], inc="ev")
        td = sch.add("vector", lambda e: e.tensor_copy(bb_b[:, :], pL2[:, :]), waits=[tb], inc="ev")
        ln_bank_free[0] = tc
        ln_bank_free[1] = td
        tok_xn = []
        for c in range(DC):
            xc = x_tile_of(c)
            dst = xn_buf[:, c * T:(c + 1) * T]
            tm = sch.add("vector", lambda e, o=dst, i=xc: e.tensor_tensor(o, i, ab_b[:, :], ALU.mult),
                         waits=[tc] + (list(xn_guard) if c == 0 else []), inc="ev")
            tn = sch.add("vector", lambda e, o=dst: e.tensor_tensor(o, o, bb_b[:, :], ALU.add),
                         waits=[td, tm], inc="ev")
            tok_xn.append(tn)
        return tok_xn

    bank_free = {k: None for k in ["pA", "pB", "pC", "pD", "pK1", "pK2"]}
    ln_bank_free = [None, None]
    xn_free = [None, None]
    _lt0 = [None]
    lnsq_tok = _LnsqTok = type("LT", (), {})()
    lnsq_tok = _lt0 * 1 if False else [None, None]
    sqs_tok = [None, None]
    temps_free = [None, None, None]

    tok_bias = sch.add("sync", lambda e: e.dma_start(out=bias_sb[:, :], in_=bias_h.ap()), inc="ds", n=16)
    tok_cb = sch.add("sync", lambda e: e.dma_start(out=cb[:, :], in_=cb_h.ap()), inc="ds", n=16)
    tok_sel = sch.add("gpsimd", lambda e: e.dma_start(out=sel_sb[:, :], in_=sel_h.ap()), inc="dl", n=16)
    t_o1 = sch.add("vector", lambda e: e.memset(ones1[:, :], 1.0), inc="ev")
    t_o2 = sch.add("vector", lambda e: e.memset(negones[:, :], -1.0), inc="ev")
    t_o3 = sch.add("vector", lambda e: e.memset(onescol[:, :], 1.0), inc="ev")

    tok_bq, _ = quantize(bias_sb[:, 40:48], bq_sb[:, :], man=10, signed=True, bufi=0, n=8,
                         waits=[tok_bias, tok_cb])

    spill_toks = []

    def spill(dst_view, src_view, stage_ap, waits):
        tg = sch.add("gpsimd", lambda e, o=stage_ap, i=src_view: e.dma_start(out=o, in_=i),
                     waits=waits, inc="dl", n=16)
        ts_ = sch.add("sync", lambda e, o=dst_view, i=stage_ap: e.dma_start(out=o, in_=i),
                      waits=[tg], inc="ds", n=16)
        return ts_

    stage_a = wstream[:, 0:4096]
    stage_b = wstream[:, 4096:8192]
    prev = [None, None]
    for jc in range(8):
        src = wqkv_h.ap()[:, jc * 128:(jc + 1) * 128].rearrange("(c p) q -> p c q", p=128)
        stg_ap = (stage_a if jc % 2 == 0 else stage_b)[:, 0:1024].rearrange("p (c q) -> p c q", c=8)
        tok = spill(sp_q.ap()[jc], src, stg_ap, [w for w in [prev[jc % 2]] if w])
        prev[jc % 2] = tok
        spill_toks.append(tok)
    for jc in range(8):
        src = wproj_h.ap()[:, jc * 128:(jc + 1) * 128].rearrange("(c p) q -> p c q", p=128)
        stg_ap = (stage_a if jc % 2 == 0 else stage_b)[:, 0:1024].rearrange("p (c q) -> p c q", c=8)
        tok = spill(sp_pr.ap()[jc], src, stg_ap, [prev[jc % 2]])
        prev[jc % 2] = tok
        spill_toks.append(tok)
    for s_ in range(4):
        src = wqkv_h.ap()[:, 1024 + s_ * 512: 1024 + (s_ + 1) * 512].rearrange("(c p) j -> p c j", p=128)
        stg_ap = (stage_a if s_ % 2 == 0 else stage_b).rearrange("p (c j) -> p c j", c=8)
        tok = spill(sp_kv.ap()[s_], src, stg_ap, [prev[s_ % 2]])
        prev[s_ % 2] = tok
        spill_toks.append(tok)
    for c in range(8):
        src = w1_h.ap()[c * 128:(c + 1) * 128, :]
        stg_flat = (stage_a if c % 2 == 0 else stage_b)
        tok_ld = sch.add("gpsimd", lambda e, o=stg_flat, i=src: e.dma_start(out=o, in_=i),
                         waits=[prev[c % 2]], inc="dl", n=16)
        tok = None
        for pc in range(8):
            dst = sp_w1.ap()[pc].rearrange("p (jj rest) -> p jj rest", jj=4)[:, :, c * 128:(c + 1) * 128]
            src2 = stg_flat[:, pc * 512:(pc + 1) * 512].rearrange("p (jj q) -> p jj q", jj=4)
            tok = sch.add("sync", lambda e, o=dst, i=src2: e.dma_start(out=o, in_=i),
                          waits=[tok_ld], inc="ds", n=16)
        prev[c % 2] = tok
        spill_toks.append(tok)
    w2stage = [xa_hq[:, 0:256].bitcast(F8E4), xa_hq[:, 256:512].bitcast(F8E4)]
    prevq = [None, None]
    for jc in range(HJC):
        for h in range(2):
            bi = (2 * jc + h) % 2
            src = w2_h.ap()[jc * 128:(jc + 1) * 128, h * 512:(h + 1) * 512]
            tok_ld = sch.add("sync", lambda e, o=sqs[bi][:, :], i=src: e.dma_start(out=o, in_=i),
                             waits=[w for w in [prevq[bi], sqs_tok[bi],
                                                tok_bq if (jc == 0) else None] if w],
                             inc="ds", n=16)
            tq, ty = quantize(sqs[bi][:, :], w2stage[bi][:, :], man=3, signed=True, bufi=bi, n=512,
                              waits=[tok_ld])
            sqs_tok[bi] = ty
            temps_free[bi] = tq
            dst = sp_w2.ap()[4 * h:4 * h + 4, :, jc * 128:(jc + 1) * 128].rearrange("a p q -> p a q")
            src2 = w2stage[bi].rearrange("p (a q) -> p a q", a=4)
            tok = sch.add("sync", lambda e, o=dst, i=src2: e.dma_start(out=o, in_=i),
                          waits=[tq], inc="ds", n=16)
            prevq[bi] = tok
            spill_toks.append(tok)

    if debug_taps:
        sch.add("sync", lambda e: e.dma_start(out=dbg_bq.ap(), in_=bq_sb[:, :]),
                waits=[tok_bq], inc="ou", n=16)
        tok_out_dbg = [16]
        sch.add("sync", lambda e: e.dma_start(out=dbg_w1.ap(), in_=sp_w1.ap()),
                waits=[spill_toks[-1]], inc="ou", n=16)
        tok_out_dbg[0] += 16
    else:
        tok_out_dbg = [0]

    tok_xa = [None]

    def load_x(s_, waits):
        src = x_h.ap()[s_].rearrange("(c p) t -> p c t", p=128)
        tgt = xa_hq if s_ % 2 == 0 else qf
        dst = tgt[:, :].rearrange("p (c t) -> p c t", c=8)
        return sch.add("gpsimd", lambda e, o=dst, i=src: e.dma_start(out=o, in_=i),
                       waits=waits, inc="dl", n=16)

    tok_xa[0] = load_x(0, [tok_sel] + [t for t in [prevq[0], prevq[1]] if t])

    qbuf_free = [None, None]
    kvbuf_free = [None, None]
    prbuf_free = [None, None]
    w1buf_free = [None, None]
    w2buf_free = [None, None]
    last_spill = spill_toks[-1]

    def spill_done_tok():
        return last_spill

    for lst in (qbuf_free, kvbuf_free, prbuf_free, w1buf_free, w2buf_free):
        lst[0] = lst[1] = spill_done_tok()


    tok_out_total = [0]
    hq_read_done = None
    tok_stgfree = [None, None]
    kfv_free = [None] * 4
    attn_over_qf_ok = {}
    x_next_tok = tok_xa[0]
    xb_read_done = None
    kv2_read_done = None
    bd_read_done = None
    zrow_free = None
    zb_free = [None, None]
    ab_free = None

    for s_ in range(NSEQ):
        x_ready = x_next_tok
        qf_free_at = hq_read_done
        XA = xa_hq if s_ % 2 == 0 else qf
        QF = qf if s_ % 2 == 0 else xa_hq

        def x_of_t(t):
            def x_of(c, t=t):
                return XA[:, c * S + t * T: c * S + t * T + T]
            return x_of

        extra0 = [x_ready] + [w for w in temps_free if w]
        sums_t = {0: ln_sums(x_of_t(0), waits_extra=extra0)}
        rows_t = {0: ln_rows(sums_t[0])}
        tails_t = {0: ln_tail(x_of_t(0), xn[0], rows_t[0],
                              xn_guard=[w for w in [xn_free[0]] if w])}
        for t in range(NT):
            tok_xn = tails_t[t]
            if t + 1 < NT:
                sums_t[t + 1] = ln_sums(x_of_t(t + 1))
                rows_t[t + 1] = ln_rows(sums_t[t + 1])

            qrot = [pA, pB, pC, pD]
            qrot_names = ["pA", "pB", "pC", "pD"]
            for jc in range(8):
                bi = jc % 2
                wg = [qbuf_free[bi]]
                if t == 0 and jc == 0:
                    wg += w1buf_free + w2buf_free
                tok_w = sch.add("sync", lambda e, o=qbuf[bi], i=sp_q.ap()[jc]: e.dma_start(out=o, in_=i),
                                waits=[w for w in wg if w], inc="ds", n=16)
                mmw = [tok_w, bank_free[qrot_names[jc % 4]]] + tok_xn
                tok_mm = None
                for c in range(DC):
                    tok_mm = sch.add("tensor",
                                     lambda e, l_=qbuf[bi][:, c * 128:(c + 1) * 128],
                                     r=xn[t % 2][:, c * T:(c + 1) * T], p=qrot[jc % 4],
                                     s0=(c == 0), s1=(c == 7): e.matmul(p[:, :], l_, r, start=s0, stop=s1),
                                     waits=[w for w in mmw if w] if c == 0 else [],
                                     inc="mm" if c == 7 else None)
                qbuf_free[bi] = tok_mm
                sqt = sqs[bi][:, :]
                tevq = sch.add("scalar", lambda e, o=sqt, i=qrot[jc % 4][:, :]: e.activation(o, i, AF.Square),
                               waits=[tok_mm] + [w for w in [sqs_tok[bi]] if w]
                               + ([w for w in [qf_free_at] if w] if (t == 0 and jc == 0) else []),
                               inc="ea")
                bank_free[qrot_names[jc % 4]] = tevq
                dst = QF[:, jc * S + t * T: jc * S + t * T + T]
                tevq2 = sch.add("scalar", lambda e, o=dst, i=sqt: e.activation(o, i, AF.Exp, scale=-0.5),
                                waits=[tevq], inc="ea")
                sqs_tok[bi] = tevq2
                if jc == 7:
                    qf_tile_done = tevq2

            if t + 1 < NT:
                guard = [w for w in [xn_free[(t + 1) % 2]] if w]
                tails_t[t + 1] = ln_tail(x_of_t(t + 1), xn[(t + 1) % 2], rows_t[t + 1], xn_guard=guard)

            pnb = [pA, pB, pC, pD]
            pnb_names = ["pA", "pB", "pC", "pD"]
            kv_evac_toks = {}
            for sl in range(4):
                bi = sl % 2
                tok_w = sch.add("sync", lambda e, o=kvbuf[bi], i=sp_kv.ap()[sl]: e.dma_start(out=o, in_=i),
                                waits=[w for w in [kvbuf_free[bi]] if w], inc="ds", n=16)
                last_mm_of_nb = {}
                for nb in range(4):
                    tok_mm = None
                    w0 = [tok_w, bank_free[pnb_names[nb]]]
                    for c in range(DC):
                        tok_mm = sch.add("tensor",
                                         lambda e, l_=xn[t % 2][:, c * T + nb * 128: c * T + nb * 128 + 128],
                                         r=kvbuf[bi][:, c * 512:(c + 1) * 512], p=pnb[nb],
                                         s0=(c == 0), s1=(c == 7): e.matmul(p[:, :], l_, r, start=s0, stop=s1),
                                         waits=[w for w in w0 if w] if c == 0 else [],
                                         inc="mm" if c == 7 else None)
                    last_mm_of_nb[nb] = tok_mm
                kvbuf_free[bi] = last_mm_of_nb[3]
                for nb in range(4):
                    tok_mm = last_mm_of_nb[nb]
                    if sl < 2:
                        sqt = sqs[nb % 2][:, :]
                        te1 = sch.add("scalar", lambda e, o=sqt, i=pnb[nb][:, :]: e.activation(o, i, AF.Square),
                                      waits=[tok_mm] + [w for w in [sqs_tok[nb % 2]] if w],
                                      inc="ea")
                        dst = kf_nb[nb][:, sl * 512:(sl + 1) * 512]
                        kf_guard = ([kfv_free[nb]] if kfv_free[nb] else []) + \
                                   ([temps_free[nb // 2]] if temps_free[nb // 2] else [])
                        te2 = sch.add("scalar", lambda e, o=dst, i=sqt: e.activation(o, i, AF.Exp, scale=-0.5),
                                      waits=[te1] + (kf_guard if (sl == 0 and t == 0) else
                                                     ([kfv_free[nb]] if (sl == 0 and kfv_free[nb]) else [])),
                                      inc="ea")
                        sqs_tok[nb % 2] = te2
                        bank_free[pnb_names[nb]] = te1
                        kv_evac_toks[("k", sl, nb)] = te2
                    else:
                        dst = v_nb[nb][:, (sl - 2) * 512:(sl - 1) * 512]
                        te = sch.add("vector", lambda e, o=dst, i=pnb[nb][:, :]: e.tensor_copy(o, i),
                                     waits=[tok_mm] + ([kfv_free[nb]] if (sl == 2 and kfv_free[nb]) else []),
                                     inc="ev")
                        bank_free[pnb_names[nb]] = te
                        kv_evac_toks[("v", sl, nb)] = te

            wv0 = [kv_evac_toks[(kk, slx, nbx)] for nbx in range(4)
                   for (kk, slx) in (("k", 0), ("k", 1), ("v", 2), ("v", 3))]
            if t == 0:
                tz_kv = sch.add("vector", lambda e: e.memset(kvacc_sb[:, :], 0.0), waits=wv0, inc="ev")
                tz_ks = sch.add("vector", lambda e: e.memset(ksacc_sb[:, :], 0.0), waits=[tz_kv], inc="ev")
            pKband = [pK1, pK2]
            pKname = ["pK1", "pK2"]
            for hp in range(8):
                for half in range(2):
                    pb = pKband[half]
                    rows = slice(half * 64, (half + 1) * 64)
                    w0 = ([w for w in wv0 if w] if (hp == 0) else []) + \
                         [w for w in [bank_free[pKname[half]]] if w]
                    tok_kv = None
                    for nb in range(4):
                        tok_kv = sch.add("tensor",
                                         lambda e, l_=kf_nb[nb][:, hp * 128 + half * 64: hp * 128 + half * 64 + 64],
                                         r=v_nb[nb][:, hp * 128 + half * 64: hp * 128 + half * 64 + 64],
                                         p=pb[rows, 0:64],
                                         s0=(nb == 0), s1=(nb == 3), tp=(0, 64) if half else None:
                                         e.matmul(p, l_, r, start=s0, stop=s1, tile_position=tp),
                                         waits=w0 if nb == 0 else [],
                                         inc="mm" if nb == 3 else None)
                    dst = kvacc_sb[rows, hp * 64:(hp + 1) * 64]
                    tacc = sch.add("vector",
                                   lambda e, o=dst, p=pb[rows, 0:64]:
                                   e.tensor_tensor(o, o, p, ALU.add),
                                   waits=[tok_kv] + ([tz_ks] if (t == 0 and hp == 0) else []), inc="ev")
                    bank_free[pKname[half]] = tacc
            for c in range(DC):
                pb = pKband[c % 2]
                w0 = [w for w in [bank_free[pKname[c % 2]], t_o3] if w]
                tok_ks = None
                for nb in range(4):
                    tok_ks = sch.add("tensor",
                                     lambda e, l_=kf_nb[nb][:, c * 128:(c + 1) * 128],
                                     p=pb[:, 0:1], s0=(nb == 0), s1=(nb == 3):
                                     e.matmul(p, l_, onescol[:, :], start=s0, stop=s1),
                                     waits=w0 if nb == 0 else [],
                                     inc="mm" if nb == 3 else None)
                dst = ksacc_sb[:, c:c + 1]
                tacc = sch.add("vector",
                               lambda e, o=dst, p=pb[:, 0:1]: e.tensor_tensor(o, o, p, ALU.add),
                               waits=[tok_ks], inc="ev")
                bank_free[pKname[c % 2]] = tacc
            for nb in range(4):
                kfv_free[nb] = tok_ks
            xn_free[t % 2] = tok_ks
            kvks_done = bank_free["pK1"] if bank_free["pK1"][1] > bank_free["pK2"][1] else bank_free["pK2"]

        w_prep = [kvks_done] + [w for w in [kv2_read_done, bd_read_done] if w]
        t_z0 = sch.add("vector", lambda e: e.memset(kv2[:, :], 0.0), waits=w_prep, inc="ev")
        t_z1 = sch.add("vector", lambda e: e.memset(bd[:, :], 0.0), waits=[t_z0], inc="ev")
        kv2_toks = []
        for c in range(DC):
            ta_ = sch.add("vector", lambda e, c=c: e.tensor_copy(kv2[0:64, c * 128: c * 128 + 64],
                                                                 kvacc_sb[0:64, c * 64:(c + 1) * 64]),
                          waits=[t_z0], inc="ev")
            tb_ = sch.add("vector", lambda e, c=c: e.tensor_copy(kv2[64:128, c * 128 + 64: c * 128 + 128],
                                                                 kvacc_sb[64:128, c * 64:(c + 1) * 64]),
                          waits=[t_z0], inc="ev")
            kv2_toks += [ta_, tb_]
        bd_toks = []
        for c in range(DC):
            ta_ = sch.add("vector", lambda e, c=c: e.tensor_copy(bd[0:64, c * 16 + 2 * c: c * 16 + 2 * c + 1],
                                                                 ksacc_sb[0:64, c:c + 1]),
                          waits=[t_z1], inc="ev")
            tb_ = sch.add("vector", lambda e, c=c: e.tensor_copy(bd[64:128, c * 16 + 2 * c + 1: c * 16 + 2 * c + 2],
                                                                 ksacc_sb[64:128, c:c + 1]),
                          waits=[t_z1], inc="ev")
            bd_toks += [ta_, tb_]
        kv2_done = kv2_toks[-1]
        bd_done = bd_toks[-1]
        if debug_taps:
            sch.add("sync", lambda e, s_=s_: e.dma_start(out=dbg_kv2.ap()[s_], in_=kv2[:, :]),
                    waits=[kv2_done], inc="ou", n=16)
            sch.add("sync", lambda e, s_=s_: e.dma_start(out=dbg_bd.ap()[s_], in_=bd[:, :]),
                    waits=[bd_done], inc="ou", n=16)
            tok_out_dbg[0] += 32

        qf_dump_gate = []
        if debug_taps:
            tqd = sch.add("sync", lambda e, s_=s_, q=QF: e.dma_start(out=dbg_qf.ap()[s_], in_=q[:, :]),
                    waits=[qf_tile_done], inc="ou", n=16)
            tok_out_dbg[0] += 16
            qf_dump_gate = [tqd]
        proj_last = None
        attn_toks_t = {}
        for t in range(NT):
            tok_zd = None
            for c in range(DC):
                w0 = [bd_done, qf_tile_done, bank_free["pC"]] if c == 0 else []
                tok_zd = sch.add("tensor",
                                 lambda e, l_=bd[:, c * 16:(c + 1) * 16],
                                 r=QF[:, c * S + t * T: c * S + t * T + T],
                                 s0=(c == 0), s1=(c == 7): e.matmul(pC[0:16, :], l_, r, start=s0, stop=s1),
                                 waits=[w for w in w0 if w], inc="mm" if c == 7 else None)
            tz1 = sch.add("vector", lambda e: e.tensor_scalar(z_f[:, :], pC[0:16, :], 1e-6, None, op0=ALU.add),
                          waits=[tok_zd] + [w for w in [zrow_free, temps_free[2]] if w], inc="ev")
            bank_free["pC"] = tz1
            tz2 = sch.add("vector", lambda e: e.reciprocal(z_f[:, :], z_f[:, :]), waits=[tz1], inc="ev")
            tz3 = sch.add("vector", lambda e: e.tensor_copy(z_b[:, :], z_f[:, :]), waits=[tz2], inc="ev")
            zrow_free = tz3
            if debug_taps:
                tzd = sch.add("sync", lambda e, s_=s_, t=t: e.dma_start(out=dbg_z.ap()[s_, t], in_=z_b[:, :]),
                              waits=[tz3], inc="ou", n=16)
                tok_out_dbg[0] += 16
                zrow_free = tzd
            attn_toks = []
            for c in range(DC):
                bi = 0
                tz4 = sch.add("tensor", lambda e, c=c: e.matmul(pC[:, :], sel_sb[:, c * 128:(c + 1) * 128],
                                                                z_b[:, :], start=True, stop=True),
                              waits=[tz3, kv2_done, bank_free["pC"]] + qf_dump_gate, inc="mm")
                tz5 = sch.add("vector", lambda e, o=zb_t[bi][:, :]: e.tensor_copy(o, pC[:, :]),
                              waits=[tz4] + [w for w in [zb_free[bi]] if w], inc="ev")
                bank_free["pC"] = tz5
                ta_ = sch.add("tensor", lambda e, c=c, r=QF[:, c * S + t * T: c * S + t * T + T]:
                              e.matmul(pD[:, :], kv2[:, c * 128:(c + 1) * 128], r, start=True, stop=True),
                              waits=[bank_free["pD"], kv2_done], inc="mm")
                dst = QF[:, c * S + t * T: c * S + t * T + T]
                tm_ = sch.add("vector", lambda e, o=dst, b=zb_t[bi][:, :]: e.tensor_tensor(o, pD[:, :], b, ALU.mult),
                              waits=[ta_, tz5], inc="ev")
                bank_free["pD"] = tm_
                zb_free[bi] = tm_
                attn_toks.append(tm_)
            attn_toks_t[t] = attn_toks
        dbg_gate = []
        if debug_taps:
            tdump = sch.add("sync", lambda e, s_=s_, q=QF: e.dma_start(out=dbg_at.ap()[s_], in_=q[:, :]),
                            waits=[tok for tl in attn_toks_t.values() for tok in tl], inc="ou", n=16)
            tok_out_dbg[0] += 16
            dbg_gate = [tdump]
        for t in range(NT):
            attn_toks = attn_toks_t[t]
            prot = [pA, pB, pK1, pK2]
            prot_names = ["pA", "pB", "pK1", "pK2"]
            for jc in range(8):
                bi = jc % 2
                wg = [prbuf_free[bi]]
                if t == 0 and jc == 0:
                    wg += qbuf_free + kvbuf_free
                tok_w = sch.add("sync", lambda e, o=prbuf[bi], i=sp_pr.ap()[jc]: e.dma_start(out=o, in_=i),
                                waits=[w for w in wg if w], inc="ds", n=16)
                ri = jc % 4
                w0 = [tok_w, bank_free[prot_names[ri]]] + attn_toks + dbg_gate
                tok_mm = None
                for cE in range(DC):
                    tok_mm = sch.add("tensor",
                                     lambda e, l_=prbuf[bi][:, cE * 128:(cE + 1) * 128],
                                     r=QF[:, cE * S + t * T: cE * S + t * T + T], p=prot[ri],
                                     s0=(cE == 0), s1=(cE == 7): e.matmul(p[:, :], l_, r, start=s0, stop=s1),
                                     waits=[w for w in w0 if w] if cE == 0 else [],
                                     inc="mm" if cE == 7 else None)
                prbuf_free[bi] = tok_mm
                dst = xb[:, jc * S + t * T: jc * S + t * T + T]
                xw = [tok_mm] + ([xb_read_done] if (xb_read_done and t == 0 and jc == 0) else [])
                te = sch.add("vector",
                             lambda e, o=dst, p=prot[ri], b=bias_sb[:, jc:jc + 1],
                             xc=XA[:, jc * S + t * T: jc * S + t * T + T]:
                             e.scalar_tensor_tensor(o, p[:, :], b, xc, op0=ALU.add, op1=ALU.add),
                             waits=[w for w in xw if w], inc="ev")
                bank_free[prot_names[ri]] = te
                proj_last = te
        if debug_taps:
            sch.add("sync", lambda e, s_=s_: e.dma_start(out=dbg_xb.ap()[s_], in_=xb[:, :]),
                    waits=[proj_last], inc="ou", n=16)
            tok_out_dbg[0] += 16
        attn_over_qf_ok[s_] = proj_last
        kv2_read_done = ta_
        bd_read_done = tok_zd
        xa_read_done = proj_last
        if s_ + 1 < NSEQ:
            x_next_tok = load_x(s_ + 1, [proj_last])

        def xb_of_t(t):
            def xb_of(c, t=t):
                return xb[:, c * S + t * T: c * S + t * T + T]
            return xb_of

        ln2_sums = ln_sums(xb_of_t(0), waits_extra=[proj_last])
        ln2_rows = ln_rows(ln2_sums)
        ln2_tail = {0: ln_tail(xb_of_t(0), xn[0], ln2_rows,
                               xn_guard=[w for w in [xn_free[0]] if w])}
        xb_last_read = None
        hqv_par = hq8[id(XA)]
        hq_toks_t = {}
        hq_free = [None, None]
        ffn1_last_mm = {}
        first_w2 = [True]

        def emit_ffn2_chunk(tp, eo):
            ebi = eo % 2
            pe_ = pC if ebi == 0 else pD
            pe_name = "pC" if ebi == 0 else "pD"
            wg = [w2buf_free[ebi]]
            if first_w2[0]:
                wg += kvbuf_free
                first_w2[0] = False
            tok_w2 = sch.add("sync", lambda e, o=w2buf[ebi], i=sp_w2.ap()[eo]: e.dma_start(out=o, in_=i),
                             waits=[w for w in wg if w], inc="ds", n=16)
            hqv = hqv_par[tp % 2]
            toks = hq_toks_t[tp]
            w0 = [tok_w2, bank_free[pe_name]]
            tok_mm = None
            for j in range(HJC):
                jw = ([w for w in w0 if w] if j == 0 else []) + \
                     ([toks[j // 2]] if eo == 0 else [])
                tok_mm = sch.add("tensor",
                                 lambda e, l_=w2buf[ebi][:, j * 128:(j + 1) * 128],
                                 r=hqv[:, j * T:(j + 1) * T], p=pe_,
                                 s0=(j == 0), s1=(j == HJC - 1): e.matmul(p[:, :], l_, r, start=s0, stop=s1),
                                 waits=jw, inc="mm" if j == HJC - 1 else None)
            w2buf_free[ebi] = tok_mm
            if eo == 7:
                hq_free[tp % 2] = tok_mm
            vtile = sqs[ebi][:, :]
            tv = sch.add("vector", lambda e, o=vtile, p=pe_, b=bq_sb[:, eo:eo + 1]:
                         e.tensor_scalar(o, p[:, :], b, None, op0=ALU.add),
                         waits=[tok_mm] + [w for w in [tok_stgfree[ebi], sqs_tok[ebi]] if w], inc="ev")
            bank_free[pe_name] = tv
            qdst = stg[ebi][:, :]
            tq, ty = quantize(vtile, qdst, man=10, signed=True, bufi=ebi, n=512,
                              waits=[tv] + [w for w in [temps_free[ebi]] if w])
            temps_free[ebi] = tq
            sqs_tok[ebi] = ty
            tr = sch.add("vector", lambda e, o=qdst, xc=xb[:, eo * S + tp * T: eo * S + tp * T + T]:
                         e.tensor_tensor(o, o, xc, ALU.add),
                         waits=[tq], inc="ev")
            tdo = sch.add("gpsimd", lambda e, i=qdst, o=out_h.ap()[s_, eo * 128:(eo + 1) * 128, tp * T:(tp + 1) * T]:
                          e.dma_start(out=o, in_=i),
                          waits=[tr], inc="ou", n=16)
            tok_out_total[0] += 16
            tok_stgfree[ebi] = tdo
            return tok_mm, tr

        for t in range(NT):
            tok_xn2 = ln2_tail[t]
            if t + 1 < NT:
                ln2_sums_next = ln_sums(xb_of_t(t + 1))
                ln2_rows_next = ln_rows(ln2_sums_next)

            frot = [pA, pB, pK1, pK2]
            frot_names = ["pA", "pB", "pK1", "pK2"]
            hq_toks = []
            hq_toks_t[t] = hq_toks
            for grp in range(16):
                gbuf = grp % 2
                if grp % 2 == 0:
                    pc = grp // 2
                    wbi = pc % 2
                    wg = [w1buf_free[wbi]]
                    if t == 0 and grp == 0:
                        wg += prbuf_free + kvbuf_free
                    tok_w1 = sch.add("sync", lambda e, o=w1buf[wbi], i=sp_w1.ap()[pc]: e.dma_start(out=o, in_=i),
                                     waits=[w for w in wg if w], inc="ds", n=16)
                    w1tok_cur = tok_w1
                grp_last_mm = []
                for jj in range(2):
                    j = grp * 2 + jj
                    bi = j % 2
                    pcbuf = (j // 4) % 2
                    off = (j % 4) * 1024
                    ri = j % 4
                    w0 = [w1tok_cur, bank_free[frot_names[ri]]] + (tok_xn2 if j == 0 else [])
                    tok_mm = None
                    for c in range(DC):
                        tok_mm = sch.add("tensor",
                                         lambda e, l_=w1buf[pcbuf][:, off + c * 128: off + (c + 1) * 128],
                                         r=xn[t % 2][:, c * T:(c + 1) * T], p=frot[ri],
                                         s0=(c == 0), s1=(c == 7): e.matmul(p[:, :], l_, r, start=s0, stop=s1),
                                         waits=[w for w in w0 if w] if c == 0 else [],
                                         inc="mm" if c == 7 else None)
                    grp_last_mm.append(tok_mm)
                    dst = r_t[gbuf][:, jj * T:(jj + 1) * T]
                    ew = [tok_mm] + ([w for w in [temps_free[gbuf]] if w] if jj == 0 or j % 2 == 1 else [])
                    if j % 2 == 0:
                        te = sch.add("vector",
                                     lambda e, o=dst, p=frot[ri], b=bias_sb[:, 8 + j:8 + j + 1]:
                                     e.tensor_scalar(o, p[:, :], b, 0.0, op0=ALU.add, op1=ALU.max),
                                     waits=ew, inc="ev")
                    else:
                        te = sch.add("scalar",
                                     lambda e, o=dst, p=frot[ri], b=bias_sb[:, 8 + j:8 + j + 1]:
                                     e.activation(o, p[:, :], AF.Relu, bias=b),
                                     waits=ew, inc="ea")
                    bank_free[frot_names[ri]] = te
                    if jj == 0:
                        evac_done0 = te
                    else:
                        evac_done = te
                if grp % 2 == 1:
                    w1buf_free[(grp // 2) % 2] = grp_last_mm[-1]
                hq_dst = hqv_par[t % 2][:, grp * 1024:(grp + 1) * 1024]
                efw = []
                if grp == 0:
                    efw = [xa_read_done] if t <= 1 else [w for w in [hq_free[t % 2]] if w]
                tq, _ = quantize(r_t[gbuf][:, :], hq_dst, man=3, signed=False, bufi=gbuf, n=1024,
                                 waits=[evac_done, evac_done0],
                                 extra_first_waits=efw)
                temps_free[gbuf] = tq
                hq_toks.append(tq)
                if grp == 2 and t + 1 < NT:
                    xg = xn_free[1] if t == 0 else ffn1_last_mm[t - 1]
                    ln2_tail[t + 1] = ln_tail(xb_of_t(t + 1), xn[(t + 1) % 2], ln2_rows_next,
                                              xn_guard=[w for w in [xg] if w])
                if t >= 1 and grp % 2 == 1:
                    _, tr = emit_ffn2_chunk(t - 1, grp // 2)
                    xb_last_read = tr
            ffn1_last_mm[t] = grp_last_mm[-1]

        for eo in range(DC):
            mmtok, tr = emit_ffn2_chunk(NT - 1, eo)
            xb_last_read = tr
        hq_read_done = mmtok
        xn_free[0] = ffn1_last_mm[NT - 2]
        xn_free[1] = ffn1_last_mm[NT - 1]
        xb_read_done = xb_last_read

    sch.add("sync", lambda e: e.nop(), waits=[("ou", tok_out_total[0] + tok_out_dbg[0])])

    sch.check()
    with es, pes, nc.Block() as block:
        sch.emit(block, sems)
    return nc


def _host_inputs(core, L0_attn_x, W_qkv, W_proj, b_proj, w1, b1, w2, b2):
    xs = L0_attn_x[core * NSEQ:(core + 1) * NSEQ]
    x_fm = np.ascontiguousarray(xs.transpose(0, 2, 1))
    biases = np.zeros((128, 48), dtype=np.float32)
    biases[:, 0:8] = b_proj.reshape(8, 128).T
    biases[:, 8:40] = b1.reshape(32, 128).T
    biases[:, 40:48] = b2.reshape(8, 128).T
    sel = np.zeros((16, 8 * 128), dtype=np.float32)
    for c in range(8):
        sel[2 * c, c * 128: c * 128 + 64] = 1.0
        sel[2 * c + 1, c * 128 + 64: c * 128 + 128] = 1.0
    cbv = np.zeros((128, 8), dtype=np.float32)
    cbv[:, 0] = -3 * LN2C
    cbv[:, 1] = 3 * LN2C
    cbv[:, 2] = -10 * LN2C
    cbv[:, 3] = 10 * LN2C
    cbv[:, 4] = 2.0 ** -80
    cbv[:, 5] = 2.0 ** -120
    cbv[:, 6] = 1e-5
    return {
        "x": x_fm,
        "wqkvT": np.ascontiguousarray(W_qkv.T),
        "wprojT": np.ascontiguousarray(W_proj.T),
        "w1T": np.ascontiguousarray(w1.T),
        "w2T": np.ascontiguousarray(w2.T),
        "biases": biases,
        "sel": sel,
        "cb": cbv,
    }


_NC_CACHE = {}


def kernel(L0_attn_x, L0_attn_W_qkv, L0_attn_W_proj, L0_attn_b_proj,
           L0_ffn_w1, L0_ffn_b1, L0_ffn_w2, L0_ffn_b2):
    args = [np.asarray(a, dtype=np.float32) for a in
            (L0_attn_x, L0_attn_W_qkv, L0_attn_W_proj, L0_attn_b_proj,
             L0_ffn_w1, L0_ffn_b1, L0_ffn_w2, L0_ffn_b2)]
    if "nc" not in _NC_CACHE:
        _NC_CACHE["nc"] = build_nc()
    nc = _NC_CACHE["nc"]
    in_maps = [_host_inputs(c, *args) for c in range(N_CORES)]
    res = run_bass_kernel_spmd(nc, in_maps, core_ids=list(range(N_CORES)))
    outs = [r["out"].transpose(0, 2, 1) for r in res.results]
    return np.ascontiguousarray(np.concatenate(outs, axis=0))


# version 13
# speedup vs baseline: 1.1345x; 1.0771x over previous
import numpy as np
import ml_dtypes
from contextlib import ExitStack

import concourse.bass as bass
import concourse.mybir as mybir
from concourse.bass_utils import run_bass_kernel_spmd

F32 = mybir.dt.float32
BF16 = mybir.dt.bfloat16
F8E4 = mybir.dt.float8e4
PM = mybir.MatmulPerfMode
I32 = mybir.dt.int32
AF = mybir.ActivationFunctionType
ALU = mybir.AluOpType

LN2C = float(np.log(2.0))

N_CORES = 8
NSEQ = 4
S = 2048
D = 1024
DC = D // 128
T = 512
NT = S // T
HJ = 4096
HJC = HJ // 128
F8 = np.float32


class Sched:

    ENGINES = ("tensor", "vector", "scalar", "gpsimd", "sync")

    def __init__(self):
        self.ops = {e: [] for e in self.ENGINES}
        self.cnt = {}

    def add(self, eng, fn, waits=(), inc=None, n=1):
        w = [t for t in waits if t is not None]
        tok = None
        if inc is not None:
            self.cnt[inc] = self.cnt.get(inc, 0) + n
            tok = (inc, self.cnt[inc])
        self.ops[eng].append({"fn": fn, "waits": w, "inc": inc, "n": n})
        return tok

    def check(self):
        idx = {e: 0 for e in self.ENGINES}
        cnt = {s: 0 for s in self.cnt}
        total = sum(len(v) for v in self.ops.values())
        done = 0
        progress = True
        while progress:
            progress = False
            for e in self.ENGINES:
                while idx[e] < len(self.ops[e]):
                    op = self.ops[e][idx[e]]
                    if all(cnt.get(s, 0) >= v for (s, v) in op["waits"]):
                        if op["inc"]:
                            cnt[op["inc"]] = cnt.get(op["inc"], 0) + op["n"]
                        idx[e] += 1
                        done += 1
                        progress = True
                    else:
                        break
        if done != total:
            stuck = {e: (idx[e], len(self.ops[e]),
                         self.ops[e][idx[e]]["waits"] if idx[e] < len(self.ops[e]) else None)
                     for e in self.ENGINES}
            raise RuntimeError(f"schedule deadlock: done={done}/{total} stuck={stuck} cnt_sample={dict(list(cnt.items())[:10])}")

    def emit(self, block, sems):
        for ename in self.ENGINES:
            ops = self.ops[ename]

            def make(ops=ops):
                def body(eng):
                    last_wait = {}
                    for op in ops:
                        for (s, v) in op["waits"]:
                            if last_wait.get(s, -1) < v:
                                eng.wait_ge(sems[s], v)
                                last_wait[s] = v
                        ins = op["fn"](eng)
                        if op["inc"] is not None:
                            assert ins is not None, "inc op must return instruction"
                            ins.then_inc(sems[op["inc"]], op["n"])
                return body

            getattr(block, ename)(make())


def build_nc(debug_taps=False):
    nc = bass.Bass(target_bir_lowering=False)

    x_h = nc.dram_tensor("x", [NSEQ, D, S], F32, kind="ExternalInput")
    wqkv_h = nc.dram_tensor("wqkvT", [D, 3 * D], F32, kind="ExternalInput")
    wproj_h = nc.dram_tensor("wprojT", [D, D], F32, kind="ExternalInput")
    w1_h = nc.dram_tensor("w1T", [D, HJ], F32, kind="ExternalInput")
    w2_h = nc.dram_tensor("w2T", [HJ, D], F32, kind="ExternalInput")
    bias_h = nc.dram_tensor("biases", [128, 48], F32, kind="ExternalInput")
    sel_h = nc.dram_tensor("sel", [16, 8 * 128], F32, kind="ExternalInput")
    cb_h = nc.dram_tensor("cb", [128, 8], F32, kind="ExternalInput")
    out_h = nc.dram_tensor("out", [NSEQ, D, S], F32, kind="ExternalOutput")
    if debug_taps:
        dbg_bq = nc.dram_tensor("dbg_bq", [128, 8], F32, kind="ExternalOutput")
        dbg_w1 = nc.dram_tensor("dbg_w1", [8, 128, 4096], BF16, kind="ExternalOutput")
        dbg_xb = nc.dram_tensor("dbg_xb", [NSEQ, 128, 16384], BF16, kind="ExternalOutput")
        dbg_qf = nc.dram_tensor("dbg_qf", [NSEQ, 128, 16384], BF16, kind="ExternalOutput")
        dbg_kv2 = nc.dram_tensor("dbg_kv2", [NSEQ, 128, 1024], BF16, kind="ExternalOutput")
        dbg_bd = nc.dram_tensor("dbg_bd", [NSEQ, 128, 128], BF16, kind="ExternalOutput")
        dbg_at = nc.dram_tensor("dbg_at", [NSEQ, 128, 16384], BF16, kind="ExternalOutput")
        dbg_z = nc.dram_tensor("dbg_z", [NSEQ, NT, 16, T], BF16, kind="ExternalOutput")

    sp_q = nc.dram_tensor("sp_q", [8, 128, 1024], BF16, kind="Internal")
    sp_kv = nc.dram_tensor("sp_kv", [4, 128, 4096], BF16, kind="Internal")
    sp_pr = nc.dram_tensor("sp_pr", [8, 128, 1024], BF16, kind="Internal")
    sp_w1 = nc.dram_tensor("sp_w1", [8, 128, 4096], BF16, kind="Internal")
    sp_w2 = nc.dram_tensor("sp_w2", [8, 128, 4096], F8E4, kind="Internal")

    es = ExitStack()

    def sb(name, shape, dt=BF16):
        return es.enter_context(nc.sbuf_tensor(name, shape, dt))

    wstream = sb("wstream", [128, 16384], BF16)
    xa_hq = sb("xa_hq", [128, 16384], BF16)
    qf = sb("qf", [128, 16384], BF16)
    xb = sb("xb", [128, 16384], BF16)
    r_t = [sb(f"r{i}", [128, 1024], F32) for i in range(3)]
    l_t = [sb(f"l{i}", [128, 1024], F32) for i in range(2)]
    e_t = [sb(f"e{i}", [128, 1024], I32) for i in range(2)]
    inv_t = [sb(f"inv{i}", [128, 1024], BF16) for i in range(3)]
    g_t = [sb(f"g{i}", [128, 1024], BF16) for i in range(3)]
    xn0 = sb("xn0", [128, 8 * T], BF16)
    xn = [xn0, None]
    kf_nb = [inv_t[0], g_t[0], inv_t[1], g_t[1]]
    kv2 = inv_t[2]
    hq8 = {id(xa_hq): [xa_hq[:, 0:8192].bitcast(F8E4), xa_hq[:, 8192:16384].bitcast(F8E4)],
           id(qf): [qf[:, 0:8192].bitcast(F8E4), qf[:, 8192:16384].bitcast(F8E4)]}
    z_f = r_t[2][0:16, 0:T]
    kvacc_sb = r_t[2][0:128, T:2 * T]
    z_b = g_t[2][0:16, 0:T]
    zb_t = [g_t[2][0:128, T:2 * T]]
    v_nb = [sb(f"v{i}", [128, 1024], BF16) for i in range(4)]
    stg = [sb(f"stg{i}", [128, T], F32) for i in range(2)]
    sqs = [sb(f"sqs{i}", [128, T], F32) for i in range(2)]
    lnsq0 = sb("lnsq0", [128, T], BF16)
    lnsq1 = sb("lnsq1", [128, T], BF16)
    lnsq = [lnsq0, lnsq1]
    bd = sb("bd", [128, 8 * 16], BF16)
    sel_sb = sb("sel_sb", [16, 8 * 128], BF16)
    bias_sb = sb("bias_sb", [128, 48], F32)
    bq_sb = sb("bq_sb", [128, 8], F32)
    cb = sb("cb_sb", [128, 8], F32)
    ones1 = sb("ones1", [1, 128], BF16)
    negones = sb("negones", [1, 128], BF16)
    onescol = sb("onescol", [128, 1], BF16)
    rowF = sb("rowF", [1, 3 * T], F32)
    rowBf = sb("rowBf", [1, 2 * T], BF16)
    m_row = rowF[0:1, 0:T]
    s2_row = rowF[0:1, T:2 * T]
    mm_row = rowF[0:1, 2 * T:3 * T]
    rstd_row = rowF[0:1, T:2 * T]
    nb_rowf = rowF[0:1, 0:T]
    a_rowb = rowBf[0:1, 0:T]
    nb_rowb = rowBf[0:1, T:2 * T]
    ab_b = sb("ab_b", [128, T], BF16)
    bb_b = sb("bb_b", [128, T], BF16)
    ksacc_sb = sb("ksacc", [128, 8], F32)

    pes = ExitStack()

    def ps(name, shape):
        return pes.enter_context(nc.psum_tensor(name, shape, F32))

    pA = ps("pA", [128, 512])
    pB = ps("pB", [128, 512])
    pC = ps("pC", [128, 512])
    pD = ps("pD", [128, 512])
    pK1 = ps("pK1", [128, 512])
    pK2 = ps("pK2", [128, 512])
    pL1 = ps("pL1", [128, 512])
    pL2 = ps("pL2", [128, 512])

    sem_names = ["dl", "ds", "mm", "ea", "ev", "ou", "eg"]
    sems = {n: es.enter_context(nc.semaphore(n)) for n in sem_names}

    sch = Sched()

    qbuf = [wstream[:, i * 1024:(i + 1) * 1024] for i in range(2)]
    kvbuf = [wstream[:, 2048 + i * 4096: 2048 + (i + 1) * 4096] for i in range(2)]
    prbuf = [wstream[:, i * 1024:(i + 1) * 1024] for i in range(2)]
    w1buf = [wstream[:, i * 4096:(i + 1) * 4096] for i in range(2)]
    w2buf = [wstream[:, 8192 + i * 2048: 8192 + (i + 1) * 2048].bitcast(F8E4) for i in range(2)]
    xn[1] = wstream[:, 12288:16384]

    TINY4 = float(2.0 ** -80)
    TINY10 = float(2.0 ** -120)
    CLAMP4 = float(np.float32(-5.45 * LN2C))
    CLAMP10 = float(np.float32(-13.45 * 2 * LN2C))
    C2 = float(np.float32(1.0 / LN2C))
    C22 = float(np.float32(1.0 / (2 * LN2C)))
    CB_M3, CB_P3, CB_M10, CB_P10, CB_T4, CB_T10, CB_EPS = range(7)

    def cbc(i):
        return cb[:, i:i + 1]

    def quantize(src_ap, out_ap, man, signed, bufi, n, waits, extra_first_waits=(), helper="vector"):
        lt, et, it_, gt = l_t[bufi][:, :n], e_t[bufi][:, :n], inv_t[bufi][:, :n], g_t[bufi][:, :n]
        ef = l_t[bufi][:, :n]
        w0 = list(waits) + list(extra_first_waits)
        min_exp = -6 if man == 3 else -14
        cbm = CB_M3 if man == 3 else CB_M10
        cbp = CB_P3 if man == 3 else CB_P10
        if signed:
            sqt = r_t[bufi][:, :n]
            t_sq = sch.add("scalar", lambda e, o=sqt, i=src_ap: e.activation(o, i, AF.Square),
                           waits=w0, inc="ea")
            t_l = sch.add("scalar", lambda e, o=lt, i=sqt, b=cbc(CB_T10): e.activation(o, i, AF.Ln, bias=b),
                          waits=[t_sq], inc="ea")
            clamp = float(np.float32((min_exp + 0.55) * 2 * LN2C))
            cc = C22
        else:
            t_l = sch.add("scalar", lambda e, o=lt, i=src_ap, b=cbc(CB_T4): e.activation(o, i, AF.Ln, bias=b),
                          waits=w0, inc="ea")
            t_sq = None
            clamp = float(np.float32((min_exp + 0.55) * LN2C))
            cc = C2
        hsem = "ev" if helper == "vector" else "eg"
        t_e = sch.add(helper, lambda e, o=et, i=lt, c=cc: e.tensor_scalar(o, i, c, 0.5, op0=ALU.mult, op1=ALU.subtract),
                      waits=[t_l], inc=hsem)
        t_ef = sch.add(helper, lambda e, o=ef, i=et, c=float(min_exp): e.tensor_scalar(o, i, c, None, op0=ALU.max),
                       waits=[t_e], inc=hsem)
        t_inv = sch.add("scalar", lambda e, o=it_, i=ef, b=cbc(cbp): e.activation(o, i, AF.Exp, scale=-LN2C, bias=b),
                        waits=[t_ef], inc="ea")
        t_g = sch.add("scalar", lambda e, o=gt, i=ef, b=cbc(cbm): e.activation(o, i, AF.Exp, scale=LN2C, bias=b),
                      waits=[t_ef], inc="ea")
        t_y = sch.add("vector", lambda e, o=et, a=src_ap, b=it_: e.tensor_tensor(o, a, b, ALU.mult),
                      waits=[t_inv], inc="ev")
        t_q = sch.add("vector", lambda e, o=out_ap, a=et, b=gt: e.tensor_tensor(o, a, b, ALU.mult),
                      waits=[t_g, t_y], inc="ev")
        return t_q, t_y

    def ln_sums(x_tile_of, waits_extra=()):
        tok_mm1 = tok_mm2 = None
        for c in range(DC):
            xc = x_tile_of(c)
            sq = lnsq[c % 2][:, :]
            tw = [w for w in [lnsq_tok[c % 2]] if w] + (list(waits_extra) if c == 0 else [])
            tsq = sch.add("scalar", lambda e, o=sq, i=xc: e.activation(o, i, AF.Square),
                          waits=tw, inc="ea")
            w1_ = ([ln_bank_free[0], t_o3] + list(waits_extra)) if c == 0 else []
            tok_mm1 = sch.add("tensor",
                              lambda e, r=xc, s=(c == 0), p=(c == 7): e.matmul(pL1[0:1, :], onescol[:, :], r, start=s, stop=p),
                              waits=[w for w in w1_ if w], inc="mm" if c == 7 else None)
            w2_ = [tsq] + ([ln_bank_free[1]] if c == 0 and ln_bank_free[1] else [])
            tok_mm2 = sch.add("tensor",
                              lambda e, r=sq, s=(c == 0), p=(c == 7): e.matmul(pL2[0:1, :], onescol[:, :], r, start=s, stop=p),
                              waits=w2_, inc="mm")
            lnsq_tok[c % 2] = tok_mm2
        return tok_mm1, tok_mm2

    def ln_rows(sums_toks):
        tok_mm1, tok_mm2 = sums_toks
        t1 = sch.add("vector", lambda e: e.tensor_scalar(m_row[:, :], pL1[0:1, :], 1.0 / D, None, op0=ALU.mult),
                     waits=[tok_mm1], inc="ev")
        t2 = sch.add("vector", lambda e: e.tensor_scalar(s2_row[:, :], pL2[0:1, :], 1.0 / D, None, op0=ALU.mult),
                     waits=[tok_mm2], inc="ev")
        t3 = sch.add("vector", lambda e: e.tensor_tensor(mm_row[:, :], m_row[:, :], m_row[:, :], ALU.mult),
                     waits=[t1], inc="ev")
        t4 = sch.add("vector", lambda e: e.tensor_tensor(mm_row[:, :], s2_row[:, :], mm_row[:, :], ALU.subtract),
                     waits=[t2, t3], inc="ev")
        t5 = sch.add("scalar", lambda e: e.activation(mm_row[:, :], mm_row[:, :], AF.Ln, bias=cb[0:1, CB_EPS:CB_EPS + 1]),
                     waits=[t4], inc="ea")
        t6 = sch.add("scalar", lambda e: e.activation(rstd_row[:, :], mm_row[:, :], AF.Exp, scale=-0.5),
                     waits=[t5], inc="ea")
        t7 = sch.add("vector", lambda e: e.tensor_copy(a_rowb[:, :], rstd_row[:, :]), waits=[t6], inc="ev")
        t8 = sch.add("vector", lambda e: e.tensor_tensor(nb_rowf[:, :], m_row[:, :], rstd_row[:, :], ALU.mult),
                     waits=[t6], inc="ev")
        t9 = sch.add("vector", lambda e: e.tensor_copy(nb_rowb[:, :], nb_rowf[:, :]), waits=[t8], inc="ev")
        return (t1, t2, t7, t9)

    def ln_tail(x_tile_of, xn_buf, rows_toks, xn_guard=()):
        t1, t2, t7, t9 = rows_toks
        ta = sch.add("tensor", lambda e: e.matmul(pL1[:, :], ones1[0:1, 0:128], a_rowb[:, :], start=True, stop=True),
                     waits=[t7, t_o1], inc="mm")
        tb = sch.add("tensor", lambda e: e.matmul(pL2[:, :], negones[0:1, 0:128], nb_rowb[:, :], start=True, stop=True),
                     waits=[t9, t_o2], inc="mm")
        tc = sch.add("vector", lambda e: e.tensor_copy(ab_b[:, :], pL1[:, :]), waits=[ta], inc="ev")
        td = sch.add("vector", lambda e: e.tensor_copy(bb_b[:, :], pL2[:, :]), waits=[tb], inc="ev")
        ln_bank_free[0] = tc
        ln_bank_free[1] = td
        tok_xn = []
        for c in range(DC):
            xc = x_tile_of(c)
            dst = xn_buf[:, c * T:(c + 1) * T]
            tm = sch.add("vector", lambda e, o=dst, i=xc: e.tensor_tensor(o, i, ab_b[:, :], ALU.mult),
                         waits=[tc] + (list(xn_guard) if c == 0 else []), inc="ev")
            tn = sch.add("vector", lambda e, o=dst: e.tensor_tensor(o, o, bb_b[:, :], ALU.add),
                         waits=[td, tm], inc="ev")
            tok_xn.append(tn)
        return tok_xn

    bank_free = {k: None for k in ["pA", "pB", "pC", "pD", "pK1", "pK2"]}
    ln_bank_free = [None, None]
    xn_free = [None, None]
    lnsq_tok = [None, None]
    sqs_tok = [None, None]
    temps_free = [None, None, None]

    tok_bias = sch.add("sync", lambda e: e.dma_start(out=bias_sb[:, :], in_=bias_h.ap()), inc="ds", n=16)
    tok_cb = sch.add("sync", lambda e: e.dma_start(out=cb[:, :], in_=cb_h.ap()), inc="ds", n=16)
    tok_sel = sch.add("gpsimd", lambda e: e.dma_start(out=sel_sb[:, :], in_=sel_h.ap()), inc="dl", n=16)
    t_o1 = sch.add("vector", lambda e: e.memset(ones1[:, :], 1.0), inc="ev")
    t_o2 = sch.add("vector", lambda e: e.memset(negones[:, :], -1.0), inc="ev")
    t_o3 = sch.add("vector", lambda e: e.memset(onescol[:, :], 1.0), inc="ev")

    tok_bq, _ = quantize(bias_sb[:, 40:48], bq_sb[:, :], man=10, signed=True, bufi=0, n=8,
                         waits=[tok_bias, tok_cb])

    spill_toks = []

    def spill(dst_view, src_view, stage_ap, waits):
        tg = sch.add("gpsimd", lambda e, o=stage_ap, i=src_view: e.dma_start(out=o, in_=i),
                     waits=waits, inc="dl", n=16)
        ts_ = sch.add("sync", lambda e, o=dst_view, i=stage_ap: e.dma_start(out=o, in_=i),
                      waits=[tg], inc="ds", n=16)
        return ts_

    stage_a = wstream[:, 0:4096]
    stage_b = wstream[:, 4096:8192]
    prev = [None, None]
    for jc in range(8):
        src = wqkv_h.ap()[:, jc * 128:(jc + 1) * 128].rearrange("(c p) q -> p c q", p=128)
        stg_ap = (stage_a if jc % 2 == 0 else stage_b)[:, 0:1024].rearrange("p (c q) -> p c q", c=8)
        tok = spill(sp_q.ap()[jc], src, stg_ap, [w for w in [prev[jc % 2]] if w])
        prev[jc % 2] = tok
        spill_toks.append(tok)
    for jc in range(8):
        src = wproj_h.ap()[:, jc * 128:(jc + 1) * 128].rearrange("(c p) q -> p c q", p=128)
        stg_ap = (stage_a if jc % 2 == 0 else stage_b)[:, 0:1024].rearrange("p (c q) -> p c q", c=8)
        tok = spill(sp_pr.ap()[jc], src, stg_ap, [prev[jc % 2]])
        prev[jc % 2] = tok
        spill_toks.append(tok)
    for s_ in range(4):
        src = wqkv_h.ap()[:, 1024 + s_ * 512: 1024 + (s_ + 1) * 512].rearrange("(c p) j -> p c j", p=128)
        stg_ap = (stage_a if s_ % 2 == 0 else stage_b).rearrange("p (c j) -> p c j", c=8)
        tok = spill(sp_kv.ap()[s_], src, stg_ap, [prev[s_ % 2]])
        prev[s_ % 2] = tok
        spill_toks.append(tok)
    for c in range(8):
        src = w1_h.ap()[c * 128:(c + 1) * 128, :]
        stg_flat = (stage_a if c % 2 == 0 else stage_b)
        tok_ld = sch.add("gpsimd", lambda e, o=stg_flat, i=src: e.dma_start(out=o, in_=i),
                         waits=[prev[c % 2]], inc="dl", n=16)
        tok = None
        for pc in range(8):
            dst = sp_w1.ap()[pc].rearrange("p (jj rest) -> p jj rest", jj=4)[:, :, c * 128:(c + 1) * 128]
            src2 = stg_flat[:, pc * 512:(pc + 1) * 512].rearrange("p (jj q) -> p jj q", jj=4)
            tok = sch.add("sync", lambda e, o=dst, i=src2: e.dma_start(out=o, in_=i),
                          waits=[tok_ld], inc="ds", n=16)
        prev[c % 2] = tok
        spill_toks.append(tok)
    w2stage = [xa_hq[:, 0:256].bitcast(F8E4), xa_hq[:, 256:512].bitcast(F8E4)]
    prevq = [None, None]
    for jc in range(HJC):
        for h in range(2):
            bi = (2 * jc + h) % 2
            src = w2_h.ap()[jc * 128:(jc + 1) * 128, h * 512:(h + 1) * 512]
            tok_ld = sch.add("sync", lambda e, o=sqs[bi][:, :], i=src: e.dma_start(out=o, in_=i),
                             waits=[w for w in [prevq[bi], sqs_tok[bi],
                                                tok_bq if (jc == 0) else None] if w],
                             inc="ds", n=16)
            tq, ty = quantize(sqs[bi][:, :], w2stage[bi][:, :], man=3, signed=True, bufi=bi, n=512,
                              waits=[tok_ld])
            sqs_tok[bi] = ty
            temps_free[bi] = tq
            dst = sp_w2.ap()[4 * h:4 * h + 4, :, jc * 128:(jc + 1) * 128].rearrange("a p q -> p a q")
            src2 = w2stage[bi].rearrange("p (a q) -> p a q", a=4)
            tok = sch.add("sync", lambda e, o=dst, i=src2: e.dma_start(out=o, in_=i),
                          waits=[tq], inc="ds", n=16)
            prevq[bi] = tok
            spill_toks.append(tok)

    if debug_taps:
        sch.add("sync", lambda e: e.dma_start(out=dbg_bq.ap(), in_=bq_sb[:, :]),
                waits=[tok_bq], inc="ou", n=16)
        tok_out_dbg = [16]
        sch.add("sync", lambda e: e.dma_start(out=dbg_w1.ap(), in_=sp_w1.ap()),
                waits=[spill_toks[-1]], inc="ou", n=16)
        tok_out_dbg[0] += 16
    else:
        tok_out_dbg = [0]

    tok_xa = [None]

    def load_x(s_, waits):
        src = x_h.ap()[s_].rearrange("(c p) t -> p c t", p=128)
        tgt = xa_hq if s_ % 2 == 0 else qf
        dst = tgt[:, :].rearrange("p (c t) -> p c t", c=8)
        return sch.add("gpsimd", lambda e, o=dst, i=src: e.dma_start(out=o, in_=i),
                       waits=waits, inc="dl", n=16)

    tok_xa[0] = load_x(0, [tok_sel] + [t for t in [prevq[0], prevq[1]] if t])

    qbuf_free = [None, None]
    kvbuf_free = [None, None]
    prbuf_free = [None, None]
    w1buf_free = [None, None]
    w2buf_free = [None, None]
    last_spill = spill_toks[-1]

    def spill_done_tok():
        return last_spill

    for lst in (qbuf_free, kvbuf_free, prbuf_free, w1buf_free, w2buf_free):
        lst[0] = lst[1] = spill_done_tok()


    tok_out_total = [0]
    hq_read_done = None
    tok_stgfree = [None, None]
    kfv_free = [None] * 4
    attn_over_qf_ok = {}
    x_next_tok = tok_xa[0]
    xb_read_done = None
    kv2_read_done = None
    bd_read_done = None
    zrow_free = None
    zb_free = [None, None]
    ab_free = None
    ln1_boot = [None]

    for s_ in range(NSEQ):
        x_ready = x_next_tok
        qf_free_at = hq_read_done
        XA = xa_hq if s_ % 2 == 0 else qf
        QF = qf if s_ % 2 == 0 else xa_hq

        def x_of_t(t):
            def x_of(c, t=t):
                return XA[:, c * S + t * T: c * S + t * T + T]
            return x_of

        if ln1_boot[0] is not None:
            tails_t = {0: ln1_boot[0]}
            ln1_boot[0] = None
            sums_t, rows_t = {}, {}
        else:
            extra0 = [x_ready] + [w for w in temps_free if w]
            sums_t = {0: ln_sums(x_of_t(0), waits_extra=extra0)}
            rows_t = {0: ln_rows(sums_t[0])}
            tails_t = {0: ln_tail(x_of_t(0), xn[0], rows_t[0],
                                  xn_guard=[w for w in [xn_free[0]] if w])}
        for t in range(NT):
            tok_xn = tails_t[t]
            if t + 1 < NT:
                sums_t[t + 1] = ln_sums(x_of_t(t + 1))
                rows_t[t + 1] = ln_rows(sums_t[t + 1])

            qrot = [pA, pB, pC, pD]
            qrot_names = ["pA", "pB", "pC", "pD"]
            for jc in range(8):
                bi = jc % 2
                wg = [qbuf_free[bi]]
                if t == 0 and jc == 0:
                    wg += [w1buf_free[0]] + prbuf_free
                tok_w = sch.add("sync", lambda e, o=qbuf[bi], i=sp_q.ap()[jc]: e.dma_start(out=o, in_=i),
                                waits=[w for w in wg if w], inc="ds", n=16)
                mmw = [tok_w, bank_free[qrot_names[jc % 4]]] + tok_xn
                tok_mm = None
                for c in range(DC):
                    tok_mm = sch.add("tensor",
                                     lambda e, l_=qbuf[bi][:, c * 128:(c + 1) * 128],
                                     r=xn[t % 2][:, c * T:(c + 1) * T], p=qrot[jc % 4],
                                     s0=(c == 0), s1=(c == 7): e.matmul(p[:, :], l_, r, start=s0, stop=s1),
                                     waits=[w for w in mmw if w] if c == 0 else [],
                                     inc="mm" if c == 7 else None)
                qbuf_free[bi] = tok_mm
                sqt = sqs[bi][:, :]
                tevq = sch.add("scalar", lambda e, o=sqt, i=qrot[jc % 4][:, :]: e.activation(o, i, AF.Square),
                               waits=[tok_mm] + [w for w in [sqs_tok[bi]] if w]
                               + ([w for w in [qf_free_at] if w] if (t == 0 and jc == 0) else []),
                               inc="ea")
                bank_free[qrot_names[jc % 4]] = tevq
                dst = QF[:, jc * S + t * T: jc * S + t * T + T]
                tevq2 = sch.add("scalar", lambda e, o=dst, i=sqt: e.activation(o, i, AF.Exp, scale=-0.5),
                                waits=[tevq], inc="ea")
                sqs_tok[bi] = tevq2
                if jc == 7:
                    qf_tile_done = tevq2

            if t + 1 < NT:
                guard = [w for w in [xn_free[(t + 1) % 2]] if w]
                tails_t[t + 1] = ln_tail(x_of_t(t + 1), xn[(t + 1) % 2], rows_t[t + 1], xn_guard=guard)

            pnb = [pA, pB, pC, pD]
            pnb_names = ["pA", "pB", "pC", "pD"]
            kv_evac_toks = {}
            for sl in range(4):
                bi = sl % 2
                wgk = [kvbuf_free[bi]]
                if t == 0 and sl == 0:
                    wgk += [w1buf_free[1]]
                if t == 0 and sl == 1:
                    wgk += [w2buf_free[0]]
                tok_w = sch.add("sync", lambda e, o=kvbuf[bi], i=sp_kv.ap()[sl]: e.dma_start(out=o, in_=i),
                                waits=[w for w in wgk if w], inc="ds", n=16)
                last_mm_of_nb = {}
                for nb in range(4):
                    tok_mm = None
                    w0 = [tok_w, bank_free[pnb_names[nb]]]
                    for c in range(DC):
                        tok_mm = sch.add("tensor",
                                         lambda e, l_=xn[t % 2][:, c * T + nb * 128: c * T + nb * 128 + 128],
                                         r=kvbuf[bi][:, c * 512:(c + 1) * 512], p=pnb[nb],
                                         s0=(c == 0), s1=(c == 7): e.matmul(p[:, :], l_, r, start=s0, stop=s1),
                                         waits=[w for w in w0 if w] if c == 0 else [],
                                         inc="mm" if c == 7 else None)
                    last_mm_of_nb[nb] = tok_mm
                kvbuf_free[bi] = last_mm_of_nb[3]
                for nb in range(4):
                    tok_mm = last_mm_of_nb[nb]
                    if sl < 2:
                        sqt = sqs[nb % 2][:, :]
                        te1 = sch.add("scalar", lambda e, o=sqt, i=pnb[nb][:, :]: e.activation(o, i, AF.Square),
                                      waits=[tok_mm] + [w for w in [sqs_tok[nb % 2]] if w],
                                      inc="ea")
                        dst = kf_nb[nb][:, sl * 512:(sl + 1) * 512]
                        kf_guard = ([kfv_free[nb]] if kfv_free[nb] else []) + \
                                   ([temps_free[nb // 2]] if temps_free[nb // 2] else [])
                        te2 = sch.add("scalar", lambda e, o=dst, i=sqt: e.activation(o, i, AF.Exp, scale=-0.5),
                                      waits=[te1] + (kf_guard if (sl == 0 and t == 0) else
                                                     ([kfv_free[nb]] if (sl == 0 and kfv_free[nb]) else [])),
                                      inc="ea")
                        sqs_tok[nb % 2] = te2
                        bank_free[pnb_names[nb]] = te1
                        kv_evac_toks[("k", sl, nb)] = te2
                    else:
                        dst = v_nb[nb][:, (sl - 2) * 512:(sl - 1) * 512]
                        te = sch.add("vector", lambda e, o=dst, i=pnb[nb][:, :]: e.tensor_copy(o, i),
                                     waits=[tok_mm] + ([kfv_free[nb]] if (sl == 2 and kfv_free[nb]) else []),
                                     inc="ev")
                        bank_free[pnb_names[nb]] = te
                        kv_evac_toks[("v", sl, nb)] = te

            wv0 = [kv_evac_toks[(kk, slx, nbx)] for nbx in range(4)
                   for (kk, slx) in (("k", 0), ("k", 1), ("v", 2), ("v", 3))]
            if t == 0:
                tz_kv = sch.add("vector", lambda e: e.memset(kvacc_sb[:, :], 0.0), waits=wv0, inc="ev")
                tz_ks = sch.add("vector", lambda e: e.memset(ksacc_sb[:, :], 0.0), waits=[tz_kv], inc="ev")
            pKband = [pK1, pK2]
            pKname = ["pK1", "pK2"]
            for hp in range(8):
                for half in range(2):
                    pb = pKband[half]
                    rows = slice(half * 64, (half + 1) * 64)
                    w0 = ([w for w in wv0 if w] if (hp == 0) else []) + \
                         [w for w in [bank_free[pKname[half]]] if w]
                    tok_kv = None
                    for nb in range(4):
                        tok_kv = sch.add("tensor",
                                         lambda e, l_=kf_nb[nb][:, hp * 128 + half * 64: hp * 128 + half * 64 + 64],
                                         r=v_nb[nb][:, hp * 128 + half * 64: hp * 128 + half * 64 + 64],
                                         p=pb[rows, 0:64],
                                         s0=(nb == 0), s1=(nb == 3), tp=(0, 64) if half else None:
                                         e.matmul(p, l_, r, start=s0, stop=s1, tile_position=tp),
                                         waits=w0 if nb == 0 else [],
                                         inc="mm" if nb == 3 else None)
                    dst = kvacc_sb[rows, hp * 64:(hp + 1) * 64]
                    tacc = sch.add("vector",
                                   lambda e, o=dst, p=pb[rows, 0:64]:
                                   e.tensor_tensor(o, o, p, ALU.add),
                                   waits=[tok_kv] + ([tz_ks] if (t == 0 and hp == 0) else []), inc="ev")
                    bank_free[pKname[half]] = tacc
            for c in range(DC):
                pb = pKband[c % 2]
                w0 = [w for w in [bank_free[pKname[c % 2]], t_o3] if w]
                tok_ks = None
                for nb in range(4):
                    tok_ks = sch.add("tensor",
                                     lambda e, l_=kf_nb[nb][:, c * 128:(c + 1) * 128],
                                     p=pb[:, 0:1], s0=(nb == 0), s1=(nb == 3):
                                     e.matmul(p, l_, onescol[:, :], start=s0, stop=s1),
                                     waits=w0 if nb == 0 else [],
                                     inc="mm" if nb == 3 else None)
                dst = ksacc_sb[:, c:c + 1]
                tacc = sch.add("vector",
                               lambda e, o=dst, p=pb[:, 0:1]: e.tensor_tensor(o, o, p, ALU.add),
                               waits=[tok_ks], inc="ev")
                bank_free[pKname[c % 2]] = tacc
            for nb in range(4):
                kfv_free[nb] = tok_ks
            xn_free[t % 2] = tok_ks
            kvks_done = bank_free["pK1"] if bank_free["pK1"][1] > bank_free["pK2"][1] else bank_free["pK2"]

        w_prep = [kvks_done] + [w for w in [kv2_read_done, bd_read_done] if w]
        t_z0 = sch.add("vector", lambda e: e.memset(kv2[:, :], 0.0), waits=w_prep, inc="ev")
        t_z1 = sch.add("vector", lambda e: e.memset(bd[:, :], 0.0), waits=[t_z0], inc="ev")
        kv2_toks = []
        for c in range(DC):
            ta_ = sch.add("vector", lambda e, c=c: e.tensor_copy(kv2[0:64, c * 128: c * 128 + 64],
                                                                 kvacc_sb[0:64, c * 64:(c + 1) * 64]),
                          waits=[t_z0], inc="ev")
            tb_ = sch.add("vector", lambda e, c=c: e.tensor_copy(kv2[64:128, c * 128 + 64: c * 128 + 128],
                                                                 kvacc_sb[64:128, c * 64:(c + 1) * 64]),
                          waits=[t_z0], inc="ev")
            kv2_toks += [ta_, tb_]
        bd_toks = []
        for c in range(DC):
            ta_ = sch.add("vector", lambda e, c=c: e.tensor_copy(bd[0:64, c * 16 + 2 * c: c * 16 + 2 * c + 1],
                                                                 ksacc_sb[0:64, c:c + 1]),
                          waits=[t_z1], inc="ev")
            tb_ = sch.add("vector", lambda e, c=c: e.tensor_copy(bd[64:128, c * 16 + 2 * c + 1: c * 16 + 2 * c + 2],
                                                                 ksacc_sb[64:128, c:c + 1]),
                          waits=[t_z1], inc="ev")
            bd_toks += [ta_, tb_]
        kv2_done = kv2_toks[-1]
        bd_done = bd_toks[-1]
        if debug_taps:
            sch.add("sync", lambda e, s_=s_: e.dma_start(out=dbg_kv2.ap()[s_], in_=kv2[:, :]),
                    waits=[kv2_done], inc="ou", n=16)
            sch.add("sync", lambda e, s_=s_: e.dma_start(out=dbg_bd.ap()[s_], in_=bd[:, :]),
                    waits=[bd_done], inc="ou", n=16)
            tok_out_dbg[0] += 32

        qf_dump_gate = []
        if debug_taps:
            tqd = sch.add("sync", lambda e, s_=s_, q=QF: e.dma_start(out=dbg_qf.ap()[s_], in_=q[:, :]),
                    waits=[qf_tile_done], inc="ou", n=16)
            tok_out_dbg[0] += 16
            qf_dump_gate = [tqd]
        proj_last = None
        attn_toks_t = {}
        for t in range(NT):
            tok_zd = None
            for c in range(DC):
                w0 = [bd_done, qf_tile_done, bank_free["pC"]] if c == 0 else []
                tok_zd = sch.add("tensor",
                                 lambda e, l_=bd[:, c * 16:(c + 1) * 16],
                                 r=QF[:, c * S + t * T: c * S + t * T + T],
                                 s0=(c == 0), s1=(c == 7): e.matmul(pC[0:16, :], l_, r, start=s0, stop=s1),
                                 waits=[w for w in w0 if w], inc="mm" if c == 7 else None)
            tz1 = sch.add("vector", lambda e: e.tensor_scalar(z_f[:, :], pC[0:16, :], 1e-6, None, op0=ALU.add),
                          waits=[tok_zd] + [w for w in [zrow_free, temps_free[2]] if w], inc="ev")
            bank_free["pC"] = tz1
            tz2 = sch.add("vector", lambda e: e.reciprocal(z_f[:, :], z_f[:, :]), waits=[tz1], inc="ev")
            tz3 = sch.add("vector", lambda e: e.tensor_copy(z_b[:, :], z_f[:, :]), waits=[tz2], inc="ev")
            zrow_free = tz3
            if debug_taps:
                tzd = sch.add("sync", lambda e, s_=s_, t=t: e.dma_start(out=dbg_z.ap()[s_, t], in_=z_b[:, :]),
                              waits=[tz3], inc="ou", n=16)
                tok_out_dbg[0] += 16
                zrow_free = tzd
            attn_toks = []
            for c in range(DC):
                bi = 0
                tz4 = sch.add("tensor", lambda e, c=c: e.matmul(pC[:, :], sel_sb[:, c * 128:(c + 1) * 128],
                                                                z_b[:, :], start=True, stop=True),
                              waits=[tz3, kv2_done, bank_free["pC"]] + qf_dump_gate, inc="mm")
                tz5 = sch.add("vector", lambda e, o=zb_t[bi][:, :]: e.tensor_copy(o, pC[:, :]),
                              waits=[tz4] + [w for w in [zb_free[bi]] if w], inc="ev")
                bank_free["pC"] = tz5
                ta_ = sch.add("tensor", lambda e, c=c, r=QF[:, c * S + t * T: c * S + t * T + T]:
                              e.matmul(pD[:, :], kv2[:, c * 128:(c + 1) * 128], r, start=True, stop=True),
                              waits=[bank_free["pD"], kv2_done], inc="mm")
                dst = QF[:, c * S + t * T: c * S + t * T + T]
                tm_ = sch.add("vector", lambda e, o=dst, b=zb_t[bi][:, :]: e.tensor_tensor(o, pD[:, :], b, ALU.mult),
                              waits=[ta_, tz5], inc="ev")
                bank_free["pD"] = tm_
                zb_free[bi] = tm_
                attn_toks.append(tm_)
            attn_toks_t[t] = attn_toks
        dbg_gate = []
        if debug_taps:
            tdump = sch.add("sync", lambda e, s_=s_, q=QF: e.dma_start(out=dbg_at.ap()[s_], in_=q[:, :]),
                            waits=[tok for tl in attn_toks_t.values() for tok in tl], inc="ou", n=16)
            tok_out_dbg[0] += 16
            dbg_gate = [tdump]
        def xb_of_t(t):
            def xb_of(c, t=t):
                return xb[:, c * S + t * T: c * S + t * T + T]
            return xb_of

        ln2_tail = {}
        proj_te_t = {}
        for t in range(NT):
            attn_toks = attn_toks_t[t]
            prot = [pA, pB, pK1, pK2]
            prot_names = ["pA", "pB", "pK1", "pK2"]
            for jc in range(8):
                bi = jc % 2
                wg = [prbuf_free[bi]]
                if t == 0 and jc == 0:
                    wg += qbuf_free + kvbuf_free
                tok_w = sch.add("sync", lambda e, o=prbuf[bi], i=sp_pr.ap()[jc]: e.dma_start(out=o, in_=i),
                                waits=[w for w in wg if w], inc="ds", n=16)
                ri = jc % 4
                w0 = [tok_w, bank_free[prot_names[ri]]] + attn_toks + dbg_gate
                tok_mm = None
                for cE in range(DC):
                    tok_mm = sch.add("tensor",
                                     lambda e, l_=prbuf[bi][:, cE * 128:(cE + 1) * 128],
                                     r=QF[:, cE * S + t * T: cE * S + t * T + T], p=prot[ri],
                                     s0=(cE == 0), s1=(cE == 7): e.matmul(p[:, :], l_, r, start=s0, stop=s1),
                                     waits=[w for w in w0 if w] if cE == 0 else [],
                                     inc="mm" if cE == 7 else None)
                prbuf_free[bi] = tok_mm
                dst = xb[:, jc * S + t * T: jc * S + t * T + T]
                xw = [tok_mm] + ([xb_read_done] if (xb_read_done and t == 0 and jc == 0) else [])
                te = sch.add("vector",
                             lambda e, o=dst, p=prot[ri], b=bias_sb[:, jc:jc + 1],
                             xc=XA[:, jc * S + t * T: jc * S + t * T + T]:
                             e.scalar_tensor_tensor(o, p[:, :], b, xc, op0=ALU.add, op1=ALU.add),
                             waits=[w for w in xw if w], inc="ev")
                bank_free[prot_names[ri]] = te
                proj_last = te
            proj_te_t[t] = te
            if t == 0:
                ln2_sums0 = ln_sums(xb_of_t(0), waits_extra=[te])
                ln2_rows0 = ln_rows(ln2_sums0)
                ln2_tail[0] = ln_tail(xb_of_t(0), xn[0], ln2_rows0,
                                      xn_guard=[w for w in [xn_free[0]] if w])
        if debug_taps:
            sch.add("sync", lambda e, s_=s_: e.dma_start(out=dbg_xb.ap()[s_], in_=xb[:, :]),
                    waits=[proj_last], inc="ou", n=16)
            tok_out_dbg[0] += 16
        attn_over_qf_ok[s_] = proj_last
        kv2_read_done = ta_
        bd_read_done = tok_zd
        xa_read_done = proj_last
        if s_ + 1 < NSEQ:
            x_next_tok = load_x(s_ + 1, [proj_last])

        xb_last_read = None
        hqv_par = hq8[id(XA)]
        hq_toks_t = {}
        hq_free = [None, None]
        ffn1_last_mm = {}
        first_w2 = [True]

        def emit_ffn2_chunk(tp, eo):
            ebi = eo % 2
            pe_ = pC if ebi == 0 else pD
            pe_name = "pC" if ebi == 0 else "pD"
            wg = [w2buf_free[ebi]]
            if first_w2[0]:
                wg += kvbuf_free
                first_w2[0] = False
            tok_w2 = sch.add("sync", lambda e, o=w2buf[ebi], i=sp_w2.ap()[eo]: e.dma_start(out=o, in_=i),
                             waits=[w for w in wg if w], inc="ds", n=16)
            hqv = hqv_par[tp % 2]
            toks = hq_toks_t[tp]
            w0 = [tok_w2, bank_free[pe_name]]
            tok_mm = None
            for c in range(16):
                jw = ([w for w in w0 if w] if c == 0 else []) + \
                     ([toks[c]] if eo == 0 else [])
                tok_mm = sch.add("tensor",
                                 lambda e, l_=w2buf[ebi][:, c * 256:(c + 1) * 256].rearrange("p (i m) -> p i m", i=2),
                                 r=hqv[:, c * 1024:(c + 1) * 1024].rearrange("p (i n) -> p i n", i=2), p=pe_,
                                 s0=(c == 0), s1=(c == 15): e.matmul(p[:, :], l_, r, start=s0, stop=s1,
                                                                     perf_mode=PM.DoubleRow),
                                 waits=jw, inc="mm" if c == 15 else None)
            w2buf_free[ebi] = tok_mm
            if eo == 7:
                hq_free[tp % 2] = tok_mm
            vtile = sqs[ebi][:, :]
            tv = sch.add("vector", lambda e, o=vtile, p=pe_, b=bq_sb[:, eo:eo + 1]:
                         e.tensor_scalar(o, p[:, :], b, None, op0=ALU.add),
                         waits=[tok_mm] + [w for w in [tok_stgfree[ebi], sqs_tok[ebi]] if w], inc="ev")
            bank_free[pe_name] = tv
            qdst = stg[ebi][:, :]
            tq, ty = quantize(vtile, qdst, man=10, signed=True, bufi=ebi, n=512,
                              waits=[tv] + [w for w in [temps_free[ebi]] if w])
            temps_free[ebi] = tq
            sqs_tok[ebi] = ty
            tr = sch.add("vector", lambda e, o=qdst, xc=xb[:, eo * S + tp * T: eo * S + tp * T + T]:
                         e.tensor_tensor(o, o, xc, ALU.add),
                         waits=[tq], inc="ev")
            tdo = sch.add("gpsimd", lambda e, i=qdst, o=out_h.ap()[s_, eo * 128:(eo + 1) * 128, tp * T:(tp + 1) * T]:
                          e.dma_start(out=o, in_=i),
                          waits=[tr], inc="ou", n=16)
            tok_out_total[0] += 16
            tok_stgfree[ebi] = tdo
            return tok_mm, tr

        for t in range(NT):
            tok_xn2 = ln2_tail[t]
            if t + 1 < NT:
                ln2_sums_next = ln_sums(xb_of_t(t + 1), waits_extra=[proj_te_t[t + 1]])
                ln2_rows_next = ln_rows(ln2_sums_next)

            frot = [pA, pB, pK1, pK2]
            frot_names = ["pA", "pB", "pK1", "pK2"]
            hq_toks = []
            hq_toks_t[t] = hq_toks
            for grp in range(16):
                gbuf = grp % 2
                if grp % 2 == 0:
                    pc = grp // 2
                    wbi = pc % 2
                    wg = [w1buf_free[wbi]]
                    if t == 0 and grp == 0:
                        wg += prbuf_free + kvbuf_free
                    tok_w1 = sch.add("sync", lambda e, o=w1buf[wbi], i=sp_w1.ap()[pc]: e.dma_start(out=o, in_=i),
                                     waits=[w for w in wg if w], inc="ds", n=16)
                    w1tok_cur = tok_w1
                grp_last_mm = []
                for jj in range(2):
                    j = grp * 2 + jj
                    bi = j % 2
                    pcbuf = (j // 4) % 2
                    off = (j % 4) * 1024
                    ri = j % 4
                    w0 = [w1tok_cur, bank_free[frot_names[ri]]] + (tok_xn2 if j == 0 else [])
                    tok_mm = None
                    for c in range(DC):
                        tok_mm = sch.add("tensor",
                                         lambda e, l_=w1buf[pcbuf][:, off + c * 128: off + (c + 1) * 128],
                                         r=xn[t % 2][:, c * T:(c + 1) * T], p=frot[ri],
                                         s0=(c == 0), s1=(c == 7): e.matmul(p[:, :], l_, r, start=s0, stop=s1),
                                         waits=[w for w in w0 if w] if c == 0 else [],
                                         inc="mm" if c == 7 else None)
                    grp_last_mm.append(tok_mm)
                    dst = r_t[gbuf][:, jj * T:(jj + 1) * T]
                    ew = [tok_mm] + ([w for w in [temps_free[gbuf]] if w] if jj == 0 or j % 2 == 1 else [])
                    te = sch.add("scalar",
                                 lambda e, o=dst, p=frot[ri], b=bias_sb[:, 8 + j:8 + j + 1]:
                                 e.activation(o, p[:, :], AF.Relu, bias=b),
                                 waits=ew, inc="ea")
                    bank_free[frot_names[ri]] = te
                    if jj == 0:
                        evac_done0 = te
                    else:
                        evac_done = te
                if grp % 2 == 1:
                    w1buf_free[(grp // 2) % 2] = grp_last_mm[-1]
                hq_dst = hqv_par[t % 2][:, grp * 1024:(grp + 1) * 1024]
                efw = []
                if grp == 0:
                    efw = [xa_read_done] if t <= 1 else [w for w in [hq_free[t % 2]] if w]
                tq, _ = quantize(r_t[gbuf][:, :], hq_dst, man=3, signed=False, bufi=gbuf, n=1024,
                                 waits=[evac_done, evac_done0],
                                 extra_first_waits=efw)
                temps_free[gbuf] = tq
                hq_toks.append(tq)
                if grp == 2:
                    if t + 1 < NT:
                        xg = xn_free[1] if t == 0 else ffn1_last_mm[t - 1]
                        ln2_tail[t + 1] = ln_tail(xb_of_t(t + 1), xn[(t + 1) % 2], ln2_rows_next,
                                                  xn_guard=[w for w in [xg] if w])
                    elif s_ + 1 < NSEQ:
                        def x1_of(c, Q=QF):
                            return Q[:, c * S: c * S + T]
                        b_sums = ln_sums(x1_of, waits_extra=[x_next_tok])
                        b_rows = ln_rows(b_sums)
                        ln1_boot[0] = ln_tail(x1_of, xn[0], b_rows,
                                              xn_guard=[ffn1_last_mm[2]])
                if t >= 1 and grp % 2 == 1:
                    _, tr = emit_ffn2_chunk(t - 1, grp // 2)
                    xb_last_read = tr
            ffn1_last_mm[t] = grp_last_mm[-1]

        for eo in range(DC):
            mmtok, tr = emit_ffn2_chunk(NT - 1, eo)
            xb_last_read = tr
        hq_read_done = mmtok
        xn_free[0] = ffn1_last_mm[NT - 2]
        xn_free[1] = ffn1_last_mm[NT - 1]
        xb_read_done = xb_last_read

    sch.add("sync", lambda e: e.nop(), waits=[("ou", tok_out_total[0] + tok_out_dbg[0])])

    sch.check()
    with es, pes, nc.Block() as block:
        sch.emit(block, sems)
    return nc


def _host_inputs(core, L0_attn_x, W_qkv, W_proj, b_proj, w1, b1, w2, b2):
    xs = L0_attn_x[core * NSEQ:(core + 1) * NSEQ]
    x_fm = np.ascontiguousarray(xs.transpose(0, 2, 1))
    biases = np.zeros((128, 48), dtype=np.float32)
    biases[:, 0:8] = b_proj.reshape(8, 128).T
    biases[:, 8:40] = b1.reshape(32, 128).T
    biases[:, 40:48] = b2.reshape(8, 128).T
    sel = np.zeros((16, 8 * 128), dtype=np.float32)
    for c in range(8):
        sel[2 * c, c * 128: c * 128 + 64] = 1.0
        sel[2 * c + 1, c * 128 + 64: c * 128 + 128] = 1.0
    cbv = np.zeros((128, 8), dtype=np.float32)
    cbv[:, 0] = -3 * LN2C
    cbv[:, 1] = 3 * LN2C
    cbv[:, 2] = -10 * LN2C
    cbv[:, 3] = 10 * LN2C
    cbv[:, 4] = 2.0 ** -80
    cbv[:, 5] = 2.0 ** -120
    cbv[:, 6] = 1e-5
    return {
        "x": x_fm,
        "wqkvT": np.ascontiguousarray(W_qkv.T),
        "wprojT": np.ascontiguousarray(W_proj.T),
        "w1T": np.ascontiguousarray(w1.T),
        "w2T": np.ascontiguousarray(w2.T),
        "biases": biases,
        "sel": sel,
        "cb": cbv,
    }


_NC_CACHE = {}


def kernel(L0_attn_x, L0_attn_W_qkv, L0_attn_W_proj, L0_attn_b_proj,
           L0_ffn_w1, L0_ffn_b1, L0_ffn_w2, L0_ffn_b2):
    args = [np.asarray(a, dtype=np.float32) for a in
            (L0_attn_x, L0_attn_W_qkv, L0_attn_W_proj, L0_attn_b_proj,
             L0_ffn_w1, L0_ffn_b1, L0_ffn_w2, L0_ffn_b2)]
    if "nc" not in _NC_CACHE:
        _NC_CACHE["nc"] = build_nc()
    nc = _NC_CACHE["nc"]
    in_maps = [_host_inputs(c, *args) for c in range(N_CORES)]
    res = run_bass_kernel_spmd(nc, in_maps, core_ids=list(range(N_CORES)))
    outs = [r["out"].transpose(0, 2, 1) for r in res.results]
    return np.ascontiguousarray(np.concatenate(outs, axis=0))


# version 14
# speedup vs baseline: 1.1667x; 1.0284x over previous
import numpy as np
import ml_dtypes
from contextlib import ExitStack

import concourse.bass as bass
import concourse.mybir as mybir
from concourse.bass_utils import run_bass_kernel_spmd

F32 = mybir.dt.float32
BF16 = mybir.dt.bfloat16
F8E4 = mybir.dt.float8e4
PM = mybir.MatmulPerfMode
I32 = mybir.dt.int32
AF = mybir.ActivationFunctionType
ALU = mybir.AluOpType

LN2C = float(np.log(2.0))

N_CORES = 8
NSEQ = 4
S = 2048
D = 1024
DC = D // 128
T = 512
NT = S // T
HJ = 4096
HJC = HJ // 128
F8 = np.float32


class Sched:

    ENGINES = ("tensor", "vector", "scalar", "gpsimd", "sync")

    def __init__(self):
        self.ops = {e: [] for e in self.ENGINES}
        self.cnt = {}

    def add(self, eng, fn, waits=(), inc=None, n=1):
        w = [t for t in waits if t is not None]
        tok = None
        if inc is not None:
            self.cnt[inc] = self.cnt.get(inc, 0) + n
            tok = (inc, self.cnt[inc])
        self.ops[eng].append({"fn": fn, "waits": w, "inc": inc, "n": n})
        return tok

    def check(self):
        idx = {e: 0 for e in self.ENGINES}
        cnt = {s: 0 for s in self.cnt}
        total = sum(len(v) for v in self.ops.values())
        done = 0
        progress = True
        while progress:
            progress = False
            for e in self.ENGINES:
                while idx[e] < len(self.ops[e]):
                    op = self.ops[e][idx[e]]
                    if all(cnt.get(s, 0) >= v for (s, v) in op["waits"]):
                        if op["inc"]:
                            cnt[op["inc"]] = cnt.get(op["inc"], 0) + op["n"]
                        idx[e] += 1
                        done += 1
                        progress = True
                    else:
                        break
        if done != total:
            stuck = {e: (idx[e], len(self.ops[e]),
                         self.ops[e][idx[e]]["waits"] if idx[e] < len(self.ops[e]) else None)
                     for e in self.ENGINES}
            raise RuntimeError(f"schedule deadlock: done={done}/{total} stuck={stuck} cnt_sample={dict(list(cnt.items())[:10])}")

    def emit(self, block, sems):
        for ename in self.ENGINES:
            ops = self.ops[ename]

            def make(ops=ops):
                def body(eng):
                    last_wait = {}
                    for op in ops:
                        for (s, v) in op["waits"]:
                            if last_wait.get(s, -1) < v:
                                eng.wait_ge(sems[s], v)
                                last_wait[s] = v
                        ins = op["fn"](eng)
                        if op["inc"] is not None:
                            assert ins is not None, "inc op must return instruction"
                            ins.then_inc(sems[op["inc"]], op["n"])
                return body

            getattr(block, ename)(make())


def build_nc(debug_taps=False):
    nc = bass.Bass(target_bir_lowering=False)

    x_h = nc.dram_tensor("x", [NSEQ, D, S], F32, kind="ExternalInput")
    wqkv_h = nc.dram_tensor("wqkvT", [D, 3 * D], F32, kind="ExternalInput")
    wproj_h = nc.dram_tensor("wprojT", [D, D], F32, kind="ExternalInput")
    w1_h = nc.dram_tensor("w1T", [D, HJ], F32, kind="ExternalInput")
    w2_h = nc.dram_tensor("w2T", [HJ, D], F32, kind="ExternalInput")
    bias_h = nc.dram_tensor("biases", [128, 48], F32, kind="ExternalInput")
    sel_h = nc.dram_tensor("sel", [16, 8 * 128], F32, kind="ExternalInput")
    cb_h = nc.dram_tensor("cb", [128, 8], F32, kind="ExternalInput")
    out_h = nc.dram_tensor("out", [NSEQ, D, S], F32, kind="ExternalOutput")
    if debug_taps:
        dbg_bq = nc.dram_tensor("dbg_bq", [128, 8], F32, kind="ExternalOutput")
        dbg_w1 = nc.dram_tensor("dbg_w1", [8, 128, 4096], BF16, kind="ExternalOutput")
        dbg_xb = nc.dram_tensor("dbg_xb", [NSEQ, 128, 16384], BF16, kind="ExternalOutput")
        dbg_qf = nc.dram_tensor("dbg_qf", [NSEQ, 128, 16384], BF16, kind="ExternalOutput")
        dbg_kv2 = nc.dram_tensor("dbg_kv2", [NSEQ, 128, 1024], BF16, kind="ExternalOutput")
        dbg_bd = nc.dram_tensor("dbg_bd", [NSEQ, 128, 128], BF16, kind="ExternalOutput")
        dbg_at = nc.dram_tensor("dbg_at", [NSEQ, 128, 16384], BF16, kind="ExternalOutput")
        dbg_z = nc.dram_tensor("dbg_z", [NSEQ, NT, 16, T], BF16, kind="ExternalOutput")

    sp_q = nc.dram_tensor("sp_q", [8, 128, 1024], BF16, kind="Internal")
    sp_kv = nc.dram_tensor("sp_kv", [4, 128, 4096], BF16, kind="Internal")
    sp_pr = nc.dram_tensor("sp_pr", [8, 128, 1024], BF16, kind="Internal")
    sp_w1 = nc.dram_tensor("sp_w1", [8, 128, 4096], BF16, kind="Internal")
    sp_w2 = nc.dram_tensor("sp_w2", [8, 128, 4096], F8E4, kind="Internal")

    es = ExitStack()

    def sb(name, shape, dt=BF16):
        return es.enter_context(nc.sbuf_tensor(name, shape, dt))

    wstream = sb("wstream", [128, 16384], BF16)
    xa_hq = sb("xa_hq", [128, 16384], BF16)
    qf = sb("qf", [128, 16384], BF16)
    xb = sb("xb", [128, 16384], BF16)
    r_t = [sb(f"r{i}", [128, 1024], F32) for i in range(3)]
    l_t = [sb(f"l{i}", [128, 1024], F32) for i in range(2)]
    e_t = [sb(f"e{i}", [128, 1024], I32) for i in range(2)]
    inv_t = [sb(f"inv{i}", [128, 1024], BF16) for i in range(3)]
    g_t = [sb(f"g{i}", [128, 1024], BF16) for i in range(3)]
    xn0 = sb("xn0", [128, 8 * T], BF16)
    xn = [xn0, None]
    kf_nb = [inv_t[0], g_t[0], inv_t[1], g_t[1]]
    kv2 = inv_t[2]
    hq8 = {id(xa_hq): [xa_hq[:, 0:8192].bitcast(F8E4), xa_hq[:, 8192:16384].bitcast(F8E4)],
           id(qf): [qf[:, 0:8192].bitcast(F8E4), qf[:, 8192:16384].bitcast(F8E4)]}
    z_f = r_t[2][0:16, 0:T]
    kvacc_sb = r_t[2][0:128, T:2 * T]
    z_b = g_t[2][0:16, 0:T]
    zb_t = [g_t[2][0:128, T:2 * T]]
    v_nb = [sb(f"v{i}", [128, 1024], BF16) for i in range(4)]
    stg = [sb(f"stg{i}", [128, T], F32) for i in range(2)]
    sqs = [sb(f"sqs{i}", [128, T], F32) for i in range(2)]
    lnsq0 = sb("lnsq0", [128, T], BF16)
    lnsq1 = sb("lnsq1", [128, T], BF16)
    lnsq = [lnsq0, lnsq1]
    bd = sb("bd", [128, 8 * 16], BF16)
    sel_sb = sb("sel_sb", [16, 8 * 128], BF16)
    bias_sb = sb("bias_sb", [128, 48], F32)
    bq_sb = sb("bq_sb", [128, 8], F32)
    cb = sb("cb_sb", [128, 8], F32)
    ones1 = sb("ones1", [1, 128], BF16)
    negones = sb("negones", [1, 128], BF16)
    onescol = sb("onescol", [128, 1], BF16)
    rowF = sb("rowF", [1, 3 * T], F32)
    rowBf = sb("rowBf", [1, 2 * T], BF16)
    m_row = rowF[0:1, 0:T]
    s2_row = rowF[0:1, T:2 * T]
    mm_row = rowF[0:1, 2 * T:3 * T]
    rstd_row = rowF[0:1, T:2 * T]
    nb_rowf = rowF[0:1, 0:T]
    a_rowb = rowBf[0:1, 0:T]
    nb_rowb = rowBf[0:1, T:2 * T]
    ab_b = sb("ab_b", [128, T], BF16)
    bb_b = sb("bb_b", [128, T], BF16)
    ksacc_sb = sb("ksacc", [128, 8], F32)

    pes = ExitStack()

    def ps(name, shape):
        return pes.enter_context(nc.psum_tensor(name, shape, F32))

    pA = ps("pA", [128, 512])
    pB = ps("pB", [128, 512])
    pC = ps("pC", [128, 512])
    pD = ps("pD", [128, 512])
    pK1 = ps("pK1", [128, 512])
    pK2 = ps("pK2", [128, 512])
    pL1 = ps("pL1", [128, 512])
    pL2 = ps("pL2", [128, 512])

    sem_names = ["dl", "ds", "mm", "ea", "ev", "ou", "eg"]
    sems = {n: es.enter_context(nc.semaphore(n)) for n in sem_names}

    sch = Sched()

    qbuf = [wstream[:, i * 1024:(i + 1) * 1024] for i in range(2)]
    kvbuf = [wstream[:, 2048 + i * 4096: 2048 + (i + 1) * 4096] for i in range(2)]
    prbuf = [wstream[:, i * 1024:(i + 1) * 1024] for i in range(2)]
    w1buf = [wstream[:, i * 4096:(i + 1) * 4096] for i in range(2)]
    w2buf = [wstream[:, 8192 + i * 2048: 8192 + (i + 1) * 2048].bitcast(F8E4) for i in range(2)]
    xn[1] = wstream[:, 12288:16384]

    TINY4 = float(2.0 ** -80)
    TINY10 = float(2.0 ** -120)
    CLAMP4 = float(np.float32(-5.45 * LN2C))
    CLAMP10 = float(np.float32(-13.45 * 2 * LN2C))
    C2 = float(np.float32(1.0 / LN2C))
    C22 = float(np.float32(1.0 / (2 * LN2C)))
    CB_M3, CB_P3, CB_M10, CB_P10, CB_T4, CB_T10, CB_EPS = range(7)

    def cbc(i):
        return cb[:, i:i + 1]

    def quantize(src_ap, out_ap, man, signed, bufi, n, waits, extra_first_waits=(), helper="vector"):
        lt, et, it_, gt = l_t[bufi][:, :n], e_t[bufi][:, :n], inv_t[bufi][:, :n], g_t[bufi][:, :n]
        ef = l_t[bufi][:, :n]
        w0 = list(waits) + list(extra_first_waits)
        min_exp = -6 if man == 3 else -14
        cbm = CB_M3 if man == 3 else CB_M10
        cbp = CB_P3 if man == 3 else CB_P10
        if signed:
            sqt = r_t[bufi][:, :n]
            t_sq = sch.add("scalar", lambda e, o=sqt, i=src_ap: e.activation(o, i, AF.Square),
                           waits=w0, inc="ea")
            t_l = sch.add("scalar", lambda e, o=lt, i=sqt, b=cbc(CB_T10): e.activation(o, i, AF.Ln, bias=b),
                          waits=[t_sq], inc="ea")
            clamp = float(np.float32((min_exp + 0.55) * 2 * LN2C))
            cc = C22
        else:
            t_l = sch.add("scalar", lambda e, o=lt, i=src_ap, b=cbc(CB_T4): e.activation(o, i, AF.Ln, bias=b),
                          waits=w0, inc="ea")
            t_sq = None
            clamp = float(np.float32((min_exp + 0.55) * LN2C))
            cc = C2
        hsem = "ev" if helper == "vector" else "eg"
        t_e = sch.add(helper, lambda e, o=et, i=lt, c=cc: e.tensor_scalar(o, i, c, 0.5, op0=ALU.mult, op1=ALU.subtract),
                      waits=[t_l], inc=hsem)
        t_ef = sch.add(helper, lambda e, o=ef, i=et, c=float(min_exp): e.tensor_scalar(o, i, c, None, op0=ALU.max),
                       waits=[t_e], inc=hsem)
        t_inv = sch.add("scalar", lambda e, o=it_, i=ef, b=cbc(cbp): e.activation(o, i, AF.Exp, scale=-LN2C, bias=b),
                        waits=[t_ef], inc="ea")
        t_g = sch.add("scalar", lambda e, o=gt, i=ef, b=cbc(cbm): e.activation(o, i, AF.Exp, scale=LN2C, bias=b),
                      waits=[t_ef], inc="ea")
        t_y = sch.add("vector", lambda e, o=et, a=src_ap, b=it_: e.tensor_tensor(o, a, b, ALU.mult),
                      waits=[t_inv], inc="ev")
        t_q = sch.add("vector", lambda e, o=out_ap, a=et, b=gt: e.tensor_tensor(o, a, b, ALU.mult),
                      waits=[t_g, t_y], inc="ev")
        return t_q, t_y

    def ln_sums(x_tile_of, waits_extra=()):
        tok_mm1 = tok_mm2 = None
        for c in range(DC):
            xc = x_tile_of(c)
            sq = lnsq[c % 2][:, :]
            tw = [w for w in [lnsq_tok[c % 2]] if w] + (list(waits_extra) if c == 0 else [])
            tsq = sch.add("scalar", lambda e, o=sq, i=xc: e.activation(o, i, AF.Square),
                          waits=tw, inc="ea")
            w1_ = ([ln_bank_free[0], t_o3] + list(waits_extra)) if c == 0 else []
            tok_mm1 = sch.add("tensor",
                              lambda e, r=xc, s=(c == 0), p=(c == 7): e.matmul(pL1[0:1, :], onescol[:, :], r, start=s, stop=p),
                              waits=[w for w in w1_ if w], inc="mm" if c == 7 else None)
            w2_ = [tsq] + ([ln_bank_free[1]] if c == 0 and ln_bank_free[1] else [])
            tok_mm2 = sch.add("tensor",
                              lambda e, r=sq, s=(c == 0), p=(c == 7): e.matmul(pL2[0:1, :], onescol[:, :], r, start=s, stop=p),
                              waits=w2_, inc="mm")
            lnsq_tok[c % 2] = tok_mm2
        return tok_mm1, tok_mm2

    def ln_rows(sums_toks):
        tok_mm1, tok_mm2 = sums_toks
        t1 = sch.add("vector", lambda e: e.tensor_scalar(m_row[:, :], pL1[0:1, :], 1.0 / D, None, op0=ALU.mult),
                     waits=[tok_mm1], inc="ev")
        t2 = sch.add("vector", lambda e: e.tensor_scalar(s2_row[:, :], pL2[0:1, :], 1.0 / D, None, op0=ALU.mult),
                     waits=[tok_mm2], inc="ev")
        t3 = sch.add("vector", lambda e: e.tensor_tensor(mm_row[:, :], m_row[:, :], m_row[:, :], ALU.mult),
                     waits=[t1], inc="ev")
        t4 = sch.add("vector", lambda e: e.tensor_tensor(mm_row[:, :], s2_row[:, :], mm_row[:, :], ALU.subtract),
                     waits=[t2, t3], inc="ev")
        t5 = sch.add("scalar", lambda e: e.activation(mm_row[:, :], mm_row[:, :], AF.Ln, bias=cb[0:1, CB_EPS:CB_EPS + 1]),
                     waits=[t4], inc="ea")
        t6 = sch.add("scalar", lambda e: e.activation(rstd_row[:, :], mm_row[:, :], AF.Exp, scale=-0.5),
                     waits=[t5], inc="ea")
        t7 = sch.add("vector", lambda e: e.tensor_copy(a_rowb[:, :], rstd_row[:, :]), waits=[t6], inc="ev")
        t8 = sch.add("vector", lambda e: e.tensor_tensor(nb_rowf[:, :], m_row[:, :], rstd_row[:, :], ALU.mult),
                     waits=[t6], inc="ev")
        t9 = sch.add("vector", lambda e: e.tensor_copy(nb_rowb[:, :], nb_rowf[:, :]), waits=[t8], inc="ev")
        return (t1, t2, t7, t9)

    def ln_tail(x_tile_of, xn_buf, rows_toks, xn_guard=()):
        t1, t2, t7, t9 = rows_toks
        ta = sch.add("tensor", lambda e: e.matmul(pL1[:, :], ones1[0:1, 0:128], a_rowb[:, :], start=True, stop=True),
                     waits=[t7, t_o1], inc="mm")
        tb = sch.add("tensor", lambda e: e.matmul(pL2[:, :], negones[0:1, 0:128], nb_rowb[:, :], start=True, stop=True),
                     waits=[t9, t_o2], inc="mm")
        tc = sch.add("vector", lambda e: e.tensor_copy(ab_b[:, :], pL1[:, :]), waits=[ta], inc="ev")
        td = sch.add("vector", lambda e: e.tensor_copy(bb_b[:, :], pL2[:, :]), waits=[tb], inc="ev")
        ln_bank_free[0] = tc
        ln_bank_free[1] = td
        tok_xn = []
        for c in range(DC):
            xc = x_tile_of(c)
            dst = xn_buf[:, c * T:(c + 1) * T]
            tm = sch.add("vector", lambda e, o=dst, i=xc: e.tensor_tensor(o, i, ab_b[:, :], ALU.mult),
                         waits=[tc] + (list(xn_guard) if c == 0 else []), inc="ev")
            tn = sch.add("vector", lambda e, o=dst: e.tensor_tensor(o, o, bb_b[:, :], ALU.add),
                         waits=[td, tm], inc="ev")
            tok_xn.append(tn)
        return tok_xn

    bank_free = {k: None for k in ["pA", "pB", "pC", "pD", "pK1", "pK2"]}
    ln_bank_free = [None, None]
    xn_free = [None, None]
    lnsq_tok = [None, None]
    sqs_tok = [None, None]
    temps_free = [None, None, None]

    tok_bias = sch.add("sync", lambda e: e.dma_start(out=bias_sb[:, :], in_=bias_h.ap()), inc="ds", n=16)
    tok_cb = sch.add("sync", lambda e: e.dma_start(out=cb[:, :], in_=cb_h.ap()), inc="ds", n=16)
    tok_sel = sch.add("gpsimd", lambda e: e.dma_start(out=sel_sb[:, :], in_=sel_h.ap()), inc="dl", n=16)
    t_o1 = sch.add("vector", lambda e: e.memset(ones1[:, :], 1.0), inc="ev")
    t_o2 = sch.add("vector", lambda e: e.memset(negones[:, :], -1.0), inc="ev")
    t_o3 = sch.add("vector", lambda e: e.memset(onescol[:, :], 1.0), inc="ev")

    tok_bq, _ = quantize(bias_sb[:, 40:48], bq_sb[:, :], man=10, signed=True, bufi=0, n=8,
                         waits=[tok_bias, tok_cb])

    def load_x(s_, waits):
        src = x_h.ap()[s_].rearrange("(c p) t -> p c t", p=128)
        tgt = xa_hq if s_ % 2 == 0 else qf
        dst = tgt[:, :].rearrange("p (c t) -> p c t", c=8)
        toks = []
        for tt in range(NT):
            tok = sch.add("gpsimd",
                          lambda e, o=dst[:, :, tt * T:(tt + 1) * T], i=src[:, :, tt * T:(tt + 1) * T]:
                          e.dma_start(out=o, in_=i),
                          waits=(list(waits) if tt == 0 else []), inc="dl", n=16)
            toks.append(tok)
        return toks

    tok_xa0 = load_x(0, [tok_sel])

    spill_toks = []
    for jc in range(8):
        src = wqkv_h.ap()[:, jc * 128:(jc + 1) * 128].rearrange("(c p) q -> p c q", p=128)
        dst = sp_q.ap()[jc].rearrange("p (c q) -> p c q", c=8)
        tok = sch.add("gpsimd", lambda e, o=dst, i=src: e.dma_start(out=o, in_=i),
                      inc="dl", n=16)
        spill_toks.append(tok)
    spq_done = spill_toks[-1]
    for sl in range(4):
        src = wqkv_h.ap()[:, 1024 + sl * 512: 1024 + (sl + 1) * 512].rearrange("(c p) j -> p c j", p=128)
        dst = sp_kv.ap()[sl].rearrange("p (c j) -> p c j", c=8)
        tok = sch.add("gpsimd", lambda e, o=dst, i=src: e.dma_start(out=o, in_=i),
                      inc="dl", n=16)
        spill_toks.append(tok)
    spkv_done = spill_toks[-1]
    for jc in range(8):
        src = wproj_h.ap()[:, jc * 128:(jc + 1) * 128].rearrange("(c p) q -> p c q", p=128)
        dst = sp_pr.ap()[jc].rearrange("p (c q) -> p c q", c=8)
        tok = sch.add("gpsimd", lambda e, o=dst, i=src: e.dma_start(out=o, in_=i),
                      inc="dl", n=16)
        spill_toks.append(tok)
    sppr_done = spill_toks[-1]
    spw1_piece = []
    for pc in range(8):
        tok = None
        for c in range(8):
            src = w1_h.ap()[c * 128:(c + 1) * 128, pc * 512:(pc + 1) * 512].rearrange("p (jj q) -> p jj q", jj=4)
            dst = sp_w1.ap()[pc].rearrange("p (jj rest) -> p jj rest", jj=4)[:, :, c * 128:(c + 1) * 128]
            tok = sch.add("gpsimd", lambda e, o=dst, i=src: e.dma_start(out=o, in_=i),
                          inc="dl", n=16)
        spw1_piece.append(tok)
        spill_toks.append(tok)
    w2stage = [stg[0][:, 0:128].bitcast(F8E4), stg[1][:, 0:128].bitcast(F8E4)]
    prevq = [None, None]
    for jc in range(HJC):
        for h in range(2):
            bi = (2 * jc + h) % 2
            src = w2_h.ap()[jc * 128:(jc + 1) * 128, h * 512:(h + 1) * 512]
            tok_ld = sch.add("sync", lambda e, o=sqs[bi][:, :], i=src: e.dma_start(out=o, in_=i),
                             waits=[w for w in [prevq[bi], sqs_tok[bi],
                                                tok_bq if (jc == 0) else None] if w],
                             inc="ds", n=16)
            tq, ty = quantize(sqs[bi][:, :], w2stage[bi][:, :], man=3, signed=True, bufi=bi, n=512,
                              waits=[tok_ld])
            sqs_tok[bi] = ty
            temps_free[bi] = tq
            dst = sp_w2.ap()[4 * h:4 * h + 4, :, jc * 128:(jc + 1) * 128].rearrange("a p q -> p a q")
            src2 = w2stage[bi].rearrange("p (a q) -> p a q", a=4)
            tok = sch.add("sync", lambda e, o=dst, i=src2: e.dma_start(out=o, in_=i),
                          waits=[tq], inc="ds", n=16)
            prevq[bi] = tok
            spill_toks.append(tok)

    if debug_taps:
        sch.add("sync", lambda e: e.dma_start(out=dbg_bq.ap(), in_=bq_sb[:, :]),
                waits=[tok_bq], inc="ou", n=16)
        tok_out_dbg = [16]
        sch.add("sync", lambda e: e.dma_start(out=dbg_w1.ap(), in_=sp_w1.ap()),
                waits=[spill_toks[-1]], inc="ou", n=16)
        tok_out_dbg[0] += 16
    else:
        tok_out_dbg = [0]

    spw2_done = prevq[1] if prevq[1][1] > prevq[0][1] else prevq[0]
    qbuf_free = [None, None]
    kvbuf_free = [None, None]
    prbuf_free = [None, None]
    w1buf_free = [None, None]
    w2buf_free = [None, None]
    last_spill = spill_toks[-1]

    tok_out_total = [0]
    hq_read_done = None
    tok_stgfree = [prevq[0], prevq[1]]
    kfv_free = [None] * 4
    attn_over_qf_ok = {}
    x_toks_next = tok_xa0
    xb_read_done = None
    kv2_read_done = None
    bd_read_done = None
    zrow_free = None
    zb_free = [None, None]
    ab_free = None
    ln1_boot = [None]

    for s_ in range(NSEQ):
        x_toks_cur = x_toks_next
        qf_free_at = hq_read_done
        XA = xa_hq if s_ % 2 == 0 else qf
        QF = qf if s_ % 2 == 0 else xa_hq

        def x_of_t(t):
            def x_of(c, t=t):
                return XA[:, c * S + t * T: c * S + t * T + T]
            return x_of

        if ln1_boot[0] is not None:
            tails_t = {0: ln1_boot[0]}
            ln1_boot[0] = None
            sums_t, rows_t = {}, {}
        else:
            extra0 = [x_toks_cur[0]] + [w for w in temps_free if w]
            sums_t = {0: ln_sums(x_of_t(0), waits_extra=extra0)}
            rows_t = {0: ln_rows(sums_t[0])}
            tails_t = {0: ln_tail(x_of_t(0), xn[0], rows_t[0],
                                  xn_guard=[w for w in [xn_free[0]] if w])}
        for t in range(NT):
            tok_xn = tails_t[t]
            if t + 1 < NT:
                sums_t[t + 1] = ln_sums(x_of_t(t + 1), waits_extra=[x_toks_cur[t + 1]])
                rows_t[t + 1] = ln_rows(sums_t[t + 1])

            qrot = [pA, pB, pC, pD]
            qrot_names = ["pA", "pB", "pC", "pD"]
            for jc in range(8):
                bi = jc % 2
                wg = [qbuf_free[bi]]
                if t == 0 and jc == 0:
                    wg += [w1buf_free[0]] + prbuf_free
                    if s_ == 0:
                        wg += [spq_done]
                tok_w = sch.add("sync", lambda e, o=qbuf[bi], i=sp_q.ap()[jc]: e.dma_start(out=o, in_=i),
                                waits=[w for w in wg if w], inc="ds", n=16)
                mmw = [tok_w, bank_free[qrot_names[jc % 4]]] + tok_xn
                tok_mm = None
                for c in range(DC):
                    tok_mm = sch.add("tensor",
                                     lambda e, l_=qbuf[bi][:, c * 128:(c + 1) * 128],
                                     r=xn[t % 2][:, c * T:(c + 1) * T], p=qrot[jc % 4],
                                     s0=(c == 0), s1=(c == 7): e.matmul(p[:, :], l_, r, start=s0, stop=s1),
                                     waits=[w for w in mmw if w] if c == 0 else [],
                                     inc="mm" if c == 7 else None)
                qbuf_free[bi] = tok_mm
                sqt = sqs[bi][:, :]
                tevq = sch.add("scalar", lambda e, o=sqt, i=qrot[jc % 4][:, :]: e.activation(o, i, AF.Square),
                               waits=[tok_mm] + [w for w in [sqs_tok[bi]] if w]
                               + ([w for w in [qf_free_at] if w] if (t == 0 and jc == 0) else []),
                               inc="ea")
                bank_free[qrot_names[jc % 4]] = tevq
                dst = QF[:, jc * S + t * T: jc * S + t * T + T]
                tevq2 = sch.add("scalar", lambda e, o=dst, i=sqt: e.activation(o, i, AF.Exp, scale=-0.5),
                                waits=[tevq], inc="ea")
                sqs_tok[bi] = tevq2
                if jc == 7:
                    qf_tile_done = tevq2

            if t + 1 < NT:
                guard = [w for w in [xn_free[(t + 1) % 2]] if w]
                tails_t[t + 1] = ln_tail(x_of_t(t + 1), xn[(t + 1) % 2], rows_t[t + 1], xn_guard=guard)

            pnb = [pA, pB, pC, pD]
            pnb_names = ["pA", "pB", "pC", "pD"]
            kv_evac_toks = {}
            for sl in range(4):
                bi = sl % 2
                wgk = [kvbuf_free[bi]]
                if t == 0 and sl == 0:
                    wgk += [w1buf_free[1]]
                    if s_ == 0:
                        wgk += [spkv_done]
                if t == 0 and sl == 1:
                    wgk += [w2buf_free[0]]
                tok_w = sch.add("sync", lambda e, o=kvbuf[bi], i=sp_kv.ap()[sl]: e.dma_start(out=o, in_=i),
                                waits=[w for w in wgk if w], inc="ds", n=16)
                last_mm_of_nb = {}
                for nb in range(4):
                    tok_mm = None
                    w0 = [tok_w, bank_free[pnb_names[nb]]]
                    for c in range(DC):
                        tok_mm = sch.add("tensor",
                                         lambda e, l_=xn[t % 2][:, c * T + nb * 128: c * T + nb * 128 + 128],
                                         r=kvbuf[bi][:, c * 512:(c + 1) * 512], p=pnb[nb],
                                         s0=(c == 0), s1=(c == 7): e.matmul(p[:, :], l_, r, start=s0, stop=s1),
                                         waits=[w for w in w0 if w] if c == 0 else [],
                                         inc="mm" if c == 7 else None)
                    last_mm_of_nb[nb] = tok_mm
                kvbuf_free[bi] = last_mm_of_nb[3]
                for nb in range(4):
                    tok_mm = last_mm_of_nb[nb]
                    if sl < 2:
                        sqt = sqs[nb % 2][:, :]
                        te1 = sch.add("scalar", lambda e, o=sqt, i=pnb[nb][:, :]: e.activation(o, i, AF.Square),
                                      waits=[tok_mm] + [w for w in [sqs_tok[nb % 2]] if w],
                                      inc="ea")
                        dst = kf_nb[nb][:, sl * 512:(sl + 1) * 512]
                        kf_guard = ([kfv_free[nb]] if kfv_free[nb] else []) + \
                                   ([temps_free[nb // 2]] if temps_free[nb // 2] else [])
                        te2 = sch.add("scalar", lambda e, o=dst, i=sqt: e.activation(o, i, AF.Exp, scale=-0.5),
                                      waits=[te1] + (kf_guard if (sl == 0 and t == 0) else
                                                     ([kfv_free[nb]] if (sl == 0 and kfv_free[nb]) else [])),
                                      inc="ea")
                        sqs_tok[nb % 2] = te2
                        bank_free[pnb_names[nb]] = te1
                        kv_evac_toks[("k", sl, nb)] = te2
                    else:
                        dst = v_nb[nb][:, (sl - 2) * 512:(sl - 1) * 512]
                        te = sch.add("vector", lambda e, o=dst, i=pnb[nb][:, :]: e.tensor_copy(o, i),
                                     waits=[tok_mm] + ([kfv_free[nb]] if (sl == 2 and kfv_free[nb]) else []),
                                     inc="ev")
                        bank_free[pnb_names[nb]] = te
                        kv_evac_toks[("v", sl, nb)] = te

            wv0 = [kv_evac_toks[(kk, slx, nbx)] for nbx in range(4)
                   for (kk, slx) in (("k", 0), ("k", 1), ("v", 2), ("v", 3))]
            if t == 0:
                tz_kv = sch.add("vector", lambda e: e.memset(kvacc_sb[:, :], 0.0), waits=wv0, inc="ev")
                tz_ks = sch.add("vector", lambda e: e.memset(ksacc_sb[:, :], 0.0), waits=[tz_kv], inc="ev")
            pKband = [pK1, pK2]
            pKname = ["pK1", "pK2"]
            for hp in range(8):
                for half in range(2):
                    pb = pKband[half]
                    rows = slice(half * 64, (half + 1) * 64)
                    w0 = ([w for w in wv0 if w] if (hp == 0) else []) + \
                         [w for w in [bank_free[pKname[half]]] if w]
                    tok_kv = None
                    for nb in range(4):
                        tok_kv = sch.add("tensor",
                                         lambda e, l_=kf_nb[nb][:, hp * 128 + half * 64: hp * 128 + half * 64 + 64],
                                         r=v_nb[nb][:, hp * 128 + half * 64: hp * 128 + half * 64 + 64],
                                         p=pb[rows, 0:64],
                                         s0=(nb == 0), s1=(nb == 3), tp=(0, 64) if half else None:
                                         e.matmul(p, l_, r, start=s0, stop=s1, tile_position=tp),
                                         waits=w0 if nb == 0 else [],
                                         inc="mm" if nb == 3 else None)
                    dst = kvacc_sb[rows, hp * 64:(hp + 1) * 64]
                    tacc = sch.add("vector",
                                   lambda e, o=dst, p=pb[rows, 0:64]:
                                   e.tensor_tensor(o, o, p, ALU.add),
                                   waits=[tok_kv] + ([tz_ks] if (t == 0 and hp == 0) else []), inc="ev")
                    bank_free[pKname[half]] = tacc
            for c in range(DC):
                pb = pKband[c % 2]
                w0 = [w for w in [bank_free[pKname[c % 2]], t_o3] if w]
                tok_ks = None
                for nb in range(4):
                    tok_ks = sch.add("tensor",
                                     lambda e, l_=kf_nb[nb][:, c * 128:(c + 1) * 128],
                                     p=pb[:, 0:1], s0=(nb == 0), s1=(nb == 3):
                                     e.matmul(p, l_, onescol[:, :], start=s0, stop=s1),
                                     waits=w0 if nb == 0 else [],
                                     inc="mm" if nb == 3 else None)
                dst = ksacc_sb[:, c:c + 1]
                tacc = sch.add("vector",
                               lambda e, o=dst, p=pb[:, 0:1]: e.tensor_tensor(o, o, p, ALU.add),
                               waits=[tok_ks], inc="ev")
                bank_free[pKname[c % 2]] = tacc
            for nb in range(4):
                kfv_free[nb] = tok_ks
            xn_free[t % 2] = tok_ks
            kvks_done = bank_free["pK1"] if bank_free["pK1"][1] > bank_free["pK2"][1] else bank_free["pK2"]

        w_prep = [kvks_done] + [w for w in [kv2_read_done, bd_read_done] if w]
        t_z0 = sch.add("vector", lambda e: e.memset(kv2[:, :], 0.0), waits=w_prep, inc="ev")
        t_z1 = sch.add("vector", lambda e: e.memset(bd[:, :], 0.0), waits=[t_z0], inc="ev")
        kv2_toks = []
        for c in range(DC):
            ta_ = sch.add("vector", lambda e, c=c: e.tensor_copy(kv2[0:64, c * 128: c * 128 + 64],
                                                                 kvacc_sb[0:64, c * 64:(c + 1) * 64]),
                          waits=[t_z0], inc="ev")
            tb_ = sch.add("vector", lambda e, c=c: e.tensor_copy(kv2[64:128, c * 128 + 64: c * 128 + 128],
                                                                 kvacc_sb[64:128, c * 64:(c + 1) * 64]),
                          waits=[t_z0], inc="ev")
            kv2_toks += [ta_, tb_]
        bd_toks = []
        for c in range(DC):
            ta_ = sch.add("vector", lambda e, c=c: e.tensor_copy(bd[0:64, c * 16 + 2 * c: c * 16 + 2 * c + 1],
                                                                 ksacc_sb[0:64, c:c + 1]),
                          waits=[t_z1], inc="ev")
            tb_ = sch.add("vector", lambda e, c=c: e.tensor_copy(bd[64:128, c * 16 + 2 * c + 1: c * 16 + 2 * c + 2],
                                                                 ksacc_sb[64:128, c:c + 1]),
                          waits=[t_z1], inc="ev")
            bd_toks += [ta_, tb_]
        kv2_done = kv2_toks[-1]
        bd_done = bd_toks[-1]
        if debug_taps:
            sch.add("sync", lambda e, s_=s_: e.dma_start(out=dbg_kv2.ap()[s_], in_=kv2[:, :]),
                    waits=[kv2_done], inc="ou", n=16)
            sch.add("sync", lambda e, s_=s_: e.dma_start(out=dbg_bd.ap()[s_], in_=bd[:, :]),
                    waits=[bd_done], inc="ou", n=16)
            tok_out_dbg[0] += 32

        qf_dump_gate = []
        if debug_taps:
            tqd = sch.add("sync", lambda e, s_=s_, q=QF: e.dma_start(out=dbg_qf.ap()[s_], in_=q[:, :]),
                    waits=[qf_tile_done], inc="ou", n=16)
            tok_out_dbg[0] += 16
            qf_dump_gate = [tqd]
        proj_last = None
        attn_toks_t = {}
        for t in range(NT):
            tok_zd = None
            for c in range(DC):
                w0 = [bd_done, qf_tile_done, bank_free["pC"]] if c == 0 else []
                tok_zd = sch.add("tensor",
                                 lambda e, l_=bd[:, c * 16:(c + 1) * 16],
                                 r=QF[:, c * S + t * T: c * S + t * T + T],
                                 s0=(c == 0), s1=(c == 7): e.matmul(pC[0:16, :], l_, r, start=s0, stop=s1),
                                 waits=[w for w in w0 if w], inc="mm" if c == 7 else None)
            tz1 = sch.add("vector", lambda e: e.tensor_scalar(z_f[:, :], pC[0:16, :], 1e-6, None, op0=ALU.add),
                          waits=[tok_zd] + [w for w in [zrow_free, temps_free[2]] if w], inc="ev")
            bank_free["pC"] = tz1
            tz2 = sch.add("vector", lambda e: e.reciprocal(z_f[:, :], z_f[:, :]), waits=[tz1], inc="ev")
            tz3 = sch.add("vector", lambda e: e.tensor_copy(z_b[:, :], z_f[:, :]), waits=[tz2], inc="ev")
            zrow_free = tz3
            if debug_taps:
                tzd = sch.add("sync", lambda e, s_=s_, t=t: e.dma_start(out=dbg_z.ap()[s_, t], in_=z_b[:, :]),
                              waits=[tz3], inc="ou", n=16)
                tok_out_dbg[0] += 16
                zrow_free = tzd
            attn_toks = []
            for c in range(DC):
                bi = 0
                tz4 = sch.add("tensor", lambda e, c=c: e.matmul(pC[:, :], sel_sb[:, c * 128:(c + 1) * 128],
                                                                z_b[:, :], start=True, stop=True),
                              waits=[tz3, kv2_done, bank_free["pC"]] + qf_dump_gate, inc="mm")
                tz5 = sch.add("vector", lambda e, o=zb_t[bi][:, :]: e.tensor_copy(o, pC[:, :]),
                              waits=[tz4] + [w for w in [zb_free[bi]] if w], inc="ev")
                bank_free["pC"] = tz5
                ta_ = sch.add("tensor", lambda e, c=c, r=QF[:, c * S + t * T: c * S + t * T + T]:
                              e.matmul(pD[:, :], kv2[:, c * 128:(c + 1) * 128], r, start=True, stop=True),
                              waits=[bank_free["pD"], kv2_done], inc="mm")
                dst = QF[:, c * S + t * T: c * S + t * T + T]
                tm_ = sch.add("vector", lambda e, o=dst, b=zb_t[bi][:, :]: e.tensor_tensor(o, pD[:, :], b, ALU.mult),
                              waits=[ta_, tz5], inc="ev")
                bank_free["pD"] = tm_
                zb_free[bi] = tm_
                attn_toks.append(tm_)
            attn_toks_t[t] = attn_toks
        dbg_gate = []
        if debug_taps:
            tdump = sch.add("sync", lambda e, s_=s_, q=QF: e.dma_start(out=dbg_at.ap()[s_], in_=q[:, :]),
                            waits=[tok for tl in attn_toks_t.values() for tok in tl], inc="ou", n=16)
            tok_out_dbg[0] += 16
            dbg_gate = [tdump]
        def xb_of_t(t):
            def xb_of(c, t=t):
                return xb[:, c * S + t * T: c * S + t * T + T]
            return xb_of

        ln2_tail = {}
        proj_te_t = {}
        for t in range(NT):
            attn_toks = attn_toks_t[t]
            prot = [pA, pB, pK1, pK2]
            prot_names = ["pA", "pB", "pK1", "pK2"]
            for jc in range(8):
                bi = jc % 2
                wg = [prbuf_free[bi]]
                if t == 0 and jc == 0:
                    wg += qbuf_free + kvbuf_free
                    if s_ == 0:
                        wg += [sppr_done]
                tok_w = sch.add("sync", lambda e, o=prbuf[bi], i=sp_pr.ap()[jc]: e.dma_start(out=o, in_=i),
                                waits=[w for w in wg if w], inc="ds", n=16)
                ri = jc % 4
                w0 = [tok_w, bank_free[prot_names[ri]]] + attn_toks + dbg_gate
                tok_mm = None
                for cE in range(DC):
                    tok_mm = sch.add("tensor",
                                     lambda e, l_=prbuf[bi][:, cE * 128:(cE + 1) * 128],
                                     r=QF[:, cE * S + t * T: cE * S + t * T + T], p=prot[ri],
                                     s0=(cE == 0), s1=(cE == 7): e.matmul(p[:, :], l_, r, start=s0, stop=s1),
                                     waits=[w for w in w0 if w] if cE == 0 else [],
                                     inc="mm" if cE == 7 else None)
                prbuf_free[bi] = tok_mm
                dst = xb[:, jc * S + t * T: jc * S + t * T + T]
                xw = [tok_mm] + ([xb_read_done] if (xb_read_done and t == 0 and jc == 0) else [])
                te = sch.add("vector",
                             lambda e, o=dst, p=prot[ri], b=bias_sb[:, jc:jc + 1],
                             xc=XA[:, jc * S + t * T: jc * S + t * T + T]:
                             e.scalar_tensor_tensor(o, p[:, :], b, xc, op0=ALU.add, op1=ALU.add),
                             waits=[w for w in xw if w], inc="ev")
                bank_free[prot_names[ri]] = te
                proj_last = te
            proj_te_t[t] = te
            if t == 0:
                ln2_sums0 = ln_sums(xb_of_t(0), waits_extra=[te])
                ln2_rows0 = ln_rows(ln2_sums0)
                ln2_tail[0] = ln_tail(xb_of_t(0), xn[0], ln2_rows0,
                                      xn_guard=[w for w in [xn_free[0]] if w])
        if debug_taps:
            sch.add("sync", lambda e, s_=s_: e.dma_start(out=dbg_xb.ap()[s_], in_=xb[:, :]),
                    waits=[proj_last], inc="ou", n=16)
            tok_out_dbg[0] += 16
        attn_over_qf_ok[s_] = proj_last
        kv2_read_done = ta_
        bd_read_done = tok_zd
        xa_read_done = proj_last
        if s_ + 1 < NSEQ:
            x_toks_next = load_x(s_ + 1, [proj_last])

        xb_last_read = None
        hqv_par = hq8[id(XA)]
        hq_toks_t = {}
        hq_free = [None, None]
        ffn1_last_mm = {}
        first_w2 = [True]

        def emit_ffn2_chunk(tp, eo):
            ebi = eo % 2
            pe_ = pC if ebi == 0 else pD
            pe_name = "pC" if ebi == 0 else "pD"
            wg = [w2buf_free[ebi]]
            if first_w2[0]:
                wg += kvbuf_free
                if s_ == 0:
                    wg += [spw2_done]
                first_w2[0] = False
            tok_w2 = sch.add("sync", lambda e, o=w2buf[ebi], i=sp_w2.ap()[eo]: e.dma_start(out=o, in_=i),
                             waits=[w for w in wg if w], inc="ds", n=16)
            hqv = hqv_par[tp % 2]
            toks = hq_toks_t[tp]
            w0 = [tok_w2, bank_free[pe_name]]
            tok_mm = None
            for c in range(16):
                jw = ([w for w in w0 if w] if c == 0 else []) + \
                     ([toks[c]] if eo == 0 else [])
                tok_mm = sch.add("tensor",
                                 lambda e, l_=w2buf[ebi][:, c * 256:(c + 1) * 256].rearrange("p (i m) -> p i m", i=2),
                                 r=hqv[:, c * 1024:(c + 1) * 1024].rearrange("p (i n) -> p i n", i=2), p=pe_,
                                 s0=(c == 0), s1=(c == 15): e.matmul(p[:, :], l_, r, start=s0, stop=s1,
                                                                     perf_mode=PM.DoubleRow),
                                 waits=jw, inc="mm" if c == 15 else None)
            w2buf_free[ebi] = tok_mm
            if eo == 7:
                hq_free[tp % 2] = tok_mm
            vtile = sqs[ebi][:, :]
            tv = sch.add("vector", lambda e, o=vtile, p=pe_, b=bq_sb[:, eo:eo + 1]:
                         e.tensor_scalar(o, p[:, :], b, None, op0=ALU.add),
                         waits=[tok_mm] + [w for w in [tok_stgfree[ebi], sqs_tok[ebi]] if w], inc="ev")
            bank_free[pe_name] = tv
            qdst = stg[ebi][:, :]
            tq, ty = quantize(vtile, qdst, man=10, signed=True, bufi=ebi, n=512,
                              waits=[tv] + [w for w in [temps_free[ebi]] if w])
            temps_free[ebi] = tq
            sqs_tok[ebi] = ty
            tr = sch.add("vector", lambda e, o=qdst, xc=xb[:, eo * S + tp * T: eo * S + tp * T + T]:
                         e.tensor_tensor(o, o, xc, ALU.add),
                         waits=[tq], inc="ev")
            tdo = sch.add("gpsimd", lambda e, i=qdst, o=out_h.ap()[s_, eo * 128:(eo + 1) * 128, tp * T:(tp + 1) * T]:
                          e.dma_start(out=o, in_=i),
                          waits=[tr], inc="ou", n=16)
            tok_out_total[0] += 16
            tok_stgfree[ebi] = tdo
            return tok_mm, tr

        for t in range(NT):
            tok_xn2 = ln2_tail[t]
            if t + 1 < NT:
                ln2_sums_next = ln_sums(xb_of_t(t + 1), waits_extra=[proj_te_t[t + 1]])
                ln2_rows_next = ln_rows(ln2_sums_next)

            frot = [pA, pB, pK1, pK2]
            frot_names = ["pA", "pB", "pK1", "pK2"]
            hq_toks = []
            hq_toks_t[t] = hq_toks
            for grp in range(16):
                gbuf = grp % 2
                if grp % 2 == 0:
                    pc = grp // 2
                    wbi = pc % 2
                    wg = [w1buf_free[wbi]]
                    if t == 0 and grp == 0:
                        wg += prbuf_free + kvbuf_free
                    if s_ == 0 and t == 0:
                        wg += [spw1_piece[pc]]
                    tok_w1 = sch.add("sync", lambda e, o=w1buf[wbi], i=sp_w1.ap()[pc]: e.dma_start(out=o, in_=i),
                                     waits=[w for w in wg if w], inc="ds", n=16)
                    w1tok_cur = tok_w1
                grp_last_mm = []
                for jj in range(2):
                    j = grp * 2 + jj
                    bi = j % 2
                    pcbuf = (j // 4) % 2
                    off = (j % 4) * 1024
                    ri = j % 4
                    w0 = [w1tok_cur, bank_free[frot_names[ri]]] + (tok_xn2 if j == 0 else [])
                    tok_mm = None
                    for c in range(DC):
                        tok_mm = sch.add("tensor",
                                         lambda e, l_=w1buf[pcbuf][:, off + c * 128: off + (c + 1) * 128],
                                         r=xn[t % 2][:, c * T:(c + 1) * T], p=frot[ri],
                                         s0=(c == 0), s1=(c == 7): e.matmul(p[:, :], l_, r, start=s0, stop=s1),
                                         waits=[w for w in w0 if w] if c == 0 else [],
                                         inc="mm" if c == 7 else None)
                    grp_last_mm.append(tok_mm)
                    dst = r_t[gbuf][:, jj * T:(jj + 1) * T]
                    ew = [tok_mm] + ([w for w in [temps_free[gbuf]] if w] if jj == 0 or j % 2 == 1 else [])
                    te = sch.add("scalar",
                                 lambda e, o=dst, p=frot[ri], b=bias_sb[:, 8 + j:8 + j + 1]:
                                 e.activation(o, p[:, :], AF.Relu, bias=b),
                                 waits=ew, inc="ea")
                    bank_free[frot_names[ri]] = te
                    if jj == 0:
                        evac_done0 = te
                    else:
                        evac_done = te
                if grp % 2 == 1:
                    w1buf_free[(grp // 2) % 2] = grp_last_mm[-1]
                hq_dst = hqv_par[t % 2][:, grp * 1024:(grp + 1) * 1024]
                efw = []
                if grp == 0:
                    efw = [xa_read_done] if t <= 1 else [w for w in [hq_free[t % 2]] if w]
                tq, _ = quantize(r_t[gbuf][:, :], hq_dst, man=3, signed=False, bufi=gbuf, n=1024,
                                 waits=[evac_done, evac_done0],
                                 extra_first_waits=efw)
                temps_free[gbuf] = tq
                hq_toks.append(tq)
                if grp == 2:
                    if t + 1 < NT:
                        xg = xn_free[1] if t == 0 else ffn1_last_mm[t - 1]
                        ln2_tail[t + 1] = ln_tail(xb_of_t(t + 1), xn[(t + 1) % 2], ln2_rows_next,
                                                  xn_guard=[w for w in [xg] if w])
                    elif s_ + 1 < NSEQ:
                        def x1_of(c, Q=QF):
                            return Q[:, c * S: c * S + T]
                        b_sums = ln_sums(x1_of, waits_extra=[x_toks_next[0]])
                        b_rows = ln_rows(b_sums)
                        ln1_boot[0] = ln_tail(x1_of, xn[0], b_rows,
                                              xn_guard=[ffn1_last_mm[2]])
                if t >= 1 and grp % 2 == 1:
                    _, tr = emit_ffn2_chunk(t - 1, grp // 2)
                    xb_last_read = tr
            ffn1_last_mm[t] = grp_last_mm[-1]

        for eo in range(DC):
            mmtok, tr = emit_ffn2_chunk(NT - 1, eo)
            xb_last_read = tr
        hq_read_done = mmtok
        xn_free[0] = ffn1_last_mm[NT - 2]
        xn_free[1] = ffn1_last_mm[NT - 1]
        xb_read_done = xb_last_read

    sch.add("sync", lambda e: e.nop(), waits=[("ou", tok_out_total[0] + tok_out_dbg[0])])

    sch.check()
    with es, pes, nc.Block() as block:
        sch.emit(block, sems)
    return nc


def _host_inputs(core, L0_attn_x, W_qkv, W_proj, b_proj, w1, b1, w2, b2):
    xs = L0_attn_x[core * NSEQ:(core + 1) * NSEQ]
    x_fm = np.ascontiguousarray(xs.transpose(0, 2, 1))
    biases = np.zeros((128, 48), dtype=np.float32)
    biases[:, 0:8] = b_proj.reshape(8, 128).T
    biases[:, 8:40] = b1.reshape(32, 128).T
    biases[:, 40:48] = b2.reshape(8, 128).T
    sel = np.zeros((16, 8 * 128), dtype=np.float32)
    for c in range(8):
        sel[2 * c, c * 128: c * 128 + 64] = 1.0
        sel[2 * c + 1, c * 128 + 64: c * 128 + 128] = 1.0
    cbv = np.zeros((128, 8), dtype=np.float32)
    cbv[:, 0] = -3 * LN2C
    cbv[:, 1] = 3 * LN2C
    cbv[:, 2] = -10 * LN2C
    cbv[:, 3] = 10 * LN2C
    cbv[:, 4] = 2.0 ** -80
    cbv[:, 5] = 2.0 ** -120
    cbv[:, 6] = 1e-5
    return {
        "x": x_fm,
        "wqkvT": np.ascontiguousarray(W_qkv.T),
        "wprojT": np.ascontiguousarray(W_proj.T),
        "w1T": np.ascontiguousarray(w1.T),
        "w2T": np.ascontiguousarray(w2.T),
        "biases": biases,
        "sel": sel,
        "cb": cbv,
    }


_NC_CACHE = {}


def kernel(L0_attn_x, L0_attn_W_qkv, L0_attn_W_proj, L0_attn_b_proj,
           L0_ffn_w1, L0_ffn_b1, L0_ffn_w2, L0_ffn_b2):
    args = [np.asarray(a, dtype=np.float32) for a in
            (L0_attn_x, L0_attn_W_qkv, L0_attn_W_proj, L0_attn_b_proj,
             L0_ffn_w1, L0_ffn_b1, L0_ffn_w2, L0_ffn_b2)]
    if "nc" not in _NC_CACHE:
        _NC_CACHE["nc"] = build_nc()
    nc = _NC_CACHE["nc"]
    in_maps = [_host_inputs(c, *args) for c in range(N_CORES)]
    res = run_bass_kernel_spmd(nc, in_maps, core_ids=list(range(N_CORES)))
    outs = [r["out"].transpose(0, 2, 1) for r in res.results]
    return np.ascontiguousarray(np.concatenate(outs, axis=0))


# version 15
# speedup vs baseline: 1.1960x; 1.0251x over previous
import numpy as np
import ml_dtypes
from contextlib import ExitStack

import concourse.bass as bass
import concourse.mybir as mybir
from concourse.bass_utils import run_bass_kernel_spmd

F32 = mybir.dt.float32
BF16 = mybir.dt.bfloat16
F8E4 = mybir.dt.float8e4
PM = mybir.MatmulPerfMode
I32 = mybir.dt.int32
AF = mybir.ActivationFunctionType
ALU = mybir.AluOpType

LN2C = float(np.log(2.0))

N_CORES = 8
NSEQ = 4
S = 2048
D = 1024
DC = D // 128
T = 512
NT = S // T
HJ = 4096
HJC = HJ // 128
F8 = np.float32


class Sched:

    ENGINES = ("tensor", "vector", "scalar", "gpsimd", "sync")

    def __init__(self):
        self.ops = {e: [] for e in self.ENGINES}
        self.cnt = {}

    def add(self, eng, fn, waits=(), inc=None, n=1):
        w = [t for t in waits if t is not None]
        tok = None
        if inc is not None:
            self.cnt[inc] = self.cnt.get(inc, 0) + n
            tok = (inc, self.cnt[inc])
        self.ops[eng].append({"fn": fn, "waits": w, "inc": inc, "n": n})
        return tok

    def check(self):
        idx = {e: 0 for e in self.ENGINES}
        cnt = {s: 0 for s in self.cnt}
        total = sum(len(v) for v in self.ops.values())
        done = 0
        progress = True
        while progress:
            progress = False
            for e in self.ENGINES:
                while idx[e] < len(self.ops[e]):
                    op = self.ops[e][idx[e]]
                    if all(cnt.get(s, 0) >= v for (s, v) in op["waits"]):
                        if op["inc"]:
                            cnt[op["inc"]] = cnt.get(op["inc"], 0) + op["n"]
                        idx[e] += 1
                        done += 1
                        progress = True
                    else:
                        break
        if done != total:
            stuck = {e: (idx[e], len(self.ops[e]),
                         self.ops[e][idx[e]]["waits"] if idx[e] < len(self.ops[e]) else None)
                     for e in self.ENGINES}
            raise RuntimeError(f"schedule deadlock: done={done}/{total} stuck={stuck} cnt_sample={dict(list(cnt.items())[:10])}")

    def emit(self, block, sems):
        for ename in self.ENGINES:
            ops = self.ops[ename]

            def make(ops=ops):
                def body(eng):
                    last_wait = {}
                    for op in ops:
                        for (s, v) in op["waits"]:
                            if last_wait.get(s, -1) < v:
                                eng.wait_ge(sems[s], v)
                                last_wait[s] = v
                        ins = op["fn"](eng)
                        if op["inc"] is not None:
                            assert ins is not None, "inc op must return instruction"
                            ins.then_inc(sems[op["inc"]], op["n"])
                return body

            getattr(block, ename)(make())


def build_nc(debug_taps=False):
    nc = bass.Bass(target_bir_lowering=False)

    x_h = nc.dram_tensor("x", [NSEQ, D, S], F32, kind="ExternalInput")
    wqkv_h = nc.dram_tensor("wqkvT", [D, 3 * D], F32, kind="ExternalInput")
    wproj_h = nc.dram_tensor("wprojT", [D, D], F32, kind="ExternalInput")
    w1_h = nc.dram_tensor("w1T", [D, HJ], F32, kind="ExternalInput")
    w2_h = nc.dram_tensor("w2T", [HJ, D], F32, kind="ExternalInput")
    bias_h = nc.dram_tensor("biases", [128, 48], F32, kind="ExternalInput")
    sel_h = nc.dram_tensor("sel", [16, 8 * 128], F32, kind="ExternalInput")
    cb_h = nc.dram_tensor("cb", [128, 8], F32, kind="ExternalInput")
    out_h = nc.dram_tensor("out", [NSEQ, D, S], F32, kind="ExternalOutput")
    if debug_taps:
        dbg_bq = nc.dram_tensor("dbg_bq", [128, 8], F32, kind="ExternalOutput")
        dbg_w1 = nc.dram_tensor("dbg_w1", [8, 128, 4096], BF16, kind="ExternalOutput")
        dbg_xb = nc.dram_tensor("dbg_xb", [NSEQ, 128, 16384], BF16, kind="ExternalOutput")
        dbg_qf = nc.dram_tensor("dbg_qf", [NSEQ, 128, 16384], BF16, kind="ExternalOutput")
        dbg_kv2 = nc.dram_tensor("dbg_kv2", [NSEQ, 128, 1024], BF16, kind="ExternalOutput")
        dbg_bd = nc.dram_tensor("dbg_bd", [NSEQ, 128, 128], BF16, kind="ExternalOutput")
        dbg_at = nc.dram_tensor("dbg_at", [NSEQ, 128, 16384], BF16, kind="ExternalOutput")
        dbg_z = nc.dram_tensor("dbg_z", [NSEQ, NT, 16, T], BF16, kind="ExternalOutput")

    sp_q = nc.dram_tensor("sp_q", [8, 128, 1024], BF16, kind="Internal")
    sp_kv = nc.dram_tensor("sp_kv", [4, 128, 4096], BF16, kind="Internal")
    sp_pr = nc.dram_tensor("sp_pr", [8, 128, 1024], BF16, kind="Internal")
    sp_w1 = nc.dram_tensor("sp_w1", [8, 128, 4096], BF16, kind="Internal")
    sp_w2 = nc.dram_tensor("sp_w2", [8, 128, 4096], F8E4, kind="Internal")

    es = ExitStack()

    def sb(name, shape, dt=BF16):
        return es.enter_context(nc.sbuf_tensor(name, shape, dt))

    wstream = sb("wstream", [128, 16384], BF16)
    xa_hq = sb("xa_hq", [128, 16384], BF16)
    qf = sb("qf", [128, 16384], BF16)
    xb = sb("xb", [128, 16384], BF16)
    r_t = [sb(f"r{i}", [128, 1024], F32) for i in range(3)]
    l_t = [sb(f"l{i}", [128, 1024], F32) for i in range(2)]
    e_t = [sb(f"e{i}", [128, 1024], I32) for i in range(2)]
    inv_t = [sb(f"inv{i}", [128, 1024], BF16) for i in range(3)]
    g_t = [sb(f"g{i}", [128, 1024], BF16) for i in range(3)]
    xn0 = sb("xn0", [128, 8 * T], BF16)
    xn = [xn0, None]
    kf_nb = [inv_t[0], g_t[0], inv_t[1], g_t[1]]
    kv2 = inv_t[2]
    hq8 = {id(xa_hq): [xa_hq[:, 0:8192].bitcast(F8E4), xa_hq[:, 8192:16384].bitcast(F8E4)],
           id(qf): [qf[:, 0:8192].bitcast(F8E4), qf[:, 8192:16384].bitcast(F8E4)]}
    z_f = r_t[2][0:16, 0:T]
    kvacc_sb = r_t[2][0:128, T:2 * T]
    z_b = g_t[2][0:16, 0:T]
    zb_t = [g_t[2][0:128, T:2 * T]]
    v_nb = [sb(f"v{i}", [128, 1024], BF16) for i in range(4)]
    stg = [sb(f"stg{i}", [128, T], F32) for i in range(2)]
    sqs = [sb(f"sqs{i}", [128, T], F32) for i in range(2)]
    lnsq0 = sb("lnsq0", [128, T], BF16)
    lnsq1 = sb("lnsq1", [128, T], BF16)
    lnsq = [lnsq0, lnsq1]
    bd = sb("bd", [128, 8 * 16], BF16)
    sel_sb = sb("sel_sb", [16, 8 * 128], BF16)
    bias_sb = sb("bias_sb", [128, 48], F32)
    bq_sb = sb("bq_sb", [128, 8], F32)
    cb = sb("cb_sb", [128, 8], F32)
    ones1 = sb("ones1", [1, 128], BF16)
    negones = sb("negones", [1, 128], BF16)
    onescol = sb("onescol", [128, 1], BF16)
    rowF = sb("rowF", [1, 3 * T], F32)
    rowBf = sb("rowBf", [1, 2 * T], BF16)
    m_row = rowF[0:1, 0:T]
    s2_row = rowF[0:1, T:2 * T]
    mm_row = rowF[0:1, 2 * T:3 * T]
    rstd_row = rowF[0:1, T:2 * T]
    nb_rowf = rowF[0:1, 0:T]
    a_rowb = rowBf[0:1, 0:T]
    nb_rowb = rowBf[0:1, T:2 * T]
    ab_b = sb("ab_b", [128, T], BF16)
    bb_b = sb("bb_b", [128, T], BF16)
    ksacc_sb = sb("ksacc", [128, 8], F32)

    pes = ExitStack()

    def ps(name, shape):
        return pes.enter_context(nc.psum_tensor(name, shape, F32))

    pA = ps("pA", [128, 512])
    pB = ps("pB", [128, 512])
    pC = ps("pC", [128, 512])
    pD = ps("pD", [128, 512])
    pK1 = ps("pK1", [128, 512])
    pK2 = ps("pK2", [128, 512])
    pL1 = ps("pL1", [128, 512])
    pL2 = ps("pL2", [128, 512])

    sem_names = ["dl", "ds", "mm", "ea", "ev", "ou", "eg"]
    sems = {n: es.enter_context(nc.semaphore(n)) for n in sem_names}

    sch = Sched()

    qbuf = [wstream[:, i * 1024:(i + 1) * 1024] for i in range(2)]
    kvbuf = [wstream[:, 2048 + i * 4096: 2048 + (i + 1) * 4096] for i in range(2)]
    prbuf = [wstream[:, i * 1024:(i + 1) * 1024] for i in range(2)]
    w1buf = [wstream[:, i * 4096:(i + 1) * 4096] for i in range(2)]
    w2buf = [wstream[:, 8192 + i * 2048: 8192 + (i + 1) * 2048].bitcast(F8E4) for i in range(2)]
    xn[1] = wstream[:, 12288:16384]

    TINY4 = float(2.0 ** -80)
    TINY10 = float(2.0 ** -120)
    CLAMP4 = float(np.float32(-5.45 * LN2C))
    CLAMP10 = float(np.float32(-13.45 * 2 * LN2C))
    C2 = float(np.float32(1.0 / LN2C))
    C22 = float(np.float32(1.0 / (2 * LN2C)))
    CB_M3, CB_P3, CB_M10, CB_P10, CB_T4, CB_T10, CB_EPS = range(7)

    def cbc(i):
        return cb[:, i:i + 1]

    def quantize(src_ap, out_ap, man, signed, bufi, n, waits, extra_first_waits=(), helper="vector", kf=None):
        lt, et = l_t[bufi][:, :n], e_t[bufi][:, :n]
        it_, gt = (inv_t[bufi][:, :n], g_t[bufi][:, :n]) if kf is None else kf
        ef = l_t[bufi][:, :n]
        w0 = list(waits) + list(extra_first_waits)
        min_exp = -6 if man == 3 else -14
        cbm = CB_M3 if man == 3 else CB_M10
        cbp = CB_P3 if man == 3 else CB_P10
        if signed:
            sqt = r_t[bufi][:, :n]
            t_sq = sch.add("scalar", lambda e, o=sqt, i=src_ap: e.activation(o, i, AF.Square),
                           waits=w0, inc="ea")
            t_l = sch.add("scalar", lambda e, o=lt, i=sqt, b=cbc(CB_T10): e.activation(o, i, AF.Ln, bias=b),
                          waits=[t_sq], inc="ea")
            clamp = float(np.float32((min_exp + 0.55) * 2 * LN2C))
            cc = C22
        else:
            t_l = sch.add("scalar", lambda e, o=lt, i=src_ap, b=cbc(CB_T4): e.activation(o, i, AF.Ln, bias=b),
                          waits=w0, inc="ea")
            t_sq = None
            clamp = float(np.float32((min_exp + 0.55) * LN2C))
            cc = C2
        hsem = "ev" if helper == "vector" else "eg"
        t_e = sch.add(helper, lambda e, o=et, i=lt, c=cc: e.tensor_scalar(o, i, c, 0.5, op0=ALU.mult, op1=ALU.subtract),
                      waits=[t_l], inc=hsem)
        t_ef = sch.add(helper, lambda e, o=ef, i=et, c=float(min_exp): e.tensor_scalar(o, i, c, None, op0=ALU.max),
                       waits=[t_e], inc=hsem)
        t_inv = sch.add("scalar", lambda e, o=it_, i=ef, b=cbc(cbp): e.activation(o, i, AF.Exp, scale=-LN2C, bias=b),
                        waits=[t_ef], inc="ea")
        t_g = sch.add("scalar", lambda e, o=gt, i=ef, b=cbc(cbm): e.activation(o, i, AF.Exp, scale=LN2C, bias=b),
                      waits=[t_ef], inc="ea")
        t_y = sch.add("vector", lambda e, o=et, a=src_ap, b=it_: e.tensor_tensor(o, a, b, ALU.mult),
                      waits=[t_inv], inc="ev")
        t_q = sch.add("vector", lambda e, o=out_ap, a=et, b=gt: e.tensor_tensor(o, a, b, ALU.mult),
                      waits=[t_g, t_y], inc="ev")
        return t_q, t_y

    def ln_sums(x_tile_of, waits_extra=()):
        tok_mm1 = tok_mm2 = None
        for c in range(DC):
            xc = x_tile_of(c)
            sq = lnsq[c % 2][:, :]
            tw = [w for w in [lnsq_tok[c % 2]] if w] + (list(waits_extra) if c == 0 else [])
            tsq = sch.add("scalar", lambda e, o=sq, i=xc: e.activation(o, i, AF.Square),
                          waits=tw, inc="ea")
            w1_ = ([ln_bank_free[0], t_o3] + list(waits_extra)) if c == 0 else []
            tok_mm1 = sch.add("tensor",
                              lambda e, r=xc, s=(c == 0), p=(c == 7): e.matmul(pL1[0:1, :], onescol[:, :], r, start=s, stop=p),
                              waits=[w for w in w1_ if w], inc="mm" if c == 7 else None)
            w2_ = [tsq] + ([ln_bank_free[1]] if c == 0 and ln_bank_free[1] else [])
            tok_mm2 = sch.add("tensor",
                              lambda e, r=sq, s=(c == 0), p=(c == 7): e.matmul(pL2[0:1, :], onescol[:, :], r, start=s, stop=p),
                              waits=w2_, inc="mm")
            lnsq_tok[c % 2] = tok_mm2
        return tok_mm1, tok_mm2

    def ln_rows(sums_toks):
        tok_mm1, tok_mm2 = sums_toks
        t1 = sch.add("vector", lambda e: e.tensor_scalar(m_row[:, :], pL1[0:1, :], 1.0 / D, None, op0=ALU.mult),
                     waits=[tok_mm1], inc="ev")
        t2 = sch.add("vector", lambda e: e.tensor_scalar(s2_row[:, :], pL2[0:1, :], 1.0 / D, None, op0=ALU.mult),
                     waits=[tok_mm2], inc="ev")
        t3 = sch.add("vector", lambda e: e.tensor_tensor(mm_row[:, :], m_row[:, :], m_row[:, :], ALU.mult),
                     waits=[t1], inc="ev")
        t4 = sch.add("vector", lambda e: e.tensor_tensor(mm_row[:, :], s2_row[:, :], mm_row[:, :], ALU.subtract),
                     waits=[t2, t3], inc="ev")
        t5 = sch.add("scalar", lambda e: e.activation(mm_row[:, :], mm_row[:, :], AF.Ln, bias=cb[0:1, CB_EPS:CB_EPS + 1]),
                     waits=[t4], inc="ea")
        t6 = sch.add("scalar", lambda e: e.activation(rstd_row[:, :], mm_row[:, :], AF.Exp, scale=-0.5),
                     waits=[t5], inc="ea")
        t7 = sch.add("vector", lambda e: e.tensor_copy(a_rowb[:, :], rstd_row[:, :]), waits=[t6], inc="ev")
        t8 = sch.add("vector", lambda e: e.tensor_tensor(nb_rowf[:, :], m_row[:, :], rstd_row[:, :], ALU.mult),
                     waits=[t6], inc="ev")
        t9 = sch.add("vector", lambda e: e.tensor_copy(nb_rowb[:, :], nb_rowf[:, :]), waits=[t8], inc="ev")
        return (t1, t2, t7, t9)

    def ln_tail(x_tile_of, xn_buf, rows_toks, xn_guard=()):
        t1, t2, t7, t9 = rows_toks
        ta = sch.add("tensor", lambda e: e.matmul(pL1[:, :], ones1[0:1, 0:128], a_rowb[:, :], start=True, stop=True),
                     waits=[t7, t_o1], inc="mm")
        tb = sch.add("tensor", lambda e: e.matmul(pL2[:, :], negones[0:1, 0:128], nb_rowb[:, :], start=True, stop=True),
                     waits=[t9, t_o2], inc="mm")
        tc = sch.add("vector", lambda e: e.tensor_copy(ab_b[:, :], pL1[:, :]), waits=[ta], inc="ev")
        td = sch.add("vector", lambda e: e.tensor_copy(bb_b[:, :], pL2[:, :]), waits=[tb], inc="ev")
        ln_bank_free[0] = tc
        ln_bank_free[1] = td
        tok_xn = []
        for c in range(DC):
            xc = x_tile_of(c)
            dst = xn_buf[:, c * T:(c + 1) * T]
            tm = sch.add("vector", lambda e, o=dst, i=xc: e.tensor_tensor(o, i, ab_b[:, :], ALU.mult),
                         waits=[tc] + (list(xn_guard) if c == 0 else []), inc="ev")
            tn = sch.add("vector", lambda e, o=dst: e.tensor_tensor(o, o, bb_b[:, :], ALU.add),
                         waits=[td, tm], inc="ev")
            tok_xn.append(tn)
        return tok_xn

    bank_free = {k: None for k in ["pA", "pB", "pC", "pD", "pK1", "pK2"]}
    ln_bank_free = [None, None]
    xn_free = [None, None]
    lnsq_tok = [None, None]
    sqs_tok = [None, None]
    temps_free = [None, None, None]

    tok_bias = sch.add("sync", lambda e: e.dma_start(out=bias_sb[:, :], in_=bias_h.ap()), inc="ds", n=16)
    tok_cb = sch.add("sync", lambda e: e.dma_start(out=cb[:, :], in_=cb_h.ap()), inc="ds", n=16)
    tok_sel = sch.add("gpsimd", lambda e: e.dma_start(out=sel_sb[:, :], in_=sel_h.ap()), inc="dl", n=16)
    t_o1 = sch.add("vector", lambda e: e.memset(ones1[:, :], 1.0), inc="ev")
    t_o2 = sch.add("vector", lambda e: e.memset(negones[:, :], -1.0), inc="ev")
    t_o3 = sch.add("vector", lambda e: e.memset(onescol[:, :], 1.0), inc="ev")

    tok_bq, _ = quantize(bias_sb[:, 40:48], bq_sb[:, :], man=10, signed=True, bufi=0, n=8,
                         waits=[tok_bias, tok_cb])

    def load_x(s_, waits):
        src = x_h.ap()[s_].rearrange("(c p) t -> p c t", p=128)
        tgt = xa_hq if s_ % 2 == 0 else qf
        dst = tgt[:, :].rearrange("p (c t) -> p c t", c=8)
        toks = []
        for tt in range(NT):
            tok = sch.add("gpsimd",
                          lambda e, o=dst[:, :, tt * T:(tt + 1) * T], i=src[:, :, tt * T:(tt + 1) * T]:
                          e.dma_start(out=o, in_=i),
                          waits=(list(waits) if tt == 0 else []), inc="dl", n=16)
            toks.append(tok)
        return toks

    tok_xa0 = load_x(0, [tok_sel])

    spill_toks = []
    for jc in range(8):
        src = wqkv_h.ap()[:, jc * 128:(jc + 1) * 128].rearrange("(c p) q -> p c q", p=128)
        dst = sp_q.ap()[jc].rearrange("p (c q) -> p c q", c=8)
        tok = sch.add("gpsimd", lambda e, o=dst, i=src: e.dma_start(out=o, in_=i),
                      inc="dl", n=16)
        spill_toks.append(tok)
    spq_done = spill_toks[-1]
    for sl in range(4):
        src = wqkv_h.ap()[:, 1024 + sl * 512: 1024 + (sl + 1) * 512].rearrange("(c p) j -> p c j", p=128)
        dst = sp_kv.ap()[sl].rearrange("p (c j) -> p c j", c=8)
        tok = sch.add("gpsimd", lambda e, o=dst, i=src: e.dma_start(out=o, in_=i),
                      inc="dl", n=16)
        spill_toks.append(tok)
    spkv_done = spill_toks[-1]
    for jc in range(8):
        src = wproj_h.ap()[:, jc * 128:(jc + 1) * 128].rearrange("(c p) q -> p c q", p=128)
        dst = sp_pr.ap()[jc].rearrange("p (c q) -> p c q", c=8)
        tok = sch.add("gpsimd", lambda e, o=dst, i=src: e.dma_start(out=o, in_=i),
                      inc="dl", n=16)
        spill_toks.append(tok)
    sppr_done = spill_toks[-1]
    spw1_piece = []
    for pc in range(8):
        tok = None
        for c in range(8):
            src = w1_h.ap()[c * 128:(c + 1) * 128, pc * 512:(pc + 1) * 512].rearrange("p (jj q) -> p jj q", jj=4)
            dst = sp_w1.ap()[pc].rearrange("p (jj rest) -> p jj rest", jj=4)[:, :, c * 128:(c + 1) * 128]
            tok = sch.add("gpsimd", lambda e, o=dst, i=src: e.dma_start(out=o, in_=i),
                          inc="dl", n=16)
        spw1_piece.append(tok)
        spill_toks.append(tok)
    w2stage = [stg[0][:, 0:128].bitcast(F8E4), stg[1][:, 0:128].bitcast(F8E4)]
    w2_invg = [(wstream[:, 10240 + 1024 * i: 10240 + 1024 * i + 512],
                wstream[:, 10240 + 1024 * i + 512: 10240 + 1024 * (i + 1)]) for i in range(2)]
    w2_ld = [e_t[i].bitcast(F32)[:, 512:1024] for i in range(2)]
    W2 = {"i": 0, "prevq": [None, None], "ld_free": [None, None], "done": None}

    def emit_w2_step():
        k = W2["i"]
        if k >= 64:
            return
        W2["i"] = k + 1
        jc, h = k // 2, k % 2
        bi = k % 2
        src = w2_h.ap()[jc * 128:(jc + 1) * 128, h * 512:(h + 1) * 512]
        tok_ld = sch.add("sync", lambda e, o=w2_ld[bi], i=src: e.dma_start(out=o, in_=i),
                         waits=[w for w in [W2["ld_free"][bi], W2["prevq"][bi],
                                            tok_bq if k < 2 else None] if w],
                         inc="ds", n=16)
        tq, ty = quantize(w2_ld[bi], w2stage[bi][:, :], man=3, signed=True, bufi=bi, n=512,
                          waits=[tok_ld], kf=w2_invg[bi])
        W2["ld_free"][bi] = ty
        temps_free[bi] = tq
        dst = sp_w2.ap()[4 * h:4 * h + 4, :, jc * 128:(jc + 1) * 128].rearrange("a p q -> p a q")
        src2 = w2stage[bi].rearrange("p (a q) -> p a q", a=4)
        tok = sch.add("sync", lambda e, o=dst, i=src2: e.dma_start(out=o, in_=i),
                      waits=[tq], inc="ds", n=16)
        W2["prevq"][bi] = tok
        tok_stgfree[bi] = tok
        if W2["i"] == 64:
            W2["done"] = tok

    def emit_w2_steps(kk):
        for _ in range(kk):
            emit_w2_step()

    if debug_taps:
        sch.add("sync", lambda e: e.dma_start(out=dbg_bq.ap(), in_=bq_sb[:, :]),
                waits=[tok_bq], inc="ou", n=16)
        tok_out_dbg = [16]
        sch.add("sync", lambda e: e.dma_start(out=dbg_w1.ap(), in_=sp_w1.ap()),
                waits=[spill_toks[-1]], inc="ou", n=16)
        tok_out_dbg[0] += 16
    else:
        tok_out_dbg = [0]

    qbuf_free = [None, None]
    kvbuf_free = [None, None]
    prbuf_free = [None, None]
    w1buf_free = [None, None]
    w2buf_free = [None, None]
    last_spill = spill_toks[-1]

    tok_out_total = [0]
    hq_read_done = None
    tok_stgfree = [None, None]
    kfv_free = [None] * 4
    attn_over_qf_ok = {}
    x_toks_next = tok_xa0
    xb_read_done = None
    kv2_read_done = None
    bd_read_done = None
    zrow_free = None
    zb_free = [None, None]
    ab_free = None
    ln1_boot = [None]

    for s_ in range(NSEQ):
        x_toks_cur = x_toks_next
        qf_free_at = hq_read_done
        XA = xa_hq if s_ % 2 == 0 else qf
        QF = qf if s_ % 2 == 0 else xa_hq

        def x_of_t(t):
            def x_of(c, t=t):
                return XA[:, c * S + t * T: c * S + t * T + T]
            return x_of

        if ln1_boot[0] is not None:
            tails_t = {0: ln1_boot[0]}
            ln1_boot[0] = None
            sums_t, rows_t = {}, {}
        else:
            extra0 = [x_toks_cur[0]] + [w for w in temps_free if w]
            sums_t = {0: ln_sums(x_of_t(0), waits_extra=extra0)}
            rows_t = {0: ln_rows(sums_t[0])}
            tails_t = {0: ln_tail(x_of_t(0), xn[0], rows_t[0],
                                  xn_guard=[w for w in [xn_free[0]] if w])}
        for t in range(NT):
            tok_xn = tails_t[t]
            if t + 1 < NT:
                sums_t[t + 1] = ln_sums(x_of_t(t + 1), waits_extra=[x_toks_cur[t + 1]])
                rows_t[t + 1] = ln_rows(sums_t[t + 1])

            qrot = [pA, pB, pC, pD]
            qrot_names = ["pA", "pB", "pC", "pD"]
            for jc in range(8):
                bi = jc % 2
                wg = [qbuf_free[bi]]
                if t == 0 and jc == 0:
                    wg += [w1buf_free[0]] + prbuf_free
                    if s_ == 0:
                        wg += [spq_done]
                tok_w = sch.add("sync", lambda e, o=qbuf[bi], i=sp_q.ap()[jc]: e.dma_start(out=o, in_=i),
                                waits=[w for w in wg if w], inc="ds", n=16)
                mmw = [tok_w, bank_free[qrot_names[jc % 4]]] + tok_xn
                tok_mm = None
                for c in range(DC):
                    tok_mm = sch.add("tensor",
                                     lambda e, l_=qbuf[bi][:, c * 128:(c + 1) * 128],
                                     r=xn[t % 2][:, c * T:(c + 1) * T], p=qrot[jc % 4],
                                     s0=(c == 0), s1=(c == 7): e.matmul(p[:, :], l_, r, start=s0, stop=s1),
                                     waits=[w for w in mmw if w] if c == 0 else [],
                                     inc="mm" if c == 7 else None)
                qbuf_free[bi] = tok_mm
                sqt = sqs[bi][:, :]
                tevq = sch.add("scalar", lambda e, o=sqt, i=qrot[jc % 4][:, :]: e.activation(o, i, AF.Square),
                               waits=[tok_mm] + [w for w in [sqs_tok[bi]] if w]
                               + ([w for w in [qf_free_at] if w] if (t == 0 and jc == 0) else []),
                               inc="ea")
                bank_free[qrot_names[jc % 4]] = tevq
                dst = QF[:, jc * S + t * T: jc * S + t * T + T]
                tevq2 = sch.add("scalar", lambda e, o=dst, i=sqt: e.activation(o, i, AF.Exp, scale=-0.5),
                                waits=[tevq], inc="ea")
                sqs_tok[bi] = tevq2
                if jc == 7:
                    qf_tile_done = tevq2

            if t + 1 < NT:
                guard = [w for w in [xn_free[(t + 1) % 2]] if w]
                tails_t[t + 1] = ln_tail(x_of_t(t + 1), xn[(t + 1) % 2], rows_t[t + 1], xn_guard=guard)

            pnb = [pA, pB, pC, pD]
            pnb_names = ["pA", "pB", "pC", "pD"]
            kv_evac_toks = {}
            for sl in range(4):
                bi = sl % 2
                wgk = [kvbuf_free[bi]]
                if t == 0 and sl == 0:
                    wgk += [w1buf_free[1]]
                    if s_ == 0:
                        wgk += [spkv_done]
                if t == 0 and sl == 1:
                    wgk += [w2buf_free[0]]
                tok_w = sch.add("sync", lambda e, o=kvbuf[bi], i=sp_kv.ap()[sl]: e.dma_start(out=o, in_=i),
                                waits=[w for w in wgk if w], inc="ds", n=16)
                last_mm_of_nb = {}
                for nb in range(4):
                    tok_mm = None
                    w0 = [tok_w, bank_free[pnb_names[nb]]]
                    for c in range(DC):
                        tok_mm = sch.add("tensor",
                                         lambda e, l_=xn[t % 2][:, c * T + nb * 128: c * T + nb * 128 + 128],
                                         r=kvbuf[bi][:, c * 512:(c + 1) * 512], p=pnb[nb],
                                         s0=(c == 0), s1=(c == 7): e.matmul(p[:, :], l_, r, start=s0, stop=s1),
                                         waits=[w for w in w0 if w] if c == 0 else [],
                                         inc="mm" if c == 7 else None)
                    last_mm_of_nb[nb] = tok_mm
                kvbuf_free[bi] = last_mm_of_nb[3]
                for nb in range(4):
                    tok_mm = last_mm_of_nb[nb]
                    if sl < 2:
                        sqt = sqs[nb % 2][:, :]
                        te1 = sch.add("scalar", lambda e, o=sqt, i=pnb[nb][:, :]: e.activation(o, i, AF.Square),
                                      waits=[tok_mm] + [w for w in [sqs_tok[nb % 2]] if w],
                                      inc="ea")
                        dst = kf_nb[nb][:, sl * 512:(sl + 1) * 512]
                        kf_guard = ([kfv_free[nb]] if kfv_free[nb] else []) + \
                                   ([temps_free[nb // 2]] if temps_free[nb // 2] else [])
                        te2 = sch.add("scalar", lambda e, o=dst, i=sqt: e.activation(o, i, AF.Exp, scale=-0.5),
                                      waits=[te1] + (kf_guard if (sl == 0 and t == 0) else
                                                     ([kfv_free[nb]] if (sl == 0 and kfv_free[nb]) else [])),
                                      inc="ea")
                        sqs_tok[nb % 2] = te2
                        bank_free[pnb_names[nb]] = te1
                        kv_evac_toks[("k", sl, nb)] = te2
                    else:
                        dst = v_nb[nb][:, (sl - 2) * 512:(sl - 1) * 512]
                        te = sch.add("vector", lambda e, o=dst, i=pnb[nb][:, :]: e.tensor_copy(o, i),
                                     waits=[tok_mm] + ([kfv_free[nb]] if (sl == 2 and kfv_free[nb]) else []),
                                     inc="ev")
                        bank_free[pnb_names[nb]] = te
                        kv_evac_toks[("v", sl, nb)] = te

            wv0 = [kv_evac_toks[(kk, slx, nbx)] for nbx in range(4)
                   for (kk, slx) in (("k", 0), ("k", 1), ("v", 2), ("v", 3))]
            if t == 0:
                tz_kv = sch.add("vector", lambda e: e.memset(kvacc_sb[:, :], 0.0), waits=wv0, inc="ev")
                tz_ks = sch.add("vector", lambda e: e.memset(ksacc_sb[:, :], 0.0), waits=[tz_kv], inc="ev")
            pKband = [pK1, pK2]
            pKname = ["pK1", "pK2"]
            for hp in range(8):
                for half in range(2):
                    pb = pKband[half]
                    rows = slice(half * 64, (half + 1) * 64)
                    w0 = ([w for w in wv0 if w] if (hp == 0) else []) + \
                         [w for w in [bank_free[pKname[half]]] if w]
                    tok_kv = None
                    for nb in range(4):
                        tok_kv = sch.add("tensor",
                                         lambda e, l_=kf_nb[nb][:, hp * 128 + half * 64: hp * 128 + half * 64 + 64],
                                         r=v_nb[nb][:, hp * 128 + half * 64: hp * 128 + half * 64 + 64],
                                         p=pb[rows, 0:64],
                                         s0=(nb == 0), s1=(nb == 3), tp=(0, 64) if half else None:
                                         e.matmul(p, l_, r, start=s0, stop=s1, tile_position=tp),
                                         waits=w0 if nb == 0 else [],
                                         inc="mm" if nb == 3 else None)
                    dst = kvacc_sb[rows, hp * 64:(hp + 1) * 64]
                    tacc = sch.add("vector",
                                   lambda e, o=dst, p=pb[rows, 0:64]:
                                   e.tensor_tensor(o, o, p, ALU.add),
                                   waits=[tok_kv] + ([tz_ks] if (t == 0 and hp == 0) else []), inc="ev")
                    bank_free[pKname[half]] = tacc
            for c in range(DC):
                pb = pKband[c % 2]
                w0 = [w for w in [bank_free[pKname[c % 2]], t_o3] if w]
                tok_ks = None
                for nb in range(4):
                    tok_ks = sch.add("tensor",
                                     lambda e, l_=kf_nb[nb][:, c * 128:(c + 1) * 128],
                                     p=pb[:, 0:1], s0=(nb == 0), s1=(nb == 3):
                                     e.matmul(p, l_, onescol[:, :], start=s0, stop=s1),
                                     waits=w0 if nb == 0 else [],
                                     inc="mm" if nb == 3 else None)
                dst = ksacc_sb[:, c:c + 1]
                tacc = sch.add("vector",
                               lambda e, o=dst, p=pb[:, 0:1]: e.tensor_tensor(o, o, p, ALU.add),
                               waits=[tok_ks], inc="ev")
                bank_free[pKname[c % 2]] = tacc
            for nb in range(4):
                kfv_free[nb] = tok_ks
            xn_free[t % 2] = tok_ks
            kvks_done = bank_free["pK1"] if bank_free["pK1"][1] > bank_free["pK2"][1] else bank_free["pK2"]
            if s_ == 0:
                emit_w2_steps(3)

        w_prep = [kvks_done] + [w for w in [kv2_read_done, bd_read_done] if w]
        t_z0 = sch.add("vector", lambda e: e.memset(kv2[:, :], 0.0), waits=w_prep, inc="ev")
        t_z1 = sch.add("vector", lambda e: e.memset(bd[:, :], 0.0), waits=[t_z0], inc="ev")
        kv2_toks = []
        for c in range(DC):
            ta_ = sch.add("vector", lambda e, c=c: e.tensor_copy(kv2[0:64, c * 128: c * 128 + 64],
                                                                 kvacc_sb[0:64, c * 64:(c + 1) * 64]),
                          waits=[t_z0], inc="ev")
            tb_ = sch.add("vector", lambda e, c=c: e.tensor_copy(kv2[64:128, c * 128 + 64: c * 128 + 128],
                                                                 kvacc_sb[64:128, c * 64:(c + 1) * 64]),
                          waits=[t_z0], inc="ev")
            kv2_toks += [ta_, tb_]
        bd_toks = []
        for c in range(DC):
            ta_ = sch.add("vector", lambda e, c=c: e.tensor_copy(bd[0:64, c * 16 + 2 * c: c * 16 + 2 * c + 1],
                                                                 ksacc_sb[0:64, c:c + 1]),
                          waits=[t_z1], inc="ev")
            tb_ = sch.add("vector", lambda e, c=c: e.tensor_copy(bd[64:128, c * 16 + 2 * c + 1: c * 16 + 2 * c + 2],
                                                                 ksacc_sb[64:128, c:c + 1]),
                          waits=[t_z1], inc="ev")
            bd_toks += [ta_, tb_]
        kv2_done = kv2_toks[-1]
        bd_done = bd_toks[-1]
        if s_ == 0:
            emit_w2_steps(4)
        if debug_taps:
            sch.add("sync", lambda e, s_=s_: e.dma_start(out=dbg_kv2.ap()[s_], in_=kv2[:, :]),
                    waits=[kv2_done], inc="ou", n=16)
            sch.add("sync", lambda e, s_=s_: e.dma_start(out=dbg_bd.ap()[s_], in_=bd[:, :]),
                    waits=[bd_done], inc="ou", n=16)
            tok_out_dbg[0] += 32

        qf_dump_gate = []
        if debug_taps:
            tqd = sch.add("sync", lambda e, s_=s_, q=QF: e.dma_start(out=dbg_qf.ap()[s_], in_=q[:, :]),
                    waits=[qf_tile_done], inc="ou", n=16)
            tok_out_dbg[0] += 16
            qf_dump_gate = [tqd]
        proj_last = None
        attn_toks_t = {}
        for t in range(NT):
            tok_zd = None
            for c in range(DC):
                w0 = [bd_done, qf_tile_done, bank_free["pC"]] if c == 0 else []
                tok_zd = sch.add("tensor",
                                 lambda e, l_=bd[:, c * 16:(c + 1) * 16],
                                 r=QF[:, c * S + t * T: c * S + t * T + T],
                                 s0=(c == 0), s1=(c == 7): e.matmul(pC[0:16, :], l_, r, start=s0, stop=s1),
                                 waits=[w for w in w0 if w], inc="mm" if c == 7 else None)
            tz1 = sch.add("vector", lambda e: e.tensor_scalar(z_f[:, :], pC[0:16, :], 1e-6, None, op0=ALU.add),
                          waits=[tok_zd] + [w for w in [zrow_free, temps_free[2]] if w], inc="ev")
            bank_free["pC"] = tz1
            tz2 = sch.add("vector", lambda e: e.reciprocal(z_f[:, :], z_f[:, :]), waits=[tz1], inc="ev")
            tz3 = sch.add("vector", lambda e: e.tensor_copy(z_b[:, :], z_f[:, :]), waits=[tz2], inc="ev")
            zrow_free = tz3
            if debug_taps:
                tzd = sch.add("sync", lambda e, s_=s_, t=t: e.dma_start(out=dbg_z.ap()[s_, t], in_=z_b[:, :]),
                              waits=[tz3], inc="ou", n=16)
                tok_out_dbg[0] += 16
                zrow_free = tzd
            attn_toks = []
            for c in range(DC):
                bi = 0
                tz4 = sch.add("tensor", lambda e, c=c: e.matmul(pC[:, :], sel_sb[:, c * 128:(c + 1) * 128],
                                                                z_b[:, :], start=True, stop=True),
                              waits=[tz3, kv2_done, bank_free["pC"]] + qf_dump_gate, inc="mm")
                tz5 = sch.add("vector", lambda e, o=zb_t[bi][:, :]: e.tensor_copy(o, pC[:, :]),
                              waits=[tz4] + [w for w in [zb_free[bi]] if w], inc="ev")
                bank_free["pC"] = tz5
                ta_ = sch.add("tensor", lambda e, c=c, r=QF[:, c * S + t * T: c * S + t * T + T]:
                              e.matmul(pD[:, :], kv2[:, c * 128:(c + 1) * 128], r, start=True, stop=True),
                              waits=[bank_free["pD"], kv2_done], inc="mm")
                dst = QF[:, c * S + t * T: c * S + t * T + T]
                tm_ = sch.add("vector", lambda e, o=dst, b=zb_t[bi][:, :]: e.tensor_tensor(o, pD[:, :], b, ALU.mult),
                              waits=[ta_, tz5], inc="ev")
                bank_free["pD"] = tm_
                zb_free[bi] = tm_
                attn_toks.append(tm_)
            attn_toks_t[t] = attn_toks
            if s_ == 0:
                emit_w2_steps(6)
        dbg_gate = []
        if debug_taps:
            tdump = sch.add("sync", lambda e, s_=s_, q=QF: e.dma_start(out=dbg_at.ap()[s_], in_=q[:, :]),
                            waits=[tok for tl in attn_toks_t.values() for tok in tl], inc="ou", n=16)
            tok_out_dbg[0] += 16
            dbg_gate = [tdump]
        def xb_of_t(t):
            def xb_of(c, t=t):
                return xb[:, c * S + t * T: c * S + t * T + T]
            return xb_of

        ln2_tail = {}
        proj_te_t = {}
        for t in range(NT):
            attn_toks = attn_toks_t[t]
            prot = [pA, pB, pK1, pK2]
            prot_names = ["pA", "pB", "pK1", "pK2"]
            for jc in range(8):
                bi = jc % 2
                wg = [prbuf_free[bi]]
                if t == 0 and jc == 0:
                    wg += qbuf_free + kvbuf_free
                    if s_ == 0:
                        wg += [sppr_done]
                tok_w = sch.add("sync", lambda e, o=prbuf[bi], i=sp_pr.ap()[jc]: e.dma_start(out=o, in_=i),
                                waits=[w for w in wg if w], inc="ds", n=16)
                ri = jc % 4
                w0 = [tok_w, bank_free[prot_names[ri]]] + attn_toks + dbg_gate
                tok_mm = None
                for cE in range(DC):
                    tok_mm = sch.add("tensor",
                                     lambda e, l_=prbuf[bi][:, cE * 128:(cE + 1) * 128],
                                     r=QF[:, cE * S + t * T: cE * S + t * T + T], p=prot[ri],
                                     s0=(cE == 0), s1=(cE == 7): e.matmul(p[:, :], l_, r, start=s0, stop=s1),
                                     waits=[w for w in w0 if w] if cE == 0 else [],
                                     inc="mm" if cE == 7 else None)
                prbuf_free[bi] = tok_mm
                dst = xb[:, jc * S + t * T: jc * S + t * T + T]
                xw = [tok_mm] + ([xb_read_done] if (xb_read_done and t == 0 and jc == 0) else [])
                te = sch.add("vector",
                             lambda e, o=dst, p=prot[ri], b=bias_sb[:, jc:jc + 1],
                             xc=XA[:, jc * S + t * T: jc * S + t * T + T]:
                             e.scalar_tensor_tensor(o, p[:, :], b, xc, op0=ALU.add, op1=ALU.add),
                             waits=[w for w in xw if w], inc="ev")
                bank_free[prot_names[ri]] = te
                proj_last = te
            proj_te_t[t] = te
            if s_ == 0:
                emit_w2_steps(6)
            if t == 0:
                ln2_sums0 = ln_sums(xb_of_t(0), waits_extra=[te])
                ln2_rows0 = ln_rows(ln2_sums0)
                ln2_tail[0] = ln_tail(xb_of_t(0), xn[0], ln2_rows0,
                                      xn_guard=[w for w in [xn_free[0]] if w])
        if debug_taps:
            sch.add("sync", lambda e, s_=s_: e.dma_start(out=dbg_xb.ap()[s_], in_=xb[:, :]),
                    waits=[proj_last], inc="ou", n=16)
            tok_out_dbg[0] += 16
        attn_over_qf_ok[s_] = proj_last
        kv2_read_done = ta_
        bd_read_done = tok_zd
        xa_read_done = proj_last
        if s_ + 1 < NSEQ:
            x_toks_next = load_x(s_ + 1, [proj_last])

        xb_last_read = None
        hqv_par = hq8[id(XA)]
        hq_toks_t = {}
        hq_free = [None, None]
        ffn1_last_mm = {}
        first_w2 = [True]

        def emit_ffn2_chunk(tp, eo):
            ebi = eo % 2
            pe_ = pC if ebi == 0 else pD
            pe_name = "pC" if ebi == 0 else "pD"
            wg = [w2buf_free[ebi]]
            if first_w2[0]:
                wg += kvbuf_free
                if s_ == 0:
                    wg += [W2["done"]]
                first_w2[0] = False
            tok_w2 = sch.add("sync", lambda e, o=w2buf[ebi], i=sp_w2.ap()[eo]: e.dma_start(out=o, in_=i),
                             waits=[w for w in wg if w], inc="ds", n=16)
            hqv = hqv_par[tp % 2]
            toks = hq_toks_t[tp]
            w0 = [tok_w2, bank_free[pe_name]]
            tok_mm = None
            for c in range(16):
                jw = ([w for w in w0 if w] if c == 0 else []) + \
                     ([toks[c]] if eo == 0 else [])
                tok_mm = sch.add("tensor",
                                 lambda e, l_=w2buf[ebi][:, c * 256:(c + 1) * 256].rearrange("p (i m) -> p i m", i=2),
                                 r=hqv[:, c * 1024:(c + 1) * 1024].rearrange("p (i n) -> p i n", i=2), p=pe_,
                                 s0=(c == 0), s1=(c == 15): e.matmul(p[:, :], l_, r, start=s0, stop=s1,
                                                                     perf_mode=PM.DoubleRow),
                                 waits=jw, inc="mm" if c == 15 else None)
            w2buf_free[ebi] = tok_mm
            if eo == 7:
                hq_free[tp % 2] = tok_mm
            vtile = sqs[ebi][:, :]
            tv = sch.add("vector", lambda e, o=vtile, p=pe_, b=bq_sb[:, eo:eo + 1]:
                         e.tensor_scalar(o, p[:, :], b, None, op0=ALU.add),
                         waits=[tok_mm] + [w for w in [tok_stgfree[ebi], sqs_tok[ebi]] if w], inc="ev")
            bank_free[pe_name] = tv
            qdst = stg[ebi][:, :]
            tq, ty = quantize(vtile, qdst, man=10, signed=True, bufi=ebi, n=512,
                              waits=[tv] + [w for w in [temps_free[ebi]] if w])
            temps_free[ebi] = tq
            sqs_tok[ebi] = ty
            tr = sch.add("vector", lambda e, o=qdst, xc=xb[:, eo * S + tp * T: eo * S + tp * T + T]:
                         e.tensor_tensor(o, o, xc, ALU.add),
                         waits=[tq], inc="ev")
            tdo = sch.add("gpsimd", lambda e, i=qdst, o=out_h.ap()[s_, eo * 128:(eo + 1) * 128, tp * T:(tp + 1) * T]:
                          e.dma_start(out=o, in_=i),
                          waits=[tr], inc="ou", n=16)
            tok_out_total[0] += 16
            tok_stgfree[ebi] = tdo
            return tok_mm, tr

        for t in range(NT):
            tok_xn2 = ln2_tail[t]
            if t + 1 < NT:
                ln2_sums_next = ln_sums(xb_of_t(t + 1), waits_extra=[proj_te_t[t + 1]])
                ln2_rows_next = ln_rows(ln2_sums_next)

            frot = [pA, pB, pK1, pK2]
            frot_names = ["pA", "pB", "pK1", "pK2"]
            hq_toks = []
            hq_toks_t[t] = hq_toks
            for grp in range(16):
                gbuf = grp % 2
                if grp % 2 == 0:
                    pc = grp // 2
                    wbi = pc % 2
                    wg = [w1buf_free[wbi]]
                    if t == 0 and grp == 0:
                        wg += prbuf_free + kvbuf_free
                    if s_ == 0 and t == 0:
                        wg += [spw1_piece[pc]]
                    tok_w1 = sch.add("sync", lambda e, o=w1buf[wbi], i=sp_w1.ap()[pc]: e.dma_start(out=o, in_=i),
                                     waits=[w for w in wg if w], inc="ds", n=16)
                    w1tok_cur = tok_w1
                grp_last_mm = []
                for jj in range(2):
                    j = grp * 2 + jj
                    bi = j % 2
                    pcbuf = (j // 4) % 2
                    off = (j % 4) * 1024
                    ri = j % 4
                    w0 = [w1tok_cur, bank_free[frot_names[ri]]] + (tok_xn2 if j == 0 else [])
                    tok_mm = None
                    for c in range(DC):
                        tok_mm = sch.add("tensor",
                                         lambda e, l_=w1buf[pcbuf][:, off + c * 128: off + (c + 1) * 128],
                                         r=xn[t % 2][:, c * T:(c + 1) * T], p=frot[ri],
                                         s0=(c == 0), s1=(c == 7): e.matmul(p[:, :], l_, r, start=s0, stop=s1),
                                         waits=[w for w in w0 if w] if c == 0 else [],
                                         inc="mm" if c == 7 else None)
                    grp_last_mm.append(tok_mm)
                    dst = r_t[gbuf][:, jj * T:(jj + 1) * T]
                    ew = [tok_mm] + ([w for w in [temps_free[gbuf]] if w] if jj == 0 or j % 2 == 1 else [])
                    te = sch.add("scalar",
                                 lambda e, o=dst, p=frot[ri], b=bias_sb[:, 8 + j:8 + j + 1]:
                                 e.activation(o, p[:, :], AF.Relu, bias=b),
                                 waits=ew, inc="ea")
                    bank_free[frot_names[ri]] = te
                    if jj == 0:
                        evac_done0 = te
                    else:
                        evac_done = te
                if grp % 2 == 1:
                    w1buf_free[(grp // 2) % 2] = grp_last_mm[-1]
                hq_dst = hqv_par[t % 2][:, grp * 1024:(grp + 1) * 1024]
                efw = []
                if grp == 0:
                    efw = [xa_read_done] if t <= 1 else [w for w in [hq_free[t % 2]] if w]
                tq, _ = quantize(r_t[gbuf][:, :], hq_dst, man=3, signed=False, bufi=gbuf, n=1024,
                                 waits=[evac_done, evac_done0],
                                 extra_first_waits=efw)
                temps_free[gbuf] = tq
                hq_toks.append(tq)
                if grp == 2:
                    if t + 1 < NT:
                        xg = xn_free[1] if t == 0 else ffn1_last_mm[t - 1]
                        ln2_tail[t + 1] = ln_tail(xb_of_t(t + 1), xn[(t + 1) % 2], ln2_rows_next,
                                                  xn_guard=[w for w in [xg] if w])
                    elif s_ + 1 < NSEQ:
                        def x1_of(c, Q=QF):
                            return Q[:, c * S: c * S + T]
                        b_sums = ln_sums(x1_of, waits_extra=[x_toks_next[0]])
                        b_rows = ln_rows(b_sums)
                        ln1_boot[0] = ln_tail(x1_of, xn[0], b_rows,
                                              xn_guard=[ffn1_last_mm[2]])
                if t >= 1 and grp % 2 == 1:
                    _, tr = emit_ffn2_chunk(t - 1, grp // 2)
                    xb_last_read = tr
            ffn1_last_mm[t] = grp_last_mm[-1]

        for eo in range(DC):
            mmtok, tr = emit_ffn2_chunk(NT - 1, eo)
            xb_last_read = tr
        hq_read_done = mmtok
        xn_free[0] = ffn1_last_mm[NT - 2]
        xn_free[1] = ffn1_last_mm[NT - 1]
        xb_read_done = xb_last_read

    sch.add("sync", lambda e: e.nop(), waits=[("ou", tok_out_total[0] + tok_out_dbg[0])])

    sch.check()
    with es, pes, nc.Block() as block:
        sch.emit(block, sems)
    return nc


def _host_inputs(core, L0_attn_x, W_qkv, W_proj, b_proj, w1, b1, w2, b2):
    xs = L0_attn_x[core * NSEQ:(core + 1) * NSEQ]
    x_fm = np.ascontiguousarray(xs.transpose(0, 2, 1))
    biases = np.zeros((128, 48), dtype=np.float32)
    biases[:, 0:8] = b_proj.reshape(8, 128).T
    biases[:, 8:40] = b1.reshape(32, 128).T
    biases[:, 40:48] = b2.reshape(8, 128).T
    sel = np.zeros((16, 8 * 128), dtype=np.float32)
    for c in range(8):
        sel[2 * c, c * 128: c * 128 + 64] = 1.0
        sel[2 * c + 1, c * 128 + 64: c * 128 + 128] = 1.0
    cbv = np.zeros((128, 8), dtype=np.float32)
    cbv[:, 0] = -3 * LN2C
    cbv[:, 1] = 3 * LN2C
    cbv[:, 2] = -10 * LN2C
    cbv[:, 3] = 10 * LN2C
    cbv[:, 4] = 2.0 ** -80
    cbv[:, 5] = 2.0 ** -120
    cbv[:, 6] = 1e-5
    return {
        "x": x_fm,
        "wqkvT": np.ascontiguousarray(W_qkv.T),
        "wprojT": np.ascontiguousarray(W_proj.T),
        "w1T": np.ascontiguousarray(w1.T),
        "w2T": np.ascontiguousarray(w2.T),
        "biases": biases,
        "sel": sel,
        "cb": cbv,
    }


_NC_CACHE = {}


def kernel(L0_attn_x, L0_attn_W_qkv, L0_attn_W_proj, L0_attn_b_proj,
           L0_ffn_w1, L0_ffn_b1, L0_ffn_w2, L0_ffn_b2):
    args = [np.asarray(a, dtype=np.float32) for a in
            (L0_attn_x, L0_attn_W_qkv, L0_attn_W_proj, L0_attn_b_proj,
             L0_ffn_w1, L0_ffn_b1, L0_ffn_w2, L0_ffn_b2)]
    if "nc" not in _NC_CACHE:
        _NC_CACHE["nc"] = build_nc()
    nc = _NC_CACHE["nc"]
    in_maps = [_host_inputs(c, *args) for c in range(N_CORES)]
    res = run_bass_kernel_spmd(nc, in_maps, core_ids=list(range(N_CORES)))
    outs = [r["out"].transpose(0, 2, 1) for r in res.results]
    return np.ascontiguousarray(np.concatenate(outs, axis=0))


# version 16 (final)
# speedup vs baseline: 1.2073x; 1.0094x over previous
import numpy as np
import ml_dtypes
from contextlib import ExitStack

import concourse.bass as bass
import concourse.mybir as mybir
from concourse.bass_utils import run_bass_kernel_spmd

F32 = mybir.dt.float32
BF16 = mybir.dt.bfloat16
F8E4 = mybir.dt.float8e4
PM = mybir.MatmulPerfMode
I32 = mybir.dt.int32
AF = mybir.ActivationFunctionType
ALU = mybir.AluOpType

LN2C = float(np.log(2.0))

N_CORES = 8
NSEQ = 4
S = 2048
D = 1024
DC = D // 128
T = 512
NT = S // T
HJ = 4096
HJC = HJ // 128
F8 = np.float32


class Sched:

    ENGINES = ("tensor", "vector", "scalar", "gpsimd", "sync")

    def __init__(self):
        self.ops = {e: [] for e in self.ENGINES}
        self.cnt = {}

    def add(self, eng, fn, waits=(), inc=None, n=1):
        w = [t for t in waits if t is not None]
        tok = None
        if inc is not None:
            self.cnt[inc] = self.cnt.get(inc, 0) + n
            tok = (inc, self.cnt[inc])
        self.ops[eng].append({"fn": fn, "waits": w, "inc": inc, "n": n})
        return tok

    def check(self):
        idx = {e: 0 for e in self.ENGINES}
        cnt = {s: 0 for s in self.cnt}
        total = sum(len(v) for v in self.ops.values())
        done = 0
        progress = True
        while progress:
            progress = False
            for e in self.ENGINES:
                while idx[e] < len(self.ops[e]):
                    op = self.ops[e][idx[e]]
                    if all(cnt.get(s, 0) >= v for (s, v) in op["waits"]):
                        if op["inc"]:
                            cnt[op["inc"]] = cnt.get(op["inc"], 0) + op["n"]
                        idx[e] += 1
                        done += 1
                        progress = True
                    else:
                        break
        if done != total:
            stuck = {e: (idx[e], len(self.ops[e]),
                         self.ops[e][idx[e]]["waits"] if idx[e] < len(self.ops[e]) else None)
                     for e in self.ENGINES}
            raise RuntimeError(f"schedule deadlock: done={done}/{total} stuck={stuck} cnt_sample={dict(list(cnt.items())[:10])}")

    def emit(self, block, sems):
        for ename in self.ENGINES:
            ops = self.ops[ename]

            def make(ops=ops):
                def body(eng):
                    last_wait = {}
                    for op in ops:
                        for (s, v) in op["waits"]:
                            if last_wait.get(s, -1) < v:
                                eng.wait_ge(sems[s], v)
                                last_wait[s] = v
                        ins = op["fn"](eng)
                        if op["inc"] is not None:
                            assert ins is not None, "inc op must return instruction"
                            ins.then_inc(sems[op["inc"]], op["n"])
                return body

            getattr(block, ename)(make())


def build_nc(debug_taps=False):
    nc = bass.Bass(target_bir_lowering=False)

    x_h = nc.dram_tensor("x", [NSEQ, D, S], F32, kind="ExternalInput")
    wqkv_h = nc.dram_tensor("wqkvT", [D, 3 * D], F32, kind="ExternalInput")
    wproj_h = nc.dram_tensor("wprojT", [D, D], F32, kind="ExternalInput")
    w1_h = nc.dram_tensor("w1T", [D, HJ], F32, kind="ExternalInput")
    w2_h = nc.dram_tensor("w2T", [HJ, D], F32, kind="ExternalInput")
    bias_h = nc.dram_tensor("biases", [128, 48], F32, kind="ExternalInput")
    sel_h = nc.dram_tensor("sel", [16, 8 * 128], F32, kind="ExternalInput")
    cb_h = nc.dram_tensor("cb", [128, 8], F32, kind="ExternalInput")
    out_h = nc.dram_tensor("out", [NSEQ, D, S], F32, kind="ExternalOutput")
    if debug_taps:
        dbg_bq = nc.dram_tensor("dbg_bq", [128, 8], F32, kind="ExternalOutput")
        dbg_w1 = nc.dram_tensor("dbg_w1", [8, 128, 4096], BF16, kind="ExternalOutput")
        dbg_xb = nc.dram_tensor("dbg_xb", [NSEQ, 128, 16384], BF16, kind="ExternalOutput")
        dbg_qf = nc.dram_tensor("dbg_qf", [NSEQ, 128, 16384], BF16, kind="ExternalOutput")
        dbg_kv2 = nc.dram_tensor("dbg_kv2", [NSEQ, 128, 1024], BF16, kind="ExternalOutput")
        dbg_bd = nc.dram_tensor("dbg_bd", [NSEQ, 128, 128], BF16, kind="ExternalOutput")
        dbg_at = nc.dram_tensor("dbg_at", [NSEQ, 128, 16384], BF16, kind="ExternalOutput")
        dbg_z = nc.dram_tensor("dbg_z", [NSEQ, NT, 16, T], BF16, kind="ExternalOutput")

    sp_q = nc.dram_tensor("sp_q", [8, 128, 1024], BF16, kind="Internal")
    sp_kv = nc.dram_tensor("sp_kv", [4, 128, 4096], BF16, kind="Internal")
    sp_pr = nc.dram_tensor("sp_pr", [8, 128, 1024], BF16, kind="Internal")
    sp_w1 = nc.dram_tensor("sp_w1", [8, 128, 4096], BF16, kind="Internal")
    sp_w2 = nc.dram_tensor("sp_w2", [8, 128, 4096], F8E4, kind="Internal")

    es = ExitStack()

    def sb(name, shape, dt=BF16):
        return es.enter_context(nc.sbuf_tensor(name, shape, dt))

    wstream = sb("wstream", [128, 16384], BF16)
    xa_hq = sb("xa_hq", [128, 16384], BF16)
    qf = sb("qf", [128, 16384], BF16)
    xb = sb("xb", [128, 16384], BF16)
    r_t = [sb(f"r{i}", [128, 1024], F32) for i in range(3)]
    l_t = [sb(f"l{i}", [128, 1024], F32) for i in range(2)]
    e_t = [sb(f"e{i}", [128, 1024], I32) for i in range(2)]
    inv_t = [sb(f"inv{i}", [128, 1024], BF16) for i in range(3)]
    g_t = [sb(f"g{i}", [128, 1024], BF16) for i in range(3)]
    xn0 = sb("xn0", [128, 8 * T], BF16)
    xn = [xn0, None]
    kf_nb = [inv_t[0], g_t[0], inv_t[1], g_t[1]]
    kv2 = inv_t[2]
    hq8 = {id(xa_hq): [xa_hq[:, 0:8192].bitcast(F8E4), xa_hq[:, 8192:16384].bitcast(F8E4)],
           id(qf): [qf[:, 0:8192].bitcast(F8E4), qf[:, 8192:16384].bitcast(F8E4)]}
    z_f = r_t[2][0:16, 0:T]
    kvacc_sb = r_t[2][0:128, T:2 * T]
    z_b = g_t[2][0:16, 0:T]
    zb_t = [g_t[2][0:128, T:2 * T]]
    v_nb = [sb(f"v{i}", [128, 1024], BF16) for i in range(4)]
    stg = [sb(f"stg{i}", [128, T], F32) for i in range(2)]
    sqs = [sb(f"sqs{i}", [128, T], F32) for i in range(2)]
    lnsq0 = sb("lnsq0", [128, T], BF16)
    lnsq1 = sb("lnsq1", [128, T], BF16)
    lnsq = [lnsq0, lnsq1]
    bd = sb("bd", [128, 8 * 16], BF16)
    sel_sb = sb("sel_sb", [16, 8 * 128], BF16)
    bias_sb = sb("bias_sb", [128, 48], F32)
    bq_sb = sb("bq_sb", [128, 8], F32)
    cb = sb("cb_sb", [128, 8], F32)
    ones1 = sb("ones1", [1, 128], BF16)
    negones = sb("negones", [1, 128], BF16)
    onescol = sb("onescol", [128, 1], BF16)
    rowF = sb("rowF", [1, 3 * T], F32)
    rowBf = sb("rowBf", [1, 2 * T], BF16)
    m_row = rowF[0:1, 0:T]
    s2_row = rowF[0:1, T:2 * T]
    mm_row = rowF[0:1, 2 * T:3 * T]
    rstd_row = rowF[0:1, T:2 * T]
    nb_rowf = rowF[0:1, 0:T]
    a_rowb = rowBf[0:1, 0:T]
    nb_rowb = rowBf[0:1, T:2 * T]
    ab_b = sb("ab_b", [128, T], BF16)
    bb_b = sb("bb_b", [128, T], BF16)
    ksacc_sb = sb("ksacc", [128, 8], F32)

    pes = ExitStack()

    def ps(name, shape):
        return pes.enter_context(nc.psum_tensor(name, shape, F32))

    pA = ps("pA", [128, 512])
    pB = ps("pB", [128, 512])
    pC = ps("pC", [128, 512])
    pD = ps("pD", [128, 512])
    pK1 = ps("pK1", [128, 512])
    pK2 = ps("pK2", [128, 512])
    pL1 = ps("pL1", [128, 512])
    pL2 = ps("pL2", [128, 512])

    sem_names = ["dl", "ds", "mm", "ea", "ev", "ou", "eg"]
    sems = {n: es.enter_context(nc.semaphore(n)) for n in sem_names}

    sch = Sched()

    qbuf = [wstream[:, i * 1024:(i + 1) * 1024] for i in range(2)]
    kvbuf = [wstream[:, 2048 + i * 4096: 2048 + (i + 1) * 4096] for i in range(2)]
    prbuf = [wstream[:, i * 1024:(i + 1) * 1024] for i in range(2)]
    w1buf = [wstream[:, i * 4096:(i + 1) * 4096] for i in range(2)]
    w2buf = [wstream[:, 8192 + i * 2048: 8192 + (i + 1) * 2048].bitcast(F8E4) for i in range(2)]
    xn[1] = wstream[:, 12288:16384]

    TINY4 = float(2.0 ** -80)
    TINY10 = float(2.0 ** -120)
    CLAMP4 = float(np.float32(-5.45 * LN2C))
    CLAMP10 = float(np.float32(-13.45 * 2 * LN2C))
    C2 = float(np.float32(1.0 / LN2C))
    C22 = float(np.float32(1.0 / (2 * LN2C)))
    CB_M3, CB_P3, CB_M10, CB_P10, CB_T4, CB_T10, CB_EPS = range(7)

    def cbc(i):
        return cb[:, i:i + 1]

    def quantize(src_ap, out_ap, man, signed, bufi, n, waits, extra_first_waits=(), helper="vector", kf=None):
        lt, et = l_t[bufi][:, :n], e_t[bufi][:, :n]
        it_, gt = (inv_t[bufi][:, :n], g_t[bufi][:, :n]) if kf is None else kf
        ef = l_t[bufi][:, :n]
        w0 = list(waits) + list(extra_first_waits)
        min_exp = -6 if man == 3 else -14
        cbm = CB_M3 if man == 3 else CB_M10
        cbp = CB_P3 if man == 3 else CB_P10
        if signed:
            sqt = r_t[bufi][:, :n]
            t_sq = sch.add("scalar", lambda e, o=sqt, i=src_ap: e.activation(o, i, AF.Square),
                           waits=w0, inc="ea")
            t_l = sch.add("scalar", lambda e, o=lt, i=sqt, b=cbc(CB_T10): e.activation(o, i, AF.Ln, bias=b),
                          waits=[t_sq], inc="ea")
            clamp = float(np.float32((min_exp + 0.55) * 2 * LN2C))
            cc = C22
        else:
            t_l = sch.add("scalar", lambda e, o=lt, i=src_ap, b=cbc(CB_T4): e.activation(o, i, AF.Ln, bias=b),
                          waits=w0, inc="ea")
            t_sq = None
            clamp = float(np.float32((min_exp + 0.55) * LN2C))
            cc = C2
        hsem = "ev" if helper == "vector" else "eg"
        t_e = sch.add(helper, lambda e, o=et, i=lt, c=cc: e.tensor_scalar(o, i, c, 0.5, op0=ALU.mult, op1=ALU.subtract),
                      waits=[t_l], inc=hsem)
        t_ef = sch.add(helper, lambda e, o=ef, i=et, c=float(min_exp): e.tensor_scalar(o, i, c, None, op0=ALU.max),
                       waits=[t_e], inc=hsem)
        t_inv = sch.add("scalar", lambda e, o=it_, i=ef, b=cbc(cbp): e.activation(o, i, AF.Exp, scale=-LN2C, bias=b),
                        waits=[t_ef], inc="ea")
        t_g = sch.add("scalar", lambda e, o=gt, i=ef, b=cbc(cbm): e.activation(o, i, AF.Exp, scale=LN2C, bias=b),
                      waits=[t_ef], inc="ea")
        t_y = sch.add("vector", lambda e, o=et, a=src_ap, b=it_: e.tensor_tensor(o, a, b, ALU.mult),
                      waits=[t_inv], inc="ev")
        t_q = sch.add("vector", lambda e, o=out_ap, a=et, b=gt: e.tensor_tensor(o, a, b, ALU.mult),
                      waits=[t_g, t_y], inc="ev")
        return t_q, t_y

    def ln_sums(x_tile_of, waits_extra=()):
        tok_mm1 = tok_mm2 = None
        for c in range(DC):
            xc = x_tile_of(c)
            sq = lnsq[c % 2][:, :]
            tw = [w for w in [lnsq_tok[c % 2]] if w] + (list(waits_extra) if c == 0 else [])
            tsq = sch.add("scalar", lambda e, o=sq, i=xc: e.activation(o, i, AF.Square),
                          waits=tw, inc="ea")
            w1_ = ([ln_bank_free[0], t_o3] + list(waits_extra)) if c == 0 else []
            tok_mm1 = sch.add("tensor",
                              lambda e, r=xc, s=(c == 0), p=(c == 7): e.matmul(pL1[0:1, :], onescol[:, :], r, start=s, stop=p),
                              waits=[w for w in w1_ if w], inc="mm" if c == 7 else None)
            w2_ = [tsq] + ([ln_bank_free[1]] if c == 0 and ln_bank_free[1] else [])
            tok_mm2 = sch.add("tensor",
                              lambda e, r=sq, s=(c == 0), p=(c == 7): e.matmul(pL2[0:1, :], onescol[:, :], r, start=s, stop=p),
                              waits=w2_, inc="mm")
            lnsq_tok[c % 2] = tok_mm2
        return tok_mm1, tok_mm2

    def ln_rows(sums_toks):
        tok_mm1, tok_mm2 = sums_toks
        t1 = sch.add("vector", lambda e: e.tensor_scalar(m_row[:, :], pL1[0:1, :], 1.0 / D, None, op0=ALU.mult),
                     waits=[tok_mm1], inc="ev")
        t2 = sch.add("vector", lambda e: e.tensor_scalar(s2_row[:, :], pL2[0:1, :], 1.0 / D, None, op0=ALU.mult),
                     waits=[tok_mm2], inc="ev")
        t3 = sch.add("vector", lambda e: e.tensor_tensor(mm_row[:, :], m_row[:, :], m_row[:, :], ALU.mult),
                     waits=[t1], inc="ev")
        t4 = sch.add("vector", lambda e: e.tensor_tensor(mm_row[:, :], s2_row[:, :], mm_row[:, :], ALU.subtract),
                     waits=[t2, t3], inc="ev")
        t5 = sch.add("scalar", lambda e: e.activation(mm_row[:, :], mm_row[:, :], AF.Ln, bias=cb[0:1, CB_EPS:CB_EPS + 1]),
                     waits=[t4], inc="ea")
        t6 = sch.add("scalar", lambda e: e.activation(rstd_row[:, :], mm_row[:, :], AF.Exp, scale=-0.5),
                     waits=[t5], inc="ea")
        t7 = sch.add("vector", lambda e: e.tensor_copy(a_rowb[:, :], rstd_row[:, :]), waits=[t6], inc="ev")
        t8 = sch.add("vector", lambda e: e.tensor_tensor(nb_rowf[:, :], m_row[:, :], rstd_row[:, :], ALU.mult),
                     waits=[t6], inc="ev")
        t9 = sch.add("vector", lambda e: e.tensor_copy(nb_rowb[:, :], nb_rowf[:, :]), waits=[t8], inc="ev")
        return (t1, t2, t7, t9)

    def ln_tail(x_tile_of, xn_buf, rows_toks, xn_guard=()):
        t1, t2, t7, t9 = rows_toks
        ta = sch.add("tensor", lambda e: e.matmul(pL1[:, :], ones1[0:1, 0:128], a_rowb[:, :], start=True, stop=True),
                     waits=[t7, t_o1], inc="mm")
        tb = sch.add("tensor", lambda e: e.matmul(pL2[:, :], negones[0:1, 0:128], nb_rowb[:, :], start=True, stop=True),
                     waits=[t9, t_o2], inc="mm")
        tc = sch.add("vector", lambda e: e.tensor_copy(ab_b[:, :], pL1[:, :]), waits=[ta], inc="ev")
        td = sch.add("vector", lambda e: e.tensor_copy(bb_b[:, :], pL2[:, :]), waits=[tb], inc="ev")
        ln_bank_free[0] = tc
        ln_bank_free[1] = td
        tok_xn = []
        for c in range(DC):
            xc = x_tile_of(c)
            dst = xn_buf[:, c * T:(c + 1) * T]
            tm = sch.add("vector", lambda e, o=dst, i=xc: e.tensor_tensor(o, i, ab_b[:, :], ALU.mult),
                         waits=[tc] + (list(xn_guard) if c == 0 else []), inc="ev")
            tn = sch.add("vector", lambda e, o=dst: e.tensor_tensor(o, o, bb_b[:, :], ALU.add),
                         waits=[td, tm], inc="ev")
            tok_xn.append(tn)
        return tok_xn

    bank_free = {k: None for k in ["pA", "pB", "pC", "pD", "pK1", "pK2"]}
    ln_bank_free = [None, None]
    xn_free = [None, None]
    lnsq_tok = [None, None]
    sqs_tok = [None, None]
    temps_free = [None, None, None]

    tok_bias = sch.add("sync", lambda e: e.dma_start(out=bias_sb[:, :], in_=bias_h.ap()), inc="ds", n=16)
    tok_cb = sch.add("sync", lambda e: e.dma_start(out=cb[:, :], in_=cb_h.ap()), inc="ds", n=16)
    tok_sel = sch.add("gpsimd", lambda e: e.dma_start(out=sel_sb[:, :], in_=sel_h.ap()), inc="dl", n=16)
    t_o1 = sch.add("vector", lambda e: e.memset(ones1[:, :], 1.0), inc="ev")
    t_o2 = sch.add("vector", lambda e: e.memset(negones[:, :], -1.0), inc="ev")
    t_o3 = sch.add("vector", lambda e: e.memset(onescol[:, :], 1.0), inc="ev")

    tok_bq, _ = quantize(bias_sb[:, 40:48], bq_sb[:, :], man=10, signed=True, bufi=0, n=8,
                         waits=[tok_bias, tok_cb])

    def load_x(s_, waits):
        src = x_h.ap()[s_].rearrange("(c p) t -> p c t", p=128)
        tgt = xa_hq if s_ % 2 == 0 else qf
        dst = tgt[:, :].rearrange("p (c t) -> p c t", c=8)
        toks = []
        for tt in range(NT):
            tok = sch.add("gpsimd",
                          lambda e, o=dst[:, :, tt * T:(tt + 1) * T], i=src[:, :, tt * T:(tt + 1) * T]:
                          e.dma_start(out=o, in_=i),
                          waits=(list(waits) if tt == 0 else []), inc="dl", n=16)
            toks.append(tok)
        return toks

    tok_xa0 = load_x(0, [tok_sel])

    spill_toks = []
    for jc in range(8):
        src = wqkv_h.ap()[:, jc * 128:(jc + 1) * 128].rearrange("(c p) q -> p c q", p=128)
        dst = sp_q.ap()[jc].rearrange("p (c q) -> p c q", c=8)
        tok = sch.add("gpsimd", lambda e, o=dst, i=src: e.dma_start(out=o, in_=i),
                      inc="dl", n=16)
        spill_toks.append(tok)
    spq_done = spill_toks[-1]
    for sl in range(4):
        src = wqkv_h.ap()[:, 1024 + sl * 512: 1024 + (sl + 1) * 512].rearrange("(c p) j -> p c j", p=128)
        dst = sp_kv.ap()[sl].rearrange("p (c j) -> p c j", c=8)
        tok = sch.add("gpsimd", lambda e, o=dst, i=src: e.dma_start(out=o, in_=i),
                      inc="dl", n=16)
        spill_toks.append(tok)
    spkv_done = spill_toks[-1]
    for jc in range(8):
        src = wproj_h.ap()[:, jc * 128:(jc + 1) * 128].rearrange("(c p) q -> p c q", p=128)
        dst = sp_pr.ap()[jc].rearrange("p (c q) -> p c q", c=8)
        tok = sch.add("gpsimd", lambda e, o=dst, i=src: e.dma_start(out=o, in_=i),
                      inc="dl", n=16)
        spill_toks.append(tok)
    sppr_done = spill_toks[-1]
    spw1_piece = []
    for pc in range(8):
        tok = None
        for c in range(8):
            src = w1_h.ap()[c * 128:(c + 1) * 128, pc * 512:(pc + 1) * 512].rearrange("p (jj q) -> p jj q", jj=4)
            dst = sp_w1.ap()[pc].rearrange("p (jj rest) -> p jj rest", jj=4)[:, :, c * 128:(c + 1) * 128]
            tok = sch.add("gpsimd", lambda e, o=dst, i=src: e.dma_start(out=o, in_=i),
                          inc="dl", n=16)
        spw1_piece.append(tok)
        spill_toks.append(tok)
    w2stage = [stg[0][:, 0:128].bitcast(F8E4), stg[1][:, 0:128].bitcast(F8E4)]
    w2_invg = [(wstream[:, 10240 + 1024 * i: 10240 + 1024 * i + 512],
                wstream[:, 10240 + 1024 * i + 512: 10240 + 1024 * (i + 1)]) for i in range(2)]
    w2_ld = [e_t[i].bitcast(F32)[:, 512:1024] for i in range(2)]
    W2 = {"i": 0, "prevq": [None, None], "ld_free": [None, None], "done": None}

    def emit_w2_step():
        k = W2["i"]
        if k >= 64:
            return
        W2["i"] = k + 1
        jc, h = k // 2, k % 2
        bi = k % 2
        src = w2_h.ap()[jc * 128:(jc + 1) * 128, h * 512:(h + 1) * 512]
        tok_ld = sch.add("sync", lambda e, o=w2_ld[bi], i=src: e.dma_start(out=o, in_=i),
                         waits=[w for w in [W2["ld_free"][bi], W2["prevq"][bi],
                                            tok_bq if k < 2 else None] if w],
                         inc="ds", n=16)
        tq, ty = quantize(w2_ld[bi], w2stage[bi][:, :], man=3, signed=True, bufi=bi, n=512,
                          waits=[tok_ld], kf=w2_invg[bi])
        W2["ld_free"][bi] = ty
        temps_free[bi] = tq
        dst = sp_w2.ap()[4 * h:4 * h + 4, :, jc * 128:(jc + 1) * 128].rearrange("a p q -> p a q")
        src2 = w2stage[bi].rearrange("p (a q) -> p a q", a=4)
        tok = sch.add("sync", lambda e, o=dst, i=src2: e.dma_start(out=o, in_=i),
                      waits=[tq], inc="ds", n=16)
        W2["prevq"][bi] = tok
        tok_stgfree[bi] = tok
        if W2["i"] == 64:
            W2["done"] = tok

    def emit_w2_steps(kk):
        for _ in range(kk):
            emit_w2_step()

    if debug_taps:
        sch.add("sync", lambda e: e.dma_start(out=dbg_bq.ap(), in_=bq_sb[:, :]),
                waits=[tok_bq], inc="ou", n=16)
        tok_out_dbg = [16]
        sch.add("sync", lambda e: e.dma_start(out=dbg_w1.ap(), in_=sp_w1.ap()),
                waits=[spill_toks[-1]], inc="ou", n=16)
        tok_out_dbg[0] += 16
    else:
        tok_out_dbg = [0]

    qbuf_free = [None, None]
    kvbuf_free = [None, None]
    prbuf_free = [None, None]
    w1buf_free = [None, None]
    w2buf_free = [None, None]
    last_spill = spill_toks[-1]

    tok_out_total = [0]
    hq_read_done = None
    tok_stgfree = [None, None]
    kfv_free = [None] * 4
    attn_over_qf_ok = {}
    x_toks_next = tok_xa0
    xb_read_done = None
    kv2_read_done = None
    bd_read_done = None
    zrow_free4 = [None] * 4
    zb_free = [None, None]
    ab_free = None
    ln1_boot = [None]

    for s_ in range(NSEQ):
        x_toks_cur = x_toks_next
        qf_free_at = hq_read_done
        XA = xa_hq if s_ % 2 == 0 else qf
        QF = qf if s_ % 2 == 0 else xa_hq

        def x_of_t(t):
            def x_of(c, t=t):
                return XA[:, c * S + t * T: c * S + t * T + T]
            return x_of

        if ln1_boot[0] is not None:
            tails_t = {0: ln1_boot[0]}
            ln1_boot[0] = None
            sums_t, rows_t = {}, {}
        else:
            extra0 = [x_toks_cur[0]] + [w for w in temps_free if w]
            sums_t = {0: ln_sums(x_of_t(0), waits_extra=extra0)}
            rows_t = {0: ln_rows(sums_t[0])}
            tails_t = {0: ln_tail(x_of_t(0), xn[0], rows_t[0],
                                  xn_guard=[w for w in [xn_free[0]] if w])}
        for t in range(NT):
            tok_xn = tails_t[t]
            if t + 1 < NT:
                sums_t[t + 1] = ln_sums(x_of_t(t + 1), waits_extra=[x_toks_cur[t + 1]])
                rows_t[t + 1] = ln_rows(sums_t[t + 1])

            qrot = [pA, pB, pC, pD]
            qrot_names = ["pA", "pB", "pC", "pD"]
            for jc in range(8):
                bi = jc % 2
                wg = [qbuf_free[bi]]
                if t == 0 and jc == 0:
                    wg += [w1buf_free[0]] + prbuf_free
                    if s_ == 0:
                        wg += [spq_done]
                tok_w = sch.add("sync", lambda e, o=qbuf[bi], i=sp_q.ap()[jc]: e.dma_start(out=o, in_=i),
                                waits=[w for w in wg if w], inc="ds", n=16)
                mmw = [tok_w, bank_free[qrot_names[jc % 4]]] + tok_xn
                tok_mm = None
                for c in range(DC):
                    tok_mm = sch.add("tensor",
                                     lambda e, l_=qbuf[bi][:, c * 128:(c + 1) * 128],
                                     r=xn[t % 2][:, c * T:(c + 1) * T], p=qrot[jc % 4],
                                     s0=(c == 0), s1=(c == 7): e.matmul(p[:, :], l_, r, start=s0, stop=s1),
                                     waits=[w for w in mmw if w] if c == 0 else [],
                                     inc="mm" if c == 7 else None)
                qbuf_free[bi] = tok_mm
                sqt = sqs[bi][:, :]
                tevq = sch.add("scalar", lambda e, o=sqt, i=qrot[jc % 4][:, :]: e.activation(o, i, AF.Square),
                               waits=[tok_mm] + [w for w in [sqs_tok[bi]] if w]
                               + ([w for w in [qf_free_at] if w] if (t == 0 and jc == 0) else []),
                               inc="ea")
                bank_free[qrot_names[jc % 4]] = tevq
                dst = QF[:, jc * S + t * T: jc * S + t * T + T]
                tevq2 = sch.add("scalar", lambda e, o=dst, i=sqt: e.activation(o, i, AF.Exp, scale=-0.5),
                                waits=[tevq], inc="ea")
                sqs_tok[bi] = tevq2
                if jc == 7:
                    qf_tile_done = tevq2

            if t + 1 < NT:
                guard = [w for w in [xn_free[(t + 1) % 2]] if w]
                tails_t[t + 1] = ln_tail(x_of_t(t + 1), xn[(t + 1) % 2], rows_t[t + 1], xn_guard=guard)

            pnb = [pA, pB, pC, pD]
            pnb_names = ["pA", "pB", "pC", "pD"]
            kv_evac_toks = {}
            for sl in range(4):
                bi = sl % 2
                wgk = [kvbuf_free[bi]]
                if t == 0 and sl == 0:
                    wgk += [w1buf_free[1]]
                    if s_ == 0:
                        wgk += [spkv_done]
                if t == 0 and sl == 1:
                    wgk += [w2buf_free[0]]
                tok_w = sch.add("sync", lambda e, o=kvbuf[bi], i=sp_kv.ap()[sl]: e.dma_start(out=o, in_=i),
                                waits=[w for w in wgk if w], inc="ds", n=16)
                last_mm_of_nb = {}
                for nb in range(4):
                    tok_mm = None
                    w0 = [tok_w, bank_free[pnb_names[nb]]]
                    for c in range(DC):
                        tok_mm = sch.add("tensor",
                                         lambda e, l_=xn[t % 2][:, c * T + nb * 128: c * T + nb * 128 + 128],
                                         r=kvbuf[bi][:, c * 512:(c + 1) * 512], p=pnb[nb],
                                         s0=(c == 0), s1=(c == 7): e.matmul(p[:, :], l_, r, start=s0, stop=s1),
                                         waits=[w for w in w0 if w] if c == 0 else [],
                                         inc="mm" if c == 7 else None)
                    last_mm_of_nb[nb] = tok_mm
                kvbuf_free[bi] = last_mm_of_nb[3]
                for nb in range(4):
                    tok_mm = last_mm_of_nb[nb]
                    if sl < 2:
                        sqt = sqs[nb % 2][:, :]
                        te1 = sch.add("scalar", lambda e, o=sqt, i=pnb[nb][:, :]: e.activation(o, i, AF.Square),
                                      waits=[tok_mm] + [w for w in [sqs_tok[nb % 2]] if w],
                                      inc="ea")
                        dst = kf_nb[nb][:, sl * 512:(sl + 1) * 512]
                        kf_guard = ([kfv_free[nb]] if kfv_free[nb] else []) + \
                                   ([temps_free[nb // 2]] if temps_free[nb // 2] else [])
                        te2 = sch.add("scalar", lambda e, o=dst, i=sqt: e.activation(o, i, AF.Exp, scale=-0.5),
                                      waits=[te1] + (kf_guard if (sl == 0 and t == 0) else
                                                     ([kfv_free[nb]] if (sl == 0 and kfv_free[nb]) else [])),
                                      inc="ea")
                        sqs_tok[nb % 2] = te2
                        bank_free[pnb_names[nb]] = te1
                        kv_evac_toks[("k", sl, nb)] = te2
                    else:
                        dst = v_nb[nb][:, (sl - 2) * 512:(sl - 1) * 512]
                        te = sch.add("vector", lambda e, o=dst, i=pnb[nb][:, :]: e.tensor_copy(o, i),
                                     waits=[tok_mm] + ([kfv_free[nb]] if (sl == 2 and kfv_free[nb]) else []),
                                     inc="ev")
                        bank_free[pnb_names[nb]] = te
                        kv_evac_toks[("v", sl, nb)] = te

            wv0 = [kv_evac_toks[(kk, slx, nbx)] for nbx in range(4)
                   for (kk, slx) in (("k", 0), ("k", 1), ("v", 2), ("v", 3))]
            if t == 0:
                tz_kv = sch.add("vector", lambda e: e.memset(kvacc_sb[:, :], 0.0), waits=wv0, inc="ev")
                tz_ks = sch.add("vector", lambda e: e.memset(ksacc_sb[:, :], 0.0), waits=[tz_kv], inc="ev")
            pKband = [pK1, pK2]
            pKname = ["pK1", "pK2"]
            for hp in range(8):
                for half in range(2):
                    pb = pKband[half]
                    rows = slice(half * 64, (half + 1) * 64)
                    w0 = ([w for w in wv0 if w] if (hp == 0) else []) + \
                         [w for w in [bank_free[pKname[half]]] if w]
                    tok_kv = None
                    for nb in range(4):
                        tok_kv = sch.add("tensor",
                                         lambda e, l_=kf_nb[nb][:, hp * 128 + half * 64: hp * 128 + half * 64 + 64],
                                         r=v_nb[nb][:, hp * 128 + half * 64: hp * 128 + half * 64 + 64],
                                         p=pb[rows, 0:64],
                                         s0=(nb == 0), s1=(nb == 3), tp=(0, 64) if half else None:
                                         e.matmul(p, l_, r, start=s0, stop=s1, tile_position=tp),
                                         waits=w0 if nb == 0 else [],
                                         inc="mm" if nb == 3 else None)
                    dst = kvacc_sb[rows, hp * 64:(hp + 1) * 64]
                    tacc = sch.add("vector",
                                   lambda e, o=dst, p=pb[rows, 0:64]:
                                   e.tensor_tensor(o, o, p, ALU.add),
                                   waits=[tok_kv] + ([tz_ks] if (t == 0 and hp == 0) else []), inc="ev")
                    bank_free[pKname[half]] = tacc
            for c in range(DC):
                pb = pKband[c % 2]
                w0 = [w for w in [bank_free[pKname[c % 2]], t_o3] if w]
                tok_ks = None
                for nb in range(4):
                    tok_ks = sch.add("tensor",
                                     lambda e, l_=kf_nb[nb][:, c * 128:(c + 1) * 128],
                                     p=pb[:, 0:1], s0=(nb == 0), s1=(nb == 3):
                                     e.matmul(p, l_, onescol[:, :], start=s0, stop=s1),
                                     waits=w0 if nb == 0 else [],
                                     inc="mm" if nb == 3 else None)
                dst = ksacc_sb[:, c:c + 1]
                tacc = sch.add("vector",
                               lambda e, o=dst, p=pb[:, 0:1]: e.tensor_tensor(o, o, p, ALU.add),
                               waits=[tok_ks], inc="ev")
                bank_free[pKname[c % 2]] = tacc
            for nb in range(4):
                kfv_free[nb] = tok_ks
            xn_free[t % 2] = tok_ks
            kvks_done = bank_free["pK1"] if bank_free["pK1"][1] > bank_free["pK2"][1] else bank_free["pK2"]
            if s_ == 0:
                emit_w2_steps(3)

        w_prep = [kvks_done] + [w for w in [kv2_read_done, bd_read_done] if w]
        t_z0 = sch.add("vector", lambda e: e.memset(kv2[:, :], 0.0), waits=w_prep, inc="ev")
        t_z1 = sch.add("vector", lambda e: e.memset(bd[:, :], 0.0), waits=[t_z0], inc="ev")
        kv2_toks = []
        for c in range(DC):
            ta_ = sch.add("vector", lambda e, c=c: e.tensor_copy(kv2[0:64, c * 128: c * 128 + 64],
                                                                 kvacc_sb[0:64, c * 64:(c + 1) * 64]),
                          waits=[t_z0], inc="ev")
            tb_ = sch.add("vector", lambda e, c=c: e.tensor_copy(kv2[64:128, c * 128 + 64: c * 128 + 128],
                                                                 kvacc_sb[64:128, c * 64:(c + 1) * 64]),
                          waits=[t_z0], inc="ev")
            kv2_toks += [ta_, tb_]
        bd_toks = []
        for c in range(DC):
            ta_ = sch.add("vector", lambda e, c=c: e.tensor_copy(bd[0:64, c * 16 + 2 * c: c * 16 + 2 * c + 1],
                                                                 ksacc_sb[0:64, c:c + 1]),
                          waits=[t_z1], inc="ev")
            tb_ = sch.add("vector", lambda e, c=c: e.tensor_copy(bd[64:128, c * 16 + 2 * c + 1: c * 16 + 2 * c + 2],
                                                                 ksacc_sb[64:128, c:c + 1]),
                          waits=[t_z1], inc="ev")
            bd_toks += [ta_, tb_]
        kv2_done = kv2_toks[-1]
        bd_done = bd_toks[-1]
        if s_ == 0:
            emit_w2_steps(4)
        if debug_taps:
            sch.add("sync", lambda e, s_=s_: e.dma_start(out=dbg_kv2.ap()[s_], in_=kv2[:, :]),
                    waits=[kv2_done], inc="ou", n=16)
            sch.add("sync", lambda e, s_=s_: e.dma_start(out=dbg_bd.ap()[s_], in_=bd[:, :]),
                    waits=[bd_done], inc="ou", n=16)
            tok_out_dbg[0] += 32

        qf_dump_gate = []
        if debug_taps:
            tqd = sch.add("sync", lambda e, s_=s_, q=QF: e.dma_start(out=dbg_qf.ap()[s_], in_=q[:, :]),
                    waits=[qf_tile_done], inc="ou", n=16)
            tok_out_dbg[0] += 16
            qf_dump_gate = [tqd]
        proj_last = None
        attn_toks_t = {}
        z_done = {}
        for t in range(NT):
            tok_zd = None
            for c in range(DC):
                w0 = [bd_done, qf_tile_done, bank_free["pC"]] if c == 0 else []
                tok_zd = sch.add("tensor",
                                 lambda e, l_=bd[:, c * 16:(c + 1) * 16],
                                 r=QF[:, c * S + t * T: c * S + t * T + T],
                                 s0=(c == 0), s1=(c == 7): e.matmul(pC[0:16, :], l_, r, start=s0, stop=s1),
                                 waits=[w for w in w0 if w], inc="mm" if c == 7 else None)
            zf = r_t[2][32 * t:32 * t + 16, 0:T]
            zb4 = v_nb[t][0:16, 0:T]
            tz1 = sch.add("vector", lambda e, o=zf: e.tensor_scalar(o, pC[0:16, :], 1e-6, None, op0=ALU.add),
                          waits=[tok_zd] + [w for w in [zrow_free4[t]] if w], inc="ev")
            bank_free["pC"] = tz1
            tz2 = sch.add("vector", lambda e, o=zf: e.reciprocal(o, o), waits=[tz1], inc="ev")
            tz3 = sch.add("vector", lambda e, o=zb4, i=zf: e.tensor_copy(o, i),
                          waits=[tz2] + [w for w in [kfv_free[t]] if w], inc="ev")
            zrow_free4[t] = tz3
            z_done[t] = tz3
            if debug_taps:
                tzd = sch.add("sync", lambda e, s_=s_, t=t, i=zb4: e.dma_start(out=dbg_z.ap()[s_, t], in_=i),
                              waits=[tz3], inc="ou", n=16)
                tok_out_dbg[0] += 16
                zrow_free4[t] = tzd
        for t in range(NT):
            zb4 = v_nb[t][0:16, 0:T]
            attn_toks = []
            for c in range(DC):
                bi = 0
                tz4 = sch.add("tensor", lambda e, c=c, zr=zb4: e.matmul(pC[:, :], sel_sb[:, c * 128:(c + 1) * 128],
                                                                        zr, start=True, stop=True),
                              waits=[z_done[t], kv2_done, bank_free["pC"]] + qf_dump_gate, inc="mm")
                tz5 = sch.add("vector", lambda e, o=zb_t[bi][:, :]: e.tensor_copy(o, pC[:, :]),
                              waits=[tz4] + [w for w in [zb_free[bi]] if w], inc="ev")
                bank_free["pC"] = tz5
                ta_ = sch.add("tensor", lambda e, c=c, r=QF[:, c * S + t * T: c * S + t * T + T]:
                              e.matmul(pD[:, :], kv2[:, c * 128:(c + 1) * 128], r, start=True, stop=True),
                              waits=[bank_free["pD"], kv2_done], inc="mm")
                dst = QF[:, c * S + t * T: c * S + t * T + T]
                tm_ = sch.add("vector", lambda e, o=dst, b=zb_t[bi][:, :]: e.tensor_tensor(o, pD[:, :], b, ALU.mult),
                              waits=[ta_, tz5], inc="ev")
                bank_free["pD"] = tm_
                zb_free[bi] = tm_
                attn_toks.append(tm_)
            attn_toks_t[t] = attn_toks
            kfv_free[t] = tz4
            if s_ == 0:
                emit_w2_steps(6)
        dbg_gate = []
        if debug_taps:
            tdump = sch.add("sync", lambda e, s_=s_, q=QF: e.dma_start(out=dbg_at.ap()[s_], in_=q[:, :]),
                            waits=[tok for tl in attn_toks_t.values() for tok in tl], inc="ou", n=16)
            tok_out_dbg[0] += 16
            dbg_gate = [tdump]
        def xb_of_t(t):
            def xb_of(c, t=t):
                return xb[:, c * S + t * T: c * S + t * T + T]
            return xb_of

        ln2_tail = {}
        proj_te_t = {}
        for t in range(NT):
            attn_toks = attn_toks_t[t]
            prot = [pA, pB, pK1, pK2]
            prot_names = ["pA", "pB", "pK1", "pK2"]
            for jc in range(8):
                bi = jc % 2
                wg = [prbuf_free[bi]]
                if t == 0 and jc == 0:
                    wg += qbuf_free + kvbuf_free
                    if s_ == 0:
                        wg += [sppr_done]
                tok_w = sch.add("sync", lambda e, o=prbuf[bi], i=sp_pr.ap()[jc]: e.dma_start(out=o, in_=i),
                                waits=[w for w in wg if w], inc="ds", n=16)
                ri = jc % 4
                w0 = [tok_w, bank_free[prot_names[ri]]] + attn_toks + dbg_gate
                tok_mm = None
                for cE in range(DC):
                    tok_mm = sch.add("tensor",
                                     lambda e, l_=prbuf[bi][:, cE * 128:(cE + 1) * 128],
                                     r=QF[:, cE * S + t * T: cE * S + t * T + T], p=prot[ri],
                                     s0=(cE == 0), s1=(cE == 7): e.matmul(p[:, :], l_, r, start=s0, stop=s1),
                                     waits=[w for w in w0 if w] if cE == 0 else [],
                                     inc="mm" if cE == 7 else None)
                prbuf_free[bi] = tok_mm
                dst = xb[:, jc * S + t * T: jc * S + t * T + T]
                xw = [tok_mm] + ([xb_read_done] if (xb_read_done and t == 0 and jc == 0) else [])
                te = sch.add("vector",
                             lambda e, o=dst, p=prot[ri], b=bias_sb[:, jc:jc + 1],
                             xc=XA[:, jc * S + t * T: jc * S + t * T + T]:
                             e.scalar_tensor_tensor(o, p[:, :], b, xc, op0=ALU.add, op1=ALU.add),
                             waits=[w for w in xw if w], inc="ev")
                bank_free[prot_names[ri]] = te
                proj_last = te
            proj_te_t[t] = te
            if s_ == 0:
                emit_w2_steps(6)
            if t == 0:
                ln2_sums0 = ln_sums(xb_of_t(0), waits_extra=[te])
                ln2_rows0 = ln_rows(ln2_sums0)
                ln2_tail[0] = ln_tail(xb_of_t(0), xn[0], ln2_rows0,
                                      xn_guard=[w for w in [xn_free[0]] if w])
            if t == 1:
                ln2_rows_d = {1: ln_rows(ln_sums(xb_of_t(1), waits_extra=[te]))}
        if debug_taps:
            sch.add("sync", lambda e, s_=s_: e.dma_start(out=dbg_xb.ap()[s_], in_=xb[:, :]),
                    waits=[proj_last], inc="ou", n=16)
            tok_out_dbg[0] += 16
        attn_over_qf_ok[s_] = proj_last
        kv2_read_done = ta_
        bd_read_done = tok_zd
        xa_read_done = proj_last
        if s_ + 1 < NSEQ:
            x_toks_next = load_x(s_ + 1, [proj_last])

        xb_last_read = None
        hqv_par = hq8[id(XA)]
        hq_toks_t = {}
        hq_free = [None, None]
        ffn1_last_mm = {}
        first_w2 = [True]

        def emit_ffn2_chunk(tp, eo):
            ebi = eo % 2
            pe_ = pC if ebi == 0 else pD
            pe_name = "pC" if ebi == 0 else "pD"
            wg = [w2buf_free[ebi]]
            if first_w2[0]:
                wg += kvbuf_free
                if s_ == 0:
                    wg += [W2["done"]]
                first_w2[0] = False
            tok_w2 = sch.add("sync", lambda e, o=w2buf[ebi], i=sp_w2.ap()[eo]: e.dma_start(out=o, in_=i),
                             waits=[w for w in wg if w], inc="ds", n=16)
            hqv = hqv_par[tp % 2]
            toks = hq_toks_t[tp]
            w0 = [tok_w2, bank_free[pe_name]]
            tok_mm = None
            for c in range(16):
                jw = ([w for w in w0 if w] if c == 0 else []) + \
                     ([toks[c]] if eo == 0 else [])
                tok_mm = sch.add("tensor",
                                 lambda e, l_=w2buf[ebi][:, c * 256:(c + 1) * 256].rearrange("p (i m) -> p i m", i=2),
                                 r=hqv[:, c * 1024:(c + 1) * 1024].rearrange("p (i n) -> p i n", i=2), p=pe_,
                                 s0=(c == 0), s1=(c == 15): e.matmul(p[:, :], l_, r, start=s0, stop=s1,
                                                                     perf_mode=PM.DoubleRow),
                                 waits=jw, inc="mm" if c == 15 else None)
            w2buf_free[ebi] = tok_mm
            if eo == 7:
                hq_free[tp % 2] = tok_mm
            vtile = sqs[ebi][:, :]
            tv = sch.add("vector", lambda e, o=vtile, p=pe_, b=bq_sb[:, eo:eo + 1]:
                         e.tensor_scalar(o, p[:, :], b, None, op0=ALU.add),
                         waits=[tok_mm] + [w for w in [tok_stgfree[ebi], sqs_tok[ebi]] if w], inc="ev")
            bank_free[pe_name] = tv
            qdst = stg[ebi][:, :]
            tq, ty = quantize(vtile, qdst, man=10, signed=True, bufi=ebi, n=512,
                              waits=[tv] + [w for w in [temps_free[ebi]] if w])
            temps_free[ebi] = tq
            sqs_tok[ebi] = ty
            tr = sch.add("vector", lambda e, o=qdst, xc=xb[:, eo * S + tp * T: eo * S + tp * T + T]:
                         e.tensor_tensor(o, o, xc, ALU.add),
                         waits=[tq], inc="ev")
            tdo = sch.add("gpsimd", lambda e, i=qdst, o=out_h.ap()[s_, eo * 128:(eo + 1) * 128, tp * T:(tp + 1) * T]:
                          e.dma_start(out=o, in_=i),
                          waits=[tr], inc="ou", n=16)
            tok_out_total[0] += 16
            tok_stgfree[ebi] = tdo
            return tok_mm, tr

        for t in range(NT):
            tok_xn2 = ln2_tail[t]

            frot = [pA, pB, pK1, pK2]
            frot_names = ["pA", "pB", "pK1", "pK2"]
            hq_toks = []
            hq_toks_t[t] = hq_toks
            for grp in range(16):
                gbuf = grp % 2
                if grp % 2 == 0:
                    pc = grp // 2
                    wbi = pc % 2
                    wg = [w1buf_free[wbi]]
                    if t == 0 and grp == 0:
                        wg += prbuf_free + kvbuf_free
                    if s_ == 0 and t == 0:
                        wg += [spw1_piece[pc]]
                    tok_w1 = sch.add("sync", lambda e, o=w1buf[wbi], i=sp_w1.ap()[pc]: e.dma_start(out=o, in_=i),
                                     waits=[w for w in wg if w], inc="ds", n=16)
                    w1tok_cur = tok_w1
                grp_last_mm = []
                for jj in range(2):
                    j = grp * 2 + jj
                    bi = j % 2
                    pcbuf = (j // 4) % 2
                    off = (j % 4) * 1024
                    ri = j % 4
                    w0 = [w1tok_cur, bank_free[frot_names[ri]]] + (tok_xn2 if j == 0 else [])
                    tok_mm = None
                    for c in range(DC):
                        tok_mm = sch.add("tensor",
                                         lambda e, l_=w1buf[pcbuf][:, off + c * 128: off + (c + 1) * 128],
                                         r=xn[t % 2][:, c * T:(c + 1) * T], p=frot[ri],
                                         s0=(c == 0), s1=(c == 7): e.matmul(p[:, :], l_, r, start=s0, stop=s1),
                                         waits=[w for w in w0 if w] if c == 0 else [],
                                         inc="mm" if c == 7 else None)
                    grp_last_mm.append(tok_mm)
                    dst = r_t[gbuf][:, jj * T:(jj + 1) * T]
                    ew = [tok_mm] + ([w for w in [temps_free[gbuf]] if w] if jj == 0 or j % 2 == 1 else [])
                    te = sch.add("scalar",
                                 lambda e, o=dst, p=frot[ri], b=bias_sb[:, 8 + j:8 + j + 1]:
                                 e.activation(o, p[:, :], AF.Relu, bias=b),
                                 waits=ew, inc="ea")
                    bank_free[frot_names[ri]] = te
                    if jj == 0:
                        evac_done0 = te
                    else:
                        evac_done = te
                if grp % 2 == 1:
                    w1buf_free[(grp // 2) % 2] = grp_last_mm[-1]
                hq_dst = hqv_par[t % 2][:, grp * 1024:(grp + 1) * 1024]
                efw = []
                if grp == 0:
                    efw = [xa_read_done] if t <= 1 else [w for w in [hq_free[t % 2]] if w]
                tq, _ = quantize(r_t[gbuf][:, :], hq_dst, man=3, signed=False, bufi=gbuf, n=1024,
                                 waits=[evac_done, evac_done0],
                                 extra_first_waits=efw)
                temps_free[gbuf] = tq
                hq_toks.append(tq)
                if grp == 2:
                    if t + 1 < NT:
                        xg = xn_free[1] if t == 0 else ffn1_last_mm[t - 1]
                        ln2_tail[t + 1] = ln_tail(xb_of_t(t + 1), xn[(t + 1) % 2], ln2_rows_d[t + 1],
                                                  xn_guard=[w for w in [xg] if w])
                        if t + 2 < NT:
                            ln2_rows_d[t + 2] = ln_rows(ln_sums(xb_of_t(t + 2),
                                                                waits_extra=[proj_te_t[t + 2]]))
                    elif s_ + 1 < NSEQ:
                        def x1_of(c, Q=QF):
                            return Q[:, c * S: c * S + T]
                        b_sums = ln_sums(x1_of, waits_extra=[x_toks_next[0]])
                        b_rows = ln_rows(b_sums)
                        ln1_boot[0] = ln_tail(x1_of, xn[0], b_rows,
                                              xn_guard=[ffn1_last_mm[2]])
                if t >= 1 and grp % 2 == 1:
                    _, tr = emit_ffn2_chunk(t - 1, grp // 2)
                    xb_last_read = tr
            ffn1_last_mm[t] = grp_last_mm[-1]

        for eo in range(DC):
            mmtok, tr = emit_ffn2_chunk(NT - 1, eo)
            xb_last_read = tr
        hq_read_done = mmtok
        xn_free[0] = ffn1_last_mm[NT - 2]
        xn_free[1] = ffn1_last_mm[NT - 1]
        xb_read_done = xb_last_read

    sch.add("sync", lambda e: e.nop(), waits=[("ou", tok_out_total[0] + tok_out_dbg[0])])

    sch.check()
    with es, pes, nc.Block() as block:
        sch.emit(block, sems)
    return nc


def _host_inputs(core, L0_attn_x, W_qkv, W_proj, b_proj, w1, b1, w2, b2):
    xs = L0_attn_x[core * NSEQ:(core + 1) * NSEQ]
    x_fm = np.ascontiguousarray(xs.transpose(0, 2, 1))
    biases = np.zeros((128, 48), dtype=np.float32)
    biases[:, 0:8] = b_proj.reshape(8, 128).T
    biases[:, 8:40] = b1.reshape(32, 128).T
    biases[:, 40:48] = b2.reshape(8, 128).T
    sel = np.zeros((16, 8 * 128), dtype=np.float32)
    for c in range(8):
        sel[2 * c, c * 128: c * 128 + 64] = 1.0
        sel[2 * c + 1, c * 128 + 64: c * 128 + 128] = 1.0
    cbv = np.zeros((128, 8), dtype=np.float32)
    cbv[:, 0] = -3 * LN2C
    cbv[:, 1] = 3 * LN2C
    cbv[:, 2] = -10 * LN2C
    cbv[:, 3] = 10 * LN2C
    cbv[:, 4] = 2.0 ** -80
    cbv[:, 5] = 2.0 ** -120
    cbv[:, 6] = 1e-5
    return {
        "x": x_fm,
        "wqkvT": np.ascontiguousarray(W_qkv.T),
        "wprojT": np.ascontiguousarray(W_proj.T),
        "w1T": np.ascontiguousarray(w1.T),
        "w2T": np.ascontiguousarray(w2.T),
        "biases": biases,
        "sel": sel,
        "cb": cbv,
    }


_NC_CACHE = {}


def kernel(L0_attn_x, L0_attn_W_qkv, L0_attn_W_proj, L0_attn_b_proj,
           L0_ffn_w1, L0_ffn_b1, L0_ffn_w2, L0_ffn_b2):
    args = [np.asarray(a, dtype=np.float32) for a in
            (L0_attn_x, L0_attn_W_qkv, L0_attn_W_proj, L0_attn_b_proj,
             L0_ffn_w1, L0_ffn_b1, L0_ffn_w2, L0_ffn_b2)]
    if "nc" not in _NC_CACHE:
        _NC_CACHE["nc"] = build_nc()
    nc = _NC_CACHE["nc"]
    in_maps = [_host_inputs(c, *args) for c in range(N_CORES)]
    res = run_bass_kernel_spmd(nc, in_maps, core_ids=list(range(N_CORES)))
    outs = [r["out"].transpose(0, 2, 1) for r in res.results]
    return np.ascontiguousarray(np.concatenate(outs, axis=0))
